# version 02
# baseline (speedup 1.0000x reference)
from contextlib import ExitStack

import numpy as np
import concourse.bass as bass
import concourse.mybir as mybir

F32 = mybir.dt.float32
BF16 = mybir.dt.bfloat16
ALU = mybir.AluOpType
ACTF = mybir.ActivationFunctionType
AX = mybir.AxisListType

D = 1024
DI = 2048
NS = 16
DR = 64
TC = 2048
NT = D // 128
NTI = DI // 128
NCORE = 8
TFULL = TC * NCORE
EPS = 1e-5
CAP = 5120
FFN = 4096
NE = 8
DSH = DI // NCORE


class Ch:

    def __init__(self, es, nc, name, nsem=4):
        self.sems = [es.enter_context(nc.semaphore(f"{name}{i}")) for i in range(nsem)]
        self.tot = [0] * nsem
        self.guard = [0] * nsem
        self.cur = 0
        self.open = False

    def tag(self, eng, instr, k=1):
        i = self.cur
        if not self.open:
            if self.tot[i] > self.guard[i]:
                eng.wait_ge(self.sems[i], self.tot[i])
                self.guard[i] = self.tot[i]
            self.open = True
        instr.then_inc(self.sems[i], k)
        self.tot[i] += k
        return instr

    def mark(self):
        self.cur = (self.cur + 1) % len(self.sems)
        self.open = False
        return list(self.tot)

    def wait(self, eng, mark=None):
        if mark is None:
            mark = self.mark()
        for s, v in zip(self.sems, mark):
            if v > 0:
                eng.wait_ge(s, v)


class CCh:

    def __init__(self, es, nc, name):
        self.sem = es.enter_context(nc.semaphore(name))
        self.n = 0

    def tag(self, instr, k=1):
        instr.then_inc(self.sem, k)
        self.n += k
        return instr

    def mark(self):
        return self.n

    def wait(self, eng, mark=None):
        eng.wait_ge(self.sem, self.n if mark is None else mark)


class MK:
    def __init__(self, dbg=(), stop_after=None):
        self.dbg = set(dbg)
        self.stop_after = stop_after
        self.nc = bass.Bass("TRN2", target_bir_lowering=False)
        self.es = ExitStack()
        self.dram = {}

    def din(self, name, shape, dt=F32):
        self.dram[name] = self.nc.dram_tensor(name, shape, dt, kind="ExternalInput").ap()
        return self.dram[name]

    def dout(self, name, shape, dt=F32):
        self.dram[name] = self.nc.dram_tensor(name, shape, dt, kind="ExternalOutput").ap()
        return self.dram[name]

    def dint(self, name, shape, dt=BF16):
        kind = "ExternalOutput" if name in self.dbg else "Internal"
        self.dram[name] = self.nc.dram_tensor(name, shape, dt, kind=kind).ap()
        return self.dram[name]

    def sb(self, name, shape, dt=BF16, stack=None):
        return (stack or self.es).enter_context(self.nc.sbuf_tensor(name, shape, dt))

    def psum(self, name, shape, dt=F32, stack=None):
        return (stack or self.es).enter_context(self.nc.psum_tensor(name, shape, dt))

    def rk(self, eng, mult, add=0):
        if not hasattr(self, "_rkcache"):
            self._rkcache = {}
        key = (id(eng.engine) if hasattr(eng, "engine") else id(eng), mult, add)
        key = (str(eng.engine), mult, add)
        if key not in self._rkcache:
            rank = eng.partition_id()
            expr = rank * mult + add if add else rank * mult
            self._rkcache[key] = eng.compute_val(expr)
        return self._rkcache[key]


    def build(self):
        nc, es = self.nc, self.es
        self.declare_dram()
        self.b = es.enter_context(nc.Block())
        self.c_ld = Ch(self.es, self.nc, "ld", nsem=4)
        self.c_wst = Ch(self.es, self.nc, "wst", nsem=4)
        self.c_sp = Ch(self.es, self.nc, "sp", nsem=4)
        self.c_coll = Ch(self.es, self.nc, "coll", nsem=2)
        self.c_st = Ch(self.es, self.nc, "st", nsem=2)
        self.c_ldb = Ch(self.es, self.nc, "ldb", nsem=4)
        self.c_mm = CCh(self.es, self.nc, "mm")
        self.c_ev = CCh(self.es, self.nc, "ev")
        self.c_dv = CCh(self.es, self.nc, "dv")
        self.c_gp = CCh(self.es, self.nc, "gp")
        self.alloc_shared()
        self.s0_constants()
        self.s1_ln0()
        self.s2_l0_proj_conv()
        if self.stop_after != "s2":
            self.s3_collectives_l0()
            self.s4_l0_ffn()
            self.s5_scan()
            if self.stop_after != "s5":
                self.s6_gate_outproj()
                if self.stop_after != "s6":
                    self.s7_ln1()
                    self.s8_kvr()
                    if self.stop_after != "s8":
                        self.s9_rwkv()
                        if self.stop_after != "s9":
                            self.s10_wo()
                            self.s11_l1_ffn()
                            if self.stop_after != "s11":
                                self.s12_final_sum()
        self.s_final()
        es.close()
        return nc

    def alloc_shared(self):
        self.s0_stack = ExitStack()
        self.ln_stack = ExitStack()
        self.ps_stat = self.psum("ps_stat", [1, 1024], stack=self.ln_stack)
        self.ps_bc = self.psum("ps_bc", [128, 1536], stack=self.ln_stack)
        self.tmpc1 = self.sb("tmpc1", [128, 128], F32)
        self.tmpc2 = self.sb("tmpc2", [128, 128], F32)
        self.acol_stage = self.sb("acol_stage", [32, 128], F32)
        self.ps_misc = self.psum("ps_misc", [128, 512], F32, stack=self.s0_stack)
        self.identf = self.sb("identf", [128, 128], F32)
        self.sel127 = self.sb("sel127", [128, 1], F32)
        self.sqt = self.sb("sqt", [128, 512])
        self.lane = self.sb("lane", [1, 2560], F32)
        self.tnm = self.sb("tnm", [128, 512])
        self.bcds = self.sb("bcds", [96, TC])
        self.facc = [self.sb(f"facc{i}", [128, TC]) for i in range(NT)]
        self.lnld0 = self.sb("lnld0", [128, 512], F32)
        self.lnld1 = self.sb("lnld1", [128, 512], F32)
        self.lnldb0 = self.sb("lnldb0", [128, 512])
        self.lnldb1 = self.sb("lnldb1", [128, 512])

    def declare_dram(self):
        din, dout, dint = self.din, self.dout, self.dint
        din("x_t", [D, TC + 3])
        din("w_inproj", [D, 2 * DI])
        din("rms_w", [D, 1])
        din("conv_w", [DI, 4])
        din("conv_b", [DI, 1])
        din("bcd_w", [DI, 96])
        din("dup_w", [DR, DI])
        din("dup_b", [DI, 1])
        din("lma", [DI, NS])
        din("d_param", [DI, 1])
        din("w_outproj", [DI, D])
        din("r0_wT", [D, NE])
        din("r0_b", [1, NE])
        din("exp0_wT", [NE, D, D])
        din("exp0_b", [NE, D, 1])
        din("wkT", [D, D])
        din("wvT", [D, D])
        din("wrT", [D, D])
        din("woT", [D, D])
        din("tdec", [D, 1])
        din("tfirst", [D, 1])
        din("r1_wT", [D, NE])
        din("w1", [NE, D, FFN])
        din("w2", [NE, FFN, D])
        din("pmask", [NE, 1])
        dout("out", [TC, D])
        dint("xc_dram", [DI, TC])
        dint("res_dram", [DI, TC])
        dint("bcd_dram", [96, TC])
        dint("xc_all", [NCORE * DI, TC])
        dint("bcd_all", [NCORE * 96, TC])
        dint("delta_dram", [DSH, TFULL])
        dint("dx_dram", [DSH, TFULL])
        dint("y0_dram", [DSH, TFULL])
        dint("y0_all", [DI, TFULL])
        dint("res_all", [NCORE * DI, TC])
        dint("xc_my", [DSH, TFULL])
        dint("res_my", [DSH, TFULL])
        dint("k_my", [128, TFULL])
        dint("v_my", [128, TFULL])
        dint("r_my", [128, TFULL])
        dint("partial0_dram", [NCORE * D, TC])
        dint("rs0_dram", [D, TC])
        dint("partial1_dram", [NCORE * D, TC])
        dint("rs1_dram", [D, TC])
        dint("x1_dram", [D, TC])
        dint("wsel_dram", [NE, TC])
        dint("kvr_dram", [3 * D, TC])
        dint("kvr_all", [NCORE * 3 * D, TC])
        dint("y1_dram", [128, TFULL])
        dint("y1_all", [NCORE * 128, TFULL])
        dint("g1_dram", [NE, TC])
        dint("cnt_dram", [1, 2 * NE], F32)
        dint("cnt_all", [NCORE, 2 * NE], F32)
        dint("hdbg", [128, 4096])
        dint("r9dbg", [128, 512])
        dint("k9dbg", [128, 512])
        dint("kh9dbg", [128, 128])
        dint("khT9dbg", [128, 128])
        dint("vT9dbg", [128, 128])
        dint("abdbg", [128, 4096], F32)
        dint("bxdbg", [128, 4096])
        dint("lgdbg", [8, TC], F32)
        dint("xn_dram", [D, TC + 3])
        dint("xn1_dram", [D, TC])
        dint("xr_dram", [D, TC + 3])

    def s0_constants(self):
        b, nc, d = self.b, self.nc, self.dram
        c_ld, c_gp, c_dv, c_ev, c_mm = self.c_ld, self.c_gp, self.c_dv, self.c_ev, self.c_mm
        self.ident = self.sb("ident", [128, 128])
        self.utri = self.sb("utri", [128, 128])
        self.io_col = self.sb("io_col", [128, 128], F32)
        self.io_row = self.sb("io_row", [128, 128], F32)
        self.io_m16 = self.sb("io_m16", [128, 128], F32)
        self.io_m32 = self.sb("io_m32", [128, 128], F32)
        self.sum16 = self.sb("sum16", [128, 16 * 128])
        self.ones_col = self.sb("ones_col", [128, 1])
        self.ones_row = self.sb("ones_row", [1, 128])
        self.onesf_col = self.sb("onesf_col", [128, 1], F32)
        self.onesf_row = self.sb("onesf_row", [1, 128], F32)
        self.mk_diag = self.sb("mk_diag", [128, 128])
        self.mk_off = self.sb("mk_off", [128, 128])
        sm = {}
        sm["rms_w"] = self.sb("sm_rms_w", [128, NT], F32)
        sm["conv_b"] = self.sb("sm_conv_b", [128, NTI], F32)
        sm["d_param"] = self.sb("sm_d_param", [128, NTI], F32)
        sm["convw"] = self.sb("sm_convw", [128, 4 * NTI], F32)
        sm["dup_b"] = self.sb("sm_dup_b", [128, 2], F32)
        sm["d_param_my"] = self.sb("sm_d_param_my", [128, 2], F32)
        sm["exp0_b"] = self.sb("sm_exp0_b", [128, NE * NT], F32)
        sm["tdec"] = self.sb("sm_tdec", [128, 1], F32)
        sm["tfirst"] = self.sb("sm_tfirst", [128, 1], F32)
        sm["pmask"] = self.sb("sm_pmask", [NE, 1], F32)
        sm["acol"] = self.sb("sm_acol", [128, 32], F32)
        self.sm = sm

        @b.sync
        def _(sy):
            for j in range(NT):
                c_ld.tag(sy, sy.dma_start(out=sm["rms_w"][:, j:j + 1], in_=d["rms_w"][128 * j:128 * (j + 1), :]), 16)
            for j in range(NTI):
                c_ld.tag(sy, sy.dma_start(out=sm["conv_b"][:, j:j + 1], in_=d["conv_b"][128 * j:128 * (j + 1), :]), 16)
                c_ld.tag(sy, sy.dma_start(out=sm["d_param"][:, j:j + 1], in_=d["d_param"][128 * j:128 * (j + 1), :]), 16)
                c_ld.tag(sy, sy.dma_start(out=sm["convw"][:, 4 * j:4 * j + 4], in_=d["conv_w"][128 * j:128 * (j + 1), :]), 16)
            for e in range(NE):
                for i in range(NT):
                    c_ld.tag(sy, sy.dma_start(out=sm["exp0_b"][:, NT * e + i:NT * e + i + 1],
                                          in_=d["exp0_b"][e, 128 * i:128 * (i + 1), :]), 16)
            c_ld.tag(sy, sy.dma_start(out=sm["pmask"][:], in_=d["pmask"]), 16)
            off = self.rk(sy, DSH)
            c_ld.tag(sy, sy.dma_start(out=sm["dup_b"][:],
                                      in_=d["dup_b"][bass.ds(off, 256), :].rearrange("(a p) o -> p (a o)", a=2),
                                      allow_slow_non_contiguous=True), 16)
            c_ld.tag(sy, sy.dma_start(out=sm["d_param_my"][:],
                                      in_=d["d_param"][bass.ds(off, 256), :].rearrange("(a p) o -> p (a o)", a=2),
                                      allow_slow_non_contiguous=True), 16)
            c_ld.tag(sy, sy.dma_start(out=self.acol_stage[0:32, :],
                                      in_=d["lma"][bass.ds(off, 256), :].rearrange("(q g) n -> q (g n)", g=8)), 16)
            off3 = self.rk(sy, 128)
            c_ld.tag(sy, sy.dma_start(out=sm["tdec"][:], in_=d["tdec"][bass.ds(off3, 128), :]), 16)
            c_ld.tag(sy, sy.dma_start(out=sm["tfirst"][:], in_=d["tfirst"][bass.ds(off3, 128), :]), 16)

        @b.gpsimd
        def _(g):
            g.iota(self.io_col[:], pattern=[[1, 128]], base=0, channel_multiplier=0,
                   allow_small_or_imprecise_dtypes=True)
            g.iota(self.io_m16[:], pattern=[[0, 8], [1, 16]], base=0, channel_multiplier=0,
                   allow_small_or_imprecise_dtypes=True)
            g.iota(self.io_m32[:], pattern=[[0, 4], [1, 32]], base=0, channel_multiplier=0,
                   allow_small_or_imprecise_dtypes=True)
            c_gp.tag(g.iota(self.io_row[:], pattern=[[0, 128]], base=0, channel_multiplier=1,
                            allow_small_or_imprecise_dtypes=True))

        @b.vector
        def _(v):
            c_gp.wait(v)
            v.tensor_tensor(self.ident[:], self.io_col[:], self.io_row[:], ALU.is_equal)
            v.tensor_tensor(self.identf[:], self.io_col[:], self.io_row[:], ALU.is_equal)
            v.tensor_scalar(self.sel127[:], self.io_row[:, 0:1], 126.5, None, ALU.is_ge)
            v.tensor_tensor(self.utri[:], self.io_row[:], self.io_col[:], ALU.is_le)
            v.memset(self.ones_col[:], 1.0)
            v.memset(self.ones_row[:], 1.0)
            v.memset(self.onesf_col[:], 1.0)
            v.memset(self.onesf_row[:], 1.0)
            v.memset(self.sum16[:], 0.0)
            v.memset(self.mk_diag[:], 0.0)
            v.memset(self.mk_off[:], 0.0)
            t1, t2 = self.tmpc1, self.tmpc2
            v.drain()
            v.tensor_scalar(t1[:], self.io_col[:], 16.0, None, ALU.mult)
            v.drain()
            v.tensor_tensor(t2[:], self.io_row[:], t1[:], ALU.is_ge)
            v.drain()
            v.tensor_scalar(t1[:], t1[:], 16.0, None, ALU.add)
            v.drain()
            v.tensor_tensor(t1[:], self.io_row[:], t1[:], ALU.is_lt)
            v.drain()
            v.tensor_tensor(t2[:], t2[:], t1[:], ALU.mult)
            v.drain()
            for j in range(16):
                v.tensor_copy(self.sum16[:, 128 * j + 8 * j:128 * j + 8 * j + 8], t2[:, 0:8])
            v.drain()
            v.tensor_tensor(t1[:], self.io_col[:], self.io_row[:], ALU.subtract)
            v.drain()
            v.tensor_tensor(t2[:], t1[:], self.io_m16[:], ALU.is_le)
            v.drain()
            v.scalar_tensor_tensor(self.mk_diag[:], t1[:], 0.0, t2[:], ALU.is_ge, ALU.mult)
            v.drain()
            v.tensor_tensor(t2[:], t1[:], self.io_m32[:], ALU.subtract)
            v.drain()
            v.tensor_scalar(self.mk_off[:], t2[:], -15.0, 0.0, ALU.is_ge, op1=ALU.bypass) if False else None
            v.tensor_scalar(t1[:], t2[:], -15.0, None, ALU.is_ge)
            v.drain()
            v.tensor_scalar(t2[:], t2[:], 0.0, None, ALU.is_le)
            v.drain()
            v.tensor_tensor(t1[:], t1[:], t2[:], ALU.mult)
            v.drain()
            v.tensor_scalar(t2[:], self.io_m32[:], 16.0, None, ALU.is_ge)
            v.drain()
            v.tensor_tensor(self.mk_off[:], t1[:], t2[:], ALU.mult)
            v.drain()
            c_dv.tag(v.nop())

        @b.scalar
        def _(sc):
            c_ld.wait(sc)
            c_ev.tag(sc.activation(self.acol_stage[0:32, :], self.acol_stage[0:32, :], ACTF.Exp))

        @b.tensor
        def _(te):
            c_ev.wait(te)
            c_dv.wait(te)
            c_mm.tag(te.transpose(self.ps_misc[:, 0:32], self.acol_stage[0:32, :], self.identf[0:32, 0:32]))

        @b.vector
        def _(v):
            c_mm.wait(v)
            c_dv.tag(v.tensor_scalar(sm["acol"][:], self.ps_misc[:, 0:32], -1.0, None, ALU.mult))

    def s1_ln0(self):
        b, d = self.b, self.dram
        self.s1_stack = ExitStack()
        self.xn = [self.sb(f"xn{i}", [128, TC + 3]) for i in range(NT)]
        self.xr = [self.sb(f"xr{i}", [128, TC + 3], stack=self.s1_stack) for i in range(NT)]

        self.emit_ln(src=None, out_xn=self.xn, out_xr=self.xr, width=TC + 3,
                     rms_cols=self.sm["rms_w"], rms_col_idx=list(range(NT)),
                     src_dram=d["x_t"])

        if "xn_dram" in self.dbg:
            @b.sync
            def _(sy):
                self.c_dv.wait(sy)
                for i in range(NT):
                    self.c_sp.tag(sy, sy.dma_start(out=d["xn_dram"][128 * i:128 * (i + 1), :], in_=self.xn[i][:]), 16)
                    self.c_sp.tag(sy, sy.dma_start(out=d["xr_dram"][128 * i:128 * (i + 1), :], in_=self.xr[i][:]), 16)

    def emit_ln(self, src, out_xn, out_xr, width, rms_cols, rms_col_idx, src_f32=True,
                src_dram=None, dram_wait=None):
        b = self.b
        c_ld, c_mm, c_ev, c_dv = self.c_ld, self.c_mm, self.c_ev, self.c_dv
        sq, lane = self.sqt, self.lane
        ps_stat, ps_bc = self.ps_stat, self.ps_bc
        nchunk = (width + 511) // 512
        use_dram = src_dram is not None
        if use_dram:
            lnld = [self.lnld0, self.lnld1] if src_f32 else [self.lnldb0, self.lnldb1]

        def srcap(i, o, W, phase):
            if not use_dram:
                return src[i][:, o:o + W]
            return lnld[(i + phase) % 2][:, 0:W]

        for ci in range(nchunk):
            o = 512 * ci
            W = min(512, width - o)
            for i in range(NT):
                if use_dram:
                    @b.sync
                    def _(sy, i=i, o=o, W=W):
                        c_mm.wait(sy)
                        if dram_wait is not None:
                            dram_wait[0].wait(sy, mark=dram_wait[1])
                        c_ld.tag(sy, sy.dma_start(out=lnld[i % 2][:, 0:W],
                                                  in_=src_dram[128 * i:128 * (i + 1), o:o + W]), 16)
                @b.scalar
                def _(sc, i=i, o=o, W=W):
                    c_ld.wait(sc)
                    c_mm.wait(sc)
                    c_dv.wait(sc)
                    c_ev.tag(sc.activation(sq[:, 0:W], srcap(i, o, W, 0), ACTF.Square))

                @b.tensor
                def _(te, i=i, o=o, W=W):
                    c_ev.wait(te)
                    te.matmul(ps_stat[0:1, 0:W], self.onesf_col[:] if src_f32 else self.ones_col[:],
                              srcap(i, o, W, 0),
                              start=(i == 0), stop=(i == NT - 1))
                    c_mm.tag(te.matmul(ps_stat[0:1, 512:512 + W], self.ones_col[:], sq[:, 0:W],
                                       start=(i == 0), stop=(i == NT - 1)))

            @b.vector
            def _(v, o=o, W=W):
                c_mm.wait(v)
                v.tensor_scalar(lane[0:1, 0:W], ps_stat[0:1, 0:W], 1.0 / D, None, ALU.mult)
                v.tensor_scalar(lane[0:1, 512:512 + W], ps_stat[0:1, 512:512 + W], 1.0 / D, None, ALU.mult)
                v.drain()
                v.tensor_tensor(lane[0:1, 1024:1024 + W], lane[0:1, 0:W], lane[0:1, 0:W], ALU.mult)
                v.drain()
                v.tensor_tensor(lane[0:1, 1024:1024 + W], lane[0:1, 512:512 + W], lane[0:1, 1024:1024 + W], ALU.subtract)
                v.drain()
                c_dv.tag(v.tensor_scalar(lane[0:1, 1536:1536 + W], lane[0:1, 1024:1024 + W], EPS, None, ALU.add))

            @b.scalar
            def _(sc, o=o, W=W):
                c_dv.wait(sc)
                c_ev.tag(sc.activation(lane[0:1, 2048:2048 + W], lane[0:1, 1536:1536 + W], ACTF.Sqrt))

            @b.vector
            def _(v, o=o, W=W):
                c_ev.wait(v)
                v.reciprocal(lane[0:1, 512:512 + W], lane[0:1, 2048:2048 + W])
                v.drain()
                if out_xr is not None:
                    v.tensor_tensor(lane[0:1, 1536:1536 + W], lane[0:1, 1024:1024 + W], lane[0:1, 512:512 + W], ALU.mult)
                    v.drain()
                    v.tensor_tensor(lane[0:1, 1536:1536 + W], lane[0:1, 1536:1536 + W], lane[0:1, 512:512 + W], ALU.mult)
                    v.drain()
                    c_dv.tag(v.tensor_scalar(lane[0:1, 1536:1536 + W], lane[0:1, 1536:1536 + W], EPS, None, ALU.add))
                else:
                    c_dv.tag(v.nop())

            if out_xr is not None:
                @b.scalar
                def _(sc, o=o, W=W):
                    c_dv.wait(sc)
                    c_ev.tag(sc.activation(lane[0:1, 2048:2048 + W], lane[0:1, 1536:1536 + W], ACTF.Sqrt))

                @b.vector
                def _(v, o=o, W=W):
                    c_ev.wait(v)
                    v.reciprocal(lane[0:1, 1536:1536 + W], lane[0:1, 2048:2048 + W])
                    v.drain()
                    c_dv.tag(v.tensor_tensor(lane[0:1, 1536:1536 + W], lane[0:1, 1536:1536 + W], lane[0:1, 512:512 + W], ALU.mult))

            @b.tensor
            def _(te, o=o, W=W):
                c_dv.wait(te)
                c_mm.tag(te.matmul(ps_bc[:, 0:W], self.onesf_row[:], lane[0:1, 0:W], start=True, stop=True))
                c_mm.tag(te.matmul(ps_bc[:, 512:512 + W], self.onesf_row[:], lane[0:1, 512:512 + W], start=True, stop=True))
                if out_xr is not None:
                    c_mm.tag(te.matmul(ps_bc[:, 1024:1024 + W], self.onesf_row[:], lane[0:1, 1536:1536 + W], start=True, stop=True))

            for i in range(NT):
                if use_dram:
                    @b.sync
                    def _(sy, i=i, o=o, W=W):
                        c_dv.wait(sy)
                        if dram_wait is not None:
                            dram_wait[0].wait(sy, mark=dram_wait[1])
                        c_ld.tag(sy, sy.dma_start(out=lnld[i % 2][:, 0:W],
                                                  in_=src_dram[128 * i:128 * (i + 1), o:o + W]), 16)

                @b.vector
                def _(v, i=i, o=o, W=W):
                    c_mm.wait(v)
                    if use_dram:
                        c_ld.wait(v)
                    v.tensor_tensor(self.tnm[:, 0:W], srcap(i, o, W, 0), ps_bc[:, 0:W], ALU.subtract)
                    v.drain()
                    v.tensor_tensor(out_xn[i][:, o:o + W], self.tnm[:, 0:W], ps_bc[:, 512:512 + W], ALU.mult)
                    if out_xr is not None:
                        v.scalar_tensor_tensor(out_xr[i][:, o:o + W], self.tnm[:, 0:W],
                                               rms_cols[:, rms_col_idx[i]:rms_col_idx[i] + 1],
                                               ps_bc[:, 1024:1024 + W], ALU.mult, ALU.mult)
                    v.drain()
                    c_dv.tag(v.nop())

    def s2_l0_proj_conv(self):
        b, d = self.b, self.dram
        c_wst, c_mm, c_ev, c_dv, c_sp = self.c_wst, self.c_mm, self.c_ev, self.c_dv, self.c_sp
        self.s0_stack.close()
        self.ln_stack.close()
        s2 = ExitStack()
        wslab = [self.sb(f"wslab{k}", [128, D], stack=s2) for k in range(NT)]
        bcdw = self.sb("bcdw", [128, 96 * NTI], stack=s2)
        xs = self.sb("xs", [128, TC + 3], stack=s2)
        xs2 = self.sb("xs2", [128, TC + 3], stack=s2)
        cvt = self.sb("cvt", [128, TC], stack=s2)
        xct = self.sb("xct", [128, TC], stack=s2)
        xct2 = self.sb("xct2", [128, TC], stack=s2)
        rst = self.sb("rst", [128, TC], stack=s2)
        ps_p = self.psum("ps_p", [128, 1024], F32, stack=s2)
        ps_bcd = self.psum("ps_bcd", [96, TC], F32, stack=s2)

        @b.gpsimd
        def _(g):
            self.c_dv.wait(g)
            self.c_ev.wait(g)
            self.c_mm.wait(g)
            for j in range(NTI):
                c_wst.tag(g, g.dma_start(out=bcdw[:, 96 * j:96 * (j + 1)], in_=d["bcd_w"][128 * j:128 * (j + 1), :]), 16)

        CH = [(0, 512), (512, 512), (1024, 512), (1536, 512), (2048, 3)]
        xs_bufs = [xs, xs2]
        xc_bufs = [xct, xct2]
        ln_dv = self.c_dv.n
        for q in range(2):
            @b.gpsimd
            def _(g, q=q):
                self.c_dv.wait(g)
                if q > 0:
                    self.c_mm.wait(g)
                for k in range(NT):
                    c_wst.tag(g, g.dma_start(out=wslab[k][:], in_=d["w_inproj"][128 * k:128 * (k + 1), D * q:D * (q + 1)]), 16)
            wq_mark = c_wst.mark()

            for jj in range(NT):
                j = NT * q + jj
                xsb = xs_bufs[j % 2]
                xcb = xc_bufs[j % 2]
                for ci, (o, W) in enumerate(CH):
                    @b.tensor
                    def _(te, j=j, jj=jj, o=o, W=W, ci=ci, xsb=xsb, wq_mark=wq_mark):
                        c_wst.wait(te, mark=wq_mark)
                        self.c_dv.wait(te, mark=ln_dv)
                        c_ev.wait(te)
                        for k in range(NT):
                            ins = te.matmul(ps_p[:, 512 * (ci % 2):512 * (ci % 2) + W],
                                            wslab[k][:, 128 * jj:128 * (jj + 1)],
                                            self.xr[k][:, o:o + W],
                                            start=(k == 0), stop=(k == NT - 1))
                        c_mm.tag(ins)

                    @b.scalar
                    def _(sc, j=j, o=o, W=W, ci=ci, xsb=xsb):
                        c_mm.wait(sc)
                        c_dv.wait(sc)
                        c_ev.tag(sc.copy(out=xsb[:, o:o + W], in_=ps_p[:, 512 * (ci % 2):512 * (ci % 2) + W]))

                @b.vector
                def _(v, j=j, xsb=xsb):
                    c_ev.wait(v)
                    cw = self.sm["convw"]
                    v.tensor_scalar(cvt[:], xsb[:, 0:TC], cw[:, 4 * j:4 * j + 1], None, ALU.mult)
                    v.drain()
                    for k in range(1, 4):
                        v.scalar_tensor_tensor(cvt[:], xsb[:, k:k + TC], cw[:, 4 * j + k:4 * j + k + 1],
                                               cvt[:], ALU.mult, ALU.add)
                        v.drain()
                    c_dv.tag(v.nop())

                @b.scalar
                def _(sc, j=j, xcb=xcb):
                    c_dv.wait(sc)
                    c_sp.wait(sc)
                    self.c_mm.wait(sc)
                    c_ev.tag(sc.activation(xcb[:], cvt[:], ACTF.Silu, bias=self.sm["conv_b"][:, j:j + 1]))

                @b.sync
                def _(sy, j=j, xcb=xcb):
                    c_ev.wait(sy)
                    c_sp.tag(sy, sy.dma_start(out=d["xc_dram"][128 * j:128 * (j + 1), :], in_=xcb[:]), 16)

                for ci in range(4):
                    @b.tensor
                    def _(te, j=j, ci=ci, xcb=xcb):
                        c_ev.wait(te)
                        ins = te.matmul(ps_bcd[:, 512 * ci:512 * (ci + 1)], bcdw[:, 96 * j:96 * (j + 1)],
                                        xcb[:, 512 * ci:512 * (ci + 1)], start=(j == 0), stop=(j == NTI - 1))
                        if j == NTI - 1:
                            c_mm.tag(ins)

        @b.scalar
        def _(sc):
            c_mm.wait(sc)
            c_ev.tag(sc.copy(out=self.bcds[0:96, :], in_=ps_bcd[0:96, :]))

        @b.sync
        def _(sy):
            c_ev.wait(sy)
            c_sp.tag(sy, sy.dma_start(out=d["bcd_dram"], in_=self.bcds[0:96, :]), 16)

        for q in range(2):
            @b.gpsimd
            def _(g, q=q):
                self.c_mm.wait(g)
                for k in range(NT):
                    c_wst.tag(g, g.dma_start(out=wslab[k][:], in_=d["w_inproj"][128 * k:128 * (k + 1), DI + D * q:DI + D * (q + 1)]), 16)
            wq_mark = c_wst.mark()

            for jj in range(NT):
                j = NT * q + jj
                for ci in range(4):
                    o = 512 * ci

                    @b.tensor
                    def _(te, j=j, jj=jj, o=o, ci=ci, wq_mark=wq_mark):
                        c_wst.wait(te, mark=wq_mark)
                        c_ev.wait(te)
                        for k in range(NT):
                            ins = te.matmul(ps_p[:, 512 * (ci % 2):512 * (ci % 2) + 512],
                                            wslab[k][:, 128 * jj:128 * (jj + 1)],
                                            self.xr[k][:, 3 + o:3 + o + 512],
                                            start=(k == 0), stop=(k == NT - 1))
                        c_mm.tag(ins)

                    @b.scalar
                    def _(sc, j=j, o=o, ci=ci):
                        c_mm.wait(sc)
                        c_sp.wait(sc)
                        c_ev.tag(sc.activation(rst[:, o:o + 512], ps_p[:, 512 * (ci % 2):512 * (ci % 2) + 512], ACTF.Silu))

                @b.sync
                def _(sy, j=j):
                    c_ev.wait(sy)
                    c_sp.tag(sy, sy.dma_start(out=d["res_dram"][128 * j:128 * (j + 1), :], in_=rst[:]), 16)

        @b.sync
        def _(sy):
            c_sp.wait(sy)
            sy.nop()

        s2.close()
        self.s1_stack.close()

    def s3_collectives_l0(self):
        b, d = self.b, self.dram
        spm = self.c_sp.mark()

        @b.gpsimd
        def _(g):
            self.c_sp.wait(g, mark=spm)
            self.c_coll.tag(g, g.collective_compute(
                "AllGather", ALU.bypass, ins=[d["xc_dram"]], outs=[d["xc_all"]],
                replica_groups=[list(range(NCORE))]), 1)
            self.c_coll.tag(g, g.collective_compute(
                "AllGather", ALU.bypass, ins=[d["bcd_dram"]], outs=[d["bcd_all"]],
                replica_groups=[list(range(NCORE))]), 1)
        self.ag_l0_mark = self.c_coll.mark()

    def s4_l0_ffn(self):
        b, d = self.b, self.dram
        c_wst, c_mm, c_ev, c_dv, c_ld, c_sp = self.c_wst, self.c_mm, self.c_ev, self.c_dv, self.c_ld, self.c_sp
        s4 = ExitStack()
        r0t = self.sb("r0t", [128, 64], stack=s4)
        brep = self.sb("brep", [128, NE], F32, stack=s4)
        lg = self.sb("lg", [8, TC], F32, stack=s4)
        lgt = self.sb("lgt", [128, 128], F32, stack=s4)
        mx = self.sb("mx_s4", [128, 16], F32, stack=s4)
        smv = self.sb("smv_s4", [128, 16], F32, stack=s4)
        ohh = self.sb("ohh_s4", [128, 128], F32, stack=s4)
        wsel = self.sb("wsel_s4", [128, 128], F32, stack=s4)
        wst_sb = self.sb("wst_sb", [8, TC], stack=s4)
        grep = self.sb("grep", [128, TC], stack=s4)
        wsl = [self.sb(f"e0w{i}", [128, 2 * D], stack=s4) for i in range(NT)]
        tmp_e = self.sb("tmp_e", [128, 512], stack=s4)
        ps_l = self.psum("ps_l", [8, 512], F32, stack=s4)
        ps_t = self.psum("ps_t", [128, 512], F32, stack=s4)
        ps_f = self.psum("ps_f", [128, 1024], F32, stack=s4)

        @b.gpsimd
        def _(g):
            for k in range(NT):
                c_wst.tag(g, g.dma_start(out=r0t[:, 8 * k:8 * k + 8], in_=d["r0_wT"][128 * k:128 * (k + 1), :]), 16)
            c_wst.tag(g, g.dma_start(out=brep[:], in_=d["r0_b"].broadcast_to([128, NE])), 16)
        r0_mark = c_wst.mark()

        for ci in range(4):
            @b.tensor
            def _(te, ci=ci):
                c_wst.wait(te, mark=r0_mark)
                c_ev.wait(te)
                for k in range(NT):
                    ins = te.matmul(ps_l[:, 0:512], r0t[:, 8 * k:8 * k + 8],
                                    self.xn[k][:, 3 + 512 * ci:3 + 512 * (ci + 1)],
                                    start=(k == 0), stop=(k == NT - 1))
                c_mm.tag(ins)

            @b.scalar
            def _(sc, ci=ci):
                c_mm.wait(sc)
                c_ev.tag(sc.copy(out=lg[:, 512 * ci:512 * (ci + 1)], in_=ps_l[:, 0:512]))

        if "lgdbg" in self.dbg:
            @b.sync
            def _(sy):
                c_ev.wait(sy)
                c_sp.tag(sy, sy.dma_start(out=d["lgdbg"], in_=lg[0:8, :]), 16)

        for i in range(16):
            @b.tensor
            def _(te, i=i):
                c_ev.wait(te)
                c_dv.wait(te)
                c_mm.tag(te.transpose(ps_t[0:128, 0:8], lg[0:8, 128 * i:128 * (i + 1)], self.identf[0:8, 0:8]))

            @b.vector
            def _(v, i=i):
                c_mm.wait(v)
                c_dv.tag(v.tensor_tensor(lgt[:, 8 * i:8 * i + 8], ps_t[:, 0:8], brep[:], ALU.add))

        from bass_rust import AP as _AP

        @b.vector
        def _(v):
            c_dv.wait(v)
            lgt3 = _AP(lgt[:].tensor, lgt[:].offset, [[128, 128], [8, 16], [1, 8]])
            mxb = _AP(mx[:].tensor, mx[:].offset, [[16, 128], [1, 16], [0, 8]])
            v.tensor_reduce(mx[:], lgt3, axis=AX.X, op=ALU.max)
            v.drain()
            v.tensor_tensor(ohh[:], lgt3, mxb, ALU.subtract)
            v.drain()
            c_dv.tag(v.nop())

        @b.scalar
        def _(sc):
            c_dv.wait(sc)
            c_ev.tag(sc.activation(wsel[:], ohh[:], ACTF.Exp))

        @b.vector
        def _(v):
            c_ev.wait(v)
            lgt3 = _AP(lgt[:].tensor, lgt[:].offset, [[128, 128], [8, 16], [1, 8]])
            mxb = _AP(mx[:].tensor, mx[:].offset, [[16, 128], [1, 16], [0, 8]])
            smb = _AP(smv[:].tensor, smv[:].offset, [[16, 128], [1, 16], [0, 8]])
            ws3 = _AP(wsel[:].tensor, wsel[:].offset, [[128, 128], [8, 16], [1, 8]])
            v.tensor_reduce(smv[:], ws3, axis=AX.X, op=ALU.add)
            v.drain()
            v.reciprocal(smv[:], smv[:])
            v.drain()
            v.tensor_tensor(ohh[:], lgt3, mxb, ALU.is_ge)
            v.drain()
            v.tensor_tensor(wsel[:], ohh[:], smb, ALU.mult)
            v.drain()
            c_dv.tag(v.nop())

        for i in range(16):
            @b.tensor
            def _(te, i=i):
                c_dv.wait(te)
                c_ev.wait(te)
                c_mm.tag(te.transpose(ps_t[0:8, 0:128], wsel[:, 8 * i:8 * i + 8], self.identf[:]))

            @b.scalar
            def _(sc, i=i):
                c_mm.wait(sc)
                c_ev.tag(sc.copy(out=wst_sb[0:8, 128 * i:128 * (i + 1)], in_=ps_t[0:8, 0:128]))

        @b.sync
        def _(sy):
            c_ev.wait(sy)
            c_sp.tag(sy, sy.dma_start(out=d["wsel_dram"], in_=wst_sb[0:8, :]), 16)
        wsel_mark = c_sp.mark()

        for e in range(NE):
            half = e % 2

            @b.gpsimd
            def _(g, e=e, half=half):
                if e >= 2:
                    c_mm.wait(g)
                for k in range(NT):
                    c_wst.tag(g, g.dma_start(out=wsl[k][:, D * half:D * (half + 1)],
                                             in_=d["exp0_wT"][e, 128 * k:128 * (k + 1), :]), 16)
            w_mark = c_wst.mark()

            @b.sync
            def _(sy, e=e):
                c_sp.wait(sy, mark=wsel_mark)
                c_dv.wait(sy)
                c_ld.tag(sy, sy.dma_start(out=grep[:],
                                          in_=d["wsel_dram"][e:e + 1, :].broadcast_to([128, TC])), 16)
            g_mark = c_ld.mark()

            for j in range(NT):
                for ci in range(4):
                    @b.tensor
                    def _(te, e=e, j=j, ci=ci, half=half):
                        c_wst.wait(te, mark=w_mark)
                        c_dv.wait(te)
                        for k in range(NT):
                            ins = te.matmul(ps_f[:, 512 * (ci % 2):512 * (ci % 2) + 512],
                                            wsl[k][:, D * half + 128 * j:D * half + 128 * (j + 1)],
                                            self.xn[k][:, 3 + 512 * ci:3 + 512 * (ci + 1)],
                                            start=(k == 0), stop=(k == NT - 1))
                        c_mm.tag(ins)

                    @b.scalar
                    def _(sc, e=e, j=j, ci=ci):
                        c_mm.wait(sc)
                        c_ev.tag(sc.activation(tmp_e[:], ps_f[:, 512 * (ci % 2):512 * (ci % 2) + 512],
                                               ACTF.Identity, bias=self.sm["exp0_b"][:, NT * e + j:NT * e + j + 1]))

                    @b.vector
                    def _(v, e=e, j=j, ci=ci):
                        c_ev.wait(v)
                        c_ld.wait(v, mark=g_mark)
                        if e == 0:
                            v.tensor_tensor(self.facc[j][:, 512 * ci:512 * (ci + 1)], tmp_e[:],
                                            grep[:, 512 * ci:512 * (ci + 1)], ALU.mult)
                        else:
                            v.tensor_tensor(tmp_e[:], tmp_e[:], grep[:, 512 * ci:512 * (ci + 1)], ALU.mult)
                            v.drain()
                            v.tensor_tensor(self.facc[j][:, 512 * ci:512 * (ci + 1)],
                                            self.facc[j][:, 512 * ci:512 * (ci + 1)], tmp_e[:], ALU.add)
                        v.drain()
                        c_dv.tag(v.nop())
        s4.close()

    def s5_scan(self):
        b, d = self.b, self.dram
        c_wst, c_mm, c_ev, c_dv, c_ld, c_sp = self.c_wst, self.c_mm, self.c_ev, self.c_dv, self.c_ld, self.c_sp
        s5 = ExitStack()
        dupw = self.sb("dupw", [64, DSH], stack=s5)
        dlr = self.sb("dlr", [64, TC], stack=s5)
        xcd = self.sb("xcd", [128, 2 * TC], stack=s5)
        dsb = self.sb("dsb", [128, TC], stack=s5)
        spz = self.sb("spz", [128, 512], F32, stack=s5)
        spc = self.sb("spc", [128, 512], F32, stack=s5)
        dxsb = self.sb("dxsb", [128, TC], stack=s5)
        ps_d = self.psum("ps_d", [128, 1024], F32, stack=s5)

        @b.gpsimd
        def _(g):
            off = g.compute_val(g.partition_id() * DSH)
            c_wst.tag(g, g.dma_start(out=dupw[:], in_=d["dup_w"][:, bass.ds(off, DSH)]), 16)
        dupw_mark = c_wst.mark()

        @b.scalar
        def _(sc2):
            off = self.rk(sc2, DSH)
            self.c_coll.wait(sc2, mark=self.ag_l0_mark)
            xa = d["xc_all"].rearrange("(r n) t -> n r t", n=DI)
            self.c_ldb.tag(sc2, sc2.dma_start(
                out=d["xc_my"].rearrange("n (r t) -> n r t", r=NCORE),
                in_=xa[bass.ds(off, DSH), :, :]), 16)
        self.xcmy_mark = self.c_ldb.mark()

        for r in range(NCORE):
            @b.sync
            def _(sy, r=r):
                self.c_coll.wait(sy, mark=self.ag_l0_mark)
                c_dv.wait(sy)
                c_ld.tag(sy, sy.dma_start(out=dlr[:], in_=d["bcd_all"][96 * r:96 * r + 64, :]), 16)
            dlr_mark = c_ld.mark()

            @b.scalar
            def _(sc2, r=r):
                off = self.rk(sc2, DSH)
                xa = d["xc_all"].rearrange("(r n) t -> r n t", n=DI)
                c_dv.wait(sc2)
                self.c_ldb.tag(sc2, sc2.dma_start(
                    out=xcd[:].rearrange("p (a t) -> p a t", a=2), in_=xa[r][bass.ds(off, 256), :].rearrange("(a p) t -> p a t", a=2)), 16)
            xcd_mark = self.c_ldb.mark()
            for dt in range(2):

                for ci in range(4):
                    @b.tensor
                    def _(te, r=r, dt=dt, ci=ci):
                        c_wst.wait(te, mark=dupw_mark)
                        c_ld.wait(te, mark=dlr_mark)
                        c_ev.wait(te)
                        c_mm.tag(te.matmul(ps_d[:, 512 * (ci % 2):512 * (ci % 2) + 512],
                                           dupw[:, 128 * dt:128 * (dt + 1)],
                                           dlr[:, 512 * ci:512 * (ci + 1)], start=True, stop=True))

                    @b.scalar
                    def _(sc, r=r, dt=dt, ci=ci):
                        c_mm.wait(sc)
                        c_dv.wait(sc)
                        if ci == 0:
                            c_sp.wait(sc)
                        c_ev.tag(sc.activation(spz[:], ps_d[:, 512 * (ci % 2):512 * (ci % 2) + 512],
                                               ACTF.Identity, bias=self.sm["dup_b"][:, dt:dt + 1]))

                    @b.vector
                    def _(v, r=r, dt=dt, ci=ci):
                        c_ev.wait(v)
                        v.tensor_scalar(spc[:], spz[:], 17.0, None, ALU.min)
                        v.drain()
                        c_dv.tag(v.nop())

                    @b.scalar
                    def _(sc, r=r, dt=dt, ci=ci):
                        c_dv.wait(sc)
                        c_ev.tag(sc.activation(spc[:], spc[:], ACTF.Exp))

                    @b.vector
                    def _(v, r=r, dt=dt, ci=ci):
                        c_ev.wait(v)
                        v.tensor_scalar(spc[:], spc[:], 1.0, None, ALU.add)
                        v.drain()
                        c_dv.tag(v.nop())

                    @b.scalar
                    def _(sc, r=r, dt=dt, ci=ci):
                        c_dv.wait(sc)
                        c_ev.tag(sc.activation(spc[:], spc[:], ACTF.Ln))

                    @b.vector
                    def _(v, r=r, dt=dt, ci=ci):
                        c_ev.wait(v)
                        v.tensor_scalar(spz[:], spz[:], 17.0, 0.0, ALU.subtract, op1=ALU.max)
                        v.drain()
                        v.tensor_tensor(dsb[:, 512 * ci:512 * (ci + 1)], spc[:], spz[:], ALU.add)
                        v.drain()
                        c_dv.tag(v.nop())

                @b.vector
                def _(v, r=r, dt=dt):
                    c_ev.wait(v)
                    self.c_ldb.wait(v, mark=xcd_mark)
                    c_sp.wait(v)
                    v.tensor_tensor(dxsb[:], dsb[:], xcd[:, TC * dt:TC * (dt + 1)], ALU.mult)
                    v.drain()
                    c_dv.tag(v.nop())

                @b.sync
                def _(sy, r=r, dt=dt):
                    c_dv.wait(sy)
                    c_sp.tag(sy, sy.dma_start(out=d["delta_dram"][128 * dt:128 * (dt + 1), TC * r:TC * (r + 1)], in_=dsb[:]), 16)
                    c_sp.tag(sy, sy.dma_start(out=d["dx_dram"][128 * dt:128 * (dt + 1), TC * r:TC * (r + 1)], in_=dxsb[:]), 16)

        ddx_mark = c_sp.mark()
        s5.close()

        s5b = ExitStack()
        TB = 4096
        brep_t = self.sb("brep_t", [128, TB], stack=s5b)
        crep_t = self.sb("crep_t", [128, TB], stack=s5b)
        drep = self.sb("drep", [128, TB], stack=s5b)
        dxrep = self.sb("dxrep", [128, TB], stack=s5b)
        ab_t = self.sb("ab_t", [128, TB], F32, stack=s5b)
        bx_t = self.sb("bx_t", [128, TB], stack=s5b)
        h_t = self.sb("h_t", [128, TB], stack=s5b)
        yv_t = self.sb("yv_t", [128, TB], stack=s5b)
        ysb = self.sb("ysb", [128, TB], stack=s5b)
        carry = self.sb("carry", [128, 32], F32, stack=s5b)
        ps_y = self.psum("ps_y", [128, TB], F32, stack=s5b)

        for c in range(TFULL // TB):
            @b.sync
            def _(sy, c=c):
                c_sp.wait(sy, mark=ddx_mark)
                c_dv.wait(sy)
                c_mm.wait(sy)
                for g in range(8):
                    for rb in range(TB // TC):
                        r = (TB * c) // TC + rb
                        c_ld.tag(sy, sy.dma_start(
                            out=brep_t[16 * g:16 * g + 16, TC * rb:TC * (rb + 1)],
                            in_=d["bcd_all"][96 * r + 64:96 * r + 80, :]), 16)
                        c_ld.tag(sy, sy.dma_start(
                            out=crep_t[16 * g:16 * g + 16, TC * rb:TC * (rb + 1)],
                            in_=d["bcd_all"][96 * r + 80:96 * r + 96, :]), 16)
            bc_mark = c_ld.mark()

            for half in range(2):
                for rr in range(16):
                    rt = 16 * half + rr

                    @b.sync
                    def _(sy, c=c, rt=rt):
                        c_dv.wait(sy)
                        c_ev.wait(sy)
                        for g in range(8):
                            row = 8 * rt + g
                            c_ld.tag(sy, sy.dma_start(
                                out=drep[16 * g:16 * g + 16, :],
                                in_=d["delta_dram"][row:row + 1, TB * c:TB * (c + 1)].broadcast_to([16, TB])), 16)
                            c_ld.tag(sy, sy.dma_start(
                                out=dxrep[16 * g:16 * g + 16, :],
                                in_=d["dx_dram"][row:row + 1, TB * c:TB * (c + 1)].broadcast_to([16, TB])), 16)
                    rep_mark = c_ld.mark()

                    @b.scalar
                    def _(sc, c=c, rt=rt):
                        c_ld.wait(sc, mark=rep_mark)
                        if "hdbg" in self.dbg:
                            c_sp.wait(sc)
                        c_ev.tag(sc.activation(ab_t[:], drep[:], ACTF.Exp,
                                               scale=self.sm["acol"][:, rt:rt + 1]))

                    @b.vector
                    def _(v, c=c, rt=rt):
                        c_ev.wait(v)
                        c_mm.wait(v)
                        c_ld.wait(v, mark=bc_mark)
                        v.tensor_tensor(bx_t[:], dxrep[:], brep_t[:], ALU.mult)
                        v.drain()
                        init = 0.0 if c % 2 == 0 else carry[:, rt:rt + 1]
                        v.tensor_tensor_scan(h_t[:], ab_t[:], bx_t[:], init, ALU.mult, ALU.add)
                        v.drain()
                        v.tensor_copy(carry[:, rt:rt + 1], h_t[:, TB - 1:TB])
                        v.tensor_tensor(yv_t[:], h_t[:], crep_t[:], ALU.mult)
                        v.drain()
                        c_dv.tag(v.nop())

                    if c == 0 and rt == 3 and "hdbg" in self.dbg:
                        @b.sync
                        def _(sy):
                            c_dv.wait(sy)
                            c_sp.tag(sy, sy.dma_start(out=d["hdbg"], in_=h_t[:]), 16)
                            c_sp.tag(sy, sy.dma_start(out=d["abdbg"], in_=ab_t[:]), 16)
                            c_sp.tag(sy, sy.dma_start(out=d["bxdbg"], in_=bx_t[:]), 16)

                    for s in range(TB // 512):
                        @b.tensor
                        def _(te, c=c, rt=rt, rr=rr, s=s, half=half):
                            c_dv.wait(te)
                            if rr == 0 and s == 0:
                                c_ev.wait(te)
                            ins = te.matmul(ps_y[:, 512 * s:512 * (s + 1)],
                                            self.sum16[:, 128 * rr:128 * (rr + 1)],
                                            yv_t[:, 512 * s:512 * (s + 1)],
                                            start=(rr == 0), stop=(rr == 15))
                            if s == TB // 512 - 1:
                                c_mm.tag(ins)

                @b.scalar
                def _(sc, c=c, half=half):
                    c_mm.wait(sc)
                    c_sp.wait(sc)
                    c_ev.tag(sc.activation(ysb[:], ps_y[:], ACTF.Copy))

                @b.sync
                def _(sy, c=c, half=half):
                    c_ev.wait(sy)
                    c_sp.tag(sy, sy.dma_start(out=d["y0_dram"][128 * half:128 * (half + 1), TB * c:TB * (c + 1)],
                                              in_=ysb[:]), 16)
        self.y0_mark = self.c_sp.mark()
        s5b.close()

        @b.gpsimd
        def _(g):
            self.c_coll.tag(g, g.collective_compute(
                "AllGather", ALU.bypass, ins=[d["res_dram"]], outs=[d["res_all"]],
                replica_groups=[list(range(NCORE))]), 1)
        self.ag_res_mark = self.c_coll.mark()

        @b.scalar
        def _(sc2):
            off = self.rk(sc2, DSH)
            self.c_coll.wait(sc2, mark=self.ag_res_mark)
            ra = d["res_all"].rearrange("(r n) t -> n r t", n=DI)
            self.c_ldb.tag(sc2, sc2.dma_start(
                out=d["res_my"].rearrange("n (r t) -> n r t", r=NCORE),
                in_=ra[bass.ds(off, DSH), :, :]), 16)
        self.resmy_mark = self.c_ldb.mark()

        s5c = ExitStack()
        y0sl = self.sb("y0sl", [128, 2 * TC], stack=s5c)
        xcd2 = self.sb("xcd2", [128, 2 * TC], stack=s5c)
        ressl = self.sb("ressl", [128, 2 * TC], stack=s5c)
        a0in = [self.sb(f"a0in{i}", [128, TC], stack=s5c) for i in range(2)]
        woslab = self.sb("woslab", [128, 2 * D], stack=s5c)
        pstg = self.sb("pstg", [128, 512], stack=s5c)
        ps_o = self.psum("ps_o", [128, 1024], F32, stack=s5c)

        @b.gpsimd
        def _(g):
            c_mm.wait(g)
            c_dv.wait(g)
            c_ev.wait(g)
            off = g.compute_val(g.partition_id() * DSH)
            c_wst.tag(g, g.dma_start(
                out=woslab[:].rearrange("p (a q) -> p a q", a=2),
                in_=d["w_outproj"][bass.ds(off, 256), :].rearrange("(a p) q -> p a q", a=2)), 16)
        wo_mark = c_wst.mark()

        for r in range(NCORE):
            @b.sync
            def _(sy, r=r):
                c_dv.wait(sy)
                c_sp.wait(sy, mark=self.y0_mark)
                c_ld.tag(sy, sy.dma_start(out=y0sl[:].rearrange("p (a t) -> p a t", a=2),
                                          in_=d["y0_dram"][:, TC * r:TC * (r + 1)]
                                          .rearrange("(a p) t -> p a t", a=2)), 16)
            y0_ld = c_ld.mark()

            @b.scalar
            def _(sc2, r=r):
                c_dv.wait(sc2)
                self.c_ldb.wait(sc2, mark=self.resmy_mark)
                self.c_ldb.tag(sc2, sc2.dma_start(
                    out=xcd2[:].rearrange("p (a t) -> p a t", a=2),
                    in_=d["xc_my"][:, TC * r:TC * (r + 1)].rearrange("(a p) t -> p a t", a=2)), 16)
                self.c_ldb.tag(sc2, sc2.dma_start(
                    out=ressl[:].rearrange("p (a t) -> p a t", a=2),
                    in_=d["res_my"][:, TC * r:TC * (r + 1)].rearrange("(a p) t -> p a t", a=2)), 16)
            xr_ld = self.c_ldb.mark()

            @b.vector
            def _(v, r=r):
                c_ld.wait(v, mark=y0_ld)
                self.c_ldb.wait(v, mark=xr_ld)
                c_mm.wait(v)
                for dt in range(2):
                    v.scalar_tensor_tensor(a0in[dt][:], xcd2[:, TC * dt:TC * (dt + 1)],
                                           self.sm["d_param_my"][:, dt:dt + 1],
                                           y0sl[:, TC * dt:TC * (dt + 1)], ALU.mult, ALU.add)
                    v.drain()
                    v.tensor_tensor(a0in[dt][:], a0in[dt][:], ressl[:, TC * dt:TC * (dt + 1)], ALU.mult)
                    v.drain()
                c_dv.tag(v.nop())

            for jo in range(NT):
                for ci in range(4):
                    @b.tensor
                    def _(te, r=r, jo=jo, ci=ci):
                        c_wst.wait(te, mark=wo_mark)
                        c_dv.wait(te)
                        c_ev.wait(te)
                        for dt in range(2):
                            ins = te.matmul(ps_o[:, 512 * (ci % 2):512 * (ci % 2) + 512],
                                            woslab[:, 2 * 128 * jo + 256 * 0 + dt * D if False else D * dt + 128 * jo:D * dt + 128 * (jo + 1)],
                                            a0in[dt][:, 512 * ci:512 * (ci + 1)],
                                            start=(dt == 0), stop=(dt == 1))
                        c_mm.tag(ins)

                    @b.scalar
                    def _(sc, r=r, jo=jo, ci=ci):
                        c_mm.wait(sc)
                        c_sp.wait(sc)
                        c_ev.tag(sc.activation(pstg[:], ps_o[:, 512 * (ci % 2):512 * (ci % 2) + 512], ACTF.Copy))

                    @b.sync
                    def _(sy, r=r, jo=jo, ci=ci):
                        c_ev.wait(sy)
                        c_sp.tag(sy, sy.dma_start(
                            out=d["partial0_dram"][D * r + 128 * jo:D * r + 128 * (jo + 1), 512 * ci:512 * (ci + 1)],
                            in_=pstg[:]), 16)
        p0_mark = c_sp.mark()
        s5c.close()

        @b.gpsimd
        def _(g):
            self.c_sp.wait(g, mark=p0_mark)
            self.c_coll.tag(g, g.collective_compute(
                "ReduceScatter", ALU.add, ins=[d["partial0_dram"]], outs=[d["rs0_dram"]],
                replica_groups=[list(range(NCORE))]), 1)
        self.rs0_mark = self.c_coll.mark()

    def s6_gate_outproj(self):
        b, d = self.b, self.dram
        c_mm, c_ev, c_dv, c_ld, c_sp = self.c_mm, self.c_ev, self.c_dv, self.c_ld, self.c_sp
        s6 = ExitStack()
        rs0t = self.sb("rs0t", [128, TC], stack=s6)
        xf6 = self.sb("x6frag", [128, TC], F32, stack=s6)
        t6 = self.sb("t6", [128, TC], F32, stack=s6)
        x1s = self.sb("x1s", [128, TC], stack=s6)
        x1s2 = self.sb("x1s2", [128, TC], stack=s6)
        for j in range(NT):
            @b.sync
            def _(sy, j=j):
                self.c_coll.wait(sy, mark=self.rs0_mark)
                c_dv.wait(sy)
                c_ld.tag(sy, sy.dma_start(out=rs0t[:], in_=d["rs0_dram"][128 * j:128 * (j + 1), :]), 16)
                c_ld.tag(sy, sy.dma_start(out=xf6[:], in_=d["x_t"][128 * j:128 * (j + 1), 3:]), 16)
            rj_mark = c_ld.mark()
            x1b = [x1s, x1s2][j % 2]

            @b.vector
            def _(v, j=j, x1b=x1b):
                c_ld.wait(v, mark=rj_mark)
                c_sp.wait(v)
                v.tensor_tensor(t6[:], rs0t[:], self.xn[j][:, 3:], ALU.add)
                v.drain()
                v.tensor_tensor(t6[:], t6[:], self.facc[j][:], ALU.add)
                v.drain()
                v.tensor_tensor(x1b[:], t6[:], xf6[:], ALU.add)
                v.drain()
                c_dv.tag(v.nop())

            @b.sync
            def _(sy, j=j, x1b=x1b):
                c_dv.wait(sy)
                c_sp.tag(sy, sy.dma_start(out=d["x1_dram"][128 * j:128 * (j + 1), :], in_=x1b[:]), 16)
        s6.close()

    def s7_ln1(self):
        b, d = self.b, self.dram
        self.ln2_stack = ExitStack()
        self.ps_stat = self.psum("ps_stat2", [1, 1024], stack=self.ln2_stack)
        self.ps_bc = self.psum("ps_bc2", [128, 1536], stack=self.ln2_stack)
        x1_mark = self.c_sp.mark()
        self.emit_ln(src=None, out_xn=self.xn, out_xr=None, width=TC,
                     rms_cols=None, rms_col_idx=None, src_f32=False,
                     src_dram=d["x1_dram"], dram_wait=(self.c_sp, x1_mark))
        if "xn1_dram" in self.dbg:
            @b.sync
            def _(sy):
                self.c_dv.wait(sy)
                for j in range(NT):
                    self.c_sp.tag(sy, sy.dma_start(out=d["xn1_dram"][128 * j:128 * (j + 1), :],
                                                   in_=self.xn[j][:, 0:TC]), 16)

    def s8_kvr(self):
        b, d = self.b, self.dram
        c_wst, c_mm, c_ev, c_dv, c_sp = self.c_wst, self.c_mm, self.c_ev, self.c_dv, self.c_sp
        self.ln2_stack.close()
        s8 = ExitStack()
        wsl8 = [self.sb(f"w8_{k}", [128, D], stack=s8) for k in range(NT)]
        kst = self.sb("kst", [128, TC], stack=s8)
        ps_k = self.psum("ps_k", [128, 1024], F32, stack=s8)
        names = ["wkT", "wvT", "wrT"]
        for wi, wname in enumerate(names):
            @b.gpsimd
            def _(g, wi=wi, wname=wname):
                c_mm.wait(g)
                for k in range(NT):
                    c_wst.tag(g, g.dma_start(out=wsl8[k][:], in_=d[wname][128 * k:128 * (k + 1), :]), 16)
            w_mark = c_wst.mark()
            for j in range(NT):
                for ci in range(4):
                    @b.tensor
                    def _(te, wi=wi, j=j, ci=ci):
                        c_wst.wait(te, mark=w_mark)
                        c_dv.wait(te)
                        c_ev.wait(te)
                        for k in range(NT):
                            ins = te.matmul(ps_k[:, 512 * (ci % 2):512 * (ci % 2) + 512],
                                            wsl8[k][:, 128 * j:128 * (j + 1)],
                                            self.xn[k][:, 512 * ci:512 * (ci + 1)],
                                            start=(k == 0), stop=(k == NT - 1))
                        c_mm.tag(ins)

                    @b.scalar
                    def _(sc, wi=wi, j=j, ci=ci):
                        c_mm.wait(sc)
                        if ci == 0:
                            c_sp.wait(sc)
                        fn = ACTF.Sigmoid if wi == 2 else ACTF.Copy
                        c_ev.tag(sc.activation(kst[:, 512 * ci:512 * (ci + 1)],
                                               ps_k[:, 512 * (ci % 2):512 * (ci % 2) + 512], fn))

                @b.sync
                def _(sy, wi=wi, j=j):
                    c_ev.wait(sy)
                    c_sp.tag(sy, sy.dma_start(out=d["kvr_dram"][D * wi + 128 * j:D * wi + 128 * (j + 1), :],
                                              in_=kst[:]), 16)
        kvr_mark = c_sp.mark()
        s8.close()

        @b.gpsimd
        def _(g):
            c_sp.wait(g, mark=kvr_mark)
            self.c_coll.tag(g, g.collective_compute(
                "AllGather", ALU.bypass, ins=[d["kvr_dram"]], outs=[d["kvr_all"]],
                replica_groups=[list(range(NCORE))]), 1)
        self.ag_kvr_mark = self.c_coll.mark()


    def s9_rwkv(self):
        LV = getattr(self, 's9_level', 99)
        b, d = self.b, self.dram
        c_wst, c_mm, c_ev, c_dv, c_ld, c_sp = self.c_wst, self.c_mm, self.c_ev, self.c_dv, self.c_ld, self.c_sp
        s9 = ExitStack()
        kch = self.sb("kch", [128, 4096], stack=s9)
        vch = self.sb("vch", [128, 4096], stack=s9)
        rch = self.sb("rch", [128, 4096], stack=s9)
        kap = self.sb("kap", [128, 4096], stack=s9)
        wpow = self.sb("wpow", [128, 33], F32, stack=s9)
        wipow = self.sb("wipow", [128, 16], F32, stack=s9)
        p16 = self.sb("p16", [128, 128], stack=s9)
        p16i = self.sb("p16i", [128, 128], stack=s9)
        p16i_h = [self.sb(f"p16ih{h}", [128, 128], stack=s9) for h in range(2)]
        p16f_h = [self.sb(f"p16fh{h}", [128, 128], stack=s9) for h in range(2)]
        hmask = self.sb("hmask", [128, 2], F32, stack=s9)
        pmaskc = self.sb("pmaskc", [128, 4], F32, stack=s9)
        khTm = [self.sb(f"khTm{p}", [128, 128], stack=s9) for p in range(4)]
        r2h = [self.sb(f"r2h{h}", [128, 128], stack=s9) for h in range(2)]
        p16f = self.sb("p16f", [128, 128], stack=s9)
        p32 = self.sb("p32", [128, 128], stack=s9)
        p32r = self.sb("p32r", [128, 128], stack=s9)
        a1 = self.sb("a1", [128, 128], stack=s9)
        r2 = self.sb("r2", [128, 128], stack=s9)
        b1 = self.sb("b1", [128, 128], stack=s9)
        b2 = self.sb("b2", [128, 128], stack=s9)
        b1b = self.sb("b1b", [128, 128], stack=s9)
        b2b = self.sb("b2b", [128, 128], stack=s9)
        kh = self.sb("kh", [128, 128], stack=s9)
        khT = self.sb("khT", [128, 128], stack=s9)
        vT = self.sb("vT", [128, 128], stack=s9)
        msb = [self.sb(f"msb{h}", [128, 128], stack=s9) for h in range(2)]
        mtmp = self.sb("mtmp", [128, 128], stack=s9)
        shist = self.sb("shist", [128, 5 * 64], stack=s9)
        y9stg = self.sb("y9stg", [128, 4096], stack=s9)
        wo1slab = self.sb("wo1slab", [128, D], stack=s9)
        p1stg = self.sb("p1stg", [128, 512], stack=s9)
        ps_tr = self.psum("ps_tr", [128, 1024], BF16, stack=s9)
        ps_m = self.psum("ps_m", [128, 512], F32, stack=s9)
        ps_m2 = self.psum("ps_m2", [128, 512], F32, stack=s9)
        ps_kv = self.psum("ps_kv", [128, 512], F32, stack=s9)
        ps_y9 = self.psum("ps_y9", [128, 512], F32, stack=s9)
        ps_p1 = self.psum("ps_p1", [128, 1024], F32, stack=s9)

        @b.gpsimd
        def _(g):
            self.c_coll.wait(g, mark=self.ag_kvr_mark)
            c_mm.wait(g)
            c_dv.wait(g)
            c_ev.wait(g)
            kvr_a = d["kvr_all"].rearrange("(o q) t -> q o t", o=NCORE)
            for wi, nm in enumerate(["k_my", "v_my", "r_my"]):
                off = self.rk(g, 128, 1024 * wi)
                c_wst.tag(g, g.dma_start(out=d[nm].rearrange("n (o t) -> n o t", o=NCORE),
                                         in_=kvr_a[bass.ds(off, 128), :, :]), 16)
            off2 = self.rk(g, 128)
            c_wst.tag(g, g.dma_start(out=wo1slab[:], in_=d["woT"][bass.ds(off2, 128), :]), 16)
        my_mark = c_wst.mark()

        @b.vector
        def _(v):
            c_ld.wait(v)
            v.memset(wpow[:, 0:1], 1.0)
            v.drain()
            for j in range(32):
                v.tensor_scalar(wpow[:, j + 1:j + 2], wpow[:, j:j + 1], self.sm["tdec"][:, 0:1], None, ALU.mult)
                v.drain()
            v.reciprocal(wipow[:, 1:2], self.sm["tdec"][:, 0:1])
            v.memset(wipow[:, 0:1], 1.0)
            v.drain()
            for j in range(1, 15):
                v.tensor_tensor(wipow[:, j + 1:j + 2], wipow[:, j:j + 1], wipow[:, 1:2], ALU.mult)
                v.drain()
            from bass_rust import AP as _AP
            v.tensor_copy(p16[:], _AP(wpow[:].tensor, wpow[:].offset, [[33, 128], [0, 8], [1, 16]]))
            v.tensor_copy(p16i[:], _AP(wipow[:].tensor, wipow[:].offset, [[16, 128], [0, 8], [1, 16]]))
            v.tensor_copy(p16f[:], _AP(wpow[:].tensor, wpow[:].offset + 16, [[33, 128], [0, 8], [-1, 16]]))
            v.tensor_copy(p32[:], _AP(wpow[:].tensor, wpow[:].offset + 31, [[33, 128], [0, 4], [-1, 32]]))
            v.tensor_copy(p32r[:], _AP(wpow[:].tensor, wpow[:].offset + 1, [[33, 128], [0, 4], [1, 32]]))
            v.drain()
            v.tensor_scalar(hmask[:, 0:1], self.io_row[:, 0:1], 63.5, None, ALU.is_le)
            v.tensor_scalar(hmask[:, 1:2], self.io_row[:, 0:1], 63.5, None, ALU.is_gt if hasattr(ALU, 'is_gt') else ALU.is_ge)
            for pp in range(4):
                v.tensor_scalar(self.tmpc1[:, 0:1], self.io_row[:, 0:1], 32.0 * pp - 0.5, None, ALU.is_ge)
                v.drain()
                v.tensor_scalar(self.tmpc2[:, 0:1], self.io_row[:, 0:1], 32.0 * (pp + 1) - 0.5, None, ALU.is_lt)
                v.drain()
                v.tensor_tensor(pmaskc[:, pp:pp + 1], self.tmpc1[:, 0:1], self.tmpc2[:, 0:1], ALU.mult)
                v.drain()
            for h in range(2):
                v.tensor_tensor(p16i_h[h][:], p16i[:], hmask[:, h:h + 1].broadcast_to([128, 128]), ALU.mult)
                v.tensor_tensor(p16f_h[h][:], p16f[:], hmask[:, h:h + 1].broadcast_to([128, 128]), ALU.mult)
            v.drain()
            c_dv.tag(v.nop())

        for cc in range(4):
            @b.sync
            def _(sy, cc=cc):
                c_wst.wait(sy, mark=my_mark)
                c_dv.wait(sy)
                c_ev.wait(sy)
                c_mm.wait(sy)
                c_ld.tag(sy, sy.dma_start(out=kch[:], in_=d["k_my"][:, 4096 * cc:4096 * (cc + 1)]), 16)
                c_ld.tag(sy, sy.dma_start(out=vch[:], in_=d["v_my"][:, 4096 * cc:4096 * (cc + 1)]), 16)
                c_ld.tag(sy, sy.dma_start(out=rch[:], in_=d["r_my"][:, 4096 * cc:4096 * (cc + 1)]), 16)
            ch_mark = c_ld.mark()

            @b.vector
            def _(v, cc=cc):
                c_ld.wait(v, mark=ch_mark)
                v.tensor_scalar(kap[:], kch[:], self.sm["tfirst"][:, 0:1], None, ALU.mult)
                v.drain()
                if cc % 2 == 0:
                    v.memset(shist[:, 0:64], 0.0)
                    v.drain()
                c_dv.tag(v.nop())

            for tt in range(32):
                sl = slice(128 * tt, 128 * (tt + 1))

                @b.vector
                def _(v, cc=cc, tt=tt, sl=sl):
                    c_mm.wait(v)
                    c_ev.wait(v)
                    if "kh9dbg" in self.dbg:
                        c_sp.wait(v)
                    v.tensor_tensor(a1[:], rch[:, sl], p16[:], ALU.mult)
                    v.tensor_tensor(r2[:], rch[:, sl], p32r[:], ALU.mult)
                    v.tensor_tensor(b1[:], kap[:, sl], p16i_h[0][:], ALU.mult)
                    v.tensor_tensor(b1b[:], kap[:, sl], p16i_h[1][:], ALU.mult)
                    v.tensor_tensor(b2[:], kap[:, sl], p16f_h[0][:], ALU.mult)
                    v.tensor_tensor(b2b[:], kap[:, sl], p16f_h[1][:], ALU.mult)
                    v.tensor_tensor(kh[:], kap[:, sl], p32[:], ALU.mult)
                    v.drain()
                    for h in range(2):
                        v.tensor_tensor(r2h[h][:], r2[:], hmask[:, h:h + 1].broadcast_to([128, 128]), ALU.mult)
                    v.drain()
                    c_dv.tag(v.nop())

                if LV < 2:
                    continue

                @b.tensor
                def _(te, cc=cc, tt=tt, sl=sl):
                    c_dv.wait(te)
                    c_ev.wait(te)
                    te.transpose(ps_tr[:, 0:128], kh[:], self.ident[:])
                    c_mm.tag(te.transpose(ps_tr[:, 128:256], vch[:, sl], self.ident[:]))

                @b.scalar
                def _(sc, cc=cc, tt=tt):
                    c_mm.wait(sc)
                    sc.copy(out=khT[:], in_=ps_tr[:, 0:128])
                    c_ev.tag(sc.copy(out=vT[:], in_=ps_tr[:, 128:256]))

                if cc == 0 and tt == 0 and "kh9dbg" in self.dbg:
                    @b.sync
                    def _(sy):
                        c_ev.wait(sy)
                        c_sp.tag(sy, sy.dma_start(out=d["r9dbg"], in_=rch[:, 0:512]), 16)
                        c_sp.tag(sy, sy.dma_start(out=d["k9dbg"], in_=kap[:, 0:512]), 16)
                        c_sp.tag(sy, sy.dma_start(out=d["kh9dbg"], in_=kh[:]), 16)
                        c_sp.tag(sy, sy.dma_start(out=d["khT9dbg"], in_=khT[:]), 16)
                        c_sp.tag(sy, sy.dma_start(out=d["vT9dbg"], in_=vT[:]), 16)

                if LV < 3:
                    continue

                @b.vector
                def _(v, cc=cc, tt=tt):
                    c_ev.wait(v)
                    for p in range(4):
                        v.tensor_tensor(khTm[p][:], khT[:], pmaskc[:, p:p + 1].broadcast_to([128, 128]), ALU.mult)
                    v.drain()
                    c_dv.tag(v.nop())

                @b.tensor
                def _(te, cc=cc, tt=tt):
                    c_ev.wait(te)
                    c_dv.wait(te)
                    te.matmul(ps_m[:, 0:128], b1[:], a1[:], start=True, stop=True, skip_group_check=True)
                    te.matmul(ps_m[:, 128:256], b1b[:], a1[:], start=True, stop=True, skip_group_check=True)
                    te.matmul(ps_m2[:, 0:128], b2[:], a1[:], start=True, stop=True, skip_group_check=True)
                    te.matmul(ps_m2[:, 128:256], b2b[:], a1[:], start=True, stop=True, skip_group_check=True)
                    for p in range(4):
                        ins = te.matmul(ps_kv[:, 128 * p:128 * (p + 1)], khTm[p][:], vT[:],
                                        start=True, stop=True, skip_group_check=True)
                    c_mm.tag(ins)

                if LV < 4:
                    continue

                @b.vector
                def _(v, cc=cc, tt=tt):
                    c_mm.wait(v)
                    for h in range(2):
                        v.tensor_tensor(msb[h][:], ps_m[:, 128 * h:128 * (h + 1)], self.mk_diag[:], ALU.mult)
                        v.drain()
                        v.tensor_tensor(mtmp[:], ps_m2[:, 128 * h:128 * (h + 1)], self.mk_off[:], ALU.mult)
                        v.drain()
                        v.tensor_tensor(msb[h][:], msb[h][:], mtmp[:], ALU.add)
                        v.drain()
                    for p in range(4):
                        v.scalar_tensor_tensor(shist[0:64, 64 * (p + 1):64 * (p + 2)],
                                               shist[0:64, 64 * p:64 * (p + 1)],
                                               wpow[0:64, 32:33],
                                               ps_kv[0:64, 128 * p:128 * p + 64], ALU.mult, ALU.add)
                        v.scalar_tensor_tensor(shist[64:128, 64 * (p + 1):64 * (p + 2)],
                                               shist[64:128, 64 * p:64 * (p + 1)],
                                               wpow[64:128, 32:33],
                                               ps_kv[64:128, 128 * p + 64:128 * (p + 1)], ALU.mult, ALU.add)
                        v.drain()
                    c_dv.tag(v.nop())

                if LV < 5:
                    continue

                @b.tensor
                def _(te, cc=cc, tt=tt):
                    c_dv.wait(te)
                    c_ev.wait(te)
                    for h in range(2):
                        te.matmul(ps_y9[64 * h:64 * (h + 1), 0:128], vT[:, 64 * h:64 * (h + 1)],
                                  msb[h][:], start=True, stop=False,
                                  tile_position=(0, 64 * h), skip_group_check=True)
                    for h in range(2):
                        for p in range(4):
                            ins = te.matmul(ps_y9[64 * h:64 * (h + 1), 32 * p:32 * (p + 1)],
                                            shist[:, 64 * p:64 * (p + 1)],
                                            r2h[h][:, 32 * p:32 * (p + 1)],
                                            start=False, stop=(h == 1 and p == 3),
                                            tile_position=(0, 64 * h), skip_group_check=True)
                    c_mm.tag(ins)

                @b.scalar
                def _(sc, cc=cc, tt=tt, sl=sl):
                    c_mm.wait(sc)
                    c_ev.tag(sc.copy(out=y9stg[:, sl], in_=ps_y9[:, 0:128]))


                @b.vector
                def _(v, cc=cc, tt=tt):
                    c_ev.wait(v)
                    v.tensor_copy(shist[:, 0:64], shist[:, 256:320])
                    v.drain()
                    c_dv.tag(v.nop())

            for dout in range(NT):
                for ci in range(8):
                    @b.tensor
                    def _(te, cc=cc, dout=dout, ci=ci):
                        c_ev.wait(te)
                        c_dv.wait(te)
                        c_mm.tag(te.matmul(ps_p1[:, 512 * (ci % 2):512 * (ci % 2) + 512],
                                           wo1slab[:, 128 * dout:128 * (dout + 1)],
                                           y9stg[:, 512 * ci:512 * (ci + 1)],
                                           start=True, stop=True))

                    @b.scalar
                    def _(sc, cc=cc, dout=dout, ci=ci):
                        c_mm.wait(sc)
                        c_sp.wait(sc)
                        c_ev.tag(sc.activation(p1stg[:], ps_p1[:, 512 * (ci % 2):512 * (ci % 2) + 512], ACTF.Copy))

                    @b.sync
                    def _(sy, cc=cc, dout=dout, ci=ci):
                        c_ev.wait(sy)
                        gt = 4096 * cc + 512 * ci
                        owner = gt // TC
                        lt = gt - TC * owner
                        c_sp.tag(sy, sy.dma_start(
                            out=d["partial1_dram"][D * owner + 128 * dout:D * owner + 128 * (dout + 1), lt:lt + 512],
                            in_=p1stg[:]), 16)
        p1_mark = c_sp.mark()
        s9.close()

        @b.gpsimd
        def _(g):
            c_sp.wait(g, mark=p1_mark)
            self.c_coll.tag(g, g.collective_compute(
                "ReduceScatter", ALU.add, ins=[d["partial1_dram"]], outs=[d["rs1_dram"]],
                replica_groups=[list(range(NCORE))]), 1)
        self.rs1_mark = self.c_coll.mark()

    def s10_wo(self):
        pass

    def s11_l1_ffn(self):
        b, d = self.b, self.dram
        c_wst, c_mm, c_ev, c_dv, c_ld, c_sp = self.c_wst, self.c_mm, self.c_ev, self.c_dv, self.c_ld, self.c_sp
        from bass_rust import AP as _AP
        s11 = ExitStack()
        r1t = self.sb("r1t", [128, 64], stack=s11)
        lg1 = self.sb("lg1", [8, TC], F32, stack=s11)
        lgt1 = self.sb("lgt1", [128, 128], F32, stack=s11)
        mx1 = self.sb("mx1", [128, 16], F32, stack=s11)
        mx2 = self.sb("mx2", [128, 16], F32, stack=s11)
        g1v = self.sb("g1v", [128, 16], F32, stack=s11)
        g2v = self.sb("g2v", [128, 16], F32, stack=s11)
        oh1 = self.sb("oh1", [128, 128], F32, stack=s11)
        oh2 = self.sb("oh2", [128, 128], F32, stack=s11)
        l2t = self.sb("l2t", [128, 128], F32, stack=s11)
        poscum = [self.sb(f"poscum{k}", [128, 128], F32, stack=s11) for k in range(2)]
        carry1 = self.sb("carry1", [1, 2 * NE], F32, stack=s11)
        cntsb = self.sb("cntsb", [NCORE, 2 * NE], F32, stack=s11)
        prefsb = self.sb("prefsb", [128, 2 * NE], F32, stack=s11)
        gfin = self.sb("gfin", [128, 128], F32, stack=s11)
        gstg = self.sb("gstg", [8, TC], stack=s11)
        grep1 = self.sb("grep1", [128, TC], stack=s11)
        utrif = self.sb("utrif", [128, 128], F32, stack=s11)
        ps_l1 = self.psum("ps_l1", [8, 512], F32, stack=s11)
        ps_t1 = self.psum("ps_t1", [128, 512], F32, stack=s11)
        ps_c1 = self.psum("ps_c1", [128, 512], F32, stack=s11)

        @b.gpsimd
        def _(g):
            c_mm.wait(g)
            c_dv.wait(g)
            for k in range(NT):
                c_wst.tag(g, g.dma_start(out=r1t[:, 8 * k:8 * k + 8], in_=d["r1_wT"][128 * k:128 * (k + 1), :]), 16)
        r1_mark = c_wst.mark()

        @b.vector
        def _(v):
            self.c_gp.wait(v)
            v.tensor_copy(utrif[:], self.utri[:])
            v.drain()
            c_dv.tag(v.nop())

        for ci in range(4):
            @b.tensor
            def _(te, ci=ci):
                c_wst.wait(te, mark=r1_mark)
                c_ev.wait(te)
                c_dv.wait(te)
                for k in range(NT):
                    ins = te.matmul(ps_l1[:, 0:512], r1t[:, 8 * k:8 * k + 8],
                                    self.xn[k][:, 512 * ci:512 * (ci + 1)],
                                    start=(k == 0), stop=(k == NT - 1))
                c_mm.tag(ins)

            @b.scalar
            def _(sc, ci=ci):
                c_mm.wait(sc)
                c_ev.tag(sc.copy(out=lg1[:, 512 * ci:512 * (ci + 1)], in_=ps_l1[:, 0:512]))

        for i in range(16):
            @b.tensor
            def _(te, i=i):
                c_ev.wait(te)
                c_dv.wait(te)
                c_mm.tag(te.transpose(ps_t1[0:128, 0:8], lg1[0:8, 128 * i:128 * (i + 1)], self.identf[0:8, 0:8]))

            @b.vector
            def _(v, i=i):
                c_mm.wait(v)
                c_dv.tag(v.tensor_copy(lgt1[:, 8 * i:8 * i + 8], ps_t1[:, 0:8]))

        @b.vector
        def _(v):
            c_dv.wait(v)
            lgt3 = _AP(lgt1[:].tensor, lgt1[:].offset, [[128, 128], [8, 16], [1, 8]])
            mx1b = _AP(mx1[:].tensor, mx1[:].offset, [[16, 128], [1, 16], [0, 8]])
            mx2b = _AP(mx2[:].tensor, mx2[:].offset, [[16, 128], [1, 16], [0, 8]])
            v.tensor_reduce(mx1[:], lgt3, axis=AX.X, op=ALU.max)
            v.drain()
            v.tensor_tensor(oh1[:], lgt3, mx1b, ALU.is_ge)
            v.drain()
            v.tensor_scalar(l2t[:], oh1[:], -1e30, None, ALU.mult)
            v.drain()
            v.tensor_tensor(l2t[:], lgt1[:], l2t[:], ALU.add)
            v.drain()
            l2t3 = _AP(l2t[:].tensor, l2t[:].offset, [[128, 128], [8, 16], [1, 8]])
            v.tensor_reduce(mx2[:], l2t3, axis=AX.X, op=ALU.max)
            v.drain()
            v.tensor_tensor(oh2[:], l2t3, mx2b, ALU.is_ge)
            v.drain()
            v.tensor_tensor(g2v[:], mx2[:], mx1[:], ALU.subtract)
            v.drain()
            c_dv.tag(v.nop())

        @b.scalar
        def _(sc):
            c_dv.wait(sc)
            c_ev.tag(sc.activation(g2v[:], g2v[:], ACTF.Exp))

        @b.vector
        def _(v):
            c_ev.wait(v)
            v.tensor_scalar(g1v[:], g2v[:], 1.0, None, ALU.add)
            v.drain()
            v.reciprocal(g1v[:], g1v[:])
            v.drain()
            v.tensor_tensor(g2v[:], g2v[:], g1v[:], ALU.mult)
            v.drain()
            c_dv.tag(v.nop())

        @b.vector
        def _(v):
            c_dv.wait(v)
            v.memset(carry1[:], 0.0)
            v.drain()
            c_dv.tag(v.nop())

        for i in range(16):
            for k in range(2):
                oh = [oh1, oh2][k]

                @b.tensor
                def _(te, i=i, k=k, oh=oh):
                    c_dv.wait(te)
                    te.matmul(ps_c1[:, 8 * k:8 * (k + 1)], utrif[:], oh[:, 8 * i:8 * i + 8],
                              start=True, stop=False, skip_group_check=True)
                    c_mm.tag(te.matmul(ps_c1[:, 8 * k:8 * (k + 1)], self.onesf_row[:], carry1[0:1, 8 * k:8 * (k + 1)],
                                       start=False, stop=True, skip_group_check=True))

                @b.vector
                def _(v, i=i, k=k):
                    c_mm.wait(v)
                    v.tensor_copy(poscum[k][:, 8 * i:8 * i + 8], ps_c1[:, 8 * k:8 * (k + 1)])
                    v.drain()
                    c_dv.tag(v.nop())

                @b.tensor
                def _(te, i=i, k=k):
                    c_dv.wait(te)
                    c_mm.tag(te.matmul(ps_c1[0:1, 16 + 8 * k:24 + 8 * k], self.sel127[:],
                                       poscum[k][:, 8 * i:8 * i + 8], start=True, stop=True,
                                       skip_group_check=True))

                @b.vector
                def _(v, i=i, k=k):
                    c_mm.wait(v)
                    v.tensor_copy(carry1[0:1, 8 * k:8 * (k + 1)], ps_c1[0:1, 16 + 8 * k:24 + 8 * k])
                    v.drain()
                    c_dv.tag(v.nop())

        @b.sync
        def _(sy):
            c_dv.wait(sy)
            c_sp.tag(sy, sy.dma_start(out=d["cnt_dram"], in_=carry1[0:1, :]), 16)
        cnt_mark = c_sp.mark()

        @b.gpsimd
        def _(g):
            c_sp.wait(g, mark=cnt_mark)
            self.c_coll.tag(g, g.collective_compute(
                "AllGather", ALU.bypass, ins=[d["cnt_dram"]], outs=[d["cnt_all"]],
                replica_groups=[list(range(NCORE))]), 1)
        cnt_ag = self.c_coll.mark()

        @b.sync
        def _(sy):
            self.c_coll.wait(sy, mark=cnt_ag)
            c_ld.tag(sy, sy.dma_start(out=cntsb[:], in_=d["cnt_all"]), 16)
        cnt_ld = c_ld.mark()

        @b.tensor
        def _(te):
            c_ld.wait(te, mark=cnt_ld)
            c_dv.wait(te)
            c_mm.tag(te.matmul(ps_c1[0:1, 0:16], self.sm["pmask"][:], cntsb[:], start=True, stop=True, skip_group_check=True))

        @b.vector
        def _(v):
            c_mm.wait(v)
            v.tensor_copy(prefsb[0:1, :], ps_c1[0:1, 0:16])
            v.drain()
            c_dv.tag(v.nop())

        @b.tensor
        def _(te):
            c_dv.wait(te)
            c_mm.tag(te.matmul(ps_c1[:, 0:16], self.onesf_row[:], prefsb[0:1, :], start=True, stop=True, skip_group_check=True))

        @b.vector
        def _(v):
            c_mm.wait(v)
            v.tensor_copy(prefsb[:], ps_c1[:, 0:16])
            v.drain()
            for k in range(2):
                pref3 = _AP(prefsb[:].tensor, prefsb[:].offset + 8 * k, [[2 * NE, 128], [0, 16], [1, 8]])
                gk3 = _AP([g1v, g2v][k][:].tensor, [g1v, g2v][k][:].offset, [[16, 128], [1, 16], [0, 8]])
                dst = [oh1, oh2][k]
                v.tensor_tensor(poscum[k][:], poscum[k][:], pref3, ALU.add)
                v.drain()
                v.tensor_scalar(poscum[k][:], poscum[k][:], CAP + 0.5, None, ALU.is_le)
                v.drain()
                v.tensor_tensor(dst[:], dst[:], poscum[k][:], ALU.mult)
                v.drain()
                v.tensor_tensor(dst[:], dst[:], gk3, ALU.mult)
                v.drain()
            v.tensor_tensor(gfin[:], oh1[:], oh2[:], ALU.add)
            v.drain()
            c_dv.tag(v.nop())

        for i in range(16):
            @b.tensor
            def _(te, i=i):
                c_dv.wait(te)
                c_ev.wait(te)
                c_mm.tag(te.transpose(ps_t1[0:8, 0:128], gfin[:, 8 * i:8 * i + 8], self.identf[:]))

            @b.scalar
            def _(sc, i=i):
                c_mm.wait(sc)
                c_ev.tag(sc.copy(out=gstg[0:8, 128 * i:128 * (i + 1)], in_=ps_t1[0:8, 0:128]))

        @b.sync
        def _(sy):
            c_ev.wait(sy)
            c_sp.tag(sy, sy.dma_start(out=d["g1_dram"], in_=gstg[0:8, :]), 16)
        g1_mark = c_sp.mark()
        s11.close()

        s11b = ExitStack()
        w1sl = [self.sb(f"w1s{k}", [128, 2048], stack=s11b) for k in range(NT)]
        w2sl = [self.sb(f"w2s{k}", [128, D], stack=s11b) for k in range(16)]
        hsb = [self.sb(f"hsb{f}", [128, 512], stack=s11b) for f in range(16)]
        grep_b = self.sb("grep1b", [128, TC], stack=s11b)
        tmp11 = self.sb("tmp11", [128, 512], stack=s11b)
        ps_h = self.psum("ps_h", [128, 1024], F32, stack=s11b)
        ps_y1 = self.psum("ps_y1", [128, 2048], F32, stack=s11b)

        for e in range(NE):
            for fh in range(2):
                @b.gpsimd
                def _(g, e=e, fh=fh):
                    c_mm.wait(g)
                    if e == 0 and fh == 0:
                        c_sp.wait(g, mark=g1_mark)
                        c_dv.wait(g)
                        c_ev.wait(g)
                    for k in range(NT):
                        c_wst.tag(g, g.dma_start(out=w1sl[k][:],
                                                 in_=d["w1"][e, 128 * k:128 * (k + 1), 2048 * fh:2048 * (fh + 1)]), 16)
                    for k in range(16):
                        c_wst.tag(g, g.dma_start(out=w2sl[k][:],
                                                 in_=d["w2"][e, 2048 * fh + 128 * k:2048 * fh + 128 * (k + 1), 0:D]), 16)
                w_mark = c_wst.mark()

                if fh == 0:
                    @b.sync
                    def _(sy, e=e):
                        c_sp.wait(sy, mark=g1_mark)
                        c_dv.wait(sy)
                        c_ld.tag(sy, sy.dma_start(out=grep_b[:],
                                                  in_=d["g1_dram"][e:e + 1, :].broadcast_to([128, TC])), 16)
                    self._s11_gemark = c_ld.mark()

                for tc4 in range(4):
                    for f in range(16):
                        @b.tensor
                        def _(te, e=e, fh=fh, tc4=tc4, f=f):
                            c_wst.wait(te, mark=w_mark)
                            c_dv.wait(te)
                            c_ev.wait(te)
                            for k in range(NT):
                                ins = te.matmul(ps_h[:, 512 * (f % 2):512 * (f % 2) + 512],
                                                w1sl[k][:, 128 * f:128 * (f + 1)],
                                                self.xn[k][:, 512 * tc4:512 * (tc4 + 1)],
                                                start=(k == 0), stop=(k == NT - 1))
                            c_mm.tag(ins)

                        @b.scalar
                        def _(sc, e=e, fh=fh, tc4=tc4, f=f):
                            c_mm.wait(sc)
                            c_ev.tag(sc.activation(hsb[f][:], ps_h[:, 512 * (f % 2):512 * (f % 2) + 512], ACTF.Gelu))

                        @b.vector
                        def _(v, e=e, fh=fh, tc4=tc4, f=f):
                            c_ev.wait(v)
                            c_ld.wait(v, mark=self._s11_gemark)
                            v.tensor_tensor(hsb[f][:], hsb[f][:], grep_b[:, 512 * tc4:512 * (tc4 + 1)], ALU.mult)
                            v.drain()
                            c_dv.tag(v.nop())

                    for dhalf in range(2):
                        for d4 in range(4):
                            dout = 4 * dhalf + d4

                            @b.tensor
                            def _(te, e=e, fh=fh, tc4=tc4, dout=dout, d4=d4, dhalf=dhalf):
                                c_dv.wait(te)
                                c_ev.wait(te)
                                for k in range(16):
                                    ins = te.matmul(ps_y1[:, 512 * d4:512 * (d4 + 1)],
                                                    w2sl[k][:, 128 * dout:128 * (dout + 1)],
                                                    hsb[k][:], start=(k == 0), stop=(k == 15))
                                c_mm.tag(ins)

                            @b.vector
                            def _(v, e=e, fh=fh, tc4=tc4, dout=dout, d4=d4):
                                c_mm.wait(v)
                                if e == 0 and fh == 0:
                                    v.tensor_copy(self.facc[dout][:, 512 * tc4:512 * (tc4 + 1)],
                                                  ps_y1[:, 512 * d4:512 * (d4 + 1)])
                                else:
                                    v.tensor_tensor(self.facc[dout][:, 512 * tc4:512 * (tc4 + 1)],
                                                    self.facc[dout][:, 512 * tc4:512 * (tc4 + 1)],
                                                    ps_y1[:, 512 * d4:512 * (d4 + 1)], ALU.add)
                                v.drain()
                                c_dv.tag(v.nop())
        s11b.close()


    def s12_final_sum(self):
        b, d = self.b, self.dram
        c_mm, c_ev, c_dv, c_ld, c_sp = self.c_mm, self.c_ev, self.c_dv, self.c_ld, self.c_sp
        s12 = ExitStack()
        rs1t = self.sb("rs1t", [128, TC], stack=s12)
        x1ld = self.sb("x1ld", [128, TC], stack=s12)
        outf = [self.sb(f"outf{j}", [128, TC], F32, stack=s12) for j in range(NT)]
        outT = self.sb("outT", [128, D], F32, stack=s12)
        outT2 = self.sb("outT2", [128, D], F32, stack=s12)
        ps12 = self.psum("ps12", [128, 512], F32, stack=s12)

        for j in range(NT):
            @b.sync
            def _(sy, j=j):
                self.c_coll.wait(sy, mark=self.rs1_mark)
                c_dv.wait(sy)
                c_ld.tag(sy, sy.dma_start(out=rs1t[:], in_=d["rs1_dram"][128 * j:128 * (j + 1), :]), 16)
                c_ld.tag(sy, sy.dma_start(out=x1ld[:], in_=d["x1_dram"][128 * j:128 * (j + 1), :]), 16)
            r_mark = c_ld.mark()

            @b.vector
            def _(v, j=j):
                c_ld.wait(v, mark=r_mark)
                c_dv.wait(v)
                v.tensor_tensor(outf[j][:], rs1t[:], x1ld[:], ALU.add)
                v.drain()
                v.tensor_tensor(outf[j][:], outf[j][:], self.facc[j][:], ALU.add)
                v.drain()
                c_dv.tag(v.nop())

        for tg in range(16):
            ot = [outT, outT2][tg % 2]
            for j in range(NT):
                @b.tensor
                def _(te, tg=tg, j=j):
                    c_dv.wait(te)
                    c_ev.wait(te)
                    c_mm.tag(te.transpose(ps12[:, 128 * (j % 2):128 * (j % 2) + 128],
                                          outf[j][:, 128 * tg:128 * (tg + 1)], self.identf[:]))

                @b.scalar
                def _(sc, tg=tg, j=j, ot=ot):
                    c_mm.wait(sc)
                    if j == 0:
                        self.c_st.wait(sc)
                    c_ev.tag(sc.copy(out=ot[:, 128 * j:128 * (j + 1)], in_=ps12[:, 128 * (j % 2):128 * (j % 2) + 128]))

            @b.sync
            def _(sy, tg=tg, ot=ot):
                c_ev.wait(sy)
                self.c_st.tag(sy, sy.dma_start(out=d["out"][128 * tg:128 * (tg + 1), :], in_=ot[:]), 16)
        s12.close()

    def s_final(self):
        b, d = self.b, self.dram
        if self.stop_after is not None:
            zout = self.sb("zout", [128, D], F32)

            @b.vector
            def _(v):
                self.c_mm.wait(v)
                self.c_ev.wait(v)
                self.c_dv.tag(v.memset(zout[:], 0.0))

            @b.sync
            def _(sy):
                self.c_dv.wait(sy)
                for i in range(TC // 128):
                    self.c_st.tag(sy, sy.dma_start(out=d["out"][128 * i:128 * (i + 1), :], in_=zout[:]), 16)

        @b.sync
        def _(sy):
            self.c_dv.wait(sy)
            self.c_ev.wait(sy)
            self.c_mm.wait(sy)
            self.c_sp.wait(sy)
            self.c_ldb.wait(sy)
            self.c_ld.wait(sy)
            self.c_coll.wait(sy)
            self.c_st.wait(sy)


def build(dbg=(), stop_after=None):
    mk = MK(dbg=dbg, stop_after=stop_after)
    nc = mk.build()
    return nc, mk


def stage_inputs(ins):
    x = np.ascontiguousarray(ins["L0_attn_x"].reshape(TFULL, D))
    maps = []
    com = {
        "w_inproj": ins["L0_attn_in_proj_w"],
        "rms_w": ins["L0_attn_rms_w"].reshape(D, 1),
        "conv_w": ins["L0_attn_conv_w"],
        "conv_b": ins["L0_attn_conv_b"].reshape(DI, 1),
        "bcd_w": ins["L0_attn_bcd_w"],
        "dup_w": ins["L0_attn_dup_w"],
        "dup_b": ins["L0_attn_dup_b"].reshape(DI, 1),
        "lma": ins["L0_attn_log_minus_A"],
        "d_param": ins["L0_attn_D_param"].reshape(DI, 1),
        "w_outproj": ins["L0_attn_out_proj_w"],
        "r0_wT": np.ascontiguousarray(ins["L0_ffn_router_w"].T),
        "r0_b": ins["L0_ffn_router_b"].reshape(1, NE),
        "exp0_wT": np.ascontiguousarray(ins["L0_ffn_expert_w"].transpose(0, 2, 1)),
        "exp0_b": np.ascontiguousarray(ins["L0_ffn_expert_b"].reshape(NE, D, 1)),
        "wkT": np.ascontiguousarray(ins["L1_attn_Wk"].T),
        "wvT": np.ascontiguousarray(ins["L1_attn_Wv"].T),
        "wrT": np.ascontiguousarray(ins["L1_attn_Wr"].T),
        "woT": np.ascontiguousarray(ins["L1_attn_Wo"].T),
        "tdec": ins["L1_attn_time_decay"].reshape(D, 1),
        "tfirst": ins["L1_attn_time_first"].reshape(D, 1),
        "r1_wT": np.ascontiguousarray(ins["L1_ffn_router_w"].T),
        "w1": ins["L1_ffn_w1"],
        "w2": ins["L1_ffn_w2"],
    }
    com = {k: np.ascontiguousarray(v, dtype=np.float32) for k, v in com.items()}
    for c in range(NCORE):
        lo = c * TC
        xt = np.zeros((D, TC + 3), np.float32)
        xt[:, 3:] = x[lo:lo + TC].T
        if lo % 8192 != 0:
            xt[:, 0:3] = x[lo - 3:lo].T
        pm = np.zeros((NE, 1), np.float32)
        pm[:c] = 1.0
        m = dict(com)
        m["x_t"] = xt
        m["pmask"] = pm
        maps.append(m)
    return maps


_CACHED = {}


def kernel(**inputs):
    from concourse.bass_utils import run_bass_kernel_spmd
    if "nc" not in _CACHED:
        nc, _ = build(dbg=())
        _CACHED["nc"] = nc
    nc = _CACHED["nc"]
    maps = stage_inputs(inputs)
    res = run_bass_kernel_spmd(nc, maps, core_ids=list(range(NCORE)))
    out = np.concatenate([r["out"] for r in res.results], axis=0)
    return np.ascontiguousarray(out.reshape(2, 8192, D)).astype(np.float32)


# version 03
# speedup vs baseline: 1.3005x; 1.3005x over previous
from contextlib import ExitStack

import numpy as np
import concourse.bass as bass
import concourse.mybir as mybir

F32 = mybir.dt.float32
BF16 = mybir.dt.bfloat16
ALU = mybir.AluOpType
ACTF = mybir.ActivationFunctionType
AX = mybir.AxisListType

D = 1024
DI = 2048
NS = 16
DR = 64
TC = 2048
NT = D // 128
NTI = DI // 128
NCORE = 8
TFULL = TC * NCORE
EPS = 1e-5
CAP = 5120
FFN = 4096
NE = 8
DSH = DI // NCORE


class Ch:

    def __init__(self, es, nc, name, nsem=4):
        self.sems = [es.enter_context(nc.semaphore(f"{name}{i}")) for i in range(nsem)]
        self.tot = [0] * nsem
        self.guard = [0] * nsem
        self.cur = 0
        self.open = False

    def tag(self, eng, instr, k=1):
        i = self.cur
        if not self.open:
            if self.tot[i] > self.guard[i]:
                eng.wait_ge(self.sems[i], self.tot[i])
                self.guard[i] = self.tot[i]
            self.open = True
        instr.then_inc(self.sems[i], k)
        self.tot[i] += k
        return instr

    def mark(self):
        self.cur = (self.cur + 1) % len(self.sems)
        self.open = False
        return list(self.tot)

    def wait(self, eng, mark=None):
        if mark is None:
            mark = self.mark()
        for s, v in zip(self.sems, mark):
            if v > 0:
                eng.wait_ge(s, v)


class CCh:

    def __init__(self, es, nc, name):
        self.sem = es.enter_context(nc.semaphore(name))
        self.n = 0

    def tag(self, instr, k=1):
        instr.then_inc(self.sem, k)
        self.n += k
        return instr

    def mark(self):
        return self.n

    def wait(self, eng, mark=None):
        eng.wait_ge(self.sem, self.n if mark is None else mark)


class MK:
    def __init__(self, dbg=(), stop_after=None):
        self.dbg = set(dbg)
        self.stop_after = stop_after
        self.nc = bass.Bass("TRN2", target_bir_lowering=False)
        self.es = ExitStack()
        self.dram = {}

    def din(self, name, shape, dt=F32):
        self.dram[name] = self.nc.dram_tensor(name, shape, dt, kind="ExternalInput").ap()
        return self.dram[name]

    def dout(self, name, shape, dt=F32):
        self.dram[name] = self.nc.dram_tensor(name, shape, dt, kind="ExternalOutput").ap()
        return self.dram[name]

    def dint(self, name, shape, dt=BF16):
        kind = "ExternalOutput" if name in self.dbg else "Internal"
        self.dram[name] = self.nc.dram_tensor(name, shape, dt, kind=kind).ap()
        return self.dram[name]

    def sb(self, name, shape, dt=BF16, stack=None):
        return (stack or self.es).enter_context(self.nc.sbuf_tensor(name, shape, dt))

    def psum(self, name, shape, dt=F32, stack=None):
        return (stack or self.es).enter_context(self.nc.psum_tensor(name, shape, dt))

    def rk(self, eng, mult, add=0):
        if not hasattr(self, "_rkcache"):
            self._rkcache = {}
        key = (id(eng.engine) if hasattr(eng, "engine") else id(eng), mult, add)
        key = (str(eng.engine), mult, add)
        if key not in self._rkcache:
            rank = eng.partition_id()
            expr = rank * mult + add if add else rank * mult
            self._rkcache[key] = eng.compute_val(expr)
        return self._rkcache[key]


    def build(self):
        nc, es = self.nc, self.es
        self.declare_dram()
        self.b = es.enter_context(nc.Block())
        self.c_ld = Ch(self.es, self.nc, "ld", nsem=4)
        self.c_wst = Ch(self.es, self.nc, "wst", nsem=4)
        self.c_sp = Ch(self.es, self.nc, "sp", nsem=4)
        self.c_coll = Ch(self.es, self.nc, "coll", nsem=2)
        self.c_st = Ch(self.es, self.nc, "st", nsem=2)
        self.c_ldb = Ch(self.es, self.nc, "ldb", nsem=4)
        self.c_mm = CCh(self.es, self.nc, "mm")
        self.c_ev = CCh(self.es, self.nc, "ev")
        self.c_dv = CCh(self.es, self.nc, "dv")
        self.c_gp = CCh(self.es, self.nc, "gp")
        self.alloc_shared()
        self.s0_constants()
        self.s1_ln0()
        self.s2_l0_proj_conv()
        if self.stop_after != "s2":
            self.s3_collectives_l0()
            self.s4_l0_ffn()
            self.s5_scan()
            if self.stop_after != "s5":
                self.s6_gate_outproj()
                if self.stop_after != "s6":
                    self.s7_ln1()
                    self.s8_kvr()
                    if self.stop_after != "s8":
                        self.s9_rwkv()
                        if self.stop_after != "s9":
                            self.s10_wo()
                            self.s11_l1_ffn()
                            if self.stop_after != "s11":
                                self.s12_final_sum()
        self.s_final()
        es.close()
        return nc

    def alloc_shared(self):
        self.s0_stack = ExitStack()
        self.ln_stack = ExitStack()
        self.ps_stat = self.psum("ps_stat", [1, 1024], stack=self.ln_stack)
        self.ps_bc = self.psum("ps_bc", [128, 1536], stack=self.ln_stack)
        self.tmpc1 = self.sb("tmpc1", [128, 128], F32)
        self.tmpc2 = self.sb("tmpc2", [128, 128], F32)
        self.acol_stage = self.sb("acol_stage", [32, 128], F32)
        self.ps_misc = self.psum("ps_misc", [128, 512], F32, stack=self.s0_stack)
        self.identf = self.sb("identf", [128, 128], F32)
        self.sel127 = self.sb("sel127", [128, 1], F32)
        self.sqt = self.sb("sqt", [128, 512])
        self.lane = self.sb("lane", [1, 2560], F32)
        self.tnm = self.sb("tnm", [128, 512])
        self.bcds = self.sb("bcds", [96, TC])
        self.facc = [self.sb(f"facc{i}", [128, TC]) for i in range(NT)]
        self.lnld0 = self.sb("lnld0", [128, 512], F32)
        self.lnld1 = self.sb("lnld1", [128, 512], F32)
        self.lnldb0 = self.sb("lnldb0", [128, 512])
        self.lnldb1 = self.sb("lnldb1", [128, 512])

    def declare_dram(self):
        din, dout, dint = self.din, self.dout, self.dint
        din("x_t", [D, TC + 3])
        din("w_inproj", [D, 2 * DI])
        din("rms_w", [D, 1])
        din("conv_w", [DI, 4])
        din("conv_b", [DI, 1])
        din("bcd_w", [DI, 96])
        din("dup_w", [DR, DI])
        din("dup_b", [DI, 1])
        din("lma", [DI, NS])
        din("d_param", [DI, 1])
        din("w_outproj", [DI, D])
        din("r0_wT", [D, NE])
        din("r0_b", [1, NE])
        din("exp0_wT", [NE, D, D])
        din("exp0_b", [NE, D, 1])
        din("wkT", [D, D])
        din("wvT", [D, D])
        din("wrT", [D, D])
        din("woT", [D, D])
        din("tdec", [D, 1])
        din("tfirst", [D, 1])
        din("r1_wT", [D, NE])
        din("w1", [NE, D, FFN])
        din("w2", [NE, FFN, D])
        din("pmask", [NE, 1])
        dout("out", [TC, D])
        dint("xc_dram", [DI, TC])
        dint("res_dram", [DI, TC])
        dint("bcd_dram", [96, TC])
        dint("xc_all", [NCORE * DI, TC])
        dint("bcd_all", [NCORE * 96, TC])
        dint("delta_dram", [DSH, TFULL])
        dint("dx_dram", [DSH, TFULL])
        dint("y0_dram", [DSH, TFULL])
        dint("y0_all", [DI, TFULL])
        dint("res_all", [NCORE * DI, TC])
        dint("xc_my", [DSH, TFULL])
        dint("res_my", [DSH, TFULL])
        dint("k_my", [128, TFULL])
        dint("v_my", [128, TFULL])
        dint("r_my", [128, TFULL])
        dint("partial0_dram", [NCORE * D, TC])
        dint("rs0_dram", [D, TC])
        dint("partial1_dram", [NCORE * D, TC])
        dint("rs1_dram", [D, TC])
        dint("x1_dram", [D, TC])
        dint("wsel_dram", [NE, TC])
        dint("kvr_dram", [3 * D, TC])
        dint("kvr_all", [NCORE * 3 * D, TC])
        dint("y1_dram", [128, TFULL])
        dint("y1_all", [NCORE * 128, TFULL])
        dint("g1_dram", [NE, TC])
        dint("cnt_dram", [1, 2 * NE], F32)
        dint("cnt_all", [NCORE, 2 * NE], F32)
        dint("hdbg", [128, 4096])
        dint("r9dbg", [128, 512])
        dint("k9dbg", [128, 512])
        dint("kh9dbg", [128, 128])
        dint("khT9dbg", [128, 128])
        dint("vT9dbg", [128, 128])
        dint("abdbg", [128, 4096], F32)
        dint("bxdbg", [128, 4096])
        dint("lgdbg", [8, TC], F32)
        dint("xn_dram", [D, TC + 3])
        dint("xn1_dram", [D, TC])
        dint("xr_dram", [D, TC + 3])

    def s0_constants(self):
        b, nc, d = self.b, self.nc, self.dram
        c_ld, c_gp, c_dv, c_ev, c_mm = self.c_ld, self.c_gp, self.c_dv, self.c_ev, self.c_mm
        self.ident = self.sb("ident", [128, 128])
        self.utri = self.sb("utri", [128, 128])
        self.io_col = self.sb("io_col", [128, 128], F32)
        self.io_row = self.sb("io_row", [128, 128], F32)
        self.io_m16 = self.sb("io_m16", [128, 128], F32)
        self.io_m32 = self.sb("io_m32", [128, 128], F32)
        self.sum16 = self.sb("sum16", [128, 16 * 128])
        self.ones_col = self.sb("ones_col", [128, 1])
        self.ones_row = self.sb("ones_row", [1, 128])
        self.onesf_col = self.sb("onesf_col", [128, 1], F32)
        self.onesf_row = self.sb("onesf_row", [1, 128], F32)
        self.mk_diag = self.sb("mk_diag", [128, 128])
        self.mk_off = self.sb("mk_off", [128, 128])
        sm = {}
        sm["rms_w"] = self.sb("sm_rms_w", [128, NT], F32)
        sm["conv_b"] = self.sb("sm_conv_b", [128, NTI], F32)
        sm["d_param"] = self.sb("sm_d_param", [128, NTI], F32)
        sm["convw"] = self.sb("sm_convw", [128, 4 * NTI], F32)
        sm["dup_b"] = self.sb("sm_dup_b", [128, 2], F32)
        sm["d_param_my"] = self.sb("sm_d_param_my", [128, 2], F32)
        sm["exp0_b"] = self.sb("sm_exp0_b", [128, NE * NT], F32)
        sm["tdec"] = self.sb("sm_tdec", [128, 1], F32)
        sm["tfirst"] = self.sb("sm_tfirst", [128, 1], F32)
        sm["pmask"] = self.sb("sm_pmask", [NE, 1], F32)
        sm["acol"] = self.sb("sm_acol", [128, 32], F32)
        self.sm = sm

        @b.sync
        def _(sy):
            for j in range(NT):
                c_ld.tag(sy, sy.dma_start(out=sm["rms_w"][:, j:j + 1], in_=d["rms_w"][128 * j:128 * (j + 1), :]), 16)
            for j in range(NTI):
                c_ld.tag(sy, sy.dma_start(out=sm["conv_b"][:, j:j + 1], in_=d["conv_b"][128 * j:128 * (j + 1), :]), 16)
                c_ld.tag(sy, sy.dma_start(out=sm["d_param"][:, j:j + 1], in_=d["d_param"][128 * j:128 * (j + 1), :]), 16)
                c_ld.tag(sy, sy.dma_start(out=sm["convw"][:, 4 * j:4 * j + 4], in_=d["conv_w"][128 * j:128 * (j + 1), :]), 16)
            for e in range(NE):
                for i in range(NT):
                    c_ld.tag(sy, sy.dma_start(out=sm["exp0_b"][:, NT * e + i:NT * e + i + 1],
                                          in_=d["exp0_b"][e, 128 * i:128 * (i + 1), :]), 16)
            c_ld.tag(sy, sy.dma_start(out=sm["pmask"][:], in_=d["pmask"]), 16)
            off = self.rk(sy, DSH)
            c_ld.tag(sy, sy.dma_start(out=sm["dup_b"][:],
                                      in_=d["dup_b"][bass.ds(off, 256), :].rearrange("(a p) o -> p (a o)", a=2),
                                      allow_slow_non_contiguous=True), 16)
            c_ld.tag(sy, sy.dma_start(out=sm["d_param_my"][:],
                                      in_=d["d_param"][bass.ds(off, 256), :].rearrange("(a p) o -> p (a o)", a=2),
                                      allow_slow_non_contiguous=True), 16)
            c_ld.tag(sy, sy.dma_start(out=self.acol_stage[0:32, :],
                                      in_=d["lma"][bass.ds(off, 256), :].rearrange("(q g) n -> q (g n)", g=8)), 16)
            off3 = self.rk(sy, 128)
            c_ld.tag(sy, sy.dma_start(out=sm["tdec"][:], in_=d["tdec"][bass.ds(off3, 128), :]), 16)
            c_ld.tag(sy, sy.dma_start(out=sm["tfirst"][:], in_=d["tfirst"][bass.ds(off3, 128), :]), 16)

        @b.gpsimd
        def _(g):
            g.iota(self.io_col[:], pattern=[[1, 128]], base=0, channel_multiplier=0,
                   allow_small_or_imprecise_dtypes=True)
            g.iota(self.io_m16[:], pattern=[[0, 8], [1, 16]], base=0, channel_multiplier=0,
                   allow_small_or_imprecise_dtypes=True)
            g.iota(self.io_m32[:], pattern=[[0, 4], [1, 32]], base=0, channel_multiplier=0,
                   allow_small_or_imprecise_dtypes=True)
            c_gp.tag(g.iota(self.io_row[:], pattern=[[0, 128]], base=0, channel_multiplier=1,
                            allow_small_or_imprecise_dtypes=True))

        @b.vector
        def _(v):
            c_gp.wait(v)
            v.tensor_tensor(self.ident[:], self.io_col[:], self.io_row[:], ALU.is_equal)
            v.tensor_tensor(self.identf[:], self.io_col[:], self.io_row[:], ALU.is_equal)
            v.tensor_scalar(self.sel127[:], self.io_row[:, 0:1], 126.5, None, ALU.is_ge)
            v.tensor_tensor(self.utri[:], self.io_row[:], self.io_col[:], ALU.is_le)
            v.memset(self.ones_col[:], 1.0)
            v.memset(self.ones_row[:], 1.0)
            v.memset(self.onesf_col[:], 1.0)
            v.memset(self.onesf_row[:], 1.0)
            v.memset(self.sum16[:], 0.0)
            v.memset(self.mk_diag[:], 0.0)
            v.memset(self.mk_off[:], 0.0)
            t1, t2 = self.tmpc1, self.tmpc2
            v.drain()
            v.tensor_scalar(t1[:], self.io_col[:], 16.0, None, ALU.mult)
            v.drain()
            v.tensor_tensor(t2[:], self.io_row[:], t1[:], ALU.is_ge)
            v.drain()
            v.tensor_scalar(t1[:], t1[:], 16.0, None, ALU.add)
            v.drain()
            v.tensor_tensor(t1[:], self.io_row[:], t1[:], ALU.is_lt)
            v.drain()
            v.tensor_tensor(t2[:], t2[:], t1[:], ALU.mult)
            v.drain()
            for j in range(16):
                v.tensor_copy(self.sum16[:, 128 * j + 8 * j:128 * j + 8 * j + 8], t2[:, 0:8])
            v.drain()
            v.tensor_tensor(t1[:], self.io_col[:], self.io_row[:], ALU.subtract)
            v.drain()
            v.tensor_tensor(t2[:], t1[:], self.io_m16[:], ALU.is_le)
            v.drain()
            v.scalar_tensor_tensor(self.mk_diag[:], t1[:], 0.0, t2[:], ALU.is_ge, ALU.mult)
            v.drain()
            v.tensor_tensor(t2[:], t1[:], self.io_m32[:], ALU.subtract)
            v.drain()
            v.tensor_scalar(self.mk_off[:], t2[:], -15.0, 0.0, ALU.is_ge, op1=ALU.bypass) if False else None
            v.tensor_scalar(t1[:], t2[:], -15.0, None, ALU.is_ge)
            v.drain()
            v.tensor_scalar(t2[:], t2[:], 0.0, None, ALU.is_le)
            v.drain()
            v.tensor_tensor(t1[:], t1[:], t2[:], ALU.mult)
            v.drain()
            v.tensor_scalar(t2[:], self.io_m32[:], 16.0, None, ALU.is_ge)
            v.drain()
            v.tensor_tensor(self.mk_off[:], t1[:], t2[:], ALU.mult)
            v.drain()
            c_dv.tag(v.nop())

        @b.scalar
        def _(sc):
            c_ld.wait(sc)
            c_ev.tag(sc.activation(self.acol_stage[0:32, :], self.acol_stage[0:32, :], ACTF.Exp))

        @b.tensor
        def _(te):
            c_ev.wait(te)
            c_dv.wait(te)
            c_mm.tag(te.transpose(self.ps_misc[:, 0:32], self.acol_stage[0:32, :], self.identf[0:32, 0:32]))

        @b.vector
        def _(v):
            c_mm.wait(v)
            c_dv.tag(v.tensor_scalar(sm["acol"][:], self.ps_misc[:, 0:32], -1.0, None, ALU.mult))

    def s1_ln0(self):
        b, d = self.b, self.dram
        self.s1_stack = ExitStack()
        self.xn = [self.sb(f"xn{i}", [128, TC + 3]) for i in range(NT)]
        self.xr = [self.sb(f"xr{i}", [128, TC + 3], stack=self.s1_stack) for i in range(NT)]

        self.emit_ln(src=None, out_xn=self.xn, out_xr=self.xr, width=TC + 3,
                     rms_cols=self.sm["rms_w"], rms_col_idx=list(range(NT)),
                     src_dram=d["x_t"])

        if "xn_dram" in self.dbg:
            @b.sync
            def _(sy):
                self.c_dv.wait(sy)
                for i in range(NT):
                    self.c_sp.tag(sy, sy.dma_start(out=d["xn_dram"][128 * i:128 * (i + 1), :], in_=self.xn[i][:]), 16)
                    self.c_sp.tag(sy, sy.dma_start(out=d["xr_dram"][128 * i:128 * (i + 1), :], in_=self.xr[i][:]), 16)

    def emit_ln(self, src, out_xn, out_xr, width, rms_cols, rms_col_idx, src_f32=True,
                src_dram=None, dram_wait=None):
        b = self.b
        c_ld, c_mm, c_ev, c_dv = self.c_ld, self.c_mm, self.c_ev, self.c_dv
        sq, lane = self.sqt, self.lane
        ps_stat, ps_bc = self.ps_stat, self.ps_bc
        nchunk = (width + 511) // 512
        use_dram = src_dram is not None
        if use_dram:
            lnld = [self.lnld0, self.lnld1] if src_f32 else [self.lnldb0, self.lnldb1]

        def srcap(i, o, W, phase):
            if not use_dram:
                return src[i][:, o:o + W]
            return lnld[(i + phase) % 2][:, 0:W]

        for ci in range(nchunk):
            o = 512 * ci
            W = min(512, width - o)
            for i in range(NT):
                if use_dram:
                    @b.sync
                    def _(sy, i=i, o=o, W=W):
                        c_mm.wait(sy)
                        if dram_wait is not None:
                            dram_wait[0].wait(sy, mark=dram_wait[1])
                        c_ld.tag(sy, sy.dma_start(out=lnld[i % 2][:, 0:W],
                                                  in_=src_dram[128 * i:128 * (i + 1), o:o + W]), 16)
                @b.scalar
                def _(sc, i=i, o=o, W=W):
                    c_ld.wait(sc)
                    c_mm.wait(sc)
                    c_dv.wait(sc)
                    c_ev.tag(sc.activation(sq[:, 0:W], srcap(i, o, W, 0), ACTF.Square))

                @b.tensor
                def _(te, i=i, o=o, W=W):
                    c_ev.wait(te)
                    te.matmul(ps_stat[0:1, 0:W], self.onesf_col[:] if src_f32 else self.ones_col[:],
                              srcap(i, o, W, 0),
                              start=(i == 0), stop=(i == NT - 1))
                    c_mm.tag(te.matmul(ps_stat[0:1, 512:512 + W], self.ones_col[:], sq[:, 0:W],
                                       start=(i == 0), stop=(i == NT - 1)))

            @b.vector
            def _(v, o=o, W=W):
                c_mm.wait(v)
                v.tensor_scalar(lane[0:1, 0:W], ps_stat[0:1, 0:W], 1.0 / D, None, ALU.mult)
                v.tensor_scalar(lane[0:1, 512:512 + W], ps_stat[0:1, 512:512 + W], 1.0 / D, None, ALU.mult)
                v.drain()
                v.tensor_tensor(lane[0:1, 1024:1024 + W], lane[0:1, 0:W], lane[0:1, 0:W], ALU.mult)
                v.drain()
                v.tensor_tensor(lane[0:1, 1024:1024 + W], lane[0:1, 512:512 + W], lane[0:1, 1024:1024 + W], ALU.subtract)
                v.drain()
                c_dv.tag(v.tensor_scalar(lane[0:1, 1536:1536 + W], lane[0:1, 1024:1024 + W], EPS, None, ALU.add))

            @b.scalar
            def _(sc, o=o, W=W):
                c_dv.wait(sc)
                c_ev.tag(sc.activation(lane[0:1, 2048:2048 + W], lane[0:1, 1536:1536 + W], ACTF.Sqrt))

            @b.vector
            def _(v, o=o, W=W):
                c_ev.wait(v)
                v.reciprocal(lane[0:1, 512:512 + W], lane[0:1, 2048:2048 + W])
                v.drain()
                if out_xr is not None:
                    v.tensor_tensor(lane[0:1, 1536:1536 + W], lane[0:1, 1024:1024 + W], lane[0:1, 512:512 + W], ALU.mult)
                    v.drain()
                    v.tensor_tensor(lane[0:1, 1536:1536 + W], lane[0:1, 1536:1536 + W], lane[0:1, 512:512 + W], ALU.mult)
                    v.drain()
                    c_dv.tag(v.tensor_scalar(lane[0:1, 1536:1536 + W], lane[0:1, 1536:1536 + W], EPS, None, ALU.add))
                else:
                    c_dv.tag(v.nop())

            if out_xr is not None:
                @b.scalar
                def _(sc, o=o, W=W):
                    c_dv.wait(sc)
                    c_ev.tag(sc.activation(lane[0:1, 2048:2048 + W], lane[0:1, 1536:1536 + W], ACTF.Sqrt))

                @b.vector
                def _(v, o=o, W=W):
                    c_ev.wait(v)
                    v.reciprocal(lane[0:1, 1536:1536 + W], lane[0:1, 2048:2048 + W])
                    v.drain()
                    c_dv.tag(v.tensor_tensor(lane[0:1, 1536:1536 + W], lane[0:1, 1536:1536 + W], lane[0:1, 512:512 + W], ALU.mult))

            @b.tensor
            def _(te, o=o, W=W):
                c_dv.wait(te)
                c_mm.tag(te.matmul(ps_bc[:, 0:W], self.onesf_row[:], lane[0:1, 0:W], start=True, stop=True))
                c_mm.tag(te.matmul(ps_bc[:, 512:512 + W], self.onesf_row[:], lane[0:1, 512:512 + W], start=True, stop=True))
                if out_xr is not None:
                    c_mm.tag(te.matmul(ps_bc[:, 1024:1024 + W], self.onesf_row[:], lane[0:1, 1536:1536 + W], start=True, stop=True))

            for i in range(NT):
                if use_dram:
                    @b.sync
                    def _(sy, i=i, o=o, W=W):
                        c_dv.wait(sy)
                        if dram_wait is not None:
                            dram_wait[0].wait(sy, mark=dram_wait[1])
                        c_ld.tag(sy, sy.dma_start(out=lnld[i % 2][:, 0:W],
                                                  in_=src_dram[128 * i:128 * (i + 1), o:o + W]), 16)

                @b.vector
                def _(v, i=i, o=o, W=W):
                    c_mm.wait(v)
                    if use_dram:
                        c_ld.wait(v)
                    v.tensor_tensor(self.tnm[:, 0:W], srcap(i, o, W, 0), ps_bc[:, 0:W], ALU.subtract)
                    v.drain()
                    v.tensor_tensor(out_xn[i][:, o:o + W], self.tnm[:, 0:W], ps_bc[:, 512:512 + W], ALU.mult)
                    if out_xr is not None:
                        v.scalar_tensor_tensor(out_xr[i][:, o:o + W], self.tnm[:, 0:W],
                                               rms_cols[:, rms_col_idx[i]:rms_col_idx[i] + 1],
                                               ps_bc[:, 1024:1024 + W], ALU.mult, ALU.mult)
                    v.drain()
                    c_dv.tag(v.nop())

    def s2_l0_proj_conv(self):
        b, d = self.b, self.dram
        c_wst, c_mm, c_ev, c_dv, c_sp = self.c_wst, self.c_mm, self.c_ev, self.c_dv, self.c_sp
        self.s0_stack.close()
        self.ln_stack.close()
        s2 = ExitStack()
        wslab = [self.sb(f"wslab{k}", [128, D], stack=s2) for k in range(NT)]
        bcdw = self.sb("bcdw", [128, 96 * NTI], stack=s2)
        xs = self.sb("xs", [128, TC + 3], stack=s2)
        xs2 = self.sb("xs2", [128, TC + 3], stack=s2)
        cvt = self.sb("cvt", [128, TC], stack=s2)
        xct = self.sb("xct", [128, TC], stack=s2)
        xct2 = self.sb("xct2", [128, TC], stack=s2)
        rst = self.sb("rst", [128, TC], stack=s2)
        ps_p = self.psum("ps_p", [128, 1024], F32, stack=s2)
        ps_bcd = self.psum("ps_bcd", [96, TC], F32, stack=s2)

        @b.gpsimd
        def _(g):
            self.c_dv.wait(g)
            self.c_ev.wait(g)
            self.c_mm.wait(g)
            for j in range(NTI):
                c_wst.tag(g, g.dma_start(out=bcdw[:, 96 * j:96 * (j + 1)], in_=d["bcd_w"][128 * j:128 * (j + 1), :]), 16)

        CH = [(0, 512), (512, 512), (1024, 512), (1536, 512), (2048, 3)]
        xs_bufs = [xs, xs2]
        xc_bufs = [xct, xct2]
        ln_dv = self.c_dv.n
        for q in range(2):
            @b.gpsimd
            def _(g, q=q):
                self.c_dv.wait(g)
                if q > 0:
                    self.c_mm.wait(g)
                for k in range(NT):
                    c_wst.tag(g, g.dma_start(out=wslab[k][:], in_=d["w_inproj"][128 * k:128 * (k + 1), D * q:D * (q + 1)]), 16)
            wq_mark = c_wst.mark()

            for jj in range(NT):
                j = NT * q + jj
                xsb = xs_bufs[j % 2]
                xcb = xc_bufs[j % 2]
                for ci, (o, W) in enumerate(CH):
                    @b.tensor
                    def _(te, j=j, jj=jj, o=o, W=W, ci=ci, xsb=xsb, wq_mark=wq_mark):
                        c_wst.wait(te, mark=wq_mark)
                        self.c_dv.wait(te, mark=ln_dv)
                        c_ev.wait(te)
                        for k in range(NT):
                            ins = te.matmul(ps_p[:, 512 * (ci % 2):512 * (ci % 2) + W],
                                            wslab[k][:, 128 * jj:128 * (jj + 1)],
                                            self.xr[k][:, o:o + W],
                                            start=(k == 0), stop=(k == NT - 1))
                        c_mm.tag(ins)

                    @b.scalar
                    def _(sc, j=j, o=o, W=W, ci=ci, xsb=xsb):
                        c_mm.wait(sc)
                        c_dv.wait(sc)
                        c_ev.tag(sc.copy(out=xsb[:, o:o + W], in_=ps_p[:, 512 * (ci % 2):512 * (ci % 2) + W]))

                @b.vector
                def _(v, j=j, xsb=xsb):
                    c_ev.wait(v)
                    cw = self.sm["convw"]
                    v.tensor_scalar(cvt[:], xsb[:, 0:TC], cw[:, 4 * j:4 * j + 1], None, ALU.mult)
                    v.drain()
                    for k in range(1, 4):
                        v.scalar_tensor_tensor(cvt[:], xsb[:, k:k + TC], cw[:, 4 * j + k:4 * j + k + 1],
                                               cvt[:], ALU.mult, ALU.add)
                        v.drain()
                    c_dv.tag(v.nop())

                @b.scalar
                def _(sc, j=j, xcb=xcb):
                    c_dv.wait(sc)
                    c_sp.wait(sc)
                    self.c_mm.wait(sc)
                    c_ev.tag(sc.activation(xcb[:], cvt[:], ACTF.Silu, bias=self.sm["conv_b"][:, j:j + 1]))

                @b.sync
                def _(sy, j=j, xcb=xcb):
                    c_ev.wait(sy)
                    c_sp.tag(sy, sy.dma_start(out=d["xc_dram"][128 * j:128 * (j + 1), :], in_=xcb[:]), 16)

                for ci in range(4):
                    @b.tensor
                    def _(te, j=j, ci=ci, xcb=xcb):
                        c_ev.wait(te)
                        ins = te.matmul(ps_bcd[:, 512 * ci:512 * (ci + 1)], bcdw[:, 96 * j:96 * (j + 1)],
                                        xcb[:, 512 * ci:512 * (ci + 1)], start=(j == 0), stop=(j == NTI - 1))
                        if j == NTI - 1:
                            c_mm.tag(ins)

        @b.scalar
        def _(sc):
            c_mm.wait(sc)
            c_ev.tag(sc.copy(out=self.bcds[0:96, :], in_=ps_bcd[0:96, :]))

        @b.sync
        def _(sy):
            c_ev.wait(sy)
            c_sp.tag(sy, sy.dma_start(out=d["bcd_dram"], in_=self.bcds[0:96, :]), 16)

        for q in range(2):
            @b.gpsimd
            def _(g, q=q):
                self.c_mm.wait(g)
                for k in range(NT):
                    c_wst.tag(g, g.dma_start(out=wslab[k][:], in_=d["w_inproj"][128 * k:128 * (k + 1), DI + D * q:DI + D * (q + 1)]), 16)
            wq_mark = c_wst.mark()

            for jj in range(NT):
                j = NT * q + jj
                for ci in range(4):
                    o = 512 * ci

                    @b.tensor
                    def _(te, j=j, jj=jj, o=o, ci=ci, wq_mark=wq_mark):
                        c_wst.wait(te, mark=wq_mark)
                        c_ev.wait(te)
                        for k in range(NT):
                            ins = te.matmul(ps_p[:, 512 * (ci % 2):512 * (ci % 2) + 512],
                                            wslab[k][:, 128 * jj:128 * (jj + 1)],
                                            self.xr[k][:, 3 + o:3 + o + 512],
                                            start=(k == 0), stop=(k == NT - 1))
                        c_mm.tag(ins)

                    @b.scalar
                    def _(sc, j=j, o=o, ci=ci):
                        c_mm.wait(sc)
                        c_sp.wait(sc)
                        c_ev.tag(sc.activation(rst[:, o:o + 512], ps_p[:, 512 * (ci % 2):512 * (ci % 2) + 512], ACTF.Silu))

                @b.sync
                def _(sy, j=j):
                    c_ev.wait(sy)
                    c_sp.tag(sy, sy.dma_start(out=d["res_dram"][128 * j:128 * (j + 1), :], in_=rst[:]), 16)

        @b.sync
        def _(sy):
            c_sp.wait(sy)
            sy.nop()

        s2.close()
        self.s1_stack.close()

    def s3_collectives_l0(self):
        b, d = self.b, self.dram
        spm = self.c_sp.mark()

        @b.gpsimd
        def _(g):
            self.c_sp.wait(g, mark=spm)
            self.c_coll.tag(g, g.collective_compute(
                "AllGather", ALU.bypass, ins=[d["xc_dram"]], outs=[d["xc_all"]],
                replica_groups=[list(range(NCORE))]), 1)
            self.c_coll.tag(g, g.collective_compute(
                "AllGather", ALU.bypass, ins=[d["bcd_dram"]], outs=[d["bcd_all"]],
                replica_groups=[list(range(NCORE))]), 1)
        self.ag_l0_mark = self.c_coll.mark()

    def s4_l0_ffn(self):
        b, d = self.b, self.dram
        c_wst, c_mm, c_ev, c_dv, c_ld, c_sp = self.c_wst, self.c_mm, self.c_ev, self.c_dv, self.c_ld, self.c_sp
        s4 = ExitStack()
        r0t = self.sb("r0t", [128, 64], stack=s4)
        brep = self.sb("brep", [128, NE], F32, stack=s4)
        lg = self.sb("lg", [8, TC], F32, stack=s4)
        lgt = self.sb("lgt", [128, 128], F32, stack=s4)
        mx = self.sb("mx_s4", [128, 16], F32, stack=s4)
        smv = self.sb("smv_s4", [128, 16], F32, stack=s4)
        ohh = self.sb("ohh_s4", [128, 128], F32, stack=s4)
        wsel = self.sb("wsel_s4", [128, 128], F32, stack=s4)
        wst_sb = self.sb("wst_sb", [8, TC], stack=s4)
        grep = self.sb("grep", [128, TC], stack=s4)
        wsl = [self.sb(f"e0w{i}", [128, 2 * D], stack=s4) for i in range(NT)]
        tmp_e2 = [self.sb(f"tmp_e{i}", [128, 512], stack=s4) for i in range(2)]
        ps_l = self.psum("ps_l", [8, 512], F32, stack=s4)
        ps_t = self.psum("ps_t", [128, 512], F32, stack=s4)
        ps_f = self.psum("ps_f", [128, 1024], F32, stack=s4)

        @b.gpsimd
        def _(g):
            for k in range(NT):
                c_wst.tag(g, g.dma_start(out=r0t[:, 8 * k:8 * k + 8], in_=d["r0_wT"][128 * k:128 * (k + 1), :]), 16)
            c_wst.tag(g, g.dma_start(out=brep[:], in_=d["r0_b"].broadcast_to([128, NE])), 16)
        r0_mark = c_wst.mark()

        for ci in range(4):
            @b.tensor
            def _(te, ci=ci):
                c_wst.wait(te, mark=r0_mark)
                c_ev.wait(te)
                for k in range(NT):
                    ins = te.matmul(ps_l[:, 0:512], r0t[:, 8 * k:8 * k + 8],
                                    self.xn[k][:, 3 + 512 * ci:3 + 512 * (ci + 1)],
                                    start=(k == 0), stop=(k == NT - 1))
                c_mm.tag(ins)

            @b.scalar
            def _(sc, ci=ci):
                c_mm.wait(sc)
                c_ev.tag(sc.copy(out=lg[:, 512 * ci:512 * (ci + 1)], in_=ps_l[:, 0:512]))

        if "lgdbg" in self.dbg:
            @b.sync
            def _(sy):
                c_ev.wait(sy)
                c_sp.tag(sy, sy.dma_start(out=d["lgdbg"], in_=lg[0:8, :]), 16)

        for i in range(16):
            @b.tensor
            def _(te, i=i):
                c_ev.wait(te)
                c_dv.wait(te)
                c_mm.tag(te.transpose(ps_t[0:128, 0:8], lg[0:8, 128 * i:128 * (i + 1)], self.identf[0:8, 0:8]))

            @b.vector
            def _(v, i=i):
                c_mm.wait(v)
                c_dv.tag(v.tensor_tensor(lgt[:, 8 * i:8 * i + 8], ps_t[:, 0:8], brep[:], ALU.add))

        from bass_rust import AP as _AP

        @b.vector
        def _(v):
            c_dv.wait(v)
            lgt3 = _AP(lgt[:].tensor, lgt[:].offset, [[128, 128], [8, 16], [1, 8]])
            mxb = _AP(mx[:].tensor, mx[:].offset, [[16, 128], [1, 16], [0, 8]])
            v.tensor_reduce(mx[:], lgt3, axis=AX.X, op=ALU.max)
            v.drain()
            v.tensor_tensor(ohh[:], lgt3, mxb, ALU.subtract)
            v.drain()
            c_dv.tag(v.nop())

        @b.scalar
        def _(sc):
            c_dv.wait(sc)
            c_ev.tag(sc.activation(wsel[:], ohh[:], ACTF.Exp))

        @b.vector
        def _(v):
            c_ev.wait(v)
            lgt3 = _AP(lgt[:].tensor, lgt[:].offset, [[128, 128], [8, 16], [1, 8]])
            mxb = _AP(mx[:].tensor, mx[:].offset, [[16, 128], [1, 16], [0, 8]])
            smb = _AP(smv[:].tensor, smv[:].offset, [[16, 128], [1, 16], [0, 8]])
            ws3 = _AP(wsel[:].tensor, wsel[:].offset, [[128, 128], [8, 16], [1, 8]])
            v.tensor_reduce(smv[:], ws3, axis=AX.X, op=ALU.add)
            v.drain()
            v.reciprocal(smv[:], smv[:])
            v.drain()
            v.tensor_tensor(ohh[:], lgt3, mxb, ALU.is_ge)
            v.drain()
            v.tensor_tensor(wsel[:], ohh[:], smb, ALU.mult)
            v.drain()
            c_dv.tag(v.nop())

        for i in range(16):
            @b.tensor
            def _(te, i=i):
                c_dv.wait(te)
                c_ev.wait(te)
                c_mm.tag(te.transpose(ps_t[0:8, 0:128], wsel[:, 8 * i:8 * i + 8], self.identf[:]))

            @b.scalar
            def _(sc, i=i):
                c_mm.wait(sc)
                c_ev.tag(sc.copy(out=wst_sb[0:8, 128 * i:128 * (i + 1)], in_=ps_t[0:8, 0:128]))

        @b.sync
        def _(sy):
            c_ev.wait(sy)
            c_sp.tag(sy, sy.dma_start(out=d["wsel_dram"], in_=wst_sb[0:8, :]), 16)
        wsel_mark = c_sp.mark()

        for e in range(NE):
            half = e % 2

            @b.gpsimd
            def _(g, e=e, half=half):
                if e >= 2:
                    c_mm.wait(g)
                for k in range(NT):
                    c_wst.tag(g, g.dma_start(out=wsl[k][:, D * half:D * (half + 1)],
                                             in_=d["exp0_wT"][e, 128 * k:128 * (k + 1), :]), 16)
            w_mark = c_wst.mark()

            @b.sync
            def _(sy, e=e):
                c_sp.wait(sy, mark=wsel_mark)
                c_dv.wait(sy)
                c_ld.tag(sy, sy.dma_start(out=grep[:],
                                          in_=d["wsel_dram"][e:e + 1, :].broadcast_to([128, TC])), 16)
            g_mark = c_ld.mark()

            mm_g = []
            ev_g = []

            @b.tensor
            def _(te, e=e, half=half):
                c_wst.wait(te, mark=w_mark)
                ev_base = c_ev.n
                g = 0
                for j in range(NT):
                    for ci in range(4):
                        if g == 0:
                            c_ev.wait(te)
                            c_dv.wait(te, mark=max(0, c_dv.n - 2))
                        elif g >= 2:
                            te.wait_ge(c_ev.sem, ev_base + g - 1)
                        for k in range(NT):
                            ins = te.matmul(ps_f[:, 512 * (g % 2):512 * (g % 2) + 512],
                                            wsl[k][:, D * half + 128 * j:D * half + 128 * (j + 1)],
                                            self.xn[k][:, 3 + 512 * ci:3 + 512 * (ci + 1)],
                                            start=(k == 0), stop=(k == NT - 1))
                        c_mm.tag(ins)
                        mm_g.append(c_mm.n)
                        g += 1

            @b.scalar
            def _(sc, e=e):
                dv0 = getattr(self, "_s4_dv", 0)
                sc.wait_ge(self.c_dv.sem, dv0)
                dv_base0 = self.c_dv.n
                g = 0
                for j in range(NT):
                    for ci in range(4):
                        sc.wait_ge(c_mm.sem, mm_g[g])
                        if g >= 2:
                            sc.wait_ge(self.c_dv.sem, dv_base0 + g - 1)
                        c_ev.tag(sc.activation(tmp_e2[g % 2][:], ps_f[:, 512 * (g % 2):512 * (g % 2) + 512],
                                               ACTF.Identity, bias=self.sm["exp0_b"][:, NT * e + j:NT * e + j + 1]))
                        ev_g.append(c_ev.n)
                        g += 1

            @b.vector
            def _(v, e=e):
                c_ld.wait(v, mark=g_mark)
                g = 0
                for j in range(NT):
                    for ci in range(4):
                        v.wait_ge(c_ev.sem, ev_g[g])
                        if e == 0:
                            c_dv.tag(v.tensor_tensor(self.facc[j][:, 512 * ci:512 * (ci + 1)], tmp_e2[g % 2][:],
                                                     grep[:, 512 * ci:512 * (ci + 1)], ALU.mult))
                        else:
                            v.tensor_tensor(tmp_e2[g % 2][:], tmp_e2[g % 2][:], grep[:, 512 * ci:512 * (ci + 1)], ALU.mult)
                            v.drain()
                            c_dv.tag(v.tensor_tensor(self.facc[j][:, 512 * ci:512 * (ci + 1)],
                                                     self.facc[j][:, 512 * ci:512 * (ci + 1)], tmp_e2[g % 2][:], ALU.add))
                        g += 1
                v.drain()
                c_dv.tag(v.nop())
            self._s4_dv = c_dv.n
        s4.close()

    def s5_scan(self):
        b, d = self.b, self.dram
        c_wst, c_mm, c_ev, c_dv, c_ld, c_sp = self.c_wst, self.c_mm, self.c_ev, self.c_dv, self.c_ld, self.c_sp
        s5 = ExitStack()
        dupw = self.sb("dupw", [64, DSH], stack=s5)
        dlr = self.sb("dlr", [64, TC], stack=s5)
        xcd = self.sb("xcd", [128, 2 * TC], stack=s5)
        dsb = self.sb("dsb", [128, TC], stack=s5)
        spz = self.sb("spz", [128, 512], F32, stack=s5)
        spc = self.sb("spc", [128, 512], F32, stack=s5)
        dxsb = self.sb("dxsb", [128, TC], stack=s5)
        ps_d = self.psum("ps_d", [128, 1024], F32, stack=s5)

        @b.gpsimd
        def _(g):
            off = g.compute_val(g.partition_id() * DSH)
            c_wst.tag(g, g.dma_start(out=dupw[:], in_=d["dup_w"][:, bass.ds(off, DSH)]), 16)
        dupw_mark = c_wst.mark()

        @b.scalar
        def _(sc2):
            off = self.rk(sc2, DSH)
            self.c_coll.wait(sc2, mark=self.ag_l0_mark)
            xa = d["xc_all"].rearrange("(r n) t -> n r t", n=DI)
            self.c_ldb.tag(sc2, sc2.dma_start(
                out=d["xc_my"].rearrange("n (r t) -> n r t", r=NCORE),
                in_=xa[bass.ds(off, DSH), :, :]), 16)
        self.xcmy_mark = self.c_ldb.mark()

        for r in range(NCORE):
            @b.sync
            def _(sy, r=r):
                self.c_coll.wait(sy, mark=self.ag_l0_mark)
                c_dv.wait(sy)
                c_ld.tag(sy, sy.dma_start(out=dlr[:], in_=d["bcd_all"][96 * r:96 * r + 64, :]), 16)
            dlr_mark = c_ld.mark()

            @b.scalar
            def _(sc2, r=r):
                off = self.rk(sc2, DSH)
                xa = d["xc_all"].rearrange("(r n) t -> r n t", n=DI)
                c_dv.wait(sc2)
                self.c_ldb.tag(sc2, sc2.dma_start(
                    out=xcd[:].rearrange("p (a t) -> p a t", a=2), in_=xa[r][bass.ds(off, 256), :].rearrange("(a p) t -> p a t", a=2)), 16)
            xcd_mark = self.c_ldb.mark()
            for dt in range(2):

                for ci in range(4):
                    @b.tensor
                    def _(te, r=r, dt=dt, ci=ci):
                        c_wst.wait(te, mark=dupw_mark)
                        c_ld.wait(te, mark=dlr_mark)
                        c_ev.wait(te)
                        c_mm.tag(te.matmul(ps_d[:, 512 * (ci % 2):512 * (ci % 2) + 512],
                                           dupw[:, 128 * dt:128 * (dt + 1)],
                                           dlr[:, 512 * ci:512 * (ci + 1)], start=True, stop=True))

                    @b.scalar
                    def _(sc, r=r, dt=dt, ci=ci):
                        c_mm.wait(sc)
                        c_dv.wait(sc)
                        if ci == 0:
                            c_sp.wait(sc)
                        c_ev.tag(sc.activation(spz[:], ps_d[:, 512 * (ci % 2):512 * (ci % 2) + 512],
                                               ACTF.Identity, bias=self.sm["dup_b"][:, dt:dt + 1]))

                    @b.vector
                    def _(v, r=r, dt=dt, ci=ci):
                        c_ev.wait(v)
                        v.tensor_scalar(spc[:], spz[:], 17.0, None, ALU.min)
                        v.drain()
                        c_dv.tag(v.nop())

                    @b.scalar
                    def _(sc, r=r, dt=dt, ci=ci):
                        c_dv.wait(sc)
                        c_ev.tag(sc.activation(spc[:], spc[:], ACTF.Exp))

                    @b.vector
                    def _(v, r=r, dt=dt, ci=ci):
                        c_ev.wait(v)
                        v.tensor_scalar(spc[:], spc[:], 1.0, None, ALU.add)
                        v.drain()
                        c_dv.tag(v.nop())

                    @b.scalar
                    def _(sc, r=r, dt=dt, ci=ci):
                        c_dv.wait(sc)
                        c_ev.tag(sc.activation(spc[:], spc[:], ACTF.Ln))

                    @b.vector
                    def _(v, r=r, dt=dt, ci=ci):
                        c_ev.wait(v)
                        v.tensor_scalar(spz[:], spz[:], 17.0, 0.0, ALU.subtract, op1=ALU.max)
                        v.drain()
                        v.tensor_tensor(dsb[:, 512 * ci:512 * (ci + 1)], spc[:], spz[:], ALU.add)
                        v.drain()
                        c_dv.tag(v.nop())

                @b.vector
                def _(v, r=r, dt=dt):
                    c_ev.wait(v)
                    self.c_ldb.wait(v, mark=xcd_mark)
                    c_sp.wait(v)
                    v.tensor_tensor(dxsb[:], dsb[:], xcd[:, TC * dt:TC * (dt + 1)], ALU.mult)
                    v.drain()
                    c_dv.tag(v.nop())

                @b.sync
                def _(sy, r=r, dt=dt):
                    c_dv.wait(sy)
                    c_sp.tag(sy, sy.dma_start(out=d["delta_dram"][128 * dt:128 * (dt + 1), TC * r:TC * (r + 1)], in_=dsb[:]), 16)
                    c_sp.tag(sy, sy.dma_start(out=d["dx_dram"][128 * dt:128 * (dt + 1), TC * r:TC * (r + 1)], in_=dxsb[:]), 16)

        ddx_mark = c_sp.mark()
        s5.close()

        s5b = ExitStack()
        TB = 4096
        brep_t = self.sb("brep_t", [128, TB], stack=s5b)
        crep_t = self.sb("crep_t", [128, TB], stack=s5b)
        drep = self.sb("drep", [128, TB], stack=s5b)
        dxrep = self.sb("dxrep", [128, TB], stack=s5b)
        ab_t = self.sb("ab_t", [128, TB], F32, stack=s5b)
        bx_t = self.sb("bx_t", [128, TB], stack=s5b)
        h_t = self.sb("h_t", [128, TB], stack=s5b)
        yv_t = self.sb("yv_t", [128, TB], stack=s5b)
        ysb = self.sb("ysb", [128, TB], stack=s5b)
        carry = self.sb("carry", [128, 32], F32, stack=s5b)
        ps_y = self.psum("ps_y", [128, TB], F32, stack=s5b)

        for c in range(TFULL // TB):
            @b.sync
            def _(sy, c=c):
                c_sp.wait(sy, mark=ddx_mark)
                c_dv.wait(sy)
                c_mm.wait(sy)
                for g in range(8):
                    for rb in range(TB // TC):
                        r = (TB * c) // TC + rb
                        c_ld.tag(sy, sy.dma_start(
                            out=brep_t[16 * g:16 * g + 16, TC * rb:TC * (rb + 1)],
                            in_=d["bcd_all"][96 * r + 64:96 * r + 80, :]), 16)
                        c_ld.tag(sy, sy.dma_start(
                            out=crep_t[16 * g:16 * g + 16, TC * rb:TC * (rb + 1)],
                            in_=d["bcd_all"][96 * r + 80:96 * r + 96, :]), 16)
            bc_mark = c_ld.mark()

            for half in range(2):
                for rr in range(16):
                    rt = 16 * half + rr

                    @b.sync
                    def _(sy, c=c, rt=rt):
                        c_dv.wait(sy)
                        c_ev.wait(sy)
                        for g in range(8):
                            row = 8 * rt + g
                            c_ld.tag(sy, sy.dma_start(
                                out=drep[16 * g:16 * g + 16, :],
                                in_=d["delta_dram"][row:row + 1, TB * c:TB * (c + 1)].broadcast_to([16, TB])), 16)
                            c_ld.tag(sy, sy.dma_start(
                                out=dxrep[16 * g:16 * g + 16, :],
                                in_=d["dx_dram"][row:row + 1, TB * c:TB * (c + 1)].broadcast_to([16, TB])), 16)
                    rep_mark = c_ld.mark()

                    @b.scalar
                    def _(sc, c=c, rt=rt):
                        c_ld.wait(sc, mark=rep_mark)
                        if "hdbg" in self.dbg:
                            c_sp.wait(sc)
                        c_ev.tag(sc.activation(ab_t[:], drep[:], ACTF.Exp,
                                               scale=self.sm["acol"][:, rt:rt + 1]))

                    @b.vector
                    def _(v, c=c, rt=rt):
                        c_ev.wait(v)
                        c_mm.wait(v)
                        c_ld.wait(v, mark=bc_mark)
                        v.tensor_tensor(bx_t[:], dxrep[:], brep_t[:], ALU.mult)
                        v.drain()
                        init = 0.0 if c % 2 == 0 else carry[:, rt:rt + 1]
                        v.tensor_tensor_scan(h_t[:], ab_t[:], bx_t[:], init, ALU.mult, ALU.add)
                        v.drain()
                        v.tensor_copy(carry[:, rt:rt + 1], h_t[:, TB - 1:TB])
                        v.tensor_tensor(yv_t[:], h_t[:], crep_t[:], ALU.mult)
                        v.drain()
                        c_dv.tag(v.nop())

                    if c == 0 and rt == 3 and "hdbg" in self.dbg:
                        @b.sync
                        def _(sy):
                            c_dv.wait(sy)
                            c_sp.tag(sy, sy.dma_start(out=d["hdbg"], in_=h_t[:]), 16)
                            c_sp.tag(sy, sy.dma_start(out=d["abdbg"], in_=ab_t[:]), 16)
                            c_sp.tag(sy, sy.dma_start(out=d["bxdbg"], in_=bx_t[:]), 16)

                    for s in range(TB // 512):
                        @b.tensor
                        def _(te, c=c, rt=rt, rr=rr, s=s, half=half):
                            c_dv.wait(te)
                            if rr == 0 and s == 0:
                                c_ev.wait(te)
                            ins = te.matmul(ps_y[:, 512 * s:512 * (s + 1)],
                                            self.sum16[:, 128 * rr:128 * (rr + 1)],
                                            yv_t[:, 512 * s:512 * (s + 1)],
                                            start=(rr == 0), stop=(rr == 15))
                            if s == TB // 512 - 1:
                                c_mm.tag(ins)

                @b.scalar
                def _(sc, c=c, half=half):
                    c_mm.wait(sc)
                    c_sp.wait(sc)
                    c_ev.tag(sc.activation(ysb[:], ps_y[:], ACTF.Copy))

                @b.sync
                def _(sy, c=c, half=half):
                    c_ev.wait(sy)
                    c_sp.tag(sy, sy.dma_start(out=d["y0_dram"][128 * half:128 * (half + 1), TB * c:TB * (c + 1)],
                                              in_=ysb[:]), 16)
        self.y0_mark = self.c_sp.mark()
        s5b.close()

        @b.gpsimd
        def _(g):
            self.c_coll.tag(g, g.collective_compute(
                "AllGather", ALU.bypass, ins=[d["res_dram"]], outs=[d["res_all"]],
                replica_groups=[list(range(NCORE))]), 1)
        self.ag_res_mark = self.c_coll.mark()

        @b.scalar
        def _(sc2):
            off = self.rk(sc2, DSH)
            self.c_coll.wait(sc2, mark=self.ag_res_mark)
            ra = d["res_all"].rearrange("(r n) t -> n r t", n=DI)
            self.c_ldb.tag(sc2, sc2.dma_start(
                out=d["res_my"].rearrange("n (r t) -> n r t", r=NCORE),
                in_=ra[bass.ds(off, DSH), :, :]), 16)
        self.resmy_mark = self.c_ldb.mark()

        s5c = ExitStack()
        y0sl = self.sb("y0sl", [128, 2 * TC], stack=s5c)
        xcd2 = self.sb("xcd2", [128, 2 * TC], stack=s5c)
        ressl = self.sb("ressl", [128, 2 * TC], stack=s5c)
        a0in = [self.sb(f"a0in{i}", [128, TC], stack=s5c) for i in range(2)]
        woslab = self.sb("woslab", [128, 2 * D], stack=s5c)
        pstg = self.sb("pstg", [128, 512], stack=s5c)
        ps_o = self.psum("ps_o", [128, 1024], F32, stack=s5c)

        @b.gpsimd
        def _(g):
            c_mm.wait(g)
            c_dv.wait(g)
            c_ev.wait(g)
            off = g.compute_val(g.partition_id() * DSH)
            c_wst.tag(g, g.dma_start(
                out=woslab[:].rearrange("p (a q) -> p a q", a=2),
                in_=d["w_outproj"][bass.ds(off, 256), :].rearrange("(a p) q -> p a q", a=2)), 16)
        wo_mark = c_wst.mark()

        for r in range(NCORE):
            @b.sync
            def _(sy, r=r):
                c_dv.wait(sy)
                c_sp.wait(sy, mark=self.y0_mark)
                c_ld.tag(sy, sy.dma_start(out=y0sl[:].rearrange("p (a t) -> p a t", a=2),
                                          in_=d["y0_dram"][:, TC * r:TC * (r + 1)]
                                          .rearrange("(a p) t -> p a t", a=2)), 16)
            y0_ld = c_ld.mark()

            @b.scalar
            def _(sc2, r=r):
                c_dv.wait(sc2)
                self.c_ldb.wait(sc2, mark=self.resmy_mark)
                self.c_ldb.tag(sc2, sc2.dma_start(
                    out=xcd2[:].rearrange("p (a t) -> p a t", a=2),
                    in_=d["xc_my"][:, TC * r:TC * (r + 1)].rearrange("(a p) t -> p a t", a=2)), 16)
                self.c_ldb.tag(sc2, sc2.dma_start(
                    out=ressl[:].rearrange("p (a t) -> p a t", a=2),
                    in_=d["res_my"][:, TC * r:TC * (r + 1)].rearrange("(a p) t -> p a t", a=2)), 16)
            xr_ld = self.c_ldb.mark()

            @b.vector
            def _(v, r=r):
                c_ld.wait(v, mark=y0_ld)
                self.c_ldb.wait(v, mark=xr_ld)
                c_mm.wait(v)
                for dt in range(2):
                    v.scalar_tensor_tensor(a0in[dt][:], xcd2[:, TC * dt:TC * (dt + 1)],
                                           self.sm["d_param_my"][:, dt:dt + 1],
                                           y0sl[:, TC * dt:TC * (dt + 1)], ALU.mult, ALU.add)
                    v.drain()
                    v.tensor_tensor(a0in[dt][:], a0in[dt][:], ressl[:, TC * dt:TC * (dt + 1)], ALU.mult)
                    v.drain()
                c_dv.tag(v.nop())

            for jo in range(NT):
                for ci in range(4):
                    @b.tensor
                    def _(te, r=r, jo=jo, ci=ci):
                        c_wst.wait(te, mark=wo_mark)
                        c_dv.wait(te)
                        c_ev.wait(te)
                        for dt in range(2):
                            ins = te.matmul(ps_o[:, 512 * (ci % 2):512 * (ci % 2) + 512],
                                            woslab[:, 2 * 128 * jo + 256 * 0 + dt * D if False else D * dt + 128 * jo:D * dt + 128 * (jo + 1)],
                                            a0in[dt][:, 512 * ci:512 * (ci + 1)],
                                            start=(dt == 0), stop=(dt == 1))
                        c_mm.tag(ins)

                    @b.scalar
                    def _(sc, r=r, jo=jo, ci=ci):
                        c_mm.wait(sc)
                        c_sp.wait(sc)
                        c_ev.tag(sc.activation(pstg[:], ps_o[:, 512 * (ci % 2):512 * (ci % 2) + 512], ACTF.Copy))

                    @b.sync
                    def _(sy, r=r, jo=jo, ci=ci):
                        c_ev.wait(sy)
                        c_sp.tag(sy, sy.dma_start(
                            out=d["partial0_dram"][D * r + 128 * jo:D * r + 128 * (jo + 1), 512 * ci:512 * (ci + 1)],
                            in_=pstg[:]), 16)
        p0_mark = c_sp.mark()
        s5c.close()

        @b.gpsimd
        def _(g):
            self.c_sp.wait(g, mark=p0_mark)
            self.c_coll.tag(g, g.collective_compute(
                "ReduceScatter", ALU.add, ins=[d["partial0_dram"]], outs=[d["rs0_dram"]],
                replica_groups=[list(range(NCORE))]), 1)
        self.rs0_mark = self.c_coll.mark()

    def s6_gate_outproj(self):
        b, d = self.b, self.dram
        c_mm, c_ev, c_dv, c_ld, c_sp = self.c_mm, self.c_ev, self.c_dv, self.c_ld, self.c_sp
        s6 = ExitStack()
        rs0t = self.sb("rs0t", [128, TC], stack=s6)
        xf6 = self.sb("x6frag", [128, TC], F32, stack=s6)
        t6 = self.sb("t6", [128, TC], F32, stack=s6)
        x1s = self.sb("x1s", [128, TC], stack=s6)
        x1s2 = self.sb("x1s2", [128, TC], stack=s6)
        for j in range(NT):
            @b.sync
            def _(sy, j=j):
                self.c_coll.wait(sy, mark=self.rs0_mark)
                c_dv.wait(sy)
                c_ld.tag(sy, sy.dma_start(out=rs0t[:], in_=d["rs0_dram"][128 * j:128 * (j + 1), :]), 16)
                c_ld.tag(sy, sy.dma_start(out=xf6[:], in_=d["x_t"][128 * j:128 * (j + 1), 3:]), 16)
            rj_mark = c_ld.mark()
            x1b = [x1s, x1s2][j % 2]

            @b.vector
            def _(v, j=j, x1b=x1b):
                c_ld.wait(v, mark=rj_mark)
                c_sp.wait(v)
                v.tensor_tensor(t6[:], rs0t[:], self.xn[j][:, 3:], ALU.add)
                v.drain()
                v.tensor_tensor(t6[:], t6[:], self.facc[j][:], ALU.add)
                v.drain()
                v.tensor_tensor(x1b[:], t6[:], xf6[:], ALU.add)
                v.drain()
                c_dv.tag(v.nop())

            @b.sync
            def _(sy, j=j, x1b=x1b):
                c_dv.wait(sy)
                c_sp.tag(sy, sy.dma_start(out=d["x1_dram"][128 * j:128 * (j + 1), :], in_=x1b[:]), 16)
        s6.close()

    def s7_ln1(self):
        b, d = self.b, self.dram
        self.ln2_stack = ExitStack()
        self.ps_stat = self.psum("ps_stat2", [1, 1024], stack=self.ln2_stack)
        self.ps_bc = self.psum("ps_bc2", [128, 1536], stack=self.ln2_stack)
        x1_mark = self.c_sp.mark()
        self.emit_ln(src=None, out_xn=self.xn, out_xr=None, width=TC,
                     rms_cols=None, rms_col_idx=None, src_f32=False,
                     src_dram=d["x1_dram"], dram_wait=(self.c_sp, x1_mark))
        if "xn1_dram" in self.dbg:
            @b.sync
            def _(sy):
                self.c_dv.wait(sy)
                for j in range(NT):
                    self.c_sp.tag(sy, sy.dma_start(out=d["xn1_dram"][128 * j:128 * (j + 1), :],
                                                   in_=self.xn[j][:, 0:TC]), 16)

    def s8_kvr(self):
        b, d = self.b, self.dram
        c_wst, c_mm, c_ev, c_dv, c_sp = self.c_wst, self.c_mm, self.c_ev, self.c_dv, self.c_sp
        self.ln2_stack.close()
        s8 = ExitStack()
        wsl8 = [self.sb(f"w8_{k}", [128, D], stack=s8) for k in range(NT)]
        kst2 = [self.sb(f"kst{i}", [128, TC], stack=s8) for i in range(NT)]
        ps_k = self.psum("ps_k", [128, 1024], F32, stack=s8)
        names = ["wkT", "wvT", "wrT"]
        for wi, wname in enumerate(names):
            @b.gpsimd
            def _(g, wi=wi, wname=wname):
                c_mm.wait(g)
                for k in range(NT):
                    c_wst.tag(g, g.dma_start(out=wsl8[k][:], in_=d[wname][128 * k:128 * (k + 1), :]), 16)
            w_mark = c_wst.mark()
            mm_g8 = []
            ev_g8 = []

            @b.tensor
            def _(te, wi=wi):
                c_wst.wait(te, mark=w_mark)
                c_dv.wait(te)
                ev_base = c_ev.n
                g = 0
                for j in range(NT):
                    for ci in range(4):
                        if g == 0:
                            c_ev.wait(te)
                        elif g >= 2:
                            te.wait_ge(c_ev.sem, ev_base + g - 1)
                        for k in range(NT):
                            ins = te.matmul(ps_k[:, 512 * (g % 2):512 * (g % 2) + 512],
                                            wsl8[k][:, 128 * j:128 * (j + 1)],
                                            self.xn[k][:, 512 * ci:512 * (ci + 1)],
                                            start=(k == 0), stop=(k == NT - 1))
                        c_mm.tag(ins)
                        mm_g8.append(c_mm.n)
                        g += 1

            @b.scalar
            def _(sc, wi=wi):
                sp0 = getattr(self, "_s8_sp", None)
                if sp0 is not None:
                    c_sp.wait(sc, mark=sp0)
                fn = ACTF.Sigmoid if wi == 2 else ACTF.Copy
                g = 0
                for j in range(NT):
                    for ci in range(4):
                        sc.wait_ge(c_mm.sem, mm_g8[g])
                        c_ev.tag(sc.activation(kst2[j][:, 512 * ci:512 * (ci + 1)],
                                               ps_k[:, 512 * (g % 2):512 * (g % 2) + 512], fn))
                        ev_g8.append(c_ev.n)
                        g += 1

            @b.sync
            def _(sy, wi=wi):
                g = 0
                for j in range(NT):
                    sy.wait_ge(c_ev.sem, ev_g8[4 * j + 3])
                    c_sp.tag(sy, sy.dma_start(out=d["kvr_dram"][D * wi + 128 * j:D * wi + 128 * (j + 1), :],
                                              in_=kst2[j][:]), 16)
            self._s8_sp = c_sp.mark()
        kvr_mark = c_sp.mark()
        s8.close()

        @b.gpsimd
        def _(g):
            c_sp.wait(g, mark=kvr_mark)
            self.c_coll.tag(g, g.collective_compute(
                "AllGather", ALU.bypass, ins=[d["kvr_dram"]], outs=[d["kvr_all"]],
                replica_groups=[list(range(NCORE))]), 1)
        self.ag_kvr_mark = self.c_coll.mark()


    def s9_rwkv(self):
        LV = getattr(self, 's9_level', 99)
        b, d = self.b, self.dram
        c_wst, c_mm, c_ev, c_dv, c_ld, c_sp = self.c_wst, self.c_mm, self.c_ev, self.c_dv, self.c_ld, self.c_sp
        s9 = ExitStack()
        kch = self.sb("kch", [128, 4096], stack=s9)
        vch = self.sb("vch", [128, 4096], stack=s9)
        rch = self.sb("rch", [128, 4096], stack=s9)
        kap = self.sb("kap", [128, 4096], stack=s9)
        wpow = self.sb("wpow", [128, 33], F32, stack=s9)
        wipow = self.sb("wipow", [128, 16], F32, stack=s9)
        p16 = self.sb("p16", [128, 128], stack=s9)
        p16i = self.sb("p16i", [128, 128], stack=s9)
        p16i_h = [self.sb(f"p16ih{h}", [128, 128], stack=s9) for h in range(2)]
        p16f_h = [self.sb(f"p16fh{h}", [128, 128], stack=s9) for h in range(2)]
        hmask = self.sb("hmask", [128, 2], F32, stack=s9)
        pmaskc = self.sb("pmaskc", [128, 4], F32, stack=s9)
        khTm = [self.sb(f"khTm{p}", [128, 128], stack=s9) for p in range(4)]
        r2h = [self.sb(f"r2h{h}", [128, 128], stack=s9) for h in range(2)]
        p16f = self.sb("p16f", [128, 128], stack=s9)
        p32 = self.sb("p32", [128, 128], stack=s9)
        p32r = self.sb("p32r", [128, 128], stack=s9)
        a1 = self.sb("a1", [128, 128], stack=s9)
        r2 = self.sb("r2", [128, 128], stack=s9)
        b1 = self.sb("b1", [128, 128], stack=s9)
        b2 = self.sb("b2", [128, 128], stack=s9)
        b1b = self.sb("b1b", [128, 128], stack=s9)
        b2b = self.sb("b2b", [128, 128], stack=s9)
        kh = self.sb("kh", [128, 128], stack=s9)
        khT = self.sb("khT", [128, 128], stack=s9)
        vT = self.sb("vT", [128, 128], stack=s9)
        msb = [self.sb(f"msb{h}", [128, 128], stack=s9) for h in range(2)]
        mtmp = self.sb("mtmp", [128, 128], stack=s9)
        shist = self.sb("shist", [128, 5 * 64], stack=s9)
        y9stg = self.sb("y9stg", [128, 4096], stack=s9)
        wo1slab = self.sb("wo1slab", [128, D], stack=s9)
        p1stg = self.sb("p1stg", [128, 512], stack=s9)
        ps_tr = self.psum("ps_tr", [128, 1024], BF16, stack=s9)
        ps_m = self.psum("ps_m", [128, 512], F32, stack=s9)
        ps_m2 = self.psum("ps_m2", [128, 512], F32, stack=s9)
        ps_kv = self.psum("ps_kv", [128, 512], F32, stack=s9)
        ps_y9 = self.psum("ps_y9", [128, 512], F32, stack=s9)
        ps_p1 = self.psum("ps_p1", [128, 1024], F32, stack=s9)

        @b.gpsimd
        def _(g):
            self.c_coll.wait(g, mark=self.ag_kvr_mark)
            c_mm.wait(g)
            c_dv.wait(g)
            c_ev.wait(g)
            kvr_a = d["kvr_all"].rearrange("(o q) t -> q o t", o=NCORE)
            for wi, nm in enumerate(["k_my", "v_my", "r_my"]):
                off = self.rk(g, 128, 1024 * wi)
                c_wst.tag(g, g.dma_start(out=d[nm].rearrange("n (o t) -> n o t", o=NCORE),
                                         in_=kvr_a[bass.ds(off, 128), :, :]), 16)
            off2 = self.rk(g, 128)
            c_wst.tag(g, g.dma_start(out=wo1slab[:], in_=d["woT"][bass.ds(off2, 128), :]), 16)
        my_mark = c_wst.mark()

        @b.vector
        def _(v):
            c_ld.wait(v)
            c_ev.wait(v)
            c_sp.wait(v)
            c_mm.wait(v)
            v.memset(wpow[:, 0:1], 1.0)
            v.drain()
            for j in range(32):
                v.tensor_scalar(wpow[:, j + 1:j + 2], wpow[:, j:j + 1], self.sm["tdec"][:, 0:1], None, ALU.mult)
                v.drain()
            v.reciprocal(wipow[:, 1:2], self.sm["tdec"][:, 0:1])
            v.memset(wipow[:, 0:1], 1.0)
            v.drain()
            for j in range(1, 15):
                v.tensor_tensor(wipow[:, j + 1:j + 2], wipow[:, j:j + 1], wipow[:, 1:2], ALU.mult)
                v.drain()
            from bass_rust import AP as _AP
            v.tensor_copy(p16[:], _AP(wpow[:].tensor, wpow[:].offset, [[33, 128], [0, 8], [1, 16]]))
            v.tensor_copy(p16i[:], _AP(wipow[:].tensor, wipow[:].offset, [[16, 128], [0, 8], [1, 16]]))
            v.tensor_copy(p16f[:], _AP(wpow[:].tensor, wpow[:].offset + 16, [[33, 128], [0, 8], [-1, 16]]))
            v.tensor_copy(p32[:], _AP(wpow[:].tensor, wpow[:].offset + 31, [[33, 128], [0, 4], [-1, 32]]))
            v.tensor_copy(p32r[:], _AP(wpow[:].tensor, wpow[:].offset + 1, [[33, 128], [0, 4], [1, 32]]))
            v.drain()
            v.tensor_scalar(hmask[:, 0:1], self.io_row[:, 0:1], 63.5, None, ALU.is_le)
            v.tensor_scalar(hmask[:, 1:2], self.io_row[:, 0:1], 63.5, None, ALU.is_gt if hasattr(ALU, 'is_gt') else ALU.is_ge)
            for pp in range(4):
                v.tensor_scalar(self.tmpc1[:, 0:1], self.io_row[:, 0:1], 32.0 * pp - 0.5, None, ALU.is_ge)
                v.drain()
                v.tensor_scalar(self.tmpc2[:, 0:1], self.io_row[:, 0:1], 32.0 * (pp + 1) - 0.5, None, ALU.is_lt)
                v.drain()
                v.tensor_tensor(pmaskc[:, pp:pp + 1], self.tmpc1[:, 0:1], self.tmpc2[:, 0:1], ALU.mult)
                v.drain()
            for h in range(2):
                v.tensor_tensor(p16i_h[h][:], p16i[:], hmask[:, h:h + 1].broadcast_to([128, 128]), ALU.mult)
                v.tensor_tensor(p16f_h[h][:], p16f[:], hmask[:, h:h + 1].broadcast_to([128, 128]), ALU.mult)
            v.drain()
            c_dv.tag(v.nop())

        for cc in range(4):
            @b.sync
            def _(sy, cc=cc):
                c_wst.wait(sy, mark=my_mark)
                c_dv.wait(sy)
                c_ev.wait(sy)
                c_mm.wait(sy)
                c_ld.tag(sy, sy.dma_start(out=kch[:], in_=d["k_my"][:, 4096 * cc:4096 * (cc + 1)]), 16)
                c_ld.tag(sy, sy.dma_start(out=vch[:], in_=d["v_my"][:, 4096 * cc:4096 * (cc + 1)]), 16)
                c_ld.tag(sy, sy.dma_start(out=rch[:], in_=d["r_my"][:, 4096 * cc:4096 * (cc + 1)]), 16)
            ch_mark = c_ld.mark()

            @b.vector
            def _(v, cc=cc):
                c_ld.wait(v, mark=ch_mark)
                v.tensor_scalar(kap[:], kch[:], self.sm["tfirst"][:, 0:1], None, ALU.mult)
                v.drain()
                if cc % 2 == 0:
                    v.memset(shist[:, 0:64], 0.0)
                    v.drain()
                c_dv.tag(v.nop())

            for tt in range(32):
                sl = slice(128 * tt, 128 * (tt + 1))

                @b.vector
                def _(v, cc=cc, tt=tt, sl=sl):
                    c_mm.wait(v)
                    c_ev.wait(v)
                    if "kh9dbg" in self.dbg:
                        c_sp.wait(v)
                    v.tensor_tensor(a1[:], rch[:, sl], p16[:], ALU.mult)
                    v.tensor_tensor(r2[:], rch[:, sl], p32r[:], ALU.mult)
                    v.tensor_tensor(b1[:], kap[:, sl], p16i_h[0][:], ALU.mult)
                    v.tensor_tensor(b1b[:], kap[:, sl], p16i_h[1][:], ALU.mult)
                    v.tensor_tensor(b2[:], kap[:, sl], p16f_h[0][:], ALU.mult)
                    v.tensor_tensor(b2b[:], kap[:, sl], p16f_h[1][:], ALU.mult)
                    v.tensor_tensor(kh[:], kap[:, sl], p32[:], ALU.mult)
                    v.drain()
                    for h in range(2):
                        v.tensor_tensor(r2h[h][:], r2[:], hmask[:, h:h + 1].broadcast_to([128, 128]), ALU.mult)
                    v.drain()
                    c_dv.tag(v.nop())

                if LV < 2:
                    continue

                @b.tensor
                def _(te, cc=cc, tt=tt, sl=sl):
                    c_dv.wait(te)
                    c_ev.wait(te)
                    te.transpose(ps_tr[:, 0:128], kh[:], self.ident[:])
                    c_mm.tag(te.transpose(ps_tr[:, 128:256], vch[:, sl], self.ident[:]))

                @b.scalar
                def _(sc, cc=cc, tt=tt):
                    c_mm.wait(sc)
                    sc.copy(out=khT[:], in_=ps_tr[:, 0:128])
                    c_ev.tag(sc.copy(out=vT[:], in_=ps_tr[:, 128:256]))

                if cc == 0 and tt == 0 and "kh9dbg" in self.dbg:
                    @b.sync
                    def _(sy):
                        c_ev.wait(sy)
                        c_sp.tag(sy, sy.dma_start(out=d["r9dbg"], in_=rch[:, 0:512]), 16)
                        c_sp.tag(sy, sy.dma_start(out=d["k9dbg"], in_=kap[:, 0:512]), 16)
                        c_sp.tag(sy, sy.dma_start(out=d["kh9dbg"], in_=kh[:]), 16)
                        c_sp.tag(sy, sy.dma_start(out=d["khT9dbg"], in_=khT[:]), 16)
                        c_sp.tag(sy, sy.dma_start(out=d["vT9dbg"], in_=vT[:]), 16)

                if LV < 3:
                    continue

                @b.vector
                def _(v, cc=cc, tt=tt):
                    c_ev.wait(v)
                    for p in range(4):
                        v.tensor_tensor(khTm[p][:], khT[:], pmaskc[:, p:p + 1].broadcast_to([128, 128]), ALU.mult)
                    v.drain()
                    c_dv.tag(v.nop())

                @b.tensor
                def _(te, cc=cc, tt=tt):
                    c_ev.wait(te)
                    c_dv.wait(te)
                    te.matmul(ps_m[:, 0:128], b1[:], a1[:], start=True, stop=True, skip_group_check=True)
                    te.matmul(ps_m[:, 128:256], b1b[:], a1[:], start=True, stop=True, skip_group_check=True)
                    te.matmul(ps_m2[:, 0:128], b2[:], a1[:], start=True, stop=True, skip_group_check=True)
                    te.matmul(ps_m2[:, 128:256], b2b[:], a1[:], start=True, stop=True, skip_group_check=True)
                    for p in range(4):
                        ins = te.matmul(ps_kv[:, 128 * p:128 * (p + 1)], khTm[p][:], vT[:],
                                        start=True, stop=True, skip_group_check=True)
                    c_mm.tag(ins)

                if LV < 4:
                    continue

                @b.vector
                def _(v, cc=cc, tt=tt):
                    c_mm.wait(v)
                    for h in range(2):
                        v.tensor_tensor(msb[h][:], ps_m[:, 128 * h:128 * (h + 1)], self.mk_diag[:], ALU.mult)
                        v.drain()
                        v.tensor_tensor(mtmp[:], ps_m2[:, 128 * h:128 * (h + 1)], self.mk_off[:], ALU.mult)
                        v.drain()
                        v.tensor_tensor(msb[h][:], msb[h][:], mtmp[:], ALU.add)
                        v.drain()
                    for p in range(4):
                        v.scalar_tensor_tensor(shist[0:64, 64 * (p + 1):64 * (p + 2)],
                                               shist[0:64, 64 * p:64 * (p + 1)],
                                               wpow[0:64, 32:33],
                                               ps_kv[0:64, 128 * p:128 * p + 64], ALU.mult, ALU.add)
                        v.scalar_tensor_tensor(shist[64:128, 64 * (p + 1):64 * (p + 2)],
                                               shist[64:128, 64 * p:64 * (p + 1)],
                                               wpow[64:128, 32:33],
                                               ps_kv[64:128, 128 * p + 64:128 * (p + 1)], ALU.mult, ALU.add)
                        v.drain()
                    c_dv.tag(v.nop())

                if LV < 5:
                    continue

                @b.tensor
                def _(te, cc=cc, tt=tt):
                    c_dv.wait(te)
                    c_ev.wait(te)
                    for h in range(2):
                        te.matmul(ps_y9[64 * h:64 * (h + 1), 0:128], vT[:, 64 * h:64 * (h + 1)],
                                  msb[h][:], start=True, stop=False,
                                  tile_position=(0, 64 * h), skip_group_check=True)
                    for h in range(2):
                        for p in range(4):
                            ins = te.matmul(ps_y9[64 * h:64 * (h + 1), 32 * p:32 * (p + 1)],
                                            shist[:, 64 * p:64 * (p + 1)],
                                            r2h[h][:, 32 * p:32 * (p + 1)],
                                            start=False, stop=(h == 1 and p == 3),
                                            tile_position=(0, 64 * h), skip_group_check=True)
                    c_mm.tag(ins)

                @b.scalar
                def _(sc, cc=cc, tt=tt, sl=sl):
                    c_mm.wait(sc)
                    c_ev.tag(sc.copy(out=y9stg[:, sl], in_=ps_y9[:, 0:128]))


                @b.vector
                def _(v, cc=cc, tt=tt):
                    c_ev.wait(v)
                    v.tensor_copy(shist[:, 0:64], shist[:, 256:320])
                    v.drain()
                    c_dv.tag(v.nop())

            for dout in range(NT):
                for ci in range(8):
                    @b.tensor
                    def _(te, cc=cc, dout=dout, ci=ci):
                        c_ev.wait(te)
                        c_dv.wait(te)
                        c_mm.tag(te.matmul(ps_p1[:, 512 * (ci % 2):512 * (ci % 2) + 512],
                                           wo1slab[:, 128 * dout:128 * (dout + 1)],
                                           y9stg[:, 512 * ci:512 * (ci + 1)],
                                           start=True, stop=True))

                    @b.scalar
                    def _(sc, cc=cc, dout=dout, ci=ci):
                        c_mm.wait(sc)
                        c_sp.wait(sc)
                        c_ev.tag(sc.activation(p1stg[:], ps_p1[:, 512 * (ci % 2):512 * (ci % 2) + 512], ACTF.Copy))

                    @b.sync
                    def _(sy, cc=cc, dout=dout, ci=ci):
                        c_ev.wait(sy)
                        gt = 4096 * cc + 512 * ci
                        owner = gt // TC
                        lt = gt - TC * owner
                        c_sp.tag(sy, sy.dma_start(
                            out=d["partial1_dram"][D * owner + 128 * dout:D * owner + 128 * (dout + 1), lt:lt + 512],
                            in_=p1stg[:]), 16)
        p1_mark = c_sp.mark()
        s9.close()

        @b.gpsimd
        def _(g):
            c_sp.wait(g, mark=p1_mark)
            self.c_coll.tag(g, g.collective_compute(
                "ReduceScatter", ALU.add, ins=[d["partial1_dram"]], outs=[d["rs1_dram"]],
                replica_groups=[list(range(NCORE))]), 1)
        self.rs1_mark = self.c_coll.mark()

    def s10_wo(self):
        pass

    def s11_l1_ffn(self):
        b, d = self.b, self.dram
        c_wst, c_mm, c_ev, c_dv, c_ld, c_sp = self.c_wst, self.c_mm, self.c_ev, self.c_dv, self.c_ld, self.c_sp
        from bass_rust import AP as _AP
        s11 = ExitStack()
        r1t = self.sb("r1t", [128, 64], stack=s11)
        lg1 = self.sb("lg1", [8, TC], F32, stack=s11)
        lgt1 = self.sb("lgt1", [128, 128], F32, stack=s11)
        mx1 = self.sb("mx1", [128, 16], F32, stack=s11)
        mx2 = self.sb("mx2", [128, 16], F32, stack=s11)
        g1v = self.sb("g1v", [128, 16], F32, stack=s11)
        g2v = self.sb("g2v", [128, 16], F32, stack=s11)
        oh1 = self.sb("oh1", [128, 128], F32, stack=s11)
        oh2 = self.sb("oh2", [128, 128], F32, stack=s11)
        l2t = self.sb("l2t", [128, 128], F32, stack=s11)
        poscum = [self.sb(f"poscum{k}", [128, 128], F32, stack=s11) for k in range(2)]
        carry1 = self.sb("carry1", [1, 2 * NE], F32, stack=s11)
        cntsb = self.sb("cntsb", [NCORE, 2 * NE], F32, stack=s11)
        prefsb = self.sb("prefsb", [128, 2 * NE], F32, stack=s11)
        gfin = self.sb("gfin", [128, 128], F32, stack=s11)
        gstg = self.sb("gstg", [8, TC], stack=s11)
        grep1 = self.sb("grep1", [128, TC], stack=s11)
        utrif = self.sb("utrif", [128, 128], F32, stack=s11)
        ps_l1 = self.psum("ps_l1", [8, 512], F32, stack=s11)
        ps_t1 = self.psum("ps_t1", [128, 512], F32, stack=s11)
        ps_c1 = self.psum("ps_c1", [128, 512], F32, stack=s11)

        @b.gpsimd
        def _(g):
            c_mm.wait(g)
            c_dv.wait(g)
            for k in range(NT):
                c_wst.tag(g, g.dma_start(out=r1t[:, 8 * k:8 * k + 8], in_=d["r1_wT"][128 * k:128 * (k + 1), :]), 16)
        r1_mark = c_wst.mark()

        @b.vector
        def _(v):
            self.c_gp.wait(v)
            v.tensor_copy(utrif[:], self.utri[:])
            v.drain()
            c_dv.tag(v.nop())

        for ci in range(4):
            @b.tensor
            def _(te, ci=ci):
                c_wst.wait(te, mark=r1_mark)
                c_ev.wait(te)
                c_dv.wait(te)
                for k in range(NT):
                    ins = te.matmul(ps_l1[:, 0:512], r1t[:, 8 * k:8 * k + 8],
                                    self.xn[k][:, 512 * ci:512 * (ci + 1)],
                                    start=(k == 0), stop=(k == NT - 1))
                c_mm.tag(ins)

            @b.scalar
            def _(sc, ci=ci):
                c_mm.wait(sc)
                c_ev.tag(sc.copy(out=lg1[:, 512 * ci:512 * (ci + 1)], in_=ps_l1[:, 0:512]))

        for i in range(16):
            @b.tensor
            def _(te, i=i):
                c_ev.wait(te)
                c_dv.wait(te)
                c_mm.tag(te.transpose(ps_t1[0:128, 0:8], lg1[0:8, 128 * i:128 * (i + 1)], self.identf[0:8, 0:8]))

            @b.vector
            def _(v, i=i):
                c_mm.wait(v)
                c_dv.tag(v.tensor_copy(lgt1[:, 8 * i:8 * i + 8], ps_t1[:, 0:8]))

        @b.vector
        def _(v):
            c_dv.wait(v)
            lgt3 = _AP(lgt1[:].tensor, lgt1[:].offset, [[128, 128], [8, 16], [1, 8]])
            mx1b = _AP(mx1[:].tensor, mx1[:].offset, [[16, 128], [1, 16], [0, 8]])
            mx2b = _AP(mx2[:].tensor, mx2[:].offset, [[16, 128], [1, 16], [0, 8]])
            v.tensor_reduce(mx1[:], lgt3, axis=AX.X, op=ALU.max)
            v.drain()
            v.tensor_tensor(oh1[:], lgt3, mx1b, ALU.is_ge)
            v.drain()
            v.tensor_scalar(l2t[:], oh1[:], -1e30, None, ALU.mult)
            v.drain()
            v.tensor_tensor(l2t[:], lgt1[:], l2t[:], ALU.add)
            v.drain()
            l2t3 = _AP(l2t[:].tensor, l2t[:].offset, [[128, 128], [8, 16], [1, 8]])
            v.tensor_reduce(mx2[:], l2t3, axis=AX.X, op=ALU.max)
            v.drain()
            v.tensor_tensor(oh2[:], l2t3, mx2b, ALU.is_ge)
            v.drain()
            v.tensor_tensor(g2v[:], mx2[:], mx1[:], ALU.subtract)
            v.drain()
            c_dv.tag(v.nop())

        @b.scalar
        def _(sc):
            c_dv.wait(sc)
            c_ev.tag(sc.activation(g2v[:], g2v[:], ACTF.Exp))

        @b.vector
        def _(v):
            c_ev.wait(v)
            v.tensor_scalar(g1v[:], g2v[:], 1.0, None, ALU.add)
            v.drain()
            v.reciprocal(g1v[:], g1v[:])
            v.drain()
            v.tensor_tensor(g2v[:], g2v[:], g1v[:], ALU.mult)
            v.drain()
            c_dv.tag(v.nop())

        @b.vector
        def _(v):
            c_dv.wait(v)
            v.memset(carry1[:], 0.0)
            v.drain()
            c_dv.tag(v.nop())

        for i in range(16):
            for k in range(2):
                oh = [oh1, oh2][k]

                @b.tensor
                def _(te, i=i, k=k, oh=oh):
                    c_dv.wait(te)
                    te.matmul(ps_c1[:, 8 * k:8 * (k + 1)], utrif[:], oh[:, 8 * i:8 * i + 8],
                              start=True, stop=False, skip_group_check=True)
                    c_mm.tag(te.matmul(ps_c1[:, 8 * k:8 * (k + 1)], self.onesf_row[:], carry1[0:1, 8 * k:8 * (k + 1)],
                                       start=False, stop=True, skip_group_check=True))

                @b.vector
                def _(v, i=i, k=k):
                    c_mm.wait(v)
                    v.tensor_copy(poscum[k][:, 8 * i:8 * i + 8], ps_c1[:, 8 * k:8 * (k + 1)])
                    v.drain()
                    c_dv.tag(v.nop())

                @b.tensor
                def _(te, i=i, k=k):
                    c_dv.wait(te)
                    c_mm.tag(te.matmul(ps_c1[0:1, 16 + 8 * k:24 + 8 * k], self.sel127[:],
                                       poscum[k][:, 8 * i:8 * i + 8], start=True, stop=True,
                                       skip_group_check=True))

                @b.vector
                def _(v, i=i, k=k):
                    c_mm.wait(v)
                    v.tensor_copy(carry1[0:1, 8 * k:8 * (k + 1)], ps_c1[0:1, 16 + 8 * k:24 + 8 * k])
                    v.drain()
                    c_dv.tag(v.nop())

        @b.sync
        def _(sy):
            c_dv.wait(sy)
            c_sp.tag(sy, sy.dma_start(out=d["cnt_dram"], in_=carry1[0:1, :]), 16)
        cnt_mark = c_sp.mark()

        @b.gpsimd
        def _(g):
            c_sp.wait(g, mark=cnt_mark)
            self.c_coll.tag(g, g.collective_compute(
                "AllGather", ALU.bypass, ins=[d["cnt_dram"]], outs=[d["cnt_all"]],
                replica_groups=[list(range(NCORE))]), 1)
        cnt_ag = self.c_coll.mark()

        @b.sync
        def _(sy):
            self.c_coll.wait(sy, mark=cnt_ag)
            c_ld.tag(sy, sy.dma_start(out=cntsb[:], in_=d["cnt_all"]), 16)
        cnt_ld = c_ld.mark()

        @b.tensor
        def _(te):
            c_ld.wait(te, mark=cnt_ld)
            c_dv.wait(te)
            c_mm.tag(te.matmul(ps_c1[0:1, 0:16], self.sm["pmask"][:], cntsb[:], start=True, stop=True, skip_group_check=True))

        @b.vector
        def _(v):
            c_mm.wait(v)
            v.tensor_copy(prefsb[0:1, :], ps_c1[0:1, 0:16])
            v.drain()
            c_dv.tag(v.nop())

        @b.tensor
        def _(te):
            c_dv.wait(te)
            c_mm.tag(te.matmul(ps_c1[:, 0:16], self.onesf_row[:], prefsb[0:1, :], start=True, stop=True, skip_group_check=True))

        @b.vector
        def _(v):
            c_mm.wait(v)
            v.tensor_copy(prefsb[:], ps_c1[:, 0:16])
            v.drain()
            for k in range(2):
                pref3 = _AP(prefsb[:].tensor, prefsb[:].offset + 8 * k, [[2 * NE, 128], [0, 16], [1, 8]])
                gk3 = _AP([g1v, g2v][k][:].tensor, [g1v, g2v][k][:].offset, [[16, 128], [1, 16], [0, 8]])
                dst = [oh1, oh2][k]
                v.tensor_tensor(poscum[k][:], poscum[k][:], pref3, ALU.add)
                v.drain()
                v.tensor_scalar(poscum[k][:], poscum[k][:], CAP + 0.5, None, ALU.is_le)
                v.drain()
                v.tensor_tensor(dst[:], dst[:], poscum[k][:], ALU.mult)
                v.drain()
                v.tensor_tensor(dst[:], dst[:], gk3, ALU.mult)
                v.drain()
            v.tensor_tensor(gfin[:], oh1[:], oh2[:], ALU.add)
            v.drain()
            c_dv.tag(v.nop())

        for i in range(16):
            @b.tensor
            def _(te, i=i):
                c_dv.wait(te)
                c_ev.wait(te)
                c_mm.tag(te.transpose(ps_t1[0:8, 0:128], gfin[:, 8 * i:8 * i + 8], self.identf[:]))

            @b.scalar
            def _(sc, i=i):
                c_mm.wait(sc)
                c_ev.tag(sc.copy(out=gstg[0:8, 128 * i:128 * (i + 1)], in_=ps_t1[0:8, 0:128]))

        @b.sync
        def _(sy):
            c_ev.wait(sy)
            c_sp.tag(sy, sy.dma_start(out=d["g1_dram"], in_=gstg[0:8, :]), 16)
        g1_mark = c_sp.mark()
        s11.close()

        s11b = ExitStack()
        w1sl = [self.sb(f"w1s{k}", [128, 2048], stack=s11b) for k in range(NT)]
        w2sl = [self.sb(f"w2s{k}", [128, D], stack=s11b) for k in range(16)]
        hsb = [self.sb(f"hsb{f}", [128, 512], stack=s11b) for f in range(16)]
        grep_b = self.sb("grep1b", [128, TC], stack=s11b)
        tmp11 = self.sb("tmp11", [128, 512], stack=s11b)
        ps_h = self.psum("ps_h", [128, 1024], F32, stack=s11b)
        ps_y1 = self.psum("ps_y1", [128, 2048], F32, stack=s11b)

        gate_marks = {}
        for e in range(NE):
            for fh in range(2):
                @b.gpsimd
                def _(g, e=e, fh=fh):
                    c_mm.wait(g)
                    if e == 0 and fh == 0:
                        c_sp.wait(g, mark=g1_mark)
                        c_dv.wait(g)
                        c_ev.wait(g)
                    for k in range(NT):
                        c_wst.tag(g, g.dma_start(out=w1sl[k][:],
                                                 in_=d["w1"][e, 128 * k:128 * (k + 1), 2048 * fh:2048 * (fh + 1)]), 16)
                    for k in range(16):
                        c_wst.tag(g, g.dma_start(out=w2sl[k][:],
                                                 in_=d["w2"][e, 2048 * fh + 128 * k:2048 * fh + 128 * (k + 1), 0:D]), 16)
                w_mark = c_wst.mark()

                if fh == 0:
                    @b.sync
                    def _(sy, e=e):
                        c_sp.wait(sy, mark=g1_mark)
                        if e > 0:
                            c_dv.wait(sy, mark=gate_marks.get(e - 1))
                        c_ld.tag(sy, sy.dma_start(out=grep_b[:],
                                                  in_=d["g1_dram"][e:e + 1, :].broadcast_to([128, TC])), 16)
                    self._s11_gemark = c_ld.mark()

                for tc4 in range(4):
                    prev_y_mark = getattr(self, "_s11_prev_y", 0)
                    mm_h = []
                    ev_h = []

                    @b.tensor
                    def _(te, e=e, fh=fh, tc4=tc4, prev_y_mark=prev_y_mark):
                        c_wst.wait(te, mark=w_mark)
                        ev_base = c_ev.n
                        for f in range(16):
                            if f == 0:
                                c_ev.wait(te)
                            elif f >= 2:
                                te.wait_ge(c_ev.sem, ev_base + f - 1)
                            for k in range(NT):
                                ins = te.matmul(ps_h[:, 512 * (f % 2):512 * (f % 2) + 512],
                                                w1sl[k][:, 128 * f:128 * (f + 1)],
                                                self.xn[k][:, 512 * tc4:512 * (tc4 + 1)],
                                                start=(k == 0), stop=(k == NT - 1))
                            c_mm.tag(ins)
                            mm_h.append(c_mm.n)

                    @b.scalar
                    def _(sc, e=e, fh=fh, tc4=tc4, prev_y_mark=prev_y_mark):
                        sc.wait_ge(c_mm.sem, prev_y_mark)
                        for f in range(16):
                            sc.wait_ge(c_mm.sem, mm_h[f])
                            c_ev.tag(sc.activation(hsb[f][:], ps_h[:, 512 * (f % 2):512 * (f % 2) + 512], ACTF.Gelu))
                            ev_h.append(c_ev.n)

                    @b.vector
                    def _(v, e=e, fh=fh, tc4=tc4):
                        c_ld.wait(v, mark=self._s11_gemark)
                        for f in range(16):
                            v.wait_ge(c_ev.sem, ev_h[f])
                            v.tensor_tensor(hsb[f][:], hsb[f][:], grep_b[:, 512 * tc4:512 * (tc4 + 1)], ALU.mult)
                        v.drain()
                        c_dv.tag(v.nop())

                    dv_base = c_dv.n
                    mm_y = []

                    @b.tensor
                    def _(te, e=e, fh=fh, tc4=tc4, dv_base=dv_base):
                        c_dv.wait(te)
                        j = 0
                        for dhalf in range(2):
                            for d4 in range(4):
                                dout = 4 * dhalf + d4
                                if dhalf == 1:
                                    te.wait_ge(c_dv.sem, dv_base + d4 + 1)
                                for k in range(16):
                                    ins = te.matmul(ps_y1[:, 512 * d4:512 * (d4 + 1)],
                                                    w2sl[k][:, 128 * dout:128 * (dout + 1)],
                                                    hsb[k][:], start=(k == 0), stop=(k == 15))
                                c_mm.tag(ins)
                                mm_y.append((dout, d4, c_mm.n))
                                j += 1

                    @b.vector
                    def _(v, e=e, fh=fh, tc4=tc4):
                        for dout, d4, mmv in mm_y:
                            v.wait_ge(c_mm.sem, mmv)
                            if e == 0 and fh == 0:
                                c_dv.tag(v.tensor_copy(self.facc[dout][:, 512 * tc4:512 * (tc4 + 1)],
                                                       ps_y1[:, 512 * d4:512 * (d4 + 1)]))
                            else:
                                c_dv.tag(v.tensor_tensor(self.facc[dout][:, 512 * tc4:512 * (tc4 + 1)],
                                                         self.facc[dout][:, 512 * tc4:512 * (tc4 + 1)],
                                                         ps_y1[:, 512 * d4:512 * (d4 + 1)], ALU.add))
                        v.drain()
                    self._s11_prev_y = c_mm.n
                gate_marks[e] = c_dv.n
        s11b.close()


    def s12_final_sum(self):
        b, d = self.b, self.dram
        c_mm, c_ev, c_dv, c_ld, c_sp = self.c_mm, self.c_ev, self.c_dv, self.c_ld, self.c_sp
        s12 = ExitStack()
        rs1t = self.sb("rs1t", [128, TC], stack=s12)
        x1ld = self.sb("x1ld", [128, TC], stack=s12)
        outf = [self.sb(f"outf{j}", [128, TC], F32, stack=s12) for j in range(NT)]
        outT = self.sb("outT", [128, D], F32, stack=s12)
        outT2 = self.sb("outT2", [128, D], F32, stack=s12)
        ps12 = self.psum("ps12", [128, 512], F32, stack=s12)

        for j in range(NT):
            @b.sync
            def _(sy, j=j):
                self.c_coll.wait(sy, mark=self.rs1_mark)
                c_dv.wait(sy)
                c_ld.tag(sy, sy.dma_start(out=rs1t[:], in_=d["rs1_dram"][128 * j:128 * (j + 1), :]), 16)
                c_ld.tag(sy, sy.dma_start(out=x1ld[:], in_=d["x1_dram"][128 * j:128 * (j + 1), :]), 16)
            r_mark = c_ld.mark()

            @b.vector
            def _(v, j=j):
                c_ld.wait(v, mark=r_mark)
                c_dv.wait(v)
                v.tensor_tensor(outf[j][:], rs1t[:], x1ld[:], ALU.add)
                v.drain()
                v.tensor_tensor(outf[j][:], outf[j][:], self.facc[j][:], ALU.add)
                v.drain()
                c_dv.tag(v.nop())

        for tg in range(16):
            ot = [outT, outT2][tg % 2]
            for j in range(NT):
                @b.tensor
                def _(te, tg=tg, j=j):
                    c_dv.wait(te)
                    c_ev.wait(te)
                    c_mm.tag(te.transpose(ps12[:, 128 * (j % 2):128 * (j % 2) + 128],
                                          outf[j][:, 128 * tg:128 * (tg + 1)], self.identf[:]))

                @b.scalar
                def _(sc, tg=tg, j=j, ot=ot):
                    c_mm.wait(sc)
                    if j == 0:
                        self.c_st.wait(sc)
                    c_ev.tag(sc.copy(out=ot[:, 128 * j:128 * (j + 1)], in_=ps12[:, 128 * (j % 2):128 * (j % 2) + 128]))

            @b.sync
            def _(sy, tg=tg, ot=ot):
                c_ev.wait(sy)
                self.c_st.tag(sy, sy.dma_start(out=d["out"][128 * tg:128 * (tg + 1), :], in_=ot[:]), 16)
        s12.close()

    def s_final(self):
        b, d = self.b, self.dram
        if self.stop_after is not None:
            zout = self.sb("zout", [128, D], F32)

            @b.vector
            def _(v):
                self.c_mm.wait(v)
                self.c_ev.wait(v)
                self.c_dv.tag(v.memset(zout[:], 0.0))

            @b.sync
            def _(sy):
                self.c_dv.wait(sy)
                for i in range(TC // 128):
                    self.c_st.tag(sy, sy.dma_start(out=d["out"][128 * i:128 * (i + 1), :], in_=zout[:]), 16)

        @b.sync
        def _(sy):
            self.c_dv.wait(sy)
            self.c_ev.wait(sy)
            self.c_mm.wait(sy)
            self.c_sp.wait(sy)
            self.c_ldb.wait(sy)
            self.c_ld.wait(sy)
            self.c_coll.wait(sy)
            self.c_st.wait(sy)


def build(dbg=(), stop_after=None):
    mk = MK(dbg=dbg, stop_after=stop_after)
    nc = mk.build()
    return nc, mk


def stage_inputs(ins):
    x = np.ascontiguousarray(ins["L0_attn_x"].reshape(TFULL, D))
    maps = []
    com = {
        "w_inproj": ins["L0_attn_in_proj_w"],
        "rms_w": ins["L0_attn_rms_w"].reshape(D, 1),
        "conv_w": ins["L0_attn_conv_w"],
        "conv_b": ins["L0_attn_conv_b"].reshape(DI, 1),
        "bcd_w": ins["L0_attn_bcd_w"],
        "dup_w": ins["L0_attn_dup_w"],
        "dup_b": ins["L0_attn_dup_b"].reshape(DI, 1),
        "lma": ins["L0_attn_log_minus_A"],
        "d_param": ins["L0_attn_D_param"].reshape(DI, 1),
        "w_outproj": ins["L0_attn_out_proj_w"],
        "r0_wT": np.ascontiguousarray(ins["L0_ffn_router_w"].T),
        "r0_b": ins["L0_ffn_router_b"].reshape(1, NE),
        "exp0_wT": np.ascontiguousarray(ins["L0_ffn_expert_w"].transpose(0, 2, 1)),
        "exp0_b": np.ascontiguousarray(ins["L0_ffn_expert_b"].reshape(NE, D, 1)),
        "wkT": np.ascontiguousarray(ins["L1_attn_Wk"].T),
        "wvT": np.ascontiguousarray(ins["L1_attn_Wv"].T),
        "wrT": np.ascontiguousarray(ins["L1_attn_Wr"].T),
        "woT": np.ascontiguousarray(ins["L1_attn_Wo"].T),
        "tdec": ins["L1_attn_time_decay"].reshape(D, 1),
        "tfirst": ins["L1_attn_time_first"].reshape(D, 1),
        "r1_wT": np.ascontiguousarray(ins["L1_ffn_router_w"].T),
        "w1": ins["L1_ffn_w1"],
        "w2": ins["L1_ffn_w2"],
    }
    com = {k: np.ascontiguousarray(v, dtype=np.float32) for k, v in com.items()}
    for c in range(NCORE):
        lo = c * TC
        xt = np.zeros((D, TC + 3), np.float32)
        xt[:, 3:] = x[lo:lo + TC].T
        if lo % 8192 != 0:
            xt[:, 0:3] = x[lo - 3:lo].T
        pm = np.zeros((NE, 1), np.float32)
        pm[:c] = 1.0
        m = dict(com)
        m["x_t"] = xt
        m["pmask"] = pm
        maps.append(m)
    return maps


_CACHED = {}


def kernel(**inputs):
    from concourse.bass_utils import run_bass_kernel_spmd
    if "nc" not in _CACHED:
        nc, _ = build(dbg=())
        _CACHED["nc"] = nc
    nc = _CACHED["nc"]
    maps = stage_inputs(inputs)
    res = run_bass_kernel_spmd(nc, maps, core_ids=list(range(NCORE)))
    out = np.concatenate([r["out"] for r in res.results], axis=0)
    return np.ascontiguousarray(out.reshape(2, 8192, D)).astype(np.float32)


# version 04
# speedup vs baseline: 1.4773x; 1.1359x over previous
from contextlib import ExitStack

import numpy as np
import concourse.bass as bass
import concourse.mybir as mybir

F32 = mybir.dt.float32
BF16 = mybir.dt.bfloat16
ALU = mybir.AluOpType
ACTF = mybir.ActivationFunctionType
AX = mybir.AxisListType

D = 1024
DI = 2048
NS = 16
DR = 64
TC = 2048
NT = D // 128
NTI = DI // 128
NCORE = 8
TFULL = TC * NCORE
EPS = 1e-5
CAP = 5120
FFN = 4096
NE = 8
DSH = DI // NCORE


class Ch:

    def __init__(self, es, nc, name, nsem=4):
        self.sems = [es.enter_context(nc.semaphore(f"{name}{i}")) for i in range(nsem)]
        self.tot = [0] * nsem
        self.guard = [0] * nsem
        self.cur = 0
        self.open = False

    def tag(self, eng, instr, k=1):
        i = self.cur
        if not self.open:
            if self.tot[i] > self.guard[i]:
                eng.wait_ge(self.sems[i], self.tot[i])
                self.guard[i] = self.tot[i]
            self.open = True
        instr.then_inc(self.sems[i], k)
        self.tot[i] += k
        return instr

    def mark(self):
        self.cur = (self.cur + 1) % len(self.sems)
        self.open = False
        return list(self.tot)

    def wait(self, eng, mark=None):
        if mark is None:
            mark = self.mark()
        for s, v in zip(self.sems, mark):
            if v > 0:
                eng.wait_ge(s, v)


class CCh:

    def __init__(self, es, nc, name):
        self.sem = es.enter_context(nc.semaphore(name))
        self.n = 0

    def tag(self, instr, k=1):
        instr.then_inc(self.sem, k)
        self.n += k
        return instr

    def mark(self):
        return self.n

    def wait(self, eng, mark=None):
        eng.wait_ge(self.sem, self.n if mark is None else mark)


class MK:
    def __init__(self, dbg=(), stop_after=None):
        self.dbg = set(dbg)
        self.stop_after = stop_after
        self.nc = bass.Bass("TRN2", target_bir_lowering=False)
        self.es = ExitStack()
        self.dram = {}

    def din(self, name, shape, dt=F32):
        self.dram[name] = self.nc.dram_tensor(name, shape, dt, kind="ExternalInput").ap()
        return self.dram[name]

    def dout(self, name, shape, dt=F32):
        self.dram[name] = self.nc.dram_tensor(name, shape, dt, kind="ExternalOutput").ap()
        return self.dram[name]

    def dint(self, name, shape, dt=BF16):
        kind = "ExternalOutput" if name in self.dbg else "Internal"
        self.dram[name] = self.nc.dram_tensor(name, shape, dt, kind=kind).ap()
        return self.dram[name]

    def sb(self, name, shape, dt=BF16, stack=None):
        return (stack or self.es).enter_context(self.nc.sbuf_tensor(name, shape, dt))

    def psum(self, name, shape, dt=F32, stack=None):
        return (stack or self.es).enter_context(self.nc.psum_tensor(name, shape, dt))

    def rk(self, eng, mult, add=0):
        if not hasattr(self, "_rkcache"):
            self._rkcache = {}
        key = (id(eng.engine) if hasattr(eng, "engine") else id(eng), mult, add)
        key = (str(eng.engine), mult, add)
        if key not in self._rkcache:
            rank = eng.partition_id()
            expr = rank * mult + add if add else rank * mult
            self._rkcache[key] = eng.compute_val(expr)
        return self._rkcache[key]


    def build(self):
        nc, es = self.nc, self.es
        self.declare_dram()
        self.b = es.enter_context(nc.Block())
        self.c_ld = Ch(self.es, self.nc, "ld", nsem=4)
        self.c_wst = Ch(self.es, self.nc, "wst", nsem=4)
        self.c_sp = Ch(self.es, self.nc, "sp", nsem=4)
        self.c_coll = Ch(self.es, self.nc, "coll", nsem=2)
        self.c_st = Ch(self.es, self.nc, "st", nsem=2)
        self.c_ldb = Ch(self.es, self.nc, "ldb", nsem=4)
        self.c_mm = CCh(self.es, self.nc, "mm")
        self.c_ev = CCh(self.es, self.nc, "ev")
        self.c_dv = CCh(self.es, self.nc, "dv")
        self.c_gp = CCh(self.es, self.nc, "gp")
        self.alloc_shared()
        self.s0_constants()
        self.s1_ln0()
        self.s2_l0_proj_conv()
        if self.stop_after != "s2":
            self.s3_collectives_l0()
            self.s4_l0_ffn()
            self.s5_scan()
            if self.stop_after != "s5":
                self.s6_gate_outproj()
                if self.stop_after != "s6":
                    self.s7_ln1()
                    self.s8_kvr()
                    if self.stop_after != "s8":
                        self.s9_rwkv()
                        if self.stop_after != "s9":
                            self.s10_wo()
                            self.s11_l1_ffn()
                            if self.stop_after != "s11":
                                self.s12_final_sum()
        self.s_final()
        es.close()
        return nc

    def alloc_shared(self):
        self.s0_stack = ExitStack()
        self.ln_stack = ExitStack()
        self.ps_stat = self.psum("ps_stat", [1, 1024], stack=self.ln_stack)
        self.ps_bc = self.psum("ps_bc", [128, 1536], stack=self.ln_stack)
        self.tmpc1 = self.sb("tmpc1", [128, 128], F32)
        self.tmpc2 = self.sb("tmpc2", [128, 128], F32)
        self.acol_stage = self.sb("acol_stage", [32, 128], F32)
        self.ps_misc = self.psum("ps_misc", [128, 512], F32, stack=self.s0_stack)
        self.identf = self.sb("identf", [128, 128], F32)
        self.sel127 = self.sb("sel127", [128, 1], F32)
        self.sqt = self.sb("sqt", [128, 512])
        self.lane = self.sb("lane", [1, 2048], F32)
        self.tnm = self.sb("tnm", [128, 512])
        self.bcds = self.sb("bcds", [96, TC])
        self.facc = [self.sb(f"facc{i}", [128, TC]) for i in range(NT)]
        self.lnld0 = self.sb("lnld0", [128, 512], F32)
        self.lnld1 = self.sb("lnld1", [128, 512], F32)
        self.lnldb0 = self.sb("lnldb0", [128, 512])
        self.lnldb1 = self.sb("lnldb1", [128, 512])

    def declare_dram(self):
        din, dout, dint = self.din, self.dout, self.dint
        din("x_t", [D, TC + 3])
        din("w_inproj", [D, 2 * DI])
        din("rms_w", [D, 1])
        din("conv_w", [DI, 4])
        din("conv_b", [DI, 1])
        din("bcd_w", [DI, 96])
        din("dup_w", [DR, DI])
        din("dup_b", [DI, 1])
        din("lma", [DI, NS])
        din("d_param", [DI, 1])
        din("w_outproj", [DI, D])
        din("r0_wT", [D, NE])
        din("r0_b", [1, NE])
        din("exp0_wT", [NE, D, D])
        din("exp0_b", [NE, D, 1])
        din("wkT", [D, D])
        din("wvT", [D, D])
        din("wrT", [D, D])
        din("woT", [D, D])
        din("tdec", [D, 1])
        din("tfirst", [D, 1])
        din("r1_wT", [D, NE])
        din("w1", [NE, D, FFN])
        din("w2", [NE, FFN, D])
        din("pmask", [NE, 1])
        dout("out", [TC, D])
        dint("xc_dram", [DI, TC])
        dint("res_dram", [DI, TC])
        dint("bcd_dram", [96, TC])
        dint("xc_all", [NCORE * DI, TC])
        dint("bcd_all", [NCORE * 96, TC])
        dint("delta_dram", [DSH, TFULL])
        dint("dx_dram", [DSH, TFULL])
        dint("y0_dram", [DSH, TFULL])
        dint("y0_all", [DI, TFULL])
        dint("res_all", [NCORE * DI, TC])
        dint("xc_my", [DSH, TFULL])
        dint("res_my", [DSH, TFULL])
        dint("k_my", [128, TFULL])
        dint("v_my", [128, TFULL])
        dint("r_my", [128, TFULL])
        dint("partial0_dram", [NCORE * D, TC])
        dint("rs0_dram", [D, TC])
        dint("partial1_dram", [NCORE * D, TC])
        dint("rs1_dram", [D, TC])
        dint("x1_dram", [D, TC])
        dint("wsel_dram", [NE, TC])
        dint("kvr_dram", [3 * D, TC])
        dint("kvr_all", [NCORE * 3 * D, TC])
        dint("y1_dram", [128, TFULL])
        dint("y1_all", [NCORE * 128, TFULL])
        dint("g1_dram", [NE, TC])
        dint("cnt_dram", [1, 2 * NE], F32)
        dint("cnt_all", [NCORE, 2 * NE], F32)
        dint("hdbg", [128, 4096])
        dint("r9dbg", [128, 512])
        dint("k9dbg", [128, 512])
        dint("kh9dbg", [128, 128])
        dint("khT9dbg", [128, 128])
        dint("vT9dbg", [128, 128])
        dint("abdbg", [128, 4096], F32)
        dint("bxdbg", [128, 4096])
        dint("lgdbg", [8, TC], F32)
        dint("xn_dram", [D, TC + 3])
        dint("xn1_dram", [D, TC])
        dint("xr_dram", [D, TC + 3])

    def s0_constants(self):
        b, nc, d = self.b, self.nc, self.dram
        c_ld, c_gp, c_dv, c_ev, c_mm = self.c_ld, self.c_gp, self.c_dv, self.c_ev, self.c_mm
        self.ident = self.sb("ident", [128, 128])
        self.utri = self.sb("utri", [128, 128])
        self.io_col = self.sb("io_col", [128, 128], F32)
        self.io_row = self.sb("io_row", [128, 128], F32)
        self.io_m16 = self.sb("io_m16", [128, 128], F32)
        self.io_m32 = self.sb("io_m32", [128, 128], F32)
        self.sum16 = self.sb("sum16", [128, 16 * 128])
        self.ones_col = self.sb("ones_col", [128, 1])
        self.ones_row = self.sb("ones_row", [1, 128])
        self.onesf_col = self.sb("onesf_col", [128, 1], F32)
        self.onesf_row = self.sb("onesf_row", [1, 128], F32)
        self.mk_diag = self.sb("mk_diag", [128, 128])
        self.mk_off = self.sb("mk_off", [128, 128])
        sm = {}
        sm["rms_w"] = self.sb("sm_rms_w", [128, NT], F32)
        sm["conv_b"] = self.sb("sm_conv_b", [128, NTI], F32)
        sm["d_param"] = self.sb("sm_d_param", [128, NTI], F32)
        sm["convw"] = self.sb("sm_convw", [128, 4 * NTI], F32)
        sm["dup_b"] = self.sb("sm_dup_b", [128, 2], F32)
        sm["d_param_my"] = self.sb("sm_d_param_my", [128, 2], F32)
        sm["exp0_b"] = self.sb("sm_exp0_b", [128, NE * NT], F32)
        sm["tdec"] = self.sb("sm_tdec", [128, 1], F32)
        sm["tfirst"] = self.sb("sm_tfirst", [128, 1], F32)
        sm["pmask"] = self.sb("sm_pmask", [NE, 1], F32)
        sm["acol"] = self.sb("sm_acol", [128, 32], F32)
        self.sm = sm

        @b.sync
        def _(sy):
            for j in range(NT):
                c_ld.tag(sy, sy.dma_start(out=sm["rms_w"][:, j:j + 1], in_=d["rms_w"][128 * j:128 * (j + 1), :]), 16)
            for j in range(NTI):
                c_ld.tag(sy, sy.dma_start(out=sm["conv_b"][:, j:j + 1], in_=d["conv_b"][128 * j:128 * (j + 1), :]), 16)
                c_ld.tag(sy, sy.dma_start(out=sm["d_param"][:, j:j + 1], in_=d["d_param"][128 * j:128 * (j + 1), :]), 16)
                c_ld.tag(sy, sy.dma_start(out=sm["convw"][:, 4 * j:4 * j + 4], in_=d["conv_w"][128 * j:128 * (j + 1), :]), 16)
            for e in range(NE):
                for i in range(NT):
                    c_ld.tag(sy, sy.dma_start(out=sm["exp0_b"][:, NT * e + i:NT * e + i + 1],
                                          in_=d["exp0_b"][e, 128 * i:128 * (i + 1), :]), 16)
            c_ld.tag(sy, sy.dma_start(out=sm["pmask"][:], in_=d["pmask"]), 16)
            off = self.rk(sy, DSH)
            c_ld.tag(sy, sy.dma_start(out=sm["dup_b"][:],
                                      in_=d["dup_b"][bass.ds(off, 256), :].rearrange("(a p) o -> p (a o)", a=2),
                                      allow_slow_non_contiguous=True), 16)
            c_ld.tag(sy, sy.dma_start(out=sm["d_param_my"][:],
                                      in_=d["d_param"][bass.ds(off, 256), :].rearrange("(a p) o -> p (a o)", a=2),
                                      allow_slow_non_contiguous=True), 16)
            c_ld.tag(sy, sy.dma_start(out=self.acol_stage[0:32, :],
                                      in_=d["lma"][bass.ds(off, 256), :].rearrange("(q g) n -> q (g n)", g=8)), 16)
            off3 = self.rk(sy, 128)
            c_ld.tag(sy, sy.dma_start(out=sm["tdec"][:], in_=d["tdec"][bass.ds(off3, 128), :]), 16)
            c_ld.tag(sy, sy.dma_start(out=sm["tfirst"][:], in_=d["tfirst"][bass.ds(off3, 128), :]), 16)

        @b.gpsimd
        def _(g):
            g.iota(self.io_col[:], pattern=[[1, 128]], base=0, channel_multiplier=0,
                   allow_small_or_imprecise_dtypes=True)
            g.iota(self.io_m16[:], pattern=[[0, 8], [1, 16]], base=0, channel_multiplier=0,
                   allow_small_or_imprecise_dtypes=True)
            g.iota(self.io_m32[:], pattern=[[0, 4], [1, 32]], base=0, channel_multiplier=0,
                   allow_small_or_imprecise_dtypes=True)
            c_gp.tag(g.iota(self.io_row[:], pattern=[[0, 128]], base=0, channel_multiplier=1,
                            allow_small_or_imprecise_dtypes=True))

        @b.vector
        def _(v):
            c_gp.wait(v)
            v.tensor_tensor(self.ident[:], self.io_col[:], self.io_row[:], ALU.is_equal)
            v.tensor_tensor(self.identf[:], self.io_col[:], self.io_row[:], ALU.is_equal)
            v.tensor_scalar(self.sel127[:], self.io_row[:, 0:1], 126.5, None, ALU.is_ge)
            v.tensor_tensor(self.utri[:], self.io_row[:], self.io_col[:], ALU.is_le)
            v.memset(self.ones_col[:], 1.0)
            v.memset(self.ones_row[:], 1.0)
            v.memset(self.onesf_col[:], 1.0)
            v.memset(self.onesf_row[:], 1.0)
            v.memset(self.sum16[:], 0.0)
            v.memset(self.mk_diag[:], 0.0)
            v.memset(self.mk_off[:], 0.0)
            t1, t2 = self.tmpc1, self.tmpc2
            v.drain()
            v.tensor_scalar(t1[:], self.io_col[:], 16.0, None, ALU.mult)
            v.drain()
            v.tensor_tensor(t2[:], self.io_row[:], t1[:], ALU.is_ge)
            v.drain()
            v.tensor_scalar(t1[:], t1[:], 16.0, None, ALU.add)
            v.drain()
            v.tensor_tensor(t1[:], self.io_row[:], t1[:], ALU.is_lt)
            v.drain()
            v.tensor_tensor(t2[:], t2[:], t1[:], ALU.mult)
            v.drain()
            for j in range(16):
                v.tensor_copy(self.sum16[:, 128 * j + 8 * j:128 * j + 8 * j + 8], t2[:, 0:8])
            v.drain()
            v.tensor_tensor(t1[:], self.io_col[:], self.io_row[:], ALU.subtract)
            v.drain()
            v.tensor_tensor(t2[:], t1[:], self.io_m16[:], ALU.is_le)
            v.drain()
            v.scalar_tensor_tensor(self.mk_diag[:], t1[:], 0.0, t2[:], ALU.is_ge, ALU.mult)
            v.drain()
            v.tensor_tensor(t2[:], t1[:], self.io_m32[:], ALU.subtract)
            v.drain()
            v.tensor_scalar(self.mk_off[:], t2[:], -15.0, 0.0, ALU.is_ge, op1=ALU.bypass) if False else None
            v.tensor_scalar(t1[:], t2[:], -15.0, None, ALU.is_ge)
            v.drain()
            v.tensor_scalar(t2[:], t2[:], 0.0, None, ALU.is_le)
            v.drain()
            v.tensor_tensor(t1[:], t1[:], t2[:], ALU.mult)
            v.drain()
            v.tensor_scalar(t2[:], self.io_m32[:], 16.0, None, ALU.is_ge)
            v.drain()
            v.tensor_tensor(self.mk_off[:], t1[:], t2[:], ALU.mult)
            v.drain()
            c_dv.tag(v.nop())

        @b.scalar
        def _(sc):
            c_ld.wait(sc)
            c_ev.tag(sc.activation(self.acol_stage[0:32, :], self.acol_stage[0:32, :], ACTF.Exp))

        @b.tensor
        def _(te):
            c_ev.wait(te)
            c_dv.wait(te)
            c_mm.tag(te.transpose(self.ps_misc[:, 0:32], self.acol_stage[0:32, :], self.identf[0:32, 0:32]))

        @b.vector
        def _(v):
            c_mm.wait(v)
            c_dv.tag(v.tensor_scalar(sm["acol"][:], self.ps_misc[:, 0:32], -1.0, None, ALU.mult))

    def s1_ln0(self):
        b, d = self.b, self.dram
        self.s1_stack = ExitStack()
        self.xn = [self.sb(f"xn{i}", [128, TC + 3]) for i in range(NT)]
        self.xr = [self.sb(f"xr{i}", [128, TC + 3], stack=self.s1_stack) for i in range(NT)]

        self.emit_ln(src=None, out_xn=self.xn, out_xr=self.xr, width=TC + 3,
                     rms_cols=self.sm["rms_w"], rms_col_idx=list(range(NT)),
                     src_dram=d["x_t"])

        if "xn_dram" in self.dbg:
            @b.sync
            def _(sy):
                self.c_dv.wait(sy)
                for i in range(NT):
                    self.c_sp.tag(sy, sy.dma_start(out=d["xn_dram"][128 * i:128 * (i + 1), :], in_=self.xn[i][:]), 16)
                    self.c_sp.tag(sy, sy.dma_start(out=d["xr_dram"][128 * i:128 * (i + 1), :], in_=self.xr[i][:]), 16)

    def emit_ln(self, src, out_xn, out_xr, width, rms_cols, rms_col_idx, src_f32=True,
                src_dram=None, dram_wait=None):
        b = self.b
        c_ld, c_mm, c_ev, c_dv = self.c_ld, self.c_mm, self.c_ev, self.c_dv
        sq, lane = self.sqt, self.lane
        ps_stat, ps_bc = self.ps_stat, self.ps_bc
        nchunk = (width + 511) // 512
        use_dram = src_dram is not None
        if use_dram:
            lnld = [self.lnld0, self.lnld1] if src_f32 else [self.lnldb0, self.lnldb1]

        def srcap(i, o, W, phase):
            if not use_dram:
                return src[i][:, o:o + W]
            return lnld[(i + phase) % 2][:, 0:W]

        for ci in range(nchunk):
            o = 512 * ci
            W = min(512, width - o)
            for i in range(NT):
                if use_dram:
                    @b.sync
                    def _(sy, i=i, o=o, W=W):
                        c_mm.wait(sy)
                        if dram_wait is not None:
                            dram_wait[0].wait(sy, mark=dram_wait[1])
                        c_ld.tag(sy, sy.dma_start(out=lnld[i % 2][:, 0:W],
                                                  in_=src_dram[128 * i:128 * (i + 1), o:o + W]), 16)
                @b.scalar
                def _(sc, i=i, o=o, W=W):
                    c_ld.wait(sc)
                    c_mm.wait(sc)
                    c_dv.wait(sc)
                    c_ev.tag(sc.activation(sq[:, 0:W], srcap(i, o, W, 0), ACTF.Square))

                @b.tensor
                def _(te, i=i, o=o, W=W):
                    c_ev.wait(te)
                    te.matmul(ps_stat[0:1, 0:W], self.onesf_col[:] if src_f32 else self.ones_col[:],
                              srcap(i, o, W, 0),
                              start=(i == 0), stop=(i == NT - 1))
                    c_mm.tag(te.matmul(ps_stat[0:1, 512:512 + W], self.ones_col[:], sq[:, 0:W],
                                       start=(i == 0), stop=(i == NT - 1)))

            @b.vector
            def _(v, o=o, W=W):
                c_mm.wait(v)
                v.tensor_scalar(lane[0:1, 0:W], ps_stat[0:1, 0:W], 1.0 / D, None, ALU.mult)
                v.tensor_scalar(lane[0:1, 512:512 + W], ps_stat[0:1, 512:512 + W], 1.0 / D, None, ALU.mult)
                v.drain()
                v.tensor_tensor(lane[0:1, 1024:1024 + W], lane[0:1, 0:W], lane[0:1, 0:W], ALU.mult)
                v.drain()
                v.tensor_tensor(lane[0:1, 1024:1024 + W], lane[0:1, 512:512 + W], lane[0:1, 1024:1024 + W], ALU.subtract)
                v.drain()
                c_dv.tag(v.tensor_scalar(lane[0:1, 1536:1536 + W], lane[0:1, 1024:1024 + W], EPS, None, ALU.add))

            @b.scalar
            def _(sc, o=o, W=W):
                c_dv.wait(sc)
                c_ev.tag(sc.activation(lane[0:1, 1536:1536 + W], lane[0:1, 1536:1536 + W], ACTF.Sqrt))

            @b.vector
            def _(v, o=o, W=W):
                c_ev.wait(v)
                v.reciprocal(lane[0:1, 512:512 + W], lane[0:1, 1536:1536 + W])
                v.drain()
                if out_xr is not None:
                    v.tensor_tensor(lane[0:1, 1536:1536 + W], lane[0:1, 1024:1024 + W], lane[0:1, 512:512 + W], ALU.mult)
                    v.drain()
                    v.tensor_tensor(lane[0:1, 1536:1536 + W], lane[0:1, 1536:1536 + W], lane[0:1, 512:512 + W], ALU.mult)
                    v.drain()
                    c_dv.tag(v.tensor_scalar(lane[0:1, 1536:1536 + W], lane[0:1, 1536:1536 + W], EPS, None, ALU.add))
                else:
                    c_dv.tag(v.nop())

            if out_xr is not None:
                @b.scalar
                def _(sc, o=o, W=W):
                    c_dv.wait(sc)
                    c_ev.tag(sc.activation(lane[0:1, 1536:1536 + W], lane[0:1, 1536:1536 + W], ACTF.Sqrt))

                @b.vector
                def _(v, o=o, W=W):
                    c_ev.wait(v)
                    v.reciprocal(lane[0:1, 1536:1536 + W], lane[0:1, 1536:1536 + W])
                    v.drain()
                    c_dv.tag(v.tensor_tensor(lane[0:1, 1536:1536 + W], lane[0:1, 1536:1536 + W], lane[0:1, 512:512 + W], ALU.mult))

            @b.tensor
            def _(te, o=o, W=W):
                c_dv.wait(te)
                c_mm.tag(te.matmul(ps_bc[:, 0:W], self.onesf_row[:], lane[0:1, 0:W], start=True, stop=True))
                c_mm.tag(te.matmul(ps_bc[:, 512:512 + W], self.onesf_row[:], lane[0:1, 512:512 + W], start=True, stop=True))
                if out_xr is not None:
                    c_mm.tag(te.matmul(ps_bc[:, 1024:1024 + W], self.onesf_row[:], lane[0:1, 1536:1536 + W], start=True, stop=True))

            for i in range(NT):
                if use_dram:
                    @b.sync
                    def _(sy, i=i, o=o, W=W):
                        c_dv.wait(sy)
                        if dram_wait is not None:
                            dram_wait[0].wait(sy, mark=dram_wait[1])
                        c_ld.tag(sy, sy.dma_start(out=lnld[i % 2][:, 0:W],
                                                  in_=src_dram[128 * i:128 * (i + 1), o:o + W]), 16)

                @b.vector
                def _(v, i=i, o=o, W=W):
                    c_mm.wait(v)
                    if use_dram:
                        c_ld.wait(v)
                    v.tensor_tensor(self.tnm[:, 0:W], srcap(i, o, W, 0), ps_bc[:, 0:W], ALU.subtract)
                    v.drain()
                    v.tensor_tensor(out_xn[i][:, o:o + W], self.tnm[:, 0:W], ps_bc[:, 512:512 + W], ALU.mult)
                    if out_xr is not None:
                        v.scalar_tensor_tensor(out_xr[i][:, o:o + W], self.tnm[:, 0:W],
                                               rms_cols[:, rms_col_idx[i]:rms_col_idx[i] + 1],
                                               ps_bc[:, 1024:1024 + W], ALU.mult, ALU.mult)
                    v.drain()
                    c_dv.tag(v.nop())

    def s2_l0_proj_conv(self):
        b, d = self.b, self.dram
        c_wst, c_mm, c_ev, c_dv, c_sp = self.c_wst, self.c_mm, self.c_ev, self.c_dv, self.c_sp
        self.s0_stack.close()
        self.ln_stack.close()
        s2 = ExitStack()
        wslab = [self.sb(f"wslab{k}", [128, D], stack=s2) for k in range(NT)]
        bcdw = self.sb("bcdw", [128, 96 * NTI], stack=s2)
        xs = self.sb("xs", [128, TC + 3], stack=s2)
        xs2 = self.sb("xs2", [128, TC + 3], stack=s2)
        cvt = self.sb("cvt", [128, TC], stack=s2)
        xct = self.sb("xct", [128, TC], stack=s2)
        xct2 = self.sb("xct2", [128, TC], stack=s2)
        rst = self.sb("rst", [128, TC], stack=s2)
        ps_p = self.psum("ps_p", [128, 1024], F32, stack=s2)
        ps_bcd = self.psum("ps_bcd", [96, TC], F32, stack=s2)

        @b.gpsimd
        def _(g):
            self.c_dv.wait(g)
            self.c_ev.wait(g)
            self.c_mm.wait(g)
            for j in range(NTI):
                c_wst.tag(g, g.dma_start(out=bcdw[:, 96 * j:96 * (j + 1)], in_=d["bcd_w"][128 * j:128 * (j + 1), :]), 16)

        CH = [(0, 512), (512, 512), (1024, 512), (1536, 512), (2048, 3)]
        xs_bufs = [xs, xs2]
        xc_bufs = [xct, xct2]
        ln_dv = self.c_dv.n
        for q in range(2):
            @b.gpsimd
            def _(g, q=q):
                self.c_dv.wait(g)
                if q > 0:
                    self.c_mm.wait(g)
                for k in range(NT):
                    c_wst.tag(g, g.dma_start(out=wslab[k][:], in_=d["w_inproj"][128 * k:128 * (k + 1), D * q:D * (q + 1)]), 16)
            wq_mark = c_wst.mark()

            for jj in range(NT):
                j = NT * q + jj
                xsb = xs_bufs[j % 2]
                xcb = xc_bufs[j % 2]
                for ci, (o, W) in enumerate(CH):
                    @b.tensor
                    def _(te, j=j, jj=jj, o=o, W=W, ci=ci, xsb=xsb, wq_mark=wq_mark):
                        c_wst.wait(te, mark=wq_mark)
                        self.c_dv.wait(te, mark=ln_dv)
                        c_ev.wait(te)
                        for k in range(NT):
                            ins = te.matmul(ps_p[:, 512 * (ci % 2):512 * (ci % 2) + W],
                                            wslab[k][:, 128 * jj:128 * (jj + 1)],
                                            self.xr[k][:, o:o + W],
                                            start=(k == 0), stop=(k == NT - 1))
                        c_mm.tag(ins)

                    @b.scalar
                    def _(sc, j=j, o=o, W=W, ci=ci, xsb=xsb):
                        c_mm.wait(sc)
                        c_dv.wait(sc)
                        c_ev.tag(sc.copy(out=xsb[:, o:o + W], in_=ps_p[:, 512 * (ci % 2):512 * (ci % 2) + W]))

                @b.vector
                def _(v, j=j, xsb=xsb):
                    c_ev.wait(v)
                    cw = self.sm["convw"]
                    v.tensor_scalar(cvt[:], xsb[:, 0:TC], cw[:, 4 * j:4 * j + 1], None, ALU.mult)
                    v.drain()
                    for k in range(1, 4):
                        v.scalar_tensor_tensor(cvt[:], xsb[:, k:k + TC], cw[:, 4 * j + k:4 * j + k + 1],
                                               cvt[:], ALU.mult, ALU.add)
                        v.drain()
                    c_dv.tag(v.nop())

                @b.scalar
                def _(sc, j=j, xcb=xcb):
                    c_dv.wait(sc)
                    c_sp.wait(sc)
                    self.c_mm.wait(sc)
                    c_ev.tag(sc.activation(xcb[:], cvt[:], ACTF.Silu, bias=self.sm["conv_b"][:, j:j + 1]))

                @b.sync
                def _(sy, j=j, xcb=xcb):
                    c_ev.wait(sy)
                    c_sp.tag(sy, sy.dma_start(out=d["xc_dram"][128 * j:128 * (j + 1), :], in_=xcb[:]), 16)

                for ci in range(4):
                    @b.tensor
                    def _(te, j=j, ci=ci, xcb=xcb):
                        c_ev.wait(te)
                        ins = te.matmul(ps_bcd[:, 512 * ci:512 * (ci + 1)], bcdw[:, 96 * j:96 * (j + 1)],
                                        xcb[:, 512 * ci:512 * (ci + 1)], start=(j == 0), stop=(j == NTI - 1))
                        if j == NTI - 1:
                            c_mm.tag(ins)

        @b.scalar
        def _(sc):
            c_mm.wait(sc)
            c_ev.tag(sc.copy(out=self.bcds[0:96, :], in_=ps_bcd[0:96, :]))

        @b.sync
        def _(sy):
            c_ev.wait(sy)
            c_sp.tag(sy, sy.dma_start(out=d["bcd_dram"], in_=self.bcds[0:96, :]), 16)

        for q in range(2):
            @b.gpsimd
            def _(g, q=q):
                self.c_mm.wait(g)
                for k in range(NT):
                    c_wst.tag(g, g.dma_start(out=wslab[k][:], in_=d["w_inproj"][128 * k:128 * (k + 1), DI + D * q:DI + D * (q + 1)]), 16)
            wq_mark = c_wst.mark()

            for jj in range(NT):
                j = NT * q + jj
                for ci in range(4):
                    o = 512 * ci

                    @b.tensor
                    def _(te, j=j, jj=jj, o=o, ci=ci, wq_mark=wq_mark):
                        c_wst.wait(te, mark=wq_mark)
                        c_ev.wait(te)
                        for k in range(NT):
                            ins = te.matmul(ps_p[:, 512 * (ci % 2):512 * (ci % 2) + 512],
                                            wslab[k][:, 128 * jj:128 * (jj + 1)],
                                            self.xr[k][:, 3 + o:3 + o + 512],
                                            start=(k == 0), stop=(k == NT - 1))
                        c_mm.tag(ins)

                    @b.scalar
                    def _(sc, j=j, o=o, ci=ci):
                        c_mm.wait(sc)
                        c_sp.wait(sc)
                        c_ev.tag(sc.activation(rst[:, o:o + 512], ps_p[:, 512 * (ci % 2):512 * (ci % 2) + 512], ACTF.Silu))

                @b.sync
                def _(sy, j=j):
                    c_ev.wait(sy)
                    c_sp.tag(sy, sy.dma_start(out=d["res_dram"][128 * j:128 * (j + 1), :], in_=rst[:]), 16)

        @b.sync
        def _(sy):
            c_sp.wait(sy)
            sy.nop()

        s2.close()
        self.s1_stack.close()

    def s3_collectives_l0(self):
        b, d = self.b, self.dram
        spm = self.c_sp.mark()

        @b.gpsimd
        def _(g):
            self.c_sp.wait(g, mark=spm)
            self.c_coll.tag(g, g.collective_compute(
                "AllGather", ALU.bypass, ins=[d["xc_dram"]], outs=[d["xc_all"]],
                replica_groups=[list(range(NCORE))]), 1)
            self.c_coll.tag(g, g.collective_compute(
                "AllGather", ALU.bypass, ins=[d["bcd_dram"]], outs=[d["bcd_all"]],
                replica_groups=[list(range(NCORE))]), 1)
        self.ag_l0_mark = self.c_coll.mark()

    def s4_l0_ffn(self):
        b, d = self.b, self.dram
        c_wst, c_mm, c_ev, c_dv, c_ld, c_sp = self.c_wst, self.c_mm, self.c_ev, self.c_dv, self.c_ld, self.c_sp
        s4 = ExitStack()
        r0t = self.sb("r0t", [128, 64], stack=s4)
        brep = self.sb("brep", [128, NE], F32, stack=s4)
        lg = self.sb("lg", [8, TC], F32, stack=s4)
        lgt = self.sb("lgt", [128, 128], F32, stack=s4)
        mx = self.sb("mx_s4", [128, 16], F32, stack=s4)
        smv = self.sb("smv_s4", [128, 16], F32, stack=s4)
        ohh = self.sb("ohh_s4", [128, 128], F32, stack=s4)
        wsel = self.sb("wsel_s4", [128, 128], F32, stack=s4)
        wst_sb = self.sb("wst_sb", [8, TC], stack=s4)
        grep = self.sb("grep", [128, TC], stack=s4)
        wsl = [self.sb(f"e0w{i}", [128, 2 * D], stack=s4) for i in range(NT)]
        tmp_e2 = [self.sb(f"tmp_e{i}", [128, 512], stack=s4) for i in range(2)]
        ps_l = self.psum("ps_l", [8, 512], F32, stack=s4)
        ps_t = self.psum("ps_t", [128, 512], F32, stack=s4)
        ps_f = self.psum("ps_f", [128, 1024], F32, stack=s4)

        @b.gpsimd
        def _(g):
            for k in range(NT):
                c_wst.tag(g, g.dma_start(out=r0t[:, 8 * k:8 * k + 8], in_=d["r0_wT"][128 * k:128 * (k + 1), :]), 16)
            c_wst.tag(g, g.dma_start(out=brep[:], in_=d["r0_b"].broadcast_to([128, NE])), 16)
        r0_mark = c_wst.mark()

        for ci in range(4):
            @b.tensor
            def _(te, ci=ci):
                c_wst.wait(te, mark=r0_mark)
                c_ev.wait(te)
                for k in range(NT):
                    ins = te.matmul(ps_l[:, 0:512], r0t[:, 8 * k:8 * k + 8],
                                    self.xn[k][:, 3 + 512 * ci:3 + 512 * (ci + 1)],
                                    start=(k == 0), stop=(k == NT - 1))
                c_mm.tag(ins)

            @b.scalar
            def _(sc, ci=ci):
                c_mm.wait(sc)
                c_ev.tag(sc.copy(out=lg[:, 512 * ci:512 * (ci + 1)], in_=ps_l[:, 0:512]))

        if "lgdbg" in self.dbg:
            @b.sync
            def _(sy):
                c_ev.wait(sy)
                c_sp.tag(sy, sy.dma_start(out=d["lgdbg"], in_=lg[0:8, :]), 16)

        for i in range(16):
            @b.tensor
            def _(te, i=i):
                c_ev.wait(te)
                c_dv.wait(te)
                c_mm.tag(te.transpose(ps_t[0:128, 0:8], lg[0:8, 128 * i:128 * (i + 1)], self.identf[0:8, 0:8]))

            @b.vector
            def _(v, i=i):
                c_mm.wait(v)
                c_dv.tag(v.tensor_tensor(lgt[:, 8 * i:8 * i + 8], ps_t[:, 0:8], brep[:], ALU.add))

        from bass_rust import AP as _AP

        @b.vector
        def _(v):
            c_dv.wait(v)
            lgt3 = _AP(lgt[:].tensor, lgt[:].offset, [[128, 128], [8, 16], [1, 8]])
            mxb = _AP(mx[:].tensor, mx[:].offset, [[16, 128], [1, 16], [0, 8]])
            v.tensor_reduce(mx[:], lgt3, axis=AX.X, op=ALU.max)
            v.drain()
            v.tensor_tensor(ohh[:], lgt3, mxb, ALU.subtract)
            v.drain()
            c_dv.tag(v.nop())

        @b.scalar
        def _(sc):
            c_dv.wait(sc)
            c_ev.tag(sc.activation(wsel[:], ohh[:], ACTF.Exp))

        @b.vector
        def _(v):
            c_ev.wait(v)
            lgt3 = _AP(lgt[:].tensor, lgt[:].offset, [[128, 128], [8, 16], [1, 8]])
            mxb = _AP(mx[:].tensor, mx[:].offset, [[16, 128], [1, 16], [0, 8]])
            smb = _AP(smv[:].tensor, smv[:].offset, [[16, 128], [1, 16], [0, 8]])
            ws3 = _AP(wsel[:].tensor, wsel[:].offset, [[128, 128], [8, 16], [1, 8]])
            v.tensor_reduce(smv[:], ws3, axis=AX.X, op=ALU.add)
            v.drain()
            v.reciprocal(smv[:], smv[:])
            v.drain()
            v.tensor_tensor(ohh[:], lgt3, mxb, ALU.is_ge)
            v.drain()
            v.tensor_tensor(wsel[:], ohh[:], smb, ALU.mult)
            v.drain()
            c_dv.tag(v.nop())

        for i in range(16):
            @b.tensor
            def _(te, i=i):
                c_dv.wait(te)
                c_ev.wait(te)
                c_mm.tag(te.transpose(ps_t[0:8, 0:128], wsel[:, 8 * i:8 * i + 8], self.identf[:]))

            @b.scalar
            def _(sc, i=i):
                c_mm.wait(sc)
                c_ev.tag(sc.copy(out=wst_sb[0:8, 128 * i:128 * (i + 1)], in_=ps_t[0:8, 0:128]))

        @b.sync
        def _(sy):
            c_ev.wait(sy)
            c_sp.tag(sy, sy.dma_start(out=d["wsel_dram"], in_=wst_sb[0:8, :]), 16)
        wsel_mark = c_sp.mark()

        for e in range(NE):
            half = e % 2

            @b.gpsimd
            def _(g, e=e, half=half):
                if e >= 2:
                    c_mm.wait(g)
                for k in range(NT):
                    c_wst.tag(g, g.dma_start(out=wsl[k][:, D * half:D * (half + 1)],
                                             in_=d["exp0_wT"][e, 128 * k:128 * (k + 1), :]), 16)
            w_mark = c_wst.mark()

            @b.sync
            def _(sy, e=e):
                c_sp.wait(sy, mark=wsel_mark)
                c_dv.wait(sy)
                c_ld.tag(sy, sy.dma_start(out=grep[:],
                                          in_=d["wsel_dram"][e:e + 1, :].broadcast_to([128, TC])), 16)
            g_mark = c_ld.mark()

            mm_g = []
            ev_g = []

            @b.tensor
            def _(te, e=e, half=half):
                c_wst.wait(te, mark=w_mark)
                ev_base = c_ev.n
                g = 0
                for j in range(NT):
                    for ci in range(4):
                        if g == 0:
                            c_ev.wait(te)
                            c_dv.wait(te, mark=max(0, c_dv.n - 2))
                        elif g >= 2:
                            te.wait_ge(c_ev.sem, ev_base + g - 1)
                        for k in range(NT):
                            ins = te.matmul(ps_f[:, 512 * (g % 2):512 * (g % 2) + 512],
                                            wsl[k][:, D * half + 128 * j:D * half + 128 * (j + 1)],
                                            self.xn[k][:, 3 + 512 * ci:3 + 512 * (ci + 1)],
                                            start=(k == 0), stop=(k == NT - 1))
                        c_mm.tag(ins)
                        mm_g.append(c_mm.n)
                        g += 1

            @b.scalar
            def _(sc, e=e):
                dv0 = getattr(self, "_s4_dv", 0)
                sc.wait_ge(self.c_dv.sem, dv0)
                dv_base0 = self.c_dv.n
                g = 0
                for j in range(NT):
                    for ci in range(4):
                        sc.wait_ge(c_mm.sem, mm_g[g])
                        if g >= 2:
                            sc.wait_ge(self.c_dv.sem, dv_base0 + g - 1)
                        c_ev.tag(sc.activation(tmp_e2[g % 2][:], ps_f[:, 512 * (g % 2):512 * (g % 2) + 512],
                                               ACTF.Identity, bias=self.sm["exp0_b"][:, NT * e + j:NT * e + j + 1]))
                        ev_g.append(c_ev.n)
                        g += 1

            @b.vector
            def _(v, e=e):
                c_ld.wait(v, mark=g_mark)
                g = 0
                for j in range(NT):
                    for ci in range(4):
                        v.wait_ge(c_ev.sem, ev_g[g])
                        if e == 0:
                            c_dv.tag(v.tensor_tensor(self.facc[j][:, 512 * ci:512 * (ci + 1)], tmp_e2[g % 2][:],
                                                     grep[:, 512 * ci:512 * (ci + 1)], ALU.mult))
                        else:
                            v.tensor_tensor(tmp_e2[g % 2][:], tmp_e2[g % 2][:], grep[:, 512 * ci:512 * (ci + 1)], ALU.mult)
                            v.drain()
                            c_dv.tag(v.tensor_tensor(self.facc[j][:, 512 * ci:512 * (ci + 1)],
                                                     self.facc[j][:, 512 * ci:512 * (ci + 1)], tmp_e2[g % 2][:], ALU.add))
                        g += 1
                v.drain()
                c_dv.tag(v.nop())
            self._s4_dv = c_dv.n
        s4.close()

    def s5_scan(self):
        b, d = self.b, self.dram
        c_wst, c_mm, c_ev, c_dv, c_ld, c_sp = self.c_wst, self.c_mm, self.c_ev, self.c_dv, self.c_ld, self.c_sp
        s5 = ExitStack()
        dupw = self.sb("dupw", [64, DSH], stack=s5)
        dlr = self.sb("dlr", [64, TC], stack=s5)
        xcd = self.sb("xcd", [128, 2 * TC], stack=s5)
        dsb = self.sb("dsb", [128, TC], stack=s5)
        spz = self.sb("spz", [128, 512], F32, stack=s5)
        spc = self.sb("spc", [128, 512], F32, stack=s5)
        dxsb = self.sb("dxsb", [128, TC], stack=s5)
        ps_d = self.psum("ps_d", [128, 1024], F32, stack=s5)

        @b.gpsimd
        def _(g):
            off = g.compute_val(g.partition_id() * DSH)
            c_wst.tag(g, g.dma_start(out=dupw[:], in_=d["dup_w"][:, bass.ds(off, DSH)]), 16)
        dupw_mark = c_wst.mark()

        @b.scalar
        def _(sc2):
            off = self.rk(sc2, DSH)
            self.c_coll.wait(sc2, mark=self.ag_l0_mark)
            xa = d["xc_all"].rearrange("(r n) t -> n r t", n=DI)
            self.c_ldb.tag(sc2, sc2.dma_start(
                out=d["xc_my"].rearrange("n (r t) -> n r t", r=NCORE),
                in_=xa[bass.ds(off, DSH), :, :]), 16)
        self.xcmy_mark = self.c_ldb.mark()

        for r in range(NCORE):
            @b.sync
            def _(sy, r=r):
                self.c_coll.wait(sy, mark=self.ag_l0_mark)
                c_dv.wait(sy)
                c_ld.tag(sy, sy.dma_start(out=dlr[:], in_=d["bcd_all"][96 * r:96 * r + 64, :]), 16)
            dlr_mark = c_ld.mark()

            @b.scalar
            def _(sc2, r=r):
                off = self.rk(sc2, DSH)
                xa = d["xc_all"].rearrange("(r n) t -> r n t", n=DI)
                c_dv.wait(sc2)
                self.c_ldb.tag(sc2, sc2.dma_start(
                    out=xcd[:].rearrange("p (a t) -> p a t", a=2), in_=xa[r][bass.ds(off, 256), :].rearrange("(a p) t -> p a t", a=2)), 16)
            xcd_mark = self.c_ldb.mark()
            for dt in range(2):

                for ci in range(4):
                    @b.tensor
                    def _(te, r=r, dt=dt, ci=ci):
                        c_wst.wait(te, mark=dupw_mark)
                        c_ld.wait(te, mark=dlr_mark)
                        c_ev.wait(te)
                        c_mm.tag(te.matmul(ps_d[:, 512 * (ci % 2):512 * (ci % 2) + 512],
                                           dupw[:, 128 * dt:128 * (dt + 1)],
                                           dlr[:, 512 * ci:512 * (ci + 1)], start=True, stop=True))

                    @b.scalar
                    def _(sc, r=r, dt=dt, ci=ci):
                        c_mm.wait(sc)
                        c_dv.wait(sc)
                        if ci == 0:
                            c_sp.wait(sc)
                        c_ev.tag(sc.activation(spz[:], ps_d[:, 512 * (ci % 2):512 * (ci % 2) + 512],
                                               ACTF.Identity, bias=self.sm["dup_b"][:, dt:dt + 1]))

                    @b.vector
                    def _(v, r=r, dt=dt, ci=ci):
                        c_ev.wait(v)
                        v.tensor_scalar(spc[:], spz[:], 17.0, None, ALU.min)
                        v.drain()
                        c_dv.tag(v.nop())

                    @b.scalar
                    def _(sc, r=r, dt=dt, ci=ci):
                        c_dv.wait(sc)
                        c_ev.tag(sc.activation(spc[:], spc[:], ACTF.Exp))

                    @b.vector
                    def _(v, r=r, dt=dt, ci=ci):
                        c_ev.wait(v)
                        v.tensor_scalar(spc[:], spc[:], 1.0, None, ALU.add)
                        v.drain()
                        c_dv.tag(v.nop())

                    @b.scalar
                    def _(sc, r=r, dt=dt, ci=ci):
                        c_dv.wait(sc)
                        c_ev.tag(sc.activation(spc[:], spc[:], ACTF.Ln))

                    @b.vector
                    def _(v, r=r, dt=dt, ci=ci):
                        c_ev.wait(v)
                        v.tensor_scalar(spz[:], spz[:], 17.0, 0.0, ALU.subtract, op1=ALU.max)
                        v.drain()
                        v.tensor_tensor(dsb[:, 512 * ci:512 * (ci + 1)], spc[:], spz[:], ALU.add)
                        v.drain()
                        c_dv.tag(v.nop())

                @b.vector
                def _(v, r=r, dt=dt):
                    c_ev.wait(v)
                    self.c_ldb.wait(v, mark=xcd_mark)
                    c_sp.wait(v)
                    v.tensor_tensor(dxsb[:], dsb[:], xcd[:, TC * dt:TC * (dt + 1)], ALU.mult)
                    v.drain()
                    c_dv.tag(v.nop())

                @b.sync
                def _(sy, r=r, dt=dt):
                    c_dv.wait(sy)
                    c_sp.tag(sy, sy.dma_start(out=d["delta_dram"][128 * dt:128 * (dt + 1), TC * r:TC * (r + 1)], in_=dsb[:]), 16)
                    c_sp.tag(sy, sy.dma_start(out=d["dx_dram"][128 * dt:128 * (dt + 1), TC * r:TC * (r + 1)], in_=dxsb[:]), 16)

        ddx_mark = c_sp.mark()
        s5.close()

        s5b = ExitStack()
        TB = 4096
        brep_t = self.sb("brep_t", [128, TB], stack=s5b)
        crep_t = self.sb("crep_t", [128, TB], stack=s5b)
        drep = self.sb("drep", [128, TB], stack=s5b)
        dxrep = self.sb("dxrep", [128, TB], stack=s5b)
        ab_t = self.sb("ab_t", [128, TB], F32, stack=s5b)
        bx_t = self.sb("bx_t", [128, TB], stack=s5b)
        h_t = self.sb("h_t", [128, TB], stack=s5b)
        yv_t = self.sb("yv_t", [128, TB], stack=s5b)
        ysb = self.sb("ysb", [128, TB], stack=s5b)
        carry = self.sb("carry", [128, 32], F32, stack=s5b)
        ps_y = self.psum("ps_y", [128, TB], F32, stack=s5b)

        drep2 = [drep, self.sb("drepB", [128, TB], stack=s5b)]
        dxrep2 = [dxrep, self.sb("dxrepB", [128, TB], stack=s5b)]
        ab2 = [ab_t, self.sb("ab_tB", [128, TB], F32, stack=s5b)]
        yv2 = [yv_t, yv_t]
        for c in range(TFULL // TB):
            @b.sync
            def _(sy, c=c):
                c_sp.wait(sy, mark=ddx_mark)
                c_dv.wait(sy)
                c_mm.wait(sy)
                for g in range(8):
                    for rb in range(TB // TC):
                        r = (TB * c) // TC + rb
                        c_ld.tag(sy, sy.dma_start(
                            out=brep_t[16 * g:16 * g + 16, TC * rb:TC * (rb + 1)],
                            in_=d["bcd_all"][96 * r + 64:96 * r + 80, :]), 16)
                        c_ld.tag(sy, sy.dma_start(
                            out=crep_t[16 * g:16 * g + 16, TC * rb:TC * (rb + 1)],
                            in_=d["bcd_all"][96 * r + 80:96 * r + 96, :]), 16)
            bc_mark = c_ld.mark()

            for half in range(2):
                ld_marks = {}
                ev_base = c_ev.n
                dv_base = c_dv.n
                mm_base = c_mm.n

                @b.sync
                def _(sy, c=c, half=half):
                    for rr in range(16):
                        rt = 16 * half + rr
                        if rr >= 2:
                            sy.wait_ge(c_ev.sem, ev_base + rr - 1)
                            sy.wait_ge(c_dv.sem, dv_base + rr - 1)
                        elif rr == 0:
                            sy.wait_ge(c_ev.sem, ev_base)
                            sy.wait_ge(c_dv.sem, dv_base)
                        for g in range(8):
                            row = 8 * rt + g
                            c_ld.tag(sy, sy.dma_start(
                                out=drep2[rr % 2][16 * g:16 * g + 16, :],
                                in_=d["delta_dram"][row:row + 1, TB * c:TB * (c + 1)].broadcast_to([16, TB])), 16)
                            c_ld.tag(sy, sy.dma_start(
                                out=dxrep2[rr % 2][16 * g:16 * g + 16, :],
                                in_=d["dx_dram"][row:row + 1, TB * c:TB * (c + 1)].broadcast_to([16, TB])), 16)
                        ld_marks[rr] = c_ld.mark()

                @b.scalar
                def _(sc, c=c, half=half):
                    for rr in range(16):
                        rt = 16 * half + rr
                        c_ld.wait(sc, mark=ld_marks[rr])
                        if rr >= 2:
                            sc.wait_ge(c_dv.sem, dv_base + rr - 1)
                        c_ev.tag(sc.activation(ab2[rr % 2][:], drep2[rr % 2][:], ACTF.Exp,
                                               scale=self.sm["acol"][:, rt:rt + 1]))

                @b.vector
                def _(v, c=c, half=half):
                    c_ld.wait(v, mark=bc_mark)
                    for rr in range(16):
                        rt = 16 * half + rr
                        v.wait_ge(c_ev.sem, ev_base + rr + 1)
                        if rr >= 1:
                            v.wait_ge(c_mm.sem, mm_base + rr)
                        v.tensor_tensor(bx_t[:], dxrep2[rr % 2][:], brep_t[:], ALU.mult)
                        v.drain()
                        init = 0.0 if c % 2 == 0 else carry[:, rt:rt + 1]
                        v.tensor_tensor_scan(h_t[:], ab2[rr % 2][:], bx_t[:], init, ALU.mult, ALU.add)
                        v.drain()
                        v.tensor_copy(carry[:, rt:rt + 1], h_t[:, TB - 1:TB])
                        c_dv.tag(v.tensor_tensor(yv2[rr % 2][:], h_t[:], crep_t[:], ALU.mult))
                        v.drain()

                @b.tensor
                def _(te, c=c, half=half):
                    for rr in range(16):
                        te.wait_ge(c_dv.sem, dv_base + rr + 1)
                        if rr == 0:
                            c_ev.wait(te, mark=ev_base)
                        for s in range(TB // 512):
                            ins = te.matmul(ps_y[:, 512 * s:512 * (s + 1)],
                                            self.sum16[:, 128 * rr:128 * (rr + 1)],
                                            yv2[rr % 2][:, 512 * s:512 * (s + 1)],
                                            start=(rr == 0), stop=(rr == 15))
                        c_mm.tag(ins)

                @b.scalar
                def _(sc, c=c, half=half):
                    c_mm.wait(sc)
                    c_sp.wait(sc)
                    c_ev.tag(sc.activation(ysb[:], ps_y[:], ACTF.Copy))

                @b.sync
                def _(sy, c=c, half=half):
                    c_ev.wait(sy)
                    c_sp.tag(sy, sy.dma_start(out=d["y0_dram"][128 * half:128 * (half + 1), TB * c:TB * (c + 1)],
                                              in_=ysb[:]), 16)
        self.y0_mark = self.c_sp.mark()
        s5b.close()

        @b.gpsimd
        def _(g):
            self.c_coll.tag(g, g.collective_compute(
                "AllGather", ALU.bypass, ins=[d["res_dram"]], outs=[d["res_all"]],
                replica_groups=[list(range(NCORE))]), 1)
        self.ag_res_mark = self.c_coll.mark()

        @b.scalar
        def _(sc2):
            off = self.rk(sc2, DSH)
            self.c_coll.wait(sc2, mark=self.ag_res_mark)
            ra = d["res_all"].rearrange("(r n) t -> n r t", n=DI)
            self.c_ldb.tag(sc2, sc2.dma_start(
                out=d["res_my"].rearrange("n (r t) -> n r t", r=NCORE),
                in_=ra[bass.ds(off, DSH), :, :]), 16)
        self.resmy_mark = self.c_ldb.mark()

        s5c = ExitStack()
        y0sl = self.sb("y0sl", [128, 2 * TC], stack=s5c)
        xcd2 = self.sb("xcd2", [128, 2 * TC], stack=s5c)
        ressl = self.sb("ressl", [128, 2 * TC], stack=s5c)
        a0in = [self.sb(f"a0in{i}", [128, TC], stack=s5c) for i in range(2)]
        woslab = self.sb("woslab", [128, 2 * D], stack=s5c)
        pstg = self.sb("pstg", [128, 512], stack=s5c)
        ps_o = self.psum("ps_o", [128, 1024], F32, stack=s5c)

        @b.gpsimd
        def _(g):
            c_mm.wait(g)
            c_dv.wait(g)
            c_ev.wait(g)
            off = g.compute_val(g.partition_id() * DSH)
            c_wst.tag(g, g.dma_start(
                out=woslab[:].rearrange("p (a q) -> p a q", a=2),
                in_=d["w_outproj"][bass.ds(off, 256), :].rearrange("(a p) q -> p a q", a=2)), 16)
        wo_mark = c_wst.mark()

        for r in range(NCORE):
            @b.sync
            def _(sy, r=r):
                c_dv.wait(sy)
                c_sp.wait(sy, mark=self.y0_mark)
                c_ld.tag(sy, sy.dma_start(out=y0sl[:].rearrange("p (a t) -> p a t", a=2),
                                          in_=d["y0_dram"][:, TC * r:TC * (r + 1)]
                                          .rearrange("(a p) t -> p a t", a=2)), 16)
            y0_ld = c_ld.mark()

            @b.scalar
            def _(sc2, r=r):
                c_dv.wait(sc2)
                self.c_ldb.wait(sc2, mark=self.resmy_mark)
                self.c_ldb.tag(sc2, sc2.dma_start(
                    out=xcd2[:].rearrange("p (a t) -> p a t", a=2),
                    in_=d["xc_my"][:, TC * r:TC * (r + 1)].rearrange("(a p) t -> p a t", a=2)), 16)
                self.c_ldb.tag(sc2, sc2.dma_start(
                    out=ressl[:].rearrange("p (a t) -> p a t", a=2),
                    in_=d["res_my"][:, TC * r:TC * (r + 1)].rearrange("(a p) t -> p a t", a=2)), 16)
            xr_ld = self.c_ldb.mark()

            @b.vector
            def _(v, r=r):
                c_ld.wait(v, mark=y0_ld)
                self.c_ldb.wait(v, mark=xr_ld)
                c_mm.wait(v)
                for dt in range(2):
                    v.scalar_tensor_tensor(a0in[dt][:], xcd2[:, TC * dt:TC * (dt + 1)],
                                           self.sm["d_param_my"][:, dt:dt + 1],
                                           y0sl[:, TC * dt:TC * (dt + 1)], ALU.mult, ALU.add)
                    v.drain()
                    v.tensor_tensor(a0in[dt][:], a0in[dt][:], ressl[:, TC * dt:TC * (dt + 1)], ALU.mult)
                    v.drain()
                c_dv.tag(v.nop())

            for jo in range(NT):
                for ci in range(4):
                    @b.tensor
                    def _(te, r=r, jo=jo, ci=ci):
                        c_wst.wait(te, mark=wo_mark)
                        c_dv.wait(te)
                        c_ev.wait(te)
                        for dt in range(2):
                            ins = te.matmul(ps_o[:, 512 * (ci % 2):512 * (ci % 2) + 512],
                                            woslab[:, 2 * 128 * jo + 256 * 0 + dt * D if False else D * dt + 128 * jo:D * dt + 128 * (jo + 1)],
                                            a0in[dt][:, 512 * ci:512 * (ci + 1)],
                                            start=(dt == 0), stop=(dt == 1))
                        c_mm.tag(ins)

                    @b.scalar
                    def _(sc, r=r, jo=jo, ci=ci):
                        c_mm.wait(sc)
                        c_sp.wait(sc)
                        c_ev.tag(sc.activation(pstg[:], ps_o[:, 512 * (ci % 2):512 * (ci % 2) + 512], ACTF.Copy))

                    @b.sync
                    def _(sy, r=r, jo=jo, ci=ci):
                        c_ev.wait(sy)
                        c_sp.tag(sy, sy.dma_start(
                            out=d["partial0_dram"][D * r + 128 * jo:D * r + 128 * (jo + 1), 512 * ci:512 * (ci + 1)],
                            in_=pstg[:]), 16)
        p0_mark = c_sp.mark()
        s5c.close()

        @b.gpsimd
        def _(g):
            self.c_sp.wait(g, mark=p0_mark)
            self.c_coll.tag(g, g.collective_compute(
                "ReduceScatter", ALU.add, ins=[d["partial0_dram"]], outs=[d["rs0_dram"]],
                replica_groups=[list(range(NCORE))]), 1)
        self.rs0_mark = self.c_coll.mark()

    def s6_gate_outproj(self):
        b, d = self.b, self.dram
        c_mm, c_ev, c_dv, c_ld, c_sp = self.c_mm, self.c_ev, self.c_dv, self.c_ld, self.c_sp
        s6 = ExitStack()
        rs0t = self.sb("rs0t", [128, TC], stack=s6)
        xf6 = self.sb("x6frag", [128, TC], F32, stack=s6)
        t6 = self.sb("t6", [128, TC], F32, stack=s6)
        x1s = self.sb("x1s", [128, TC], stack=s6)
        x1s2 = self.sb("x1s2", [128, TC], stack=s6)
        for j in range(NT):
            @b.sync
            def _(sy, j=j):
                self.c_coll.wait(sy, mark=self.rs0_mark)
                c_dv.wait(sy)
                c_ld.tag(sy, sy.dma_start(out=rs0t[:], in_=d["rs0_dram"][128 * j:128 * (j + 1), :]), 16)
                c_ld.tag(sy, sy.dma_start(out=xf6[:], in_=d["x_t"][128 * j:128 * (j + 1), 3:]), 16)
            rj_mark = c_ld.mark()
            x1b = [x1s, x1s2][j % 2]

            @b.vector
            def _(v, j=j, x1b=x1b):
                c_ld.wait(v, mark=rj_mark)
                c_sp.wait(v)
                v.tensor_tensor(t6[:], rs0t[:], self.xn[j][:, 3:], ALU.add)
                v.drain()
                v.tensor_tensor(t6[:], t6[:], self.facc[j][:], ALU.add)
                v.drain()
                v.tensor_tensor(x1b[:], t6[:], xf6[:], ALU.add)
                v.drain()
                c_dv.tag(v.nop())

            @b.sync
            def _(sy, j=j, x1b=x1b):
                c_dv.wait(sy)
                c_sp.tag(sy, sy.dma_start(out=d["x1_dram"][128 * j:128 * (j + 1), :], in_=x1b[:]), 16)
        s6.close()

    def s7_ln1(self):
        b, d = self.b, self.dram
        self.ln2_stack = ExitStack()
        self.ps_stat = self.psum("ps_stat2", [1, 1024], stack=self.ln2_stack)
        self.ps_bc = self.psum("ps_bc2", [128, 1536], stack=self.ln2_stack)
        x1_mark = self.c_sp.mark()
        self.emit_ln(src=None, out_xn=self.xn, out_xr=None, width=TC,
                     rms_cols=None, rms_col_idx=None, src_f32=False,
                     src_dram=d["x1_dram"], dram_wait=(self.c_sp, x1_mark))
        if "xn1_dram" in self.dbg:
            @b.sync
            def _(sy):
                self.c_dv.wait(sy)
                for j in range(NT):
                    self.c_sp.tag(sy, sy.dma_start(out=d["xn1_dram"][128 * j:128 * (j + 1), :],
                                                   in_=self.xn[j][:, 0:TC]), 16)

    def s8_kvr(self):
        b, d = self.b, self.dram
        c_wst, c_mm, c_ev, c_dv, c_sp = self.c_wst, self.c_mm, self.c_ev, self.c_dv, self.c_sp
        self.ln2_stack.close()
        s8 = ExitStack()
        wsl8 = [self.sb(f"w8_{k}", [128, D], stack=s8) for k in range(NT)]
        kst2 = [self.sb(f"kst{i}", [128, TC], stack=s8) for i in range(NT)]
        ps_k = self.psum("ps_k", [128, 1024], F32, stack=s8)
        names = ["wkT", "wvT", "wrT"]
        for wi, wname in enumerate(names):
            @b.gpsimd
            def _(g, wi=wi, wname=wname):
                c_mm.wait(g)
                for k in range(NT):
                    c_wst.tag(g, g.dma_start(out=wsl8[k][:], in_=d[wname][128 * k:128 * (k + 1), :]), 16)
            w_mark = c_wst.mark()
            mm_g8 = []
            ev_g8 = []

            @b.tensor
            def _(te, wi=wi):
                c_wst.wait(te, mark=w_mark)
                c_dv.wait(te)
                ev_base = c_ev.n
                g = 0
                for j in range(NT):
                    for ci in range(4):
                        if g == 0:
                            c_ev.wait(te)
                        elif g >= 2:
                            te.wait_ge(c_ev.sem, ev_base + g - 1)
                        for k in range(NT):
                            ins = te.matmul(ps_k[:, 512 * (g % 2):512 * (g % 2) + 512],
                                            wsl8[k][:, 128 * j:128 * (j + 1)],
                                            self.xn[k][:, 512 * ci:512 * (ci + 1)],
                                            start=(k == 0), stop=(k == NT - 1))
                        c_mm.tag(ins)
                        mm_g8.append(c_mm.n)
                        g += 1

            @b.scalar
            def _(sc, wi=wi):
                sp0 = getattr(self, "_s8_sp", None)
                if sp0 is not None:
                    c_sp.wait(sc, mark=sp0)
                fn = ACTF.Sigmoid if wi == 2 else ACTF.Copy
                g = 0
                for j in range(NT):
                    for ci in range(4):
                        sc.wait_ge(c_mm.sem, mm_g8[g])
                        c_ev.tag(sc.activation(kst2[j][:, 512 * ci:512 * (ci + 1)],
                                               ps_k[:, 512 * (g % 2):512 * (g % 2) + 512], fn))
                        ev_g8.append(c_ev.n)
                        g += 1

            @b.sync
            def _(sy, wi=wi):
                g = 0
                for j in range(NT):
                    sy.wait_ge(c_ev.sem, ev_g8[4 * j + 3])
                    c_sp.tag(sy, sy.dma_start(out=d["kvr_dram"][D * wi + 128 * j:D * wi + 128 * (j + 1), :],
                                              in_=kst2[j][:]), 16)
            self._s8_sp = c_sp.mark()
        kvr_mark = c_sp.mark()
        s8.close()

        @b.gpsimd
        def _(g):
            c_sp.wait(g, mark=kvr_mark)
            self.c_coll.tag(g, g.collective_compute(
                "AllGather", ALU.bypass, ins=[d["kvr_dram"]], outs=[d["kvr_all"]],
                replica_groups=[list(range(NCORE))]), 1)
        self.ag_kvr_mark = self.c_coll.mark()


    def s9_rwkv(self):
        LV = getattr(self, 's9_level', 99)
        b, d = self.b, self.dram
        c_wst, c_mm, c_ev, c_dv, c_ld, c_sp = self.c_wst, self.c_mm, self.c_ev, self.c_dv, self.c_ld, self.c_sp
        s9 = ExitStack()
        kch = self.sb("kch", [128, 4096], stack=s9)
        vch = self.sb("vch", [128, 4096], stack=s9)
        rch = self.sb("rch", [128, 4096], stack=s9)
        kap = self.sb("kap", [128, 4096], stack=s9)
        wpow = self.sb("wpow", [128, 33], F32, stack=s9)
        wipow = self.sb("wipow", [128, 16], F32, stack=s9)
        p16 = self.sb("p16", [128, 128], stack=s9)
        p16i = self.sb("p16i", [128, 128], stack=s9)
        p16i_h = [self.sb(f"p16ih{h}", [128, 128], stack=s9) for h in range(2)]
        p16f_h = [self.sb(f"p16fh{h}", [128, 128], stack=s9) for h in range(2)]
        hmask = self.sb("hmask", [128, 2], F32, stack=s9)
        pmaskc = self.sb("pmaskc", [128, 4], F32, stack=s9)
        khTm = [self.sb(f"khTm{p}", [128, 128], stack=s9) for p in range(4)]
        r2h = [self.sb(f"r2h{h}", [128, 128], stack=s9) for h in range(2)]
        p16f = self.sb("p16f", [128, 128], stack=s9)
        p32 = self.sb("p32", [128, 128], stack=s9)
        p32r = self.sb("p32r", [128, 128], stack=s9)
        a1 = self.sb("a1", [128, 128], stack=s9)
        r2 = self.sb("r2", [128, 128], stack=s9)
        b1 = self.sb("b1", [128, 128], stack=s9)
        b2 = self.sb("b2", [128, 128], stack=s9)
        b1b = self.sb("b1b", [128, 128], stack=s9)
        b2b = self.sb("b2b", [128, 128], stack=s9)
        kh = self.sb("kh", [128, 128], stack=s9)
        khT = self.sb("khT", [128, 128], stack=s9)
        vT = self.sb("vT", [128, 128], stack=s9)
        msb = [self.sb(f"msb{h}", [128, 128], stack=s9) for h in range(2)]
        mtmp = self.sb("mtmp", [128, 128], stack=s9)
        shist = self.sb("shist", [128, 5 * 64], stack=s9)
        y9stg = self.sb("y9stg", [128, 4096], stack=s9)
        wo1slab = self.sb("wo1slab", [128, D], stack=s9)
        p1stg = self.sb("p1stg", [128, 512], stack=s9)
        ps_tr = self.psum("ps_tr", [128, 1024], BF16, stack=s9)
        ps_m = self.psum("ps_m", [128, 512], F32, stack=s9)
        ps_m2 = self.psum("ps_m2", [128, 512], F32, stack=s9)
        ps_kv = self.psum("ps_kv", [128, 512], F32, stack=s9)
        ps_y9 = self.psum("ps_y9", [128, 512], F32, stack=s9)
        ps_p1 = self.psum("ps_p1", [128, 1024], F32, stack=s9)

        @b.gpsimd
        def _(g):
            self.c_coll.wait(g, mark=self.ag_kvr_mark)
            c_mm.wait(g)
            c_dv.wait(g)
            c_ev.wait(g)
            kvr_a = d["kvr_all"].rearrange("(o q) t -> q o t", o=NCORE)
            for wi, nm in enumerate(["k_my", "v_my", "r_my"]):
                off = self.rk(g, 128, 1024 * wi)
                c_wst.tag(g, g.dma_start(out=d[nm].rearrange("n (o t) -> n o t", o=NCORE),
                                         in_=kvr_a[bass.ds(off, 128), :, :]), 16)
            off2 = self.rk(g, 128)
            c_wst.tag(g, g.dma_start(out=wo1slab[:], in_=d["woT"][bass.ds(off2, 128), :]), 16)
        my_mark = c_wst.mark()

        @b.vector
        def _(v):
            c_ld.wait(v)
            c_ev.wait(v)
            c_sp.wait(v)
            c_mm.wait(v)
            v.memset(wpow[:, 0:1], 1.0)
            v.drain()
            for j in range(32):
                v.tensor_scalar(wpow[:, j + 1:j + 2], wpow[:, j:j + 1], self.sm["tdec"][:, 0:1], None, ALU.mult)
                v.drain()
            v.reciprocal(wipow[:, 1:2], self.sm["tdec"][:, 0:1])
            v.memset(wipow[:, 0:1], 1.0)
            v.drain()
            for j in range(1, 15):
                v.tensor_tensor(wipow[:, j + 1:j + 2], wipow[:, j:j + 1], wipow[:, 1:2], ALU.mult)
                v.drain()
            from bass_rust import AP as _AP
            v.tensor_copy(p16[:], _AP(wpow[:].tensor, wpow[:].offset, [[33, 128], [0, 8], [1, 16]]))
            v.tensor_copy(p16i[:], _AP(wipow[:].tensor, wipow[:].offset, [[16, 128], [0, 8], [1, 16]]))
            v.tensor_copy(p16f[:], _AP(wpow[:].tensor, wpow[:].offset + 16, [[33, 128], [0, 8], [-1, 16]]))
            v.tensor_copy(p32[:], _AP(wpow[:].tensor, wpow[:].offset + 31, [[33, 128], [0, 4], [-1, 32]]))
            v.tensor_copy(p32r[:], _AP(wpow[:].tensor, wpow[:].offset + 1, [[33, 128], [0, 4], [1, 32]]))
            v.drain()
            v.tensor_scalar(hmask[:, 0:1], self.io_row[:, 0:1], 63.5, None, ALU.is_le)
            v.tensor_scalar(hmask[:, 1:2], self.io_row[:, 0:1], 63.5, None, ALU.is_gt if hasattr(ALU, 'is_gt') else ALU.is_ge)
            for pp in range(4):
                v.tensor_scalar(self.tmpc1[:, 0:1], self.io_row[:, 0:1], 32.0 * pp - 0.5, None, ALU.is_ge)
                v.drain()
                v.tensor_scalar(self.tmpc2[:, 0:1], self.io_row[:, 0:1], 32.0 * (pp + 1) - 0.5, None, ALU.is_lt)
                v.drain()
                v.tensor_tensor(pmaskc[:, pp:pp + 1], self.tmpc1[:, 0:1], self.tmpc2[:, 0:1], ALU.mult)
                v.drain()
            for h in range(2):
                v.tensor_tensor(p16i_h[h][:], p16i[:], hmask[:, h:h + 1].broadcast_to([128, 128]), ALU.mult)
                v.tensor_tensor(p16f_h[h][:], p16f[:], hmask[:, h:h + 1].broadcast_to([128, 128]), ALU.mult)
            v.drain()
            c_dv.tag(v.nop())

        for cc in range(4):
            @b.sync
            def _(sy, cc=cc):
                c_wst.wait(sy, mark=my_mark)
                c_dv.wait(sy)
                c_ev.wait(sy)
                c_mm.wait(sy)
                c_ld.tag(sy, sy.dma_start(out=kch[:], in_=d["k_my"][:, 4096 * cc:4096 * (cc + 1)]), 16)
                c_ld.tag(sy, sy.dma_start(out=vch[:], in_=d["v_my"][:, 4096 * cc:4096 * (cc + 1)]), 16)
                c_ld.tag(sy, sy.dma_start(out=rch[:], in_=d["r_my"][:, 4096 * cc:4096 * (cc + 1)]), 16)
            ch_mark = c_ld.mark()

            @b.vector
            def _(v, cc=cc):
                c_ld.wait(v, mark=ch_mark)
                v.tensor_scalar(kap[:], kch[:], self.sm["tfirst"][:, 0:1], None, ALU.mult)
                v.drain()
                if cc % 2 == 0:
                    v.memset(shist[:, 0:64], 0.0)
                    v.drain()
                c_dv.tag(v.nop())

            for tt in range(32):
                sl = slice(128 * tt, 128 * (tt + 1))

                @b.vector
                def _(v, cc=cc, tt=tt, sl=sl):
                    c_mm.wait(v)
                    c_ev.wait(v)
                    if "kh9dbg" in self.dbg:
                        c_sp.wait(v)
                    v.tensor_tensor(a1[:], rch[:, sl], p16[:], ALU.mult)
                    v.tensor_tensor(r2[:], rch[:, sl], p32r[:], ALU.mult)
                    v.tensor_tensor(b1[:], kap[:, sl], p16i_h[0][:], ALU.mult)
                    v.tensor_tensor(b1b[:], kap[:, sl], p16i_h[1][:], ALU.mult)
                    v.tensor_tensor(b2[:], kap[:, sl], p16f_h[0][:], ALU.mult)
                    v.tensor_tensor(b2b[:], kap[:, sl], p16f_h[1][:], ALU.mult)
                    v.tensor_tensor(kh[:], kap[:, sl], p32[:], ALU.mult)
                    v.drain()
                    for h in range(2):
                        v.tensor_tensor(r2h[h][:], r2[:], hmask[:, h:h + 1].broadcast_to([128, 128]), ALU.mult)
                    v.drain()
                    c_dv.tag(v.nop())

                if LV < 2:
                    continue

                @b.tensor
                def _(te, cc=cc, tt=tt, sl=sl):
                    c_dv.wait(te)
                    c_ev.wait(te)
                    te.transpose(ps_tr[:, 0:128], kh[:], self.ident[:])
                    c_mm.tag(te.transpose(ps_tr[:, 128:256], vch[:, sl], self.ident[:]))

                @b.scalar
                def _(sc, cc=cc, tt=tt):
                    c_mm.wait(sc)
                    sc.copy(out=khT[:], in_=ps_tr[:, 0:128])
                    c_ev.tag(sc.copy(out=vT[:], in_=ps_tr[:, 128:256]))

                if cc == 0 and tt == 0 and "kh9dbg" in self.dbg:
                    @b.sync
                    def _(sy):
                        c_ev.wait(sy)
                        c_sp.tag(sy, sy.dma_start(out=d["r9dbg"], in_=rch[:, 0:512]), 16)
                        c_sp.tag(sy, sy.dma_start(out=d["k9dbg"], in_=kap[:, 0:512]), 16)
                        c_sp.tag(sy, sy.dma_start(out=d["kh9dbg"], in_=kh[:]), 16)
                        c_sp.tag(sy, sy.dma_start(out=d["khT9dbg"], in_=khT[:]), 16)
                        c_sp.tag(sy, sy.dma_start(out=d["vT9dbg"], in_=vT[:]), 16)

                if LV < 3:
                    continue

                @b.vector
                def _(v, cc=cc, tt=tt):
                    c_ev.wait(v)
                    for p in range(4):
                        v.tensor_tensor(khTm[p][:], khT[:], pmaskc[:, p:p + 1].broadcast_to([128, 128]), ALU.mult)
                    v.drain()
                    c_dv.tag(v.nop())

                @b.tensor
                def _(te, cc=cc, tt=tt):
                    c_ev.wait(te)
                    c_dv.wait(te)
                    te.matmul(ps_m[:, 0:128], b1[:], a1[:], start=True, stop=True, skip_group_check=True)
                    te.matmul(ps_m[:, 128:256], b1b[:], a1[:], start=True, stop=True, skip_group_check=True)
                    te.matmul(ps_m2[:, 0:128], b2[:], a1[:], start=True, stop=True, skip_group_check=True)
                    te.matmul(ps_m2[:, 128:256], b2b[:], a1[:], start=True, stop=True, skip_group_check=True)
                    for p in range(4):
                        ins = te.matmul(ps_kv[:, 128 * p:128 * (p + 1)], khTm[p][:], vT[:],
                                        start=True, stop=True, skip_group_check=True)
                    c_mm.tag(ins)

                if LV < 4:
                    continue

                @b.vector
                def _(v, cc=cc, tt=tt):
                    c_mm.wait(v)
                    for h in range(2):
                        v.tensor_tensor(msb[h][:], ps_m[:, 128 * h:128 * (h + 1)], self.mk_diag[:], ALU.mult)
                        v.drain()
                        v.tensor_tensor(mtmp[:], ps_m2[:, 128 * h:128 * (h + 1)], self.mk_off[:], ALU.mult)
                        v.drain()
                        v.tensor_tensor(msb[h][:], msb[h][:], mtmp[:], ALU.add)
                        v.drain()
                    for p in range(4):
                        v.scalar_tensor_tensor(shist[0:64, 64 * (p + 1):64 * (p + 2)],
                                               shist[0:64, 64 * p:64 * (p + 1)],
                                               wpow[0:64, 32:33],
                                               ps_kv[0:64, 128 * p:128 * p + 64], ALU.mult, ALU.add)
                        v.scalar_tensor_tensor(shist[64:128, 64 * (p + 1):64 * (p + 2)],
                                               shist[64:128, 64 * p:64 * (p + 1)],
                                               wpow[64:128, 32:33],
                                               ps_kv[64:128, 128 * p + 64:128 * (p + 1)], ALU.mult, ALU.add)
                        v.drain()
                    c_dv.tag(v.nop())

                if LV < 5:
                    continue

                @b.tensor
                def _(te, cc=cc, tt=tt):
                    c_dv.wait(te)
                    c_ev.wait(te)
                    for h in range(2):
                        te.matmul(ps_y9[64 * h:64 * (h + 1), 0:128], vT[:, 64 * h:64 * (h + 1)],
                                  msb[h][:], start=True, stop=False,
                                  tile_position=(0, 64 * h), skip_group_check=True)
                    for h in range(2):
                        for p in range(4):
                            ins = te.matmul(ps_y9[64 * h:64 * (h + 1), 32 * p:32 * (p + 1)],
                                            shist[:, 64 * p:64 * (p + 1)],
                                            r2h[h][:, 32 * p:32 * (p + 1)],
                                            start=False, stop=(h == 1 and p == 3),
                                            tile_position=(0, 64 * h), skip_group_check=True)
                    c_mm.tag(ins)

                @b.scalar
                def _(sc, cc=cc, tt=tt, sl=sl):
                    c_mm.wait(sc)
                    c_ev.tag(sc.copy(out=y9stg[:, sl], in_=ps_y9[:, 0:128]))


                @b.vector
                def _(v, cc=cc, tt=tt):
                    c_ev.wait(v)
                    v.tensor_copy(shist[:, 0:64], shist[:, 256:320])
                    v.drain()
                    c_dv.tag(v.nop())

            for dout in range(NT):
                for ci in range(8):
                    @b.tensor
                    def _(te, cc=cc, dout=dout, ci=ci):
                        c_ev.wait(te)
                        c_dv.wait(te)
                        c_mm.tag(te.matmul(ps_p1[:, 512 * (ci % 2):512 * (ci % 2) + 512],
                                           wo1slab[:, 128 * dout:128 * (dout + 1)],
                                           y9stg[:, 512 * ci:512 * (ci + 1)],
                                           start=True, stop=True))

                    @b.scalar
                    def _(sc, cc=cc, dout=dout, ci=ci):
                        c_mm.wait(sc)
                        c_sp.wait(sc)
                        c_ev.tag(sc.activation(p1stg[:], ps_p1[:, 512 * (ci % 2):512 * (ci % 2) + 512], ACTF.Copy))

                    @b.sync
                    def _(sy, cc=cc, dout=dout, ci=ci):
                        c_ev.wait(sy)
                        gt = 4096 * cc + 512 * ci
                        owner = gt // TC
                        lt = gt - TC * owner
                        c_sp.tag(sy, sy.dma_start(
                            out=d["partial1_dram"][D * owner + 128 * dout:D * owner + 128 * (dout + 1), lt:lt + 512],
                            in_=p1stg[:]), 16)
        p1_mark = c_sp.mark()
        s9.close()

        @b.gpsimd
        def _(g):
            c_sp.wait(g, mark=p1_mark)
            self.c_coll.tag(g, g.collective_compute(
                "ReduceScatter", ALU.add, ins=[d["partial1_dram"]], outs=[d["rs1_dram"]],
                replica_groups=[list(range(NCORE))]), 1)
        self.rs1_mark = self.c_coll.mark()

    def s10_wo(self):
        pass

    def s11_l1_ffn(self):
        b, d = self.b, self.dram
        c_wst, c_mm, c_ev, c_dv, c_ld, c_sp = self.c_wst, self.c_mm, self.c_ev, self.c_dv, self.c_ld, self.c_sp
        from bass_rust import AP as _AP
        s11 = ExitStack()
        r1t = self.sb("r1t", [128, 64], stack=s11)
        lg1 = self.sb("lg1", [8, TC], F32, stack=s11)
        lgt1 = self.sb("lgt1", [128, 128], F32, stack=s11)
        mx1 = self.sb("mx1", [128, 16], F32, stack=s11)
        mx2 = self.sb("mx2", [128, 16], F32, stack=s11)
        g1v = self.sb("g1v", [128, 16], F32, stack=s11)
        g2v = self.sb("g2v", [128, 16], F32, stack=s11)
        oh1 = self.sb("oh1", [128, 128], F32, stack=s11)
        oh2 = self.sb("oh2", [128, 128], F32, stack=s11)
        l2t = self.sb("l2t", [128, 128], F32, stack=s11)
        poscum = [self.sb(f"poscum{k}", [128, 128], F32, stack=s11) for k in range(2)]
        carry1 = self.sb("carry1", [1, 2 * NE], F32, stack=s11)
        cntsb = self.sb("cntsb", [NCORE, 2 * NE], F32, stack=s11)
        prefsb = self.sb("prefsb", [128, 2 * NE], F32, stack=s11)
        gfin = self.sb("gfin", [128, 128], F32, stack=s11)
        gstg = self.sb("gstg", [8, TC], stack=s11)
        grep1 = self.sb("grep1", [128, TC], stack=s11)
        utrif = self.sb("utrif", [128, 128], F32, stack=s11)
        ps_l1 = self.psum("ps_l1", [8, 512], F32, stack=s11)
        ps_t1 = self.psum("ps_t1", [128, 512], F32, stack=s11)
        ps_c1 = self.psum("ps_c1", [128, 512], F32, stack=s11)

        @b.gpsimd
        def _(g):
            c_mm.wait(g)
            c_dv.wait(g)
            for k in range(NT):
                c_wst.tag(g, g.dma_start(out=r1t[:, 8 * k:8 * k + 8], in_=d["r1_wT"][128 * k:128 * (k + 1), :]), 16)
        r1_mark = c_wst.mark()

        @b.vector
        def _(v):
            self.c_gp.wait(v)
            v.tensor_copy(utrif[:], self.utri[:])
            v.drain()
            c_dv.tag(v.nop())

        for ci in range(4):
            @b.tensor
            def _(te, ci=ci):
                c_wst.wait(te, mark=r1_mark)
                c_ev.wait(te)
                c_dv.wait(te)
                for k in range(NT):
                    ins = te.matmul(ps_l1[:, 0:512], r1t[:, 8 * k:8 * k + 8],
                                    self.xn[k][:, 512 * ci:512 * (ci + 1)],
                                    start=(k == 0), stop=(k == NT - 1))
                c_mm.tag(ins)

            @b.scalar
            def _(sc, ci=ci):
                c_mm.wait(sc)
                c_ev.tag(sc.copy(out=lg1[:, 512 * ci:512 * (ci + 1)], in_=ps_l1[:, 0:512]))

        for i in range(16):
            @b.tensor
            def _(te, i=i):
                c_ev.wait(te)
                c_dv.wait(te)
                c_mm.tag(te.transpose(ps_t1[0:128, 0:8], lg1[0:8, 128 * i:128 * (i + 1)], self.identf[0:8, 0:8]))

            @b.vector
            def _(v, i=i):
                c_mm.wait(v)
                c_dv.tag(v.tensor_copy(lgt1[:, 8 * i:8 * i + 8], ps_t1[:, 0:8]))

        @b.vector
        def _(v):
            c_dv.wait(v)
            lgt3 = _AP(lgt1[:].tensor, lgt1[:].offset, [[128, 128], [8, 16], [1, 8]])
            mx1b = _AP(mx1[:].tensor, mx1[:].offset, [[16, 128], [1, 16], [0, 8]])
            mx2b = _AP(mx2[:].tensor, mx2[:].offset, [[16, 128], [1, 16], [0, 8]])
            v.tensor_reduce(mx1[:], lgt3, axis=AX.X, op=ALU.max)
            v.drain()
            v.tensor_tensor(oh1[:], lgt3, mx1b, ALU.is_ge)
            v.drain()
            v.tensor_scalar(l2t[:], oh1[:], -1e30, None, ALU.mult)
            v.drain()
            v.tensor_tensor(l2t[:], lgt1[:], l2t[:], ALU.add)
            v.drain()
            l2t3 = _AP(l2t[:].tensor, l2t[:].offset, [[128, 128], [8, 16], [1, 8]])
            v.tensor_reduce(mx2[:], l2t3, axis=AX.X, op=ALU.max)
            v.drain()
            v.tensor_tensor(oh2[:], l2t3, mx2b, ALU.is_ge)
            v.drain()
            v.tensor_tensor(g2v[:], mx2[:], mx1[:], ALU.subtract)
            v.drain()
            c_dv.tag(v.nop())

        @b.scalar
        def _(sc):
            c_dv.wait(sc)
            c_ev.tag(sc.activation(g2v[:], g2v[:], ACTF.Exp))

        @b.vector
        def _(v):
            c_ev.wait(v)
            v.tensor_scalar(g1v[:], g2v[:], 1.0, None, ALU.add)
            v.drain()
            v.reciprocal(g1v[:], g1v[:])
            v.drain()
            v.tensor_tensor(g2v[:], g2v[:], g1v[:], ALU.mult)
            v.drain()
            c_dv.tag(v.nop())

        @b.vector
        def _(v):
            c_dv.wait(v)
            v.memset(carry1[:], 0.0)
            v.drain()
            c_dv.tag(v.nop())

        for i in range(16):
            for k in range(2):
                oh = [oh1, oh2][k]

                @b.tensor
                def _(te, i=i, k=k, oh=oh):
                    c_dv.wait(te)
                    te.matmul(ps_c1[:, 8 * k:8 * (k + 1)], utrif[:], oh[:, 8 * i:8 * i + 8],
                              start=True, stop=False, skip_group_check=True)
                    c_mm.tag(te.matmul(ps_c1[:, 8 * k:8 * (k + 1)], self.onesf_row[:], carry1[0:1, 8 * k:8 * (k + 1)],
                                       start=False, stop=True, skip_group_check=True))

                @b.vector
                def _(v, i=i, k=k):
                    c_mm.wait(v)
                    v.tensor_copy(poscum[k][:, 8 * i:8 * i + 8], ps_c1[:, 8 * k:8 * (k + 1)])
                    v.drain()
                    c_dv.tag(v.nop())

                @b.tensor
                def _(te, i=i, k=k):
                    c_dv.wait(te)
                    c_mm.tag(te.matmul(ps_c1[0:1, 16 + 8 * k:24 + 8 * k], self.sel127[:],
                                       poscum[k][:, 8 * i:8 * i + 8], start=True, stop=True,
                                       skip_group_check=True))

                @b.vector
                def _(v, i=i, k=k):
                    c_mm.wait(v)
                    v.tensor_copy(carry1[0:1, 8 * k:8 * (k + 1)], ps_c1[0:1, 16 + 8 * k:24 + 8 * k])
                    v.drain()
                    c_dv.tag(v.nop())

        @b.sync
        def _(sy):
            c_dv.wait(sy)
            c_sp.tag(sy, sy.dma_start(out=d["cnt_dram"], in_=carry1[0:1, :]), 16)
        cnt_mark = c_sp.mark()

        @b.gpsimd
        def _(g):
            c_sp.wait(g, mark=cnt_mark)
            self.c_coll.tag(g, g.collective_compute(
                "AllGather", ALU.bypass, ins=[d["cnt_dram"]], outs=[d["cnt_all"]],
                replica_groups=[list(range(NCORE))]), 1)
        cnt_ag = self.c_coll.mark()

        @b.sync
        def _(sy):
            self.c_coll.wait(sy, mark=cnt_ag)
            c_ld.tag(sy, sy.dma_start(out=cntsb[:], in_=d["cnt_all"]), 16)
        cnt_ld = c_ld.mark()

        @b.tensor
        def _(te):
            c_ld.wait(te, mark=cnt_ld)
            c_dv.wait(te)
            c_mm.tag(te.matmul(ps_c1[0:1, 0:16], self.sm["pmask"][:], cntsb[:], start=True, stop=True, skip_group_check=True))

        @b.vector
        def _(v):
            c_mm.wait(v)
            v.tensor_copy(prefsb[0:1, :], ps_c1[0:1, 0:16])
            v.drain()
            c_dv.tag(v.nop())

        @b.tensor
        def _(te):
            c_dv.wait(te)
            c_mm.tag(te.matmul(ps_c1[:, 0:16], self.onesf_row[:], prefsb[0:1, :], start=True, stop=True, skip_group_check=True))

        @b.vector
        def _(v):
            c_mm.wait(v)
            v.tensor_copy(prefsb[:], ps_c1[:, 0:16])
            v.drain()
            for k in range(2):
                pref3 = _AP(prefsb[:].tensor, prefsb[:].offset + 8 * k, [[2 * NE, 128], [0, 16], [1, 8]])
                gk3 = _AP([g1v, g2v][k][:].tensor, [g1v, g2v][k][:].offset, [[16, 128], [1, 16], [0, 8]])
                dst = [oh1, oh2][k]
                v.tensor_tensor(poscum[k][:], poscum[k][:], pref3, ALU.add)
                v.drain()
                v.tensor_scalar(poscum[k][:], poscum[k][:], CAP + 0.5, None, ALU.is_le)
                v.drain()
                v.tensor_tensor(dst[:], dst[:], poscum[k][:], ALU.mult)
                v.drain()
                v.tensor_tensor(dst[:], dst[:], gk3, ALU.mult)
                v.drain()
            v.tensor_tensor(gfin[:], oh1[:], oh2[:], ALU.add)
            v.drain()
            c_dv.tag(v.nop())

        for i in range(16):
            @b.tensor
            def _(te, i=i):
                c_dv.wait(te)
                c_ev.wait(te)
                c_mm.tag(te.transpose(ps_t1[0:8, 0:128], gfin[:, 8 * i:8 * i + 8], self.identf[:]))

            @b.scalar
            def _(sc, i=i):
                c_mm.wait(sc)
                c_ev.tag(sc.copy(out=gstg[0:8, 128 * i:128 * (i + 1)], in_=ps_t1[0:8, 0:128]))

        @b.sync
        def _(sy):
            c_ev.wait(sy)
            c_sp.tag(sy, sy.dma_start(out=d["g1_dram"], in_=gstg[0:8, :]), 16)
        g1_mark = c_sp.mark()
        s11.close()

        s11b = ExitStack()
        w1sl = [self.sb(f"w1s{k}", [128, 2048], stack=s11b) for k in range(NT)]
        w2sl = [self.sb(f"w2s{k}", [128, D], stack=s11b) for k in range(16)]
        hsb = [self.sb(f"hsb{f}", [128, 512], stack=s11b) for f in range(16)]
        grep_b = self.sb("grep1b", [128, TC], stack=s11b)
        tmp11 = self.sb("tmp11", [128, 512], stack=s11b)
        ps_h = self.psum("ps_h", [128, 1024], F32, stack=s11b)
        ps_y1 = self.psum("ps_y1", [128, 2048], F32, stack=s11b)

        gate_marks = {}
        for e in range(NE):
            for fh in range(2):
                @b.gpsimd
                def _(g, e=e, fh=fh):
                    c_mm.wait(g)
                    if e == 0 and fh == 0:
                        c_sp.wait(g, mark=g1_mark)
                        c_dv.wait(g)
                        c_ev.wait(g)
                    for k in range(NT):
                        c_wst.tag(g, g.dma_start(out=w1sl[k][:],
                                                 in_=d["w1"][e, 128 * k:128 * (k + 1), 2048 * fh:2048 * (fh + 1)]), 16)
                    for k in range(16):
                        c_wst.tag(g, g.dma_start(out=w2sl[k][:],
                                                 in_=d["w2"][e, 2048 * fh + 128 * k:2048 * fh + 128 * (k + 1), 0:D]), 16)
                w_mark = c_wst.mark()

                if fh == 0:
                    @b.sync
                    def _(sy, e=e):
                        c_sp.wait(sy, mark=g1_mark)
                        if e > 0:
                            c_dv.wait(sy, mark=gate_marks.get(e - 1))
                        c_ld.tag(sy, sy.dma_start(out=grep_b[:],
                                                  in_=d["g1_dram"][e:e + 1, :].broadcast_to([128, TC])), 16)
                    self._s11_gemark = c_ld.mark()

                for tc4 in range(4):
                    prev_y_mark = getattr(self, "_s11_prev_y", 0)
                    mm_h = []
                    ev_h = []

                    @b.tensor
                    def _(te, e=e, fh=fh, tc4=tc4, prev_y_mark=prev_y_mark):
                        c_wst.wait(te, mark=w_mark)
                        ev_base = c_ev.n
                        for f in range(16):
                            if f == 0:
                                c_ev.wait(te)
                            elif f >= 2:
                                te.wait_ge(c_ev.sem, ev_base + f - 1)
                            for k in range(NT):
                                ins = te.matmul(ps_h[:, 512 * (f % 2):512 * (f % 2) + 512],
                                                w1sl[k][:, 128 * f:128 * (f + 1)],
                                                self.xn[k][:, 512 * tc4:512 * (tc4 + 1)],
                                                start=(k == 0), stop=(k == NT - 1))
                            c_mm.tag(ins)
                            mm_h.append(c_mm.n)

                    @b.scalar
                    def _(sc, e=e, fh=fh, tc4=tc4, prev_y_mark=prev_y_mark):
                        sc.wait_ge(c_mm.sem, prev_y_mark)
                        for f in range(16):
                            sc.wait_ge(c_mm.sem, mm_h[f])
                            c_ev.tag(sc.activation(hsb[f][:], ps_h[:, 512 * (f % 2):512 * (f % 2) + 512], ACTF.Gelu))
                            ev_h.append(c_ev.n)

                    @b.vector
                    def _(v, e=e, fh=fh, tc4=tc4):
                        c_ld.wait(v, mark=self._s11_gemark)
                        for f in range(16):
                            v.wait_ge(c_ev.sem, ev_h[f])
                            v.tensor_tensor(hsb[f][:], hsb[f][:], grep_b[:, 512 * tc4:512 * (tc4 + 1)], ALU.mult)
                        v.drain()
                        c_dv.tag(v.nop())

                    dv_base = c_dv.n
                    mm_y = []

                    @b.tensor
                    def _(te, e=e, fh=fh, tc4=tc4, dv_base=dv_base):
                        c_dv.wait(te)
                        j = 0
                        for dhalf in range(2):
                            for d4 in range(4):
                                dout = 4 * dhalf + d4
                                if dhalf == 1:
                                    te.wait_ge(c_dv.sem, dv_base + d4 + 1)
                                for k in range(16):
                                    ins = te.matmul(ps_y1[:, 512 * d4:512 * (d4 + 1)],
                                                    w2sl[k][:, 128 * dout:128 * (dout + 1)],
                                                    hsb[k][:], start=(k == 0), stop=(k == 15))
                                c_mm.tag(ins)
                                mm_y.append((dout, d4, c_mm.n))
                                j += 1

                    @b.vector
                    def _(v, e=e, fh=fh, tc4=tc4):
                        for dout, d4, mmv in mm_y:
                            v.wait_ge(c_mm.sem, mmv)
                            if e == 0 and fh == 0:
                                c_dv.tag(v.tensor_copy(self.facc[dout][:, 512 * tc4:512 * (tc4 + 1)],
                                                       ps_y1[:, 512 * d4:512 * (d4 + 1)]))
                            else:
                                c_dv.tag(v.tensor_tensor(self.facc[dout][:, 512 * tc4:512 * (tc4 + 1)],
                                                         self.facc[dout][:, 512 * tc4:512 * (tc4 + 1)],
                                                         ps_y1[:, 512 * d4:512 * (d4 + 1)], ALU.add))
                        v.drain()
                    self._s11_prev_y = c_mm.n
                gate_marks[e] = c_dv.n
        s11b.close()


    def s12_final_sum(self):
        b, d = self.b, self.dram
        c_mm, c_ev, c_dv, c_ld, c_sp = self.c_mm, self.c_ev, self.c_dv, self.c_ld, self.c_sp
        s12 = ExitStack()
        rs1t = self.sb("rs1t", [128, TC], stack=s12)
        x1ld = self.sb("x1ld", [128, TC], stack=s12)
        outf = [self.sb(f"outf{j}", [128, TC], F32, stack=s12) for j in range(NT)]
        outT = self.sb("outT", [128, D], F32, stack=s12)
        outT2 = self.sb("outT2", [128, D], F32, stack=s12)
        ps12 = self.psum("ps12", [128, 512], F32, stack=s12)

        for j in range(NT):
            @b.sync
            def _(sy, j=j):
                self.c_coll.wait(sy, mark=self.rs1_mark)
                c_dv.wait(sy)
                c_ld.tag(sy, sy.dma_start(out=rs1t[:], in_=d["rs1_dram"][128 * j:128 * (j + 1), :]), 16)
                c_ld.tag(sy, sy.dma_start(out=x1ld[:], in_=d["x1_dram"][128 * j:128 * (j + 1), :]), 16)
            r_mark = c_ld.mark()

            @b.vector
            def _(v, j=j):
                c_ld.wait(v, mark=r_mark)
                c_dv.wait(v)
                v.tensor_tensor(outf[j][:], rs1t[:], x1ld[:], ALU.add)
                v.drain()
                v.tensor_tensor(outf[j][:], outf[j][:], self.facc[j][:], ALU.add)
                v.drain()
                c_dv.tag(v.nop())

        for tg in range(16):
            ot = [outT, outT2][tg % 2]
            for j in range(NT):
                @b.tensor
                def _(te, tg=tg, j=j):
                    c_dv.wait(te)
                    c_ev.wait(te)
                    c_mm.tag(te.transpose(ps12[:, 128 * (j % 2):128 * (j % 2) + 128],
                                          outf[j][:, 128 * tg:128 * (tg + 1)], self.identf[:]))

                @b.scalar
                def _(sc, tg=tg, j=j, ot=ot):
                    c_mm.wait(sc)
                    if j == 0:
                        self.c_st.wait(sc)
                    c_ev.tag(sc.copy(out=ot[:, 128 * j:128 * (j + 1)], in_=ps12[:, 128 * (j % 2):128 * (j % 2) + 128]))

            @b.sync
            def _(sy, tg=tg, ot=ot):
                c_ev.wait(sy)
                self.c_st.tag(sy, sy.dma_start(out=d["out"][128 * tg:128 * (tg + 1), :], in_=ot[:]), 16)
        s12.close()

    def s_final(self):
        b, d = self.b, self.dram
        if self.stop_after is not None:
            zout = self.sb("zout", [128, D], F32)

            @b.vector
            def _(v):
                self.c_mm.wait(v)
                self.c_ev.wait(v)
                self.c_dv.tag(v.memset(zout[:], 0.0))

            @b.sync
            def _(sy):
                self.c_dv.wait(sy)
                for i in range(TC // 128):
                    self.c_st.tag(sy, sy.dma_start(out=d["out"][128 * i:128 * (i + 1), :], in_=zout[:]), 16)

        @b.sync
        def _(sy):
            self.c_dv.wait(sy)
            self.c_ev.wait(sy)
            self.c_mm.wait(sy)
            self.c_sp.wait(sy)
            self.c_ldb.wait(sy)
            self.c_ld.wait(sy)
            self.c_coll.wait(sy)
            self.c_st.wait(sy)


def build(dbg=(), stop_after=None):
    mk = MK(dbg=dbg, stop_after=stop_after)
    nc = mk.build()
    return nc, mk


def stage_inputs(ins):
    x = np.ascontiguousarray(ins["L0_attn_x"].reshape(TFULL, D))
    maps = []
    com = {
        "w_inproj": ins["L0_attn_in_proj_w"],
        "rms_w": ins["L0_attn_rms_w"].reshape(D, 1),
        "conv_w": ins["L0_attn_conv_w"],
        "conv_b": ins["L0_attn_conv_b"].reshape(DI, 1),
        "bcd_w": ins["L0_attn_bcd_w"],
        "dup_w": ins["L0_attn_dup_w"],
        "dup_b": ins["L0_attn_dup_b"].reshape(DI, 1),
        "lma": ins["L0_attn_log_minus_A"],
        "d_param": ins["L0_attn_D_param"].reshape(DI, 1),
        "w_outproj": ins["L0_attn_out_proj_w"],
        "r0_wT": np.ascontiguousarray(ins["L0_ffn_router_w"].T),
        "r0_b": ins["L0_ffn_router_b"].reshape(1, NE),
        "exp0_wT": np.ascontiguousarray(ins["L0_ffn_expert_w"].transpose(0, 2, 1)),
        "exp0_b": np.ascontiguousarray(ins["L0_ffn_expert_b"].reshape(NE, D, 1)),
        "wkT": np.ascontiguousarray(ins["L1_attn_Wk"].T),
        "wvT": np.ascontiguousarray(ins["L1_attn_Wv"].T),
        "wrT": np.ascontiguousarray(ins["L1_attn_Wr"].T),
        "woT": np.ascontiguousarray(ins["L1_attn_Wo"].T),
        "tdec": ins["L1_attn_time_decay"].reshape(D, 1),
        "tfirst": ins["L1_attn_time_first"].reshape(D, 1),
        "r1_wT": np.ascontiguousarray(ins["L1_ffn_router_w"].T),
        "w1": ins["L1_ffn_w1"],
        "w2": ins["L1_ffn_w2"],
    }
    com = {k: np.ascontiguousarray(v, dtype=np.float32) for k, v in com.items()}
    for c in range(NCORE):
        lo = c * TC
        xt = np.zeros((D, TC + 3), np.float32)
        xt[:, 3:] = x[lo:lo + TC].T
        if lo % 8192 != 0:
            xt[:, 0:3] = x[lo - 3:lo].T
        pm = np.zeros((NE, 1), np.float32)
        pm[:c] = 1.0
        m = dict(com)
        m["x_t"] = xt
        m["pmask"] = pm
        maps.append(m)
    return maps


_CACHED = {}


def kernel(**inputs):
    from concourse.bass_utils import run_bass_kernel_spmd
    if "nc" not in _CACHED:
        nc, _ = build(dbg=())
        _CACHED["nc"] = nc
    nc = _CACHED["nc"]
    maps = stage_inputs(inputs)
    res = run_bass_kernel_spmd(nc, maps, core_ids=list(range(NCORE)))
    out = np.concatenate([r["out"] for r in res.results], axis=0)
    return np.ascontiguousarray(out.reshape(2, 8192, D)).astype(np.float32)


# version 05 (final)
# speedup vs baseline: 1.5416x; 1.0435x over previous
from contextlib import ExitStack

import numpy as np
import concourse.bass as bass
import concourse.mybir as mybir

F32 = mybir.dt.float32
BF16 = mybir.dt.bfloat16
ALU = mybir.AluOpType
ACTF = mybir.ActivationFunctionType
AX = mybir.AxisListType

D = 1024
DI = 2048
NS = 16
DR = 64
TC = 2048
NT = D // 128
NTI = DI // 128
NCORE = 8
TFULL = TC * NCORE
EPS = 1e-5
CAP = 5120
FFN = 4096
NE = 8
DSH = DI // NCORE


class Ch:

    def __init__(self, es, nc, name, nsem=4):
        self.sems = [es.enter_context(nc.semaphore(f"{name}{i}")) for i in range(nsem)]
        self.tot = [0] * nsem
        self.guard = [0] * nsem
        self.cur = 0
        self.open = False

    def tag(self, eng, instr, k=1):
        i = self.cur
        if not self.open:
            if self.tot[i] > self.guard[i]:
                eng.wait_ge(self.sems[i], self.tot[i])
                self.guard[i] = self.tot[i]
            self.open = True
        instr.then_inc(self.sems[i], k)
        self.tot[i] += k
        return instr

    def mark(self):
        self.cur = (self.cur + 1) % len(self.sems)
        self.open = False
        return list(self.tot)

    def wait(self, eng, mark=None):
        if mark is None:
            mark = self.mark()
        for s, v in zip(self.sems, mark):
            if v > 0:
                eng.wait_ge(s, v)


class CCh:

    def __init__(self, es, nc, name):
        self.sem = es.enter_context(nc.semaphore(name))
        self.n = 0

    def tag(self, instr, k=1):
        instr.then_inc(self.sem, k)
        self.n += k
        return instr

    def mark(self):
        return self.n

    def wait(self, eng, mark=None):
        eng.wait_ge(self.sem, self.n if mark is None else mark)


class MK:
    def __init__(self, dbg=(), stop_after=None):
        self.dbg = set(dbg)
        self.stop_after = stop_after
        self.nc = bass.Bass("TRN2", target_bir_lowering=False)
        self.es = ExitStack()
        self.dram = {}

    def din(self, name, shape, dt=F32):
        self.dram[name] = self.nc.dram_tensor(name, shape, dt, kind="ExternalInput").ap()
        return self.dram[name]

    def dout(self, name, shape, dt=F32):
        self.dram[name] = self.nc.dram_tensor(name, shape, dt, kind="ExternalOutput").ap()
        return self.dram[name]

    def dint(self, name, shape, dt=BF16):
        kind = "ExternalOutput" if name in self.dbg else "Internal"
        self.dram[name] = self.nc.dram_tensor(name, shape, dt, kind=kind).ap()
        return self.dram[name]

    def sb(self, name, shape, dt=BF16, stack=None):
        return (stack or self.es).enter_context(self.nc.sbuf_tensor(name, shape, dt))

    def psum(self, name, shape, dt=F32, stack=None):
        return (stack or self.es).enter_context(self.nc.psum_tensor(name, shape, dt))

    def rk(self, eng, mult, add=0):
        if not hasattr(self, "_rkcache"):
            self._rkcache = {}
        key = (id(eng.engine) if hasattr(eng, "engine") else id(eng), mult, add)
        key = (str(eng.engine), mult, add)
        if key not in self._rkcache:
            rank = eng.partition_id()
            expr = rank * mult + add if add else rank * mult
            self._rkcache[key] = eng.compute_val(expr)
        return self._rkcache[key]


    def build(self):
        nc, es = self.nc, self.es
        self.declare_dram()
        self.b = es.enter_context(nc.Block())
        self.c_ld = Ch(self.es, self.nc, "ld", nsem=4)
        self.c_wst = Ch(self.es, self.nc, "wst", nsem=4)
        self.c_sp = Ch(self.es, self.nc, "sp", nsem=4)
        self.c_coll = Ch(self.es, self.nc, "coll", nsem=2)
        self.c_st = Ch(self.es, self.nc, "st", nsem=2)
        self.c_ldb = Ch(self.es, self.nc, "ldb", nsem=4)
        self.c_mm = CCh(self.es, self.nc, "mm")
        self.c_ev = CCh(self.es, self.nc, "ev")
        self.c_dv = CCh(self.es, self.nc, "dv")
        self.c_gp = CCh(self.es, self.nc, "gp")
        self.alloc_shared()
        self.s0_constants()
        self.s1_ln0()
        self.s2_l0_proj_conv()
        if self.stop_after != "s2":
            self.s3_collectives_l0()
            self.s4_l0_ffn()
            self.s5_scan()
            if self.stop_after != "s5":
                self.s6_gate_outproj()
                if self.stop_after != "s6":
                    self.s7_ln1()
                    self.s8_kvr()
                    if self.stop_after != "s8":
                        self.s9_rwkv()
                        if self.stop_after != "s9":
                            self.s10_wo()
                            self.s11_l1_ffn()
                            if self.stop_after != "s11":
                                self.s12_final_sum()
        self.s_final()
        es.close()
        return nc

    def alloc_shared(self):
        self.s0_stack = ExitStack()
        self.ln_stack = ExitStack()
        self.ps_stat = self.psum("ps_stat", [1, 1024], stack=self.ln_stack)
        self.ps_bc = self.psum("ps_bc", [128, 1536], stack=self.ln_stack)
        self.tmpc1 = self.sb("tmpc1", [128, 128], F32)
        self.tmpc2 = self.sb("tmpc2", [128, 128], F32)
        self.acol_stage = self.sb("acol_stage", [32, 128], F32)
        self.ps_misc = self.psum("ps_misc", [128, 512], F32, stack=self.s0_stack)
        self.identf = self.sb("identf", [128, 128], F32)
        self.sel127 = self.sb("sel127", [128, 1], F32)
        self.sqt = self.sb("sqt", [128, 512])
        self.lane = self.sb("lane", [1, 2048], F32)
        self.tnm = self.sb("tnm", [128, 512])
        self.bcds = self.sb("bcds", [96, TC])
        self.facc = [self.sb(f"facc{i}", [128, TC]) for i in range(NT)]
        self.lnld0 = self.sb("lnld0", [128, 512], F32)
        self.lnld1 = self.sb("lnld1", [128, 512], F32)
        self.lnldb0 = self.sb("lnldb0", [128, 512])
        self.lnldb1 = self.sb("lnldb1", [128, 512])

    def declare_dram(self):
        din, dout, dint = self.din, self.dout, self.dint
        din("x_t", [D, TC + 3])
        din("w_inproj", [D, 2 * DI])
        din("rms_w", [D, 1])
        din("conv_w", [DI, 4])
        din("conv_b", [DI, 1])
        din("bcd_w", [DI, 96])
        din("dup_w", [DR, DI])
        din("dup_b", [DI, 1])
        din("lma", [DI, NS])
        din("d_param", [DI, 1])
        din("w_outproj", [DI, D])
        din("r0_wT", [D, NE])
        din("r0_b", [1, NE])
        din("exp0_wT", [NE, D, D])
        din("exp0_b", [NE, D, 1])
        din("wkT", [D, D])
        din("wvT", [D, D])
        din("wrT", [D, D])
        din("woT", [D, D])
        din("tdec", [D, 1])
        din("tfirst", [D, 1])
        din("r1_wT", [D, NE])
        din("w1", [NE, D, FFN])
        din("w2", [NE, FFN, D])
        din("pmask", [NE, 1])
        dout("out", [TC, D])
        dint("xc_dram", [DI, TC])
        dint("res_dram", [DI, TC])
        dint("bcd_dram", [96, TC])
        dint("xc_all", [NCORE * DI, TC])
        dint("bcd_all", [NCORE * 96, TC])
        dint("delta_dram", [DSH, TFULL])
        dint("dx_dram", [DSH, TFULL])
        dint("y0_dram", [DSH, TFULL])
        dint("y0_all", [DI, TFULL])
        dint("res_all", [NCORE * DI, TC])
        dint("xc_my", [DSH, TFULL])
        dint("res_my", [DSH, TFULL])
        dint("k_my", [128, TFULL])
        dint("v_my", [128, TFULL])
        dint("r_my", [128, TFULL])
        dint("partial0_dram", [NCORE * D, TC])
        dint("rs0_dram", [D, TC])
        dint("partial1_dram", [NCORE * D, TC])
        dint("rs1_dram", [D, TC])
        dint("x1_dram", [D, TC])
        dint("wsel_dram", [NE, TC])
        dint("kvr_dram", [3 * D, TC])
        dint("kvr_all", [NCORE * 3 * D, TC])
        dint("y1_dram", [128, TFULL])
        dint("y1_all", [NCORE * 128, TFULL])
        dint("g1_dram", [NE, TC])
        dint("cnt_dram", [1, 2 * NE], F32)
        dint("cnt_all", [NCORE, 2 * NE], F32)
        dint("hdbg", [128, 4096])
        dint("r9dbg", [128, 512])
        dint("k9dbg", [128, 512])
        dint("kh9dbg", [128, 128])
        dint("khT9dbg", [128, 128])
        dint("vT9dbg", [128, 128])
        dint("abdbg", [128, 4096], F32)
        dint("bxdbg", [128, 4096])
        dint("lgdbg", [8, TC], F32)
        dint("xn_dram", [D, TC + 3])
        dint("xn1_dram", [D, TC])
        dint("xr_dram", [D, TC + 3])

    def s0_constants(self):
        b, nc, d = self.b, self.nc, self.dram
        c_ld, c_gp, c_dv, c_ev, c_mm = self.c_ld, self.c_gp, self.c_dv, self.c_ev, self.c_mm
        self.ident = self.sb("ident", [128, 128])
        self.utri = self.sb("utri", [128, 128])
        self.io_col = self.sb("io_col", [128, 128], F32)
        self.io_row = self.sb("io_row", [128, 128], F32)
        self.io_m16 = self.sb("io_m16", [128, 128], F32)
        self.io_m32 = self.sb("io_m32", [128, 128], F32)
        self.sum16 = self.sb("sum16", [128, 16 * 128])
        self.ones_col = self.sb("ones_col", [128, 1])
        self.ones_row = self.sb("ones_row", [1, 128])
        self.onesf_col = self.sb("onesf_col", [128, 1], F32)
        self.onesf_row = self.sb("onesf_row", [1, 128], F32)
        self.mk_diag = self.sb("mk_diag", [128, 128])
        self.mk_off = self.sb("mk_off", [128, 128])
        sm = {}
        sm["rms_w"] = self.sb("sm_rms_w", [128, NT], F32)
        sm["conv_b"] = self.sb("sm_conv_b", [128, NTI], F32)
        sm["d_param"] = self.sb("sm_d_param", [128, NTI], F32)
        sm["convw"] = self.sb("sm_convw", [128, 4 * NTI], F32)
        sm["dup_b"] = self.sb("sm_dup_b", [128, 2], F32)
        sm["d_param_my"] = self.sb("sm_d_param_my", [128, 2], F32)
        sm["exp0_b"] = self.sb("sm_exp0_b", [128, NE * NT], F32)
        sm["tdec"] = self.sb("sm_tdec", [128, 1], F32)
        sm["tfirst"] = self.sb("sm_tfirst", [128, 1], F32)
        sm["pmask"] = self.sb("sm_pmask", [NE, 1], F32)
        sm["acol"] = self.sb("sm_acol", [128, 32], F32)
        self.sm = sm

        @b.sync
        def _(sy):
            for j in range(NT):
                c_ld.tag(sy, sy.dma_start(out=sm["rms_w"][:, j:j + 1], in_=d["rms_w"][128 * j:128 * (j + 1), :]), 16)
            for j in range(NTI):
                c_ld.tag(sy, sy.dma_start(out=sm["conv_b"][:, j:j + 1], in_=d["conv_b"][128 * j:128 * (j + 1), :]), 16)
                c_ld.tag(sy, sy.dma_start(out=sm["d_param"][:, j:j + 1], in_=d["d_param"][128 * j:128 * (j + 1), :]), 16)
                c_ld.tag(sy, sy.dma_start(out=sm["convw"][:, 4 * j:4 * j + 4], in_=d["conv_w"][128 * j:128 * (j + 1), :]), 16)
            for e in range(NE):
                for i in range(NT):
                    c_ld.tag(sy, sy.dma_start(out=sm["exp0_b"][:, NT * e + i:NT * e + i + 1],
                                          in_=d["exp0_b"][e, 128 * i:128 * (i + 1), :]), 16)
            c_ld.tag(sy, sy.dma_start(out=sm["pmask"][:], in_=d["pmask"]), 16)
            off = self.rk(sy, DSH)
            c_ld.tag(sy, sy.dma_start(out=sm["dup_b"][:],
                                      in_=d["dup_b"][bass.ds(off, 256), :].rearrange("(a p) o -> p (a o)", a=2),
                                      allow_slow_non_contiguous=True), 16)
            c_ld.tag(sy, sy.dma_start(out=sm["d_param_my"][:],
                                      in_=d["d_param"][bass.ds(off, 256), :].rearrange("(a p) o -> p (a o)", a=2),
                                      allow_slow_non_contiguous=True), 16)
            c_ld.tag(sy, sy.dma_start(out=self.acol_stage[0:32, :],
                                      in_=d["lma"][bass.ds(off, 256), :].rearrange("(q g) n -> q (g n)", g=8)), 16)
            off3 = self.rk(sy, 128)
            c_ld.tag(sy, sy.dma_start(out=sm["tdec"][:], in_=d["tdec"][bass.ds(off3, 128), :]), 16)
            c_ld.tag(sy, sy.dma_start(out=sm["tfirst"][:], in_=d["tfirst"][bass.ds(off3, 128), :]), 16)

        @b.gpsimd
        def _(g):
            g.iota(self.io_col[:], pattern=[[1, 128]], base=0, channel_multiplier=0,
                   allow_small_or_imprecise_dtypes=True)
            g.iota(self.io_m16[:], pattern=[[0, 8], [1, 16]], base=0, channel_multiplier=0,
                   allow_small_or_imprecise_dtypes=True)
            g.iota(self.io_m32[:], pattern=[[0, 4], [1, 32]], base=0, channel_multiplier=0,
                   allow_small_or_imprecise_dtypes=True)
            c_gp.tag(g.iota(self.io_row[:], pattern=[[0, 128]], base=0, channel_multiplier=1,
                            allow_small_or_imprecise_dtypes=True))

        @b.vector
        def _(v):
            c_gp.wait(v)
            v.tensor_tensor(self.ident[:], self.io_col[:], self.io_row[:], ALU.is_equal)
            v.tensor_tensor(self.identf[:], self.io_col[:], self.io_row[:], ALU.is_equal)
            v.tensor_scalar(self.sel127[:], self.io_row[:, 0:1], 126.5, None, ALU.is_ge)
            v.tensor_tensor(self.utri[:], self.io_row[:], self.io_col[:], ALU.is_le)
            v.memset(self.ones_col[:], 1.0)
            v.memset(self.ones_row[:], 1.0)
            v.memset(self.onesf_col[:], 1.0)
            v.memset(self.onesf_row[:], 1.0)
            v.memset(self.sum16[:], 0.0)
            v.memset(self.mk_diag[:], 0.0)
            v.memset(self.mk_off[:], 0.0)
            t1, t2 = self.tmpc1, self.tmpc2
            v.drain()
            v.tensor_scalar(t1[:], self.io_col[:], 16.0, None, ALU.mult)
            v.drain()
            v.tensor_tensor(t2[:], self.io_row[:], t1[:], ALU.is_ge)
            v.drain()
            v.tensor_scalar(t1[:], t1[:], 16.0, None, ALU.add)
            v.drain()
            v.tensor_tensor(t1[:], self.io_row[:], t1[:], ALU.is_lt)
            v.drain()
            v.tensor_tensor(t2[:], t2[:], t1[:], ALU.mult)
            v.drain()
            for j in range(16):
                v.tensor_copy(self.sum16[:, 128 * j + 8 * j:128 * j + 8 * j + 8], t2[:, 0:8])
            v.drain()
            v.tensor_tensor(t1[:], self.io_col[:], self.io_row[:], ALU.subtract)
            v.drain()
            v.tensor_tensor(t2[:], t1[:], self.io_m16[:], ALU.is_le)
            v.drain()
            v.scalar_tensor_tensor(self.mk_diag[:], t1[:], 0.0, t2[:], ALU.is_ge, ALU.mult)
            v.drain()
            v.tensor_tensor(t2[:], t1[:], self.io_m32[:], ALU.subtract)
            v.drain()
            v.tensor_scalar(self.mk_off[:], t2[:], -15.0, 0.0, ALU.is_ge, op1=ALU.bypass) if False else None
            v.tensor_scalar(t1[:], t2[:], -15.0, None, ALU.is_ge)
            v.drain()
            v.tensor_scalar(t2[:], t2[:], 0.0, None, ALU.is_le)
            v.drain()
            v.tensor_tensor(t1[:], t1[:], t2[:], ALU.mult)
            v.drain()
            v.tensor_scalar(t2[:], self.io_m32[:], 16.0, None, ALU.is_ge)
            v.drain()
            v.tensor_tensor(self.mk_off[:], t1[:], t2[:], ALU.mult)
            v.drain()
            c_dv.tag(v.nop())

        @b.scalar
        def _(sc):
            c_ld.wait(sc)
            c_ev.tag(sc.activation(self.acol_stage[0:32, :], self.acol_stage[0:32, :], ACTF.Exp))

        @b.tensor
        def _(te):
            c_ev.wait(te)
            c_dv.wait(te)
            c_mm.tag(te.transpose(self.ps_misc[:, 0:32], self.acol_stage[0:32, :], self.identf[0:32, 0:32]))

        @b.vector
        def _(v):
            c_mm.wait(v)
            c_dv.tag(v.tensor_scalar(sm["acol"][:], self.ps_misc[:, 0:32], -1.0, None, ALU.mult))

    def s1_ln0(self):
        b, d = self.b, self.dram
        self.s1_stack = ExitStack()
        self.xn = [self.sb(f"xn{i}", [128, TC + 3]) for i in range(NT)]
        self.xr = [self.sb(f"xr{i}", [128, TC + 3], stack=self.s1_stack) for i in range(NT)]

        self.emit_ln(src=None, out_xn=self.xn, out_xr=self.xr, width=TC + 3,
                     rms_cols=self.sm["rms_w"], rms_col_idx=list(range(NT)),
                     src_dram=d["x_t"])

        if "xn_dram" in self.dbg:
            @b.sync
            def _(sy):
                self.c_dv.wait(sy)
                for i in range(NT):
                    self.c_sp.tag(sy, sy.dma_start(out=d["xn_dram"][128 * i:128 * (i + 1), :], in_=self.xn[i][:]), 16)
                    self.c_sp.tag(sy, sy.dma_start(out=d["xr_dram"][128 * i:128 * (i + 1), :], in_=self.xr[i][:]), 16)

    def emit_ln(self, src, out_xn, out_xr, width, rms_cols, rms_col_idx, src_f32=True,
                src_dram=None, dram_wait=None):
        b = self.b
        c_ld, c_mm, c_ev, c_dv = self.c_ld, self.c_mm, self.c_ev, self.c_dv
        sq, lane = self.sqt, self.lane
        ps_stat, ps_bc = self.ps_stat, self.ps_bc
        nchunk = (width + 511) // 512
        use_dram = src_dram is not None
        if use_dram:
            lnld = [self.lnld0, self.lnld1] if src_f32 else [self.lnldb0, self.lnldb1]

        def srcap(i, o, W, phase):
            if not use_dram:
                return src[i][:, o:o + W]
            return lnld[(i + phase) % 2][:, 0:W]

        for ci in range(nchunk):
            o = 512 * ci
            W = min(512, width - o)
            for i in range(NT):
                if use_dram:
                    @b.sync
                    def _(sy, i=i, o=o, W=W):
                        c_mm.wait(sy)
                        if dram_wait is not None:
                            dram_wait[0].wait(sy, mark=dram_wait[1])
                        c_ld.tag(sy, sy.dma_start(out=lnld[i % 2][:, 0:W],
                                                  in_=src_dram[128 * i:128 * (i + 1), o:o + W]), 16)
                @b.scalar
                def _(sc, i=i, o=o, W=W):
                    c_ld.wait(sc)
                    c_mm.wait(sc)
                    c_dv.wait(sc)
                    c_ev.tag(sc.activation(sq[:, 0:W], srcap(i, o, W, 0), ACTF.Square))

                @b.tensor
                def _(te, i=i, o=o, W=W):
                    c_ev.wait(te)
                    te.matmul(ps_stat[0:1, 0:W], self.onesf_col[:] if src_f32 else self.ones_col[:],
                              srcap(i, o, W, 0),
                              start=(i == 0), stop=(i == NT - 1))
                    c_mm.tag(te.matmul(ps_stat[0:1, 512:512 + W], self.ones_col[:], sq[:, 0:W],
                                       start=(i == 0), stop=(i == NT - 1)))

            @b.vector
            def _(v, o=o, W=W):
                c_mm.wait(v)
                v.tensor_scalar(lane[0:1, 0:W], ps_stat[0:1, 0:W], 1.0 / D, None, ALU.mult)
                v.tensor_scalar(lane[0:1, 512:512 + W], ps_stat[0:1, 512:512 + W], 1.0 / D, None, ALU.mult)
                v.drain()
                v.tensor_tensor(lane[0:1, 1024:1024 + W], lane[0:1, 0:W], lane[0:1, 0:W], ALU.mult)
                v.drain()
                v.tensor_tensor(lane[0:1, 1024:1024 + W], lane[0:1, 512:512 + W], lane[0:1, 1024:1024 + W], ALU.subtract)
                v.drain()
                c_dv.tag(v.tensor_scalar(lane[0:1, 1536:1536 + W], lane[0:1, 1024:1024 + W], EPS, None, ALU.add))

            @b.scalar
            def _(sc, o=o, W=W):
                c_dv.wait(sc)
                c_ev.tag(sc.activation(lane[0:1, 1536:1536 + W], lane[0:1, 1536:1536 + W], ACTF.Sqrt))

            @b.vector
            def _(v, o=o, W=W):
                c_ev.wait(v)
                v.reciprocal(lane[0:1, 512:512 + W], lane[0:1, 1536:1536 + W])
                v.drain()
                if out_xr is not None:
                    v.tensor_tensor(lane[0:1, 1536:1536 + W], lane[0:1, 1024:1024 + W], lane[0:1, 512:512 + W], ALU.mult)
                    v.drain()
                    v.tensor_tensor(lane[0:1, 1536:1536 + W], lane[0:1, 1536:1536 + W], lane[0:1, 512:512 + W], ALU.mult)
                    v.drain()
                    c_dv.tag(v.tensor_scalar(lane[0:1, 1536:1536 + W], lane[0:1, 1536:1536 + W], EPS, None, ALU.add))
                else:
                    c_dv.tag(v.nop())

            if out_xr is not None:
                @b.scalar
                def _(sc, o=o, W=W):
                    c_dv.wait(sc)
                    c_ev.tag(sc.activation(lane[0:1, 1536:1536 + W], lane[0:1, 1536:1536 + W], ACTF.Sqrt))

                @b.vector
                def _(v, o=o, W=W):
                    c_ev.wait(v)
                    v.reciprocal(lane[0:1, 1536:1536 + W], lane[0:1, 1536:1536 + W])
                    v.drain()
                    c_dv.tag(v.tensor_tensor(lane[0:1, 1536:1536 + W], lane[0:1, 1536:1536 + W], lane[0:1, 512:512 + W], ALU.mult))

            @b.tensor
            def _(te, o=o, W=W):
                c_dv.wait(te)
                c_mm.tag(te.matmul(ps_bc[:, 0:W], self.onesf_row[:], lane[0:1, 0:W], start=True, stop=True))
                c_mm.tag(te.matmul(ps_bc[:, 512:512 + W], self.onesf_row[:], lane[0:1, 512:512 + W], start=True, stop=True))
                if out_xr is not None:
                    c_mm.tag(te.matmul(ps_bc[:, 1024:1024 + W], self.onesf_row[:], lane[0:1, 1536:1536 + W], start=True, stop=True))

            for i in range(NT):
                if use_dram:
                    @b.sync
                    def _(sy, i=i, o=o, W=W):
                        c_dv.wait(sy)
                        if dram_wait is not None:
                            dram_wait[0].wait(sy, mark=dram_wait[1])
                        c_ld.tag(sy, sy.dma_start(out=lnld[i % 2][:, 0:W],
                                                  in_=src_dram[128 * i:128 * (i + 1), o:o + W]), 16)

                @b.vector
                def _(v, i=i, o=o, W=W):
                    c_mm.wait(v)
                    if use_dram:
                        c_ld.wait(v)
                    v.tensor_tensor(self.tnm[:, 0:W], srcap(i, o, W, 0), ps_bc[:, 0:W], ALU.subtract)
                    v.drain()
                    v.tensor_tensor(out_xn[i][:, o:o + W], self.tnm[:, 0:W], ps_bc[:, 512:512 + W], ALU.mult)
                    if out_xr is not None:
                        v.scalar_tensor_tensor(out_xr[i][:, o:o + W], self.tnm[:, 0:W],
                                               rms_cols[:, rms_col_idx[i]:rms_col_idx[i] + 1],
                                               ps_bc[:, 1024:1024 + W], ALU.mult, ALU.mult)
                    v.drain()
                    c_dv.tag(v.nop())

    def s2_l0_proj_conv(self):
        b, d = self.b, self.dram
        c_wst, c_mm, c_ev, c_dv, c_sp = self.c_wst, self.c_mm, self.c_ev, self.c_dv, self.c_sp
        self.s0_stack.close()
        self.ln_stack.close()
        s2 = ExitStack()
        wslab = [self.sb(f"wslab{k}", [128, D], stack=s2) for k in range(NT)]
        bcdw = self.sb("bcdw", [128, 96 * NTI], stack=s2)
        xs = self.sb("xs", [128, TC + 3], stack=s2)
        xs2 = self.sb("xs2", [128, TC + 3], stack=s2)
        cvt = self.sb("cvt", [128, TC], stack=s2)
        xct = self.sb("xct", [128, TC], stack=s2)
        xct2 = self.sb("xct2", [128, TC], stack=s2)
        rst = self.sb("rst", [128, TC], stack=s2)
        ps_p = self.psum("ps_p", [128, 1024], F32, stack=s2)
        ps_bcd = self.psum("ps_bcd", [96, TC], F32, stack=s2)

        @b.gpsimd
        def _(g):
            self.c_dv.wait(g)
            self.c_ev.wait(g)
            self.c_mm.wait(g)
            for j in range(NTI):
                c_wst.tag(g, g.dma_start(out=bcdw[:, 96 * j:96 * (j + 1)], in_=d["bcd_w"][128 * j:128 * (j + 1), :]), 16)

        CH = [(0, 512), (512, 512), (1024, 512), (1536, 512), (2048, 3)]
        xs_bufs = [xs, xs2]
        xc_bufs = [xct, xct2]
        ln_dv = self.c_dv.n
        for q in range(2):
            @b.gpsimd
            def _(g, q=q):
                self.c_dv.wait(g)
                if q > 0:
                    self.c_mm.wait(g)
                for k in range(NT):
                    c_wst.tag(g, g.dma_start(out=wslab[k][:], in_=d["w_inproj"][128 * k:128 * (k + 1), D * q:D * (q + 1)]), 16)
            wq_mark = c_wst.mark()

            for jj in range(NT):
                j = NT * q + jj
                xsb = xs_bufs[j % 2]
                xcb = xc_bufs[j % 2]
                for ci, (o, W) in enumerate(CH):
                    @b.tensor
                    def _(te, j=j, jj=jj, o=o, W=W, ci=ci, xsb=xsb, wq_mark=wq_mark):
                        c_wst.wait(te, mark=wq_mark)
                        self.c_dv.wait(te, mark=ln_dv)
                        c_ev.wait(te)
                        for k in range(NT):
                            ins = te.matmul(ps_p[:, 512 * (ci % 2):512 * (ci % 2) + W],
                                            wslab[k][:, 128 * jj:128 * (jj + 1)],
                                            self.xr[k][:, o:o + W],
                                            start=(k == 0), stop=(k == NT - 1))
                        c_mm.tag(ins)

                    @b.scalar
                    def _(sc, j=j, o=o, W=W, ci=ci, xsb=xsb):
                        c_mm.wait(sc)
                        c_dv.wait(sc)
                        c_ev.tag(sc.copy(out=xsb[:, o:o + W], in_=ps_p[:, 512 * (ci % 2):512 * (ci % 2) + W]))

                @b.vector
                def _(v, j=j, xsb=xsb):
                    c_ev.wait(v)
                    cw = self.sm["convw"]
                    v.tensor_scalar(cvt[:], xsb[:, 0:TC], cw[:, 4 * j:4 * j + 1], None, ALU.mult)
                    v.drain()
                    for k in range(1, 4):
                        v.scalar_tensor_tensor(cvt[:], xsb[:, k:k + TC], cw[:, 4 * j + k:4 * j + k + 1],
                                               cvt[:], ALU.mult, ALU.add)
                        v.drain()
                    c_dv.tag(v.nop())

                @b.scalar
                def _(sc, j=j, xcb=xcb):
                    c_dv.wait(sc)
                    c_sp.wait(sc)
                    self.c_mm.wait(sc)
                    c_ev.tag(sc.activation(xcb[:], cvt[:], ACTF.Silu, bias=self.sm["conv_b"][:, j:j + 1]))

                @b.sync
                def _(sy, j=j, xcb=xcb):
                    c_ev.wait(sy)
                    c_sp.tag(sy, sy.dma_start(out=d["xc_dram"][128 * j:128 * (j + 1), :], in_=xcb[:]), 16)

                for ci in range(4):
                    @b.tensor
                    def _(te, j=j, ci=ci, xcb=xcb):
                        c_ev.wait(te)
                        ins = te.matmul(ps_bcd[:, 512 * ci:512 * (ci + 1)], bcdw[:, 96 * j:96 * (j + 1)],
                                        xcb[:, 512 * ci:512 * (ci + 1)], start=(j == 0), stop=(j == NTI - 1))
                        if j == NTI - 1:
                            c_mm.tag(ins)

        @b.scalar
        def _(sc):
            c_mm.wait(sc)
            c_ev.tag(sc.copy(out=self.bcds[0:96, :], in_=ps_bcd[0:96, :]))

        @b.sync
        def _(sy):
            c_ev.wait(sy)
            c_sp.tag(sy, sy.dma_start(out=d["bcd_dram"], in_=self.bcds[0:96, :]), 16)

        for q in range(2):
            @b.gpsimd
            def _(g, q=q):
                self.c_mm.wait(g)
                for k in range(NT):
                    c_wst.tag(g, g.dma_start(out=wslab[k][:], in_=d["w_inproj"][128 * k:128 * (k + 1), DI + D * q:DI + D * (q + 1)]), 16)
            wq_mark = c_wst.mark()

            for jj in range(NT):
                j = NT * q + jj
                for ci in range(4):
                    o = 512 * ci

                    @b.tensor
                    def _(te, j=j, jj=jj, o=o, ci=ci, wq_mark=wq_mark):
                        c_wst.wait(te, mark=wq_mark)
                        c_ev.wait(te)
                        for k in range(NT):
                            ins = te.matmul(ps_p[:, 512 * (ci % 2):512 * (ci % 2) + 512],
                                            wslab[k][:, 128 * jj:128 * (jj + 1)],
                                            self.xr[k][:, 3 + o:3 + o + 512],
                                            start=(k == 0), stop=(k == NT - 1))
                        c_mm.tag(ins)

                    @b.scalar
                    def _(sc, j=j, o=o, ci=ci):
                        c_mm.wait(sc)
                        c_sp.wait(sc)
                        c_ev.tag(sc.activation(rst[:, o:o + 512], ps_p[:, 512 * (ci % 2):512 * (ci % 2) + 512], ACTF.Silu))

                @b.sync
                def _(sy, j=j):
                    c_ev.wait(sy)
                    c_sp.tag(sy, sy.dma_start(out=d["res_dram"][128 * j:128 * (j + 1), :], in_=rst[:]), 16)

        @b.sync
        def _(sy):
            c_sp.wait(sy)
            sy.nop()

        s2.close()
        self.s1_stack.close()

    def s3_collectives_l0(self):
        b, d = self.b, self.dram
        spm = self.c_sp.mark()

        @b.gpsimd
        def _(g):
            self.c_sp.wait(g, mark=spm)
            self.c_coll.tag(g, g.collective_compute(
                "AllGather", ALU.bypass, ins=[d["xc_dram"]], outs=[d["xc_all"]],
                replica_groups=[list(range(NCORE))]), 1)
            self.c_coll.tag(g, g.collective_compute(
                "AllGather", ALU.bypass, ins=[d["bcd_dram"]], outs=[d["bcd_all"]],
                replica_groups=[list(range(NCORE))]), 1)
        self.ag_l0_mark = self.c_coll.mark()

    def s4_l0_ffn(self):
        b, d = self.b, self.dram
        c_wst, c_mm, c_ev, c_dv, c_ld, c_sp = self.c_wst, self.c_mm, self.c_ev, self.c_dv, self.c_ld, self.c_sp
        s4 = ExitStack()
        r0t = self.sb("r0t", [128, 64], stack=s4)
        brep = self.sb("brep", [128, NE], F32, stack=s4)
        lg = self.sb("lg", [8, TC], F32, stack=s4)
        lgt = self.sb("lgt", [128, 128], F32, stack=s4)
        mx = self.sb("mx_s4", [128, 16], F32, stack=s4)
        smv = self.sb("smv_s4", [128, 16], F32, stack=s4)
        ohh = self.sb("ohh_s4", [128, 128], F32, stack=s4)
        wsel = self.sb("wsel_s4", [128, 128], F32, stack=s4)
        wst_sb = self.sb("wst_sb", [8, TC], stack=s4)
        grep = self.sb("grep", [128, TC], stack=s4)
        wsl = [self.sb(f"e0w{i}", [128, 2 * D], stack=s4) for i in range(NT)]
        tmp_e2 = [self.sb(f"tmp_e{i}", [128, 512], stack=s4) for i in range(2)]
        ps_l = self.psum("ps_l", [8, 512], F32, stack=s4)
        ps_t = self.psum("ps_t", [128, 512], F32, stack=s4)
        ps_f = self.psum("ps_f", [128, 1024], F32, stack=s4)

        @b.gpsimd
        def _(g):
            for k in range(NT):
                c_wst.tag(g, g.dma_start(out=r0t[:, 8 * k:8 * k + 8], in_=d["r0_wT"][128 * k:128 * (k + 1), :]), 16)
            c_wst.tag(g, g.dma_start(out=brep[:], in_=d["r0_b"].broadcast_to([128, NE])), 16)
        r0_mark = c_wst.mark()

        for ci in range(4):
            @b.tensor
            def _(te, ci=ci):
                c_wst.wait(te, mark=r0_mark)
                c_ev.wait(te)
                for k in range(NT):
                    ins = te.matmul(ps_l[:, 0:512], r0t[:, 8 * k:8 * k + 8],
                                    self.xn[k][:, 3 + 512 * ci:3 + 512 * (ci + 1)],
                                    start=(k == 0), stop=(k == NT - 1))
                c_mm.tag(ins)

            @b.scalar
            def _(sc, ci=ci):
                c_mm.wait(sc)
                c_ev.tag(sc.copy(out=lg[:, 512 * ci:512 * (ci + 1)], in_=ps_l[:, 0:512]))

        if "lgdbg" in self.dbg:
            @b.sync
            def _(sy):
                c_ev.wait(sy)
                c_sp.tag(sy, sy.dma_start(out=d["lgdbg"], in_=lg[0:8, :]), 16)

        for i in range(16):
            @b.tensor
            def _(te, i=i):
                c_ev.wait(te)
                c_dv.wait(te)
                c_mm.tag(te.transpose(ps_t[0:128, 0:8], lg[0:8, 128 * i:128 * (i + 1)], self.identf[0:8, 0:8]))

            @b.vector
            def _(v, i=i):
                c_mm.wait(v)
                c_dv.tag(v.tensor_tensor(lgt[:, 8 * i:8 * i + 8], ps_t[:, 0:8], brep[:], ALU.add))

        from bass_rust import AP as _AP

        @b.vector
        def _(v):
            c_dv.wait(v)
            lgt3 = _AP(lgt[:].tensor, lgt[:].offset, [[128, 128], [8, 16], [1, 8]])
            mxb = _AP(mx[:].tensor, mx[:].offset, [[16, 128], [1, 16], [0, 8]])
            v.tensor_reduce(mx[:], lgt3, axis=AX.X, op=ALU.max)
            v.drain()
            v.tensor_tensor(ohh[:], lgt3, mxb, ALU.subtract)
            v.drain()
            c_dv.tag(v.nop())

        @b.scalar
        def _(sc):
            c_dv.wait(sc)
            c_ev.tag(sc.activation(wsel[:], ohh[:], ACTF.Exp))

        @b.vector
        def _(v):
            c_ev.wait(v)
            lgt3 = _AP(lgt[:].tensor, lgt[:].offset, [[128, 128], [8, 16], [1, 8]])
            mxb = _AP(mx[:].tensor, mx[:].offset, [[16, 128], [1, 16], [0, 8]])
            smb = _AP(smv[:].tensor, smv[:].offset, [[16, 128], [1, 16], [0, 8]])
            ws3 = _AP(wsel[:].tensor, wsel[:].offset, [[128, 128], [8, 16], [1, 8]])
            v.tensor_reduce(smv[:], ws3, axis=AX.X, op=ALU.add)
            v.drain()
            v.reciprocal(smv[:], smv[:])
            v.drain()
            v.tensor_tensor(ohh[:], lgt3, mxb, ALU.is_ge)
            v.drain()
            v.tensor_tensor(wsel[:], ohh[:], smb, ALU.mult)
            v.drain()
            c_dv.tag(v.nop())

        for i in range(16):
            @b.tensor
            def _(te, i=i):
                c_dv.wait(te)
                c_ev.wait(te)
                c_mm.tag(te.transpose(ps_t[0:8, 0:128], wsel[:, 8 * i:8 * i + 8], self.identf[:]))

            @b.scalar
            def _(sc, i=i):
                c_mm.wait(sc)
                c_ev.tag(sc.copy(out=wst_sb[0:8, 128 * i:128 * (i + 1)], in_=ps_t[0:8, 0:128]))

        @b.sync
        def _(sy):
            c_ev.wait(sy)
            c_sp.tag(sy, sy.dma_start(out=d["wsel_dram"], in_=wst_sb[0:8, :]), 16)
        wsel_mark = c_sp.mark()

        for e in range(NE):
            half = e % 2

            @b.gpsimd
            def _(g, e=e, half=half):
                if e >= 2:
                    c_mm.wait(g)
                for k in range(NT):
                    c_wst.tag(g, g.dma_start(out=wsl[k][:, D * half:D * (half + 1)],
                                             in_=d["exp0_wT"][e, 128 * k:128 * (k + 1), :]), 16)
            w_mark = c_wst.mark()

            @b.sync
            def _(sy, e=e):
                c_sp.wait(sy, mark=wsel_mark)
                c_dv.wait(sy)
                c_ld.tag(sy, sy.dma_start(out=grep[:],
                                          in_=d["wsel_dram"][e:e + 1, :].broadcast_to([128, TC])), 16)
            g_mark = c_ld.mark()

            mm_g = []
            ev_g = []

            @b.tensor
            def _(te, e=e, half=half):
                c_wst.wait(te, mark=w_mark)
                ev_base = c_ev.n
                g = 0
                for j in range(NT):
                    for ci in range(4):
                        if g == 0:
                            c_ev.wait(te)
                            c_dv.wait(te, mark=max(0, c_dv.n - 2))
                        elif g >= 2:
                            te.wait_ge(c_ev.sem, ev_base + g - 1)
                        for k in range(NT):
                            ins = te.matmul(ps_f[:, 512 * (g % 2):512 * (g % 2) + 512],
                                            wsl[k][:, D * half + 128 * j:D * half + 128 * (j + 1)],
                                            self.xn[k][:, 3 + 512 * ci:3 + 512 * (ci + 1)],
                                            start=(k == 0), stop=(k == NT - 1))
                        c_mm.tag(ins)
                        mm_g.append(c_mm.n)
                        g += 1

            @b.scalar
            def _(sc, e=e):
                dv0 = getattr(self, "_s4_dv", 0)
                sc.wait_ge(self.c_dv.sem, dv0)
                dv_base0 = self.c_dv.n
                g = 0
                for j in range(NT):
                    for ci in range(4):
                        sc.wait_ge(c_mm.sem, mm_g[g])
                        if g >= 2:
                            sc.wait_ge(self.c_dv.sem, dv_base0 + g - 1)
                        c_ev.tag(sc.activation(tmp_e2[g % 2][:], ps_f[:, 512 * (g % 2):512 * (g % 2) + 512],
                                               ACTF.Identity, bias=self.sm["exp0_b"][:, NT * e + j:NT * e + j + 1]))
                        ev_g.append(c_ev.n)
                        g += 1

            @b.vector
            def _(v, e=e):
                c_ld.wait(v, mark=g_mark)
                g = 0
                for j in range(NT):
                    for ci in range(4):
                        v.wait_ge(c_ev.sem, ev_g[g])
                        if e == 0:
                            c_dv.tag(v.tensor_tensor(self.facc[j][:, 512 * ci:512 * (ci + 1)], tmp_e2[g % 2][:],
                                                     grep[:, 512 * ci:512 * (ci + 1)], ALU.mult))
                        else:
                            v.tensor_tensor(tmp_e2[g % 2][:], tmp_e2[g % 2][:], grep[:, 512 * ci:512 * (ci + 1)], ALU.mult)
                            v.drain()
                            c_dv.tag(v.tensor_tensor(self.facc[j][:, 512 * ci:512 * (ci + 1)],
                                                     self.facc[j][:, 512 * ci:512 * (ci + 1)], tmp_e2[g % 2][:], ALU.add))
                        g += 1
                v.drain()
                c_dv.tag(v.nop())
            self._s4_dv = c_dv.n
        s4.close()

    def s5_scan(self):
        b, d = self.b, self.dram
        c_wst, c_mm, c_ev, c_dv, c_ld, c_sp = self.c_wst, self.c_mm, self.c_ev, self.c_dv, self.c_ld, self.c_sp
        s5 = ExitStack()
        dupw = self.sb("dupw", [64, DSH], stack=s5)
        dlr = self.sb("dlr", [64, TC], stack=s5)
        xcd = self.sb("xcd", [128, 2 * TC], stack=s5)
        dsb = self.sb("dsb", [128, TC], stack=s5)
        spz = self.sb("spz", [128, 512], F32, stack=s5)
        spc = self.sb("spc", [128, 512], F32, stack=s5)
        dxsb = self.sb("dxsb", [128, TC], stack=s5)
        ps_d = self.psum("ps_d", [128, 1024], F32, stack=s5)

        @b.gpsimd
        def _(g):
            off = g.compute_val(g.partition_id() * DSH)
            c_wst.tag(g, g.dma_start(out=dupw[:], in_=d["dup_w"][:, bass.ds(off, DSH)]), 16)
        dupw_mark = c_wst.mark()

        @b.scalar
        def _(sc2):
            off = self.rk(sc2, DSH)
            self.c_coll.wait(sc2, mark=self.ag_l0_mark)
            xa = d["xc_all"].rearrange("(r n) t -> n r t", n=DI)
            self.c_ldb.tag(sc2, sc2.dma_start(
                out=d["xc_my"].rearrange("n (r t) -> n r t", r=NCORE),
                in_=xa[bass.ds(off, DSH), :, :]), 16)
        self.xcmy_mark = self.c_ldb.mark()

        for r in range(NCORE):
            @b.sync
            def _(sy, r=r):
                self.c_coll.wait(sy, mark=self.ag_l0_mark)
                c_dv.wait(sy)
                c_ld.tag(sy, sy.dma_start(out=dlr[:], in_=d["bcd_all"][96 * r:96 * r + 64, :]), 16)
            dlr_mark = c_ld.mark()

            @b.scalar
            def _(sc2, r=r):
                off = self.rk(sc2, DSH)
                xa = d["xc_all"].rearrange("(r n) t -> r n t", n=DI)
                c_dv.wait(sc2)
                self.c_ldb.tag(sc2, sc2.dma_start(
                    out=xcd[:].rearrange("p (a t) -> p a t", a=2), in_=xa[r][bass.ds(off, 256), :].rearrange("(a p) t -> p a t", a=2)), 16)
            xcd_mark = self.c_ldb.mark()
            for dt in range(2):

                for ci in range(4):
                    @b.tensor
                    def _(te, r=r, dt=dt, ci=ci):
                        c_wst.wait(te, mark=dupw_mark)
                        c_ld.wait(te, mark=dlr_mark)
                        c_ev.wait(te)
                        c_mm.tag(te.matmul(ps_d[:, 512 * (ci % 2):512 * (ci % 2) + 512],
                                           dupw[:, 128 * dt:128 * (dt + 1)],
                                           dlr[:, 512 * ci:512 * (ci + 1)], start=True, stop=True))

                    @b.scalar
                    def _(sc, r=r, dt=dt, ci=ci):
                        c_mm.wait(sc)
                        c_dv.wait(sc)
                        if ci == 0:
                            c_sp.wait(sc)
                        c_ev.tag(sc.activation(spz[:], ps_d[:, 512 * (ci % 2):512 * (ci % 2) + 512],
                                               ACTF.Identity, bias=self.sm["dup_b"][:, dt:dt + 1]))

                    @b.vector
                    def _(v, r=r, dt=dt, ci=ci):
                        c_ev.wait(v)
                        v.tensor_scalar(spc[:], spz[:], 17.0, None, ALU.min)
                        v.drain()
                        c_dv.tag(v.nop())

                    @b.scalar
                    def _(sc, r=r, dt=dt, ci=ci):
                        c_dv.wait(sc)
                        c_ev.tag(sc.activation(spc[:], spc[:], ACTF.Exp))

                    @b.vector
                    def _(v, r=r, dt=dt, ci=ci):
                        c_ev.wait(v)
                        v.tensor_scalar(spc[:], spc[:], 1.0, None, ALU.add)
                        v.drain()
                        c_dv.tag(v.nop())

                    @b.scalar
                    def _(sc, r=r, dt=dt, ci=ci):
                        c_dv.wait(sc)
                        c_ev.tag(sc.activation(spc[:], spc[:], ACTF.Ln))

                    @b.vector
                    def _(v, r=r, dt=dt, ci=ci):
                        c_ev.wait(v)
                        v.tensor_scalar(spz[:], spz[:], 17.0, 0.0, ALU.subtract, op1=ALU.max)
                        v.drain()
                        v.tensor_tensor(dsb[:, 512 * ci:512 * (ci + 1)], spc[:], spz[:], ALU.add)
                        v.drain()
                        c_dv.tag(v.nop())

                @b.vector
                def _(v, r=r, dt=dt):
                    c_ev.wait(v)
                    self.c_ldb.wait(v, mark=xcd_mark)
                    c_sp.wait(v)
                    v.tensor_tensor(dxsb[:], dsb[:], xcd[:, TC * dt:TC * (dt + 1)], ALU.mult)
                    v.drain()
                    c_dv.tag(v.nop())

                @b.sync
                def _(sy, r=r, dt=dt):
                    c_dv.wait(sy)
                    c_sp.tag(sy, sy.dma_start(out=d["delta_dram"][128 * dt:128 * (dt + 1), TC * r:TC * (r + 1)], in_=dsb[:]), 16)
                    c_sp.tag(sy, sy.dma_start(out=d["dx_dram"][128 * dt:128 * (dt + 1), TC * r:TC * (r + 1)], in_=dxsb[:]), 16)

        ddx_mark = c_sp.mark()
        s5.close()

        s5b = ExitStack()
        TB = 4096
        brep_t = self.sb("brep_t", [128, TB], stack=s5b)
        crep_t = self.sb("crep_t", [128, TB], stack=s5b)
        drep = self.sb("drep", [128, TB], stack=s5b)
        dxrep = self.sb("dxrep", [128, TB], stack=s5b)
        ab_t = self.sb("ab_t", [128, TB], F32, stack=s5b)
        bx_t = self.sb("bx_t", [128, TB], stack=s5b)
        h_t = self.sb("h_t", [128, TB], stack=s5b)
        yv_t = self.sb("yv_t", [128, TB], stack=s5b)
        ysb = self.sb("ysb", [128, TB], stack=s5b)
        carry = self.sb("carry", [128, 32], F32, stack=s5b)
        ps_y = self.psum("ps_y", [128, TB], F32, stack=s5b)

        drep2 = [drep, self.sb("drepB", [128, TB], stack=s5b)]
        dxrep2 = [dxrep, self.sb("dxrepB", [128, TB], stack=s5b)]
        ab2 = [ab_t, self.sb("ab_tB", [128, TB], F32, stack=s5b)]
        yv2 = [yv_t, yv_t]
        for c in range(TFULL // TB):
            @b.sync
            def _(sy, c=c):
                c_sp.wait(sy, mark=ddx_mark)
                c_dv.wait(sy)
                c_mm.wait(sy)
                for g in range(8):
                    for rb in range(TB // TC):
                        r = (TB * c) // TC + rb
                        c_ld.tag(sy, sy.dma_start(
                            out=brep_t[16 * g:16 * g + 16, TC * rb:TC * (rb + 1)],
                            in_=d["bcd_all"][96 * r + 64:96 * r + 80, :]), 16)
                        c_ld.tag(sy, sy.dma_start(
                            out=crep_t[16 * g:16 * g + 16, TC * rb:TC * (rb + 1)],
                            in_=d["bcd_all"][96 * r + 80:96 * r + 96, :]), 16)
            bc_mark = c_ld.mark()

            for half in range(2):
                ld_marks = {}
                ev_base = c_ev.n
                dv_base = c_dv.n
                mm_base = c_mm.n

                @b.sync
                def _(sy, c=c, half=half):
                    for rr in range(16):
                        rt = 16 * half + rr
                        if rr >= 2:
                            sy.wait_ge(c_ev.sem, ev_base + rr - 1)
                            sy.wait_ge(c_dv.sem, dv_base + rr - 1)
                        elif rr == 0:
                            sy.wait_ge(c_ev.sem, ev_base)
                            sy.wait_ge(c_dv.sem, dv_base)
                        for g in range(8):
                            row = 8 * rt + g
                            c_ld.tag(sy, sy.dma_start(
                                out=drep2[rr % 2][16 * g:16 * g + 16, :],
                                in_=d["delta_dram"][row:row + 1, TB * c:TB * (c + 1)].broadcast_to([16, TB])), 16)
                            c_ld.tag(sy, sy.dma_start(
                                out=dxrep2[rr % 2][16 * g:16 * g + 16, :],
                                in_=d["dx_dram"][row:row + 1, TB * c:TB * (c + 1)].broadcast_to([16, TB])), 16)
                        ld_marks[rr] = c_ld.mark()

                @b.scalar
                def _(sc, c=c, half=half):
                    for rr in range(16):
                        rt = 16 * half + rr
                        c_ld.wait(sc, mark=ld_marks[rr])
                        if rr >= 2:
                            sc.wait_ge(c_dv.sem, dv_base + rr - 1)
                        c_ev.tag(sc.activation(ab2[rr % 2][:], drep2[rr % 2][:], ACTF.Exp,
                                               scale=self.sm["acol"][:, rt:rt + 1]))

                @b.vector
                def _(v, c=c, half=half):
                    c_ld.wait(v, mark=bc_mark)
                    for rr in range(16):
                        rt = 16 * half + rr
                        v.wait_ge(c_ev.sem, ev_base + rr + 1)
                        if rr >= 1:
                            v.wait_ge(c_mm.sem, mm_base + rr)
                        v.tensor_tensor(bx_t[:], dxrep2[rr % 2][:], brep_t[:], ALU.mult)
                        v.drain()
                        init = 0.0 if c % 2 == 0 else carry[:, rt:rt + 1]
                        v.tensor_tensor_scan(h_t[:], ab2[rr % 2][:], bx_t[:], init, ALU.mult, ALU.add)
                        v.drain()
                        v.tensor_copy(carry[:, rt:rt + 1], h_t[:, TB - 1:TB])
                        c_dv.tag(v.tensor_tensor(yv2[rr % 2][:], h_t[:], crep_t[:], ALU.mult))
                        v.drain()

                @b.tensor
                def _(te, c=c, half=half):
                    for rr in range(16):
                        te.wait_ge(c_dv.sem, dv_base + rr + 1)
                        if rr == 0:
                            c_ev.wait(te, mark=ev_base)
                        for s in range(TB // 512):
                            ins = te.matmul(ps_y[:, 512 * s:512 * (s + 1)],
                                            self.sum16[:, 128 * rr:128 * (rr + 1)],
                                            yv2[rr % 2][:, 512 * s:512 * (s + 1)],
                                            start=(rr == 0), stop=(rr == 15))
                        c_mm.tag(ins)

                @b.scalar
                def _(sc, c=c, half=half):
                    c_mm.wait(sc)
                    c_sp.wait(sc)
                    c_ev.tag(sc.activation(ysb[:], ps_y[:], ACTF.Copy))

                @b.sync
                def _(sy, c=c, half=half):
                    c_ev.wait(sy)
                    c_sp.tag(sy, sy.dma_start(out=d["y0_dram"][128 * half:128 * (half + 1), TB * c:TB * (c + 1)],
                                              in_=ysb[:]), 16)
        self.y0_mark = self.c_sp.mark()
        s5b.close()

        @b.gpsimd
        def _(g):
            self.c_coll.tag(g, g.collective_compute(
                "AllGather", ALU.bypass, ins=[d["res_dram"]], outs=[d["res_all"]],
                replica_groups=[list(range(NCORE))]), 1)
        self.ag_res_mark = self.c_coll.mark()

        @b.scalar
        def _(sc2):
            off = self.rk(sc2, DSH)
            self.c_coll.wait(sc2, mark=self.ag_res_mark)
            ra = d["res_all"].rearrange("(r n) t -> n r t", n=DI)
            self.c_ldb.tag(sc2, sc2.dma_start(
                out=d["res_my"].rearrange("n (r t) -> n r t", r=NCORE),
                in_=ra[bass.ds(off, DSH), :, :]), 16)
        self.resmy_mark = self.c_ldb.mark()

        s5c = ExitStack()
        y0sl = self.sb("y0sl", [128, 2 * TC], stack=s5c)
        xcd2 = self.sb("xcd2", [128, 2 * TC], stack=s5c)
        ressl = self.sb("ressl", [128, 2 * TC], stack=s5c)
        a0in = [self.sb(f"a0in{i}", [128, TC], stack=s5c) for i in range(2)]
        woslab = self.sb("woslab", [128, 2 * D], stack=s5c)
        pstg2 = [self.sb(f"pstg{i}", [128, 512], stack=s5c) for i in range(2)]
        ps_o = self.psum("ps_o", [128, 1024], F32, stack=s5c)

        @b.gpsimd
        def _(g):
            c_mm.wait(g)
            c_dv.wait(g)
            c_ev.wait(g)
            off = g.compute_val(g.partition_id() * DSH)
            c_wst.tag(g, g.dma_start(
                out=woslab[:].rearrange("p (a q) -> p a q", a=2),
                in_=d["w_outproj"][bass.ds(off, 256), :].rearrange("(a p) q -> p a q", a=2)), 16)
        wo_mark = c_wst.mark()

        for r in range(NCORE):
            @b.sync
            def _(sy, r=r):
                c_dv.wait(sy)
                c_sp.wait(sy, mark=self.y0_mark)
                c_ld.tag(sy, sy.dma_start(out=y0sl[:].rearrange("p (a t) -> p a t", a=2),
                                          in_=d["y0_dram"][:, TC * r:TC * (r + 1)]
                                          .rearrange("(a p) t -> p a t", a=2)), 16)
            y0_ld = c_ld.mark()

            @b.scalar
            def _(sc2, r=r):
                c_dv.wait(sc2)
                self.c_ldb.wait(sc2, mark=self.resmy_mark)
                self.c_ldb.tag(sc2, sc2.dma_start(
                    out=xcd2[:].rearrange("p (a t) -> p a t", a=2),
                    in_=d["xc_my"][:, TC * r:TC * (r + 1)].rearrange("(a p) t -> p a t", a=2)), 16)
                self.c_ldb.tag(sc2, sc2.dma_start(
                    out=ressl[:].rearrange("p (a t) -> p a t", a=2),
                    in_=d["res_my"][:, TC * r:TC * (r + 1)].rearrange("(a p) t -> p a t", a=2)), 16)
            xr_ld = self.c_ldb.mark()

            @b.vector
            def _(v, r=r):
                c_ld.wait(v, mark=y0_ld)
                self.c_ldb.wait(v, mark=xr_ld)
                c_mm.wait(v)
                for dt in range(2):
                    v.scalar_tensor_tensor(a0in[dt][:], xcd2[:, TC * dt:TC * (dt + 1)],
                                           self.sm["d_param_my"][:, dt:dt + 1],
                                           y0sl[:, TC * dt:TC * (dt + 1)], ALU.mult, ALU.add)
                    v.drain()
                    v.tensor_tensor(a0in[dt][:], a0in[dt][:], ressl[:, TC * dt:TC * (dt + 1)], ALU.mult)
                    v.drain()
                c_dv.tag(v.nop())

            mm_p0 = []
            ev_p0 = []

            @b.tensor
            def _(te, r=r):
                c_wst.wait(te, mark=wo_mark)
                c_dv.wait(te)
                ev_base = c_ev.n
                g = 0
                for jo in range(NT):
                    for ci in range(4):
                        if g == 0:
                            c_ev.wait(te)
                        elif g >= 2:
                            te.wait_ge(c_ev.sem, ev_base + g - 1)
                        for dt in range(2):
                            ins = te.matmul(ps_o[:, 512 * (g % 2):512 * (g % 2) + 512],
                                            woslab[:, D * dt + 128 * jo:D * dt + 128 * (jo + 1)],
                                            a0in[dt][:, 512 * ci:512 * (ci + 1)],
                                            start=(dt == 0), stop=(dt == 1))
                        c_mm.tag(ins)
                        mm_p0.append(c_mm.n)
                        g += 1

            @b.scalar
            def _(sc, r=r):
                sp0 = getattr(self, "_s5c_sp", None)
                if sp0 is not None:
                    c_sp.wait(sc, mark=sp0)
                spsem = c_sp.sems[c_sp.cur]
                spbase = c_sp.tot[c_sp.cur]
                g = 0
                for jo in range(NT):
                    for ci in range(4):
                        sc.wait_ge(c_mm.sem, mm_p0[g])
                        if g >= 2:
                            sc.wait_ge(spsem, spbase + 16 * (g - 1))
                        c_ev.tag(sc.activation(pstg2[g % 2][:], ps_o[:, 512 * (g % 2):512 * (g % 2) + 512], ACTF.Copy))
                        ev_p0.append(c_ev.n)
                        g += 1

            @b.sync
            def _(sy, r=r):
                g = 0
                for jo in range(NT):
                    for ci in range(4):
                        sy.wait_ge(c_ev.sem, ev_p0[g])
                        c_sp.tag(sy, sy.dma_start(
                            out=d["partial0_dram"][D * r + 128 * jo:D * r + 128 * (jo + 1), 512 * ci:512 * (ci + 1)],
                            in_=pstg2[g % 2][:]), 16)
                        g += 1
            self._s5c_sp = c_sp.mark()
        p0_mark = c_sp.mark()
        s5c.close()

        @b.gpsimd
        def _(g):
            self.c_sp.wait(g, mark=p0_mark)
            self.c_coll.tag(g, g.collective_compute(
                "ReduceScatter", ALU.add, ins=[d["partial0_dram"]], outs=[d["rs0_dram"]],
                replica_groups=[list(range(NCORE))]), 1)
        self.rs0_mark = self.c_coll.mark()

    def s6_gate_outproj(self):
        b, d = self.b, self.dram
        c_mm, c_ev, c_dv, c_ld, c_sp = self.c_mm, self.c_ev, self.c_dv, self.c_ld, self.c_sp
        s6 = ExitStack()
        rs0t = self.sb("rs0t", [128, TC], stack=s6)
        xf6 = self.sb("x6frag", [128, TC], F32, stack=s6)
        t6 = self.sb("t6", [128, TC], F32, stack=s6)
        x1s = self.sb("x1s", [128, TC], stack=s6)
        x1s2 = self.sb("x1s2", [128, TC], stack=s6)
        for j in range(NT):
            @b.sync
            def _(sy, j=j):
                self.c_coll.wait(sy, mark=self.rs0_mark)
                c_dv.wait(sy)
                c_ld.tag(sy, sy.dma_start(out=rs0t[:], in_=d["rs0_dram"][128 * j:128 * (j + 1), :]), 16)
                c_ld.tag(sy, sy.dma_start(out=xf6[:], in_=d["x_t"][128 * j:128 * (j + 1), 3:]), 16)
            rj_mark = c_ld.mark()
            x1b = [x1s, x1s2][j % 2]

            @b.vector
            def _(v, j=j, x1b=x1b):
                c_ld.wait(v, mark=rj_mark)
                c_sp.wait(v)
                v.tensor_tensor(t6[:], rs0t[:], self.xn[j][:, 3:], ALU.add)
                v.drain()
                v.tensor_tensor(t6[:], t6[:], self.facc[j][:], ALU.add)
                v.drain()
                v.tensor_tensor(x1b[:], t6[:], xf6[:], ALU.add)
                v.drain()
                c_dv.tag(v.nop())

            @b.sync
            def _(sy, j=j, x1b=x1b):
                c_dv.wait(sy)
                c_sp.tag(sy, sy.dma_start(out=d["x1_dram"][128 * j:128 * (j + 1), :], in_=x1b[:]), 16)
        s6.close()

    def s7_ln1(self):
        b, d = self.b, self.dram
        self.ln2_stack = ExitStack()
        self.ps_stat = self.psum("ps_stat2", [1, 1024], stack=self.ln2_stack)
        self.ps_bc = self.psum("ps_bc2", [128, 1536], stack=self.ln2_stack)
        x1_mark = self.c_sp.mark()
        self.emit_ln(src=None, out_xn=self.xn, out_xr=None, width=TC,
                     rms_cols=None, rms_col_idx=None, src_f32=False,
                     src_dram=d["x1_dram"], dram_wait=(self.c_sp, x1_mark))
        if "xn1_dram" in self.dbg:
            @b.sync
            def _(sy):
                self.c_dv.wait(sy)
                for j in range(NT):
                    self.c_sp.tag(sy, sy.dma_start(out=d["xn1_dram"][128 * j:128 * (j + 1), :],
                                                   in_=self.xn[j][:, 0:TC]), 16)

    def s8_kvr(self):
        b, d = self.b, self.dram
        c_wst, c_mm, c_ev, c_dv, c_sp = self.c_wst, self.c_mm, self.c_ev, self.c_dv, self.c_sp
        self.ln2_stack.close()
        s8 = ExitStack()
        wsl8 = [self.sb(f"w8_{k}", [128, D], stack=s8) for k in range(NT)]
        kst2 = [self.sb(f"kst{i}", [128, TC], stack=s8) for i in range(NT)]
        ps_k = self.psum("ps_k", [128, 1024], F32, stack=s8)
        names = ["wkT", "wvT", "wrT"]
        for wi, wname in enumerate(names):
            @b.gpsimd
            def _(g, wi=wi, wname=wname):
                c_mm.wait(g)
                for k in range(NT):
                    c_wst.tag(g, g.dma_start(out=wsl8[k][:], in_=d[wname][128 * k:128 * (k + 1), :]), 16)
            w_mark = c_wst.mark()
            mm_g8 = []
            ev_g8 = []

            @b.tensor
            def _(te, wi=wi):
                c_wst.wait(te, mark=w_mark)
                c_dv.wait(te)
                ev_base = c_ev.n
                g = 0
                for j in range(NT):
                    for ci in range(4):
                        if g == 0:
                            c_ev.wait(te)
                        elif g >= 2:
                            te.wait_ge(c_ev.sem, ev_base + g - 1)
                        for k in range(NT):
                            ins = te.matmul(ps_k[:, 512 * (g % 2):512 * (g % 2) + 512],
                                            wsl8[k][:, 128 * j:128 * (j + 1)],
                                            self.xn[k][:, 512 * ci:512 * (ci + 1)],
                                            start=(k == 0), stop=(k == NT - 1))
                        c_mm.tag(ins)
                        mm_g8.append(c_mm.n)
                        g += 1

            @b.scalar
            def _(sc, wi=wi):
                sp0 = getattr(self, "_s8_sp", None)
                if sp0 is not None:
                    c_sp.wait(sc, mark=sp0)
                fn = ACTF.Sigmoid if wi == 2 else ACTF.Copy
                g = 0
                for j in range(NT):
                    for ci in range(4):
                        sc.wait_ge(c_mm.sem, mm_g8[g])
                        c_ev.tag(sc.activation(kst2[j][:, 512 * ci:512 * (ci + 1)],
                                               ps_k[:, 512 * (g % 2):512 * (g % 2) + 512], fn))
                        ev_g8.append(c_ev.n)
                        g += 1

            @b.sync
            def _(sy, wi=wi):
                g = 0
                for j in range(NT):
                    sy.wait_ge(c_ev.sem, ev_g8[4 * j + 3])
                    c_sp.tag(sy, sy.dma_start(out=d["kvr_dram"][D * wi + 128 * j:D * wi + 128 * (j + 1), :],
                                              in_=kst2[j][:]), 16)
            self._s8_sp = c_sp.mark()
        kvr_mark = c_sp.mark()
        s8.close()

        @b.gpsimd
        def _(g):
            c_sp.wait(g, mark=kvr_mark)
            self.c_coll.tag(g, g.collective_compute(
                "AllGather", ALU.bypass, ins=[d["kvr_dram"]], outs=[d["kvr_all"]],
                replica_groups=[list(range(NCORE))]), 1)
        self.ag_kvr_mark = self.c_coll.mark()


    def s9_rwkv(self):
        LV = getattr(self, 's9_level', 99)
        b, d = self.b, self.dram
        c_wst, c_mm, c_ev, c_dv, c_ld, c_sp = self.c_wst, self.c_mm, self.c_ev, self.c_dv, self.c_ld, self.c_sp
        s9 = ExitStack()
        kch = self.sb("kch", [128, 4096], stack=s9)
        vch = self.sb("vch", [128, 4096], stack=s9)
        rch = self.sb("rch", [128, 4096], stack=s9)
        kap = self.sb("kap", [128, 4096], stack=s9)
        wpow = self.sb("wpow", [128, 33], F32, stack=s9)
        wipow = self.sb("wipow", [128, 16], F32, stack=s9)
        p16 = self.sb("p16", [128, 128], stack=s9)
        p16i = self.sb("p16i", [128, 128], stack=s9)
        p16i_h = [self.sb(f"p16ih{h}", [128, 128], stack=s9) for h in range(2)]
        p16f_h = [self.sb(f"p16fh{h}", [128, 128], stack=s9) for h in range(2)]
        hmask = self.sb("hmask", [128, 2], F32, stack=s9)
        pmaskc = self.sb("pmaskc", [128, 4], F32, stack=s9)
        khTm = [self.sb(f"khTm{p}", [128, 128], stack=s9) for p in range(4)]
        r2h = [self.sb(f"r2h{h}", [128, 128], stack=s9) for h in range(2)]
        p16f = self.sb("p16f", [128, 128], stack=s9)
        p32 = self.sb("p32", [128, 128], stack=s9)
        p32r = self.sb("p32r", [128, 128], stack=s9)
        a1 = self.sb("a1", [128, 128], stack=s9)
        r2 = self.sb("r2", [128, 128], stack=s9)
        b1 = self.sb("b1", [128, 128], stack=s9)
        b2 = self.sb("b2", [128, 128], stack=s9)
        b1b = self.sb("b1b", [128, 128], stack=s9)
        b2b = self.sb("b2b", [128, 128], stack=s9)
        kh = self.sb("kh", [128, 128], stack=s9)
        khT = self.sb("khT", [128, 128], stack=s9)
        vT = self.sb("vT", [128, 128], stack=s9)
        msb = [self.sb(f"msb{h}", [128, 128], stack=s9) for h in range(2)]
        mtmp = self.sb("mtmp", [128, 128], stack=s9)
        shist = self.sb("shist", [128, 5 * 64], stack=s9)
        y9stg = self.sb("y9stg", [128, 4096], stack=s9)
        wo1slab = self.sb("wo1slab", [128, D], stack=s9)
        p1stg = self.sb("p1stg", [128, 512], stack=s9)
        ps_tr = self.psum("ps_tr", [128, 1024], BF16, stack=s9)
        ps_m = self.psum("ps_m", [128, 512], F32, stack=s9)
        ps_m2 = self.psum("ps_m2", [128, 512], F32, stack=s9)
        ps_kv = self.psum("ps_kv", [128, 512], F32, stack=s9)
        ps_y9 = self.psum("ps_y9", [128, 512], F32, stack=s9)
        ps_p1 = self.psum("ps_p1", [128, 1024], F32, stack=s9)

        @b.gpsimd
        def _(g):
            self.c_coll.wait(g, mark=self.ag_kvr_mark)
            c_mm.wait(g)
            c_dv.wait(g)
            c_ev.wait(g)
            kvr_a = d["kvr_all"].rearrange("(o q) t -> q o t", o=NCORE)
            for wi, nm in enumerate(["k_my", "v_my", "r_my"]):
                off = self.rk(g, 128, 1024 * wi)
                c_wst.tag(g, g.dma_start(out=d[nm].rearrange("n (o t) -> n o t", o=NCORE),
                                         in_=kvr_a[bass.ds(off, 128), :, :]), 16)
            off2 = self.rk(g, 128)
            c_wst.tag(g, g.dma_start(out=wo1slab[:], in_=d["woT"][bass.ds(off2, 128), :]), 16)
        my_mark = c_wst.mark()

        @b.vector
        def _(v):
            c_ld.wait(v)
            c_ev.wait(v)
            c_sp.wait(v)
            c_mm.wait(v)
            v.memset(wpow[:, 0:1], 1.0)
            v.drain()
            for j in range(32):
                v.tensor_scalar(wpow[:, j + 1:j + 2], wpow[:, j:j + 1], self.sm["tdec"][:, 0:1], None, ALU.mult)
                v.drain()
            v.reciprocal(wipow[:, 1:2], self.sm["tdec"][:, 0:1])
            v.memset(wipow[:, 0:1], 1.0)
            v.drain()
            for j in range(1, 15):
                v.tensor_tensor(wipow[:, j + 1:j + 2], wipow[:, j:j + 1], wipow[:, 1:2], ALU.mult)
                v.drain()
            from bass_rust import AP as _AP
            v.tensor_copy(p16[:], _AP(wpow[:].tensor, wpow[:].offset, [[33, 128], [0, 8], [1, 16]]))
            v.tensor_copy(p16i[:], _AP(wipow[:].tensor, wipow[:].offset, [[16, 128], [0, 8], [1, 16]]))
            v.tensor_copy(p16f[:], _AP(wpow[:].tensor, wpow[:].offset + 16, [[33, 128], [0, 8], [-1, 16]]))
            v.tensor_copy(p32[:], _AP(wpow[:].tensor, wpow[:].offset + 31, [[33, 128], [0, 4], [-1, 32]]))
            v.tensor_copy(p32r[:], _AP(wpow[:].tensor, wpow[:].offset + 1, [[33, 128], [0, 4], [1, 32]]))
            v.drain()
            v.tensor_scalar(hmask[:, 0:1], self.io_row[:, 0:1], 63.5, None, ALU.is_le)
            v.tensor_scalar(hmask[:, 1:2], self.io_row[:, 0:1], 63.5, None, ALU.is_gt if hasattr(ALU, 'is_gt') else ALU.is_ge)
            for pp in range(4):
                v.tensor_scalar(self.tmpc1[:, 0:1], self.io_row[:, 0:1], 32.0 * pp - 0.5, None, ALU.is_ge)
                v.drain()
                v.tensor_scalar(self.tmpc2[:, 0:1], self.io_row[:, 0:1], 32.0 * (pp + 1) - 0.5, None, ALU.is_lt)
                v.drain()
                v.tensor_tensor(pmaskc[:, pp:pp + 1], self.tmpc1[:, 0:1], self.tmpc2[:, 0:1], ALU.mult)
                v.drain()
            for h in range(2):
                v.tensor_tensor(p16i_h[h][:], p16i[:], hmask[:, h:h + 1].broadcast_to([128, 128]), ALU.mult)
                v.tensor_tensor(p16f_h[h][:], p16f[:], hmask[:, h:h + 1].broadcast_to([128, 128]), ALU.mult)
            v.drain()
            c_dv.tag(v.nop())

        for cc in range(4):
            @b.sync
            def _(sy, cc=cc):
                c_wst.wait(sy, mark=my_mark)
                c_dv.wait(sy)
                c_ev.wait(sy)
                c_mm.wait(sy)
                c_ld.tag(sy, sy.dma_start(out=kch[:], in_=d["k_my"][:, 4096 * cc:4096 * (cc + 1)]), 16)
                c_ld.tag(sy, sy.dma_start(out=vch[:], in_=d["v_my"][:, 4096 * cc:4096 * (cc + 1)]), 16)
                c_ld.tag(sy, sy.dma_start(out=rch[:], in_=d["r_my"][:, 4096 * cc:4096 * (cc + 1)]), 16)
            ch_mark = c_ld.mark()

            @b.vector
            def _(v, cc=cc):
                c_ld.wait(v, mark=ch_mark)
                v.tensor_scalar(kap[:], kch[:], self.sm["tfirst"][:, 0:1], None, ALU.mult)
                v.drain()
                if cc % 2 == 0:
                    v.memset(shist[:, 0:64], 0.0)
                    v.drain()
                c_dv.tag(v.nop())

            for tt in range(32):
                sl = slice(128 * tt, 128 * (tt + 1))

                @b.vector
                def _(v, cc=cc, tt=tt, sl=sl):
                    c_mm.wait(v)
                    c_ev.wait(v)
                    if "kh9dbg" in self.dbg:
                        c_sp.wait(v)
                    v.tensor_tensor(a1[:], rch[:, sl], p16[:], ALU.mult)
                    v.tensor_tensor(r2[:], rch[:, sl], p32r[:], ALU.mult)
                    v.tensor_tensor(b1[:], kap[:, sl], p16i_h[0][:], ALU.mult)
                    v.tensor_tensor(b1b[:], kap[:, sl], p16i_h[1][:], ALU.mult)
                    v.tensor_tensor(b2[:], kap[:, sl], p16f_h[0][:], ALU.mult)
                    v.tensor_tensor(b2b[:], kap[:, sl], p16f_h[1][:], ALU.mult)
                    v.tensor_tensor(kh[:], kap[:, sl], p32[:], ALU.mult)
                    v.drain()
                    for h in range(2):
                        v.tensor_tensor(r2h[h][:], r2[:], hmask[:, h:h + 1].broadcast_to([128, 128]), ALU.mult)
                    v.drain()
                    c_dv.tag(v.nop())

                if LV < 2:
                    continue

                @b.tensor
                def _(te, cc=cc, tt=tt, sl=sl):
                    c_dv.wait(te)
                    c_ev.wait(te)
                    te.transpose(ps_tr[:, 0:128], kh[:], self.ident[:])
                    c_mm.tag(te.transpose(ps_tr[:, 128:256], vch[:, sl], self.ident[:]))

                @b.scalar
                def _(sc, cc=cc, tt=tt):
                    c_mm.wait(sc)
                    sc.copy(out=khT[:], in_=ps_tr[:, 0:128])
                    c_ev.tag(sc.copy(out=vT[:], in_=ps_tr[:, 128:256]))

                if cc == 0 and tt == 0 and "kh9dbg" in self.dbg:
                    @b.sync
                    def _(sy):
                        c_ev.wait(sy)
                        c_sp.tag(sy, sy.dma_start(out=d["r9dbg"], in_=rch[:, 0:512]), 16)
                        c_sp.tag(sy, sy.dma_start(out=d["k9dbg"], in_=kap[:, 0:512]), 16)
                        c_sp.tag(sy, sy.dma_start(out=d["kh9dbg"], in_=kh[:]), 16)
                        c_sp.tag(sy, sy.dma_start(out=d["khT9dbg"], in_=khT[:]), 16)
                        c_sp.tag(sy, sy.dma_start(out=d["vT9dbg"], in_=vT[:]), 16)

                if LV < 3:
                    continue

                @b.vector
                def _(v, cc=cc, tt=tt):
                    c_ev.wait(v)
                    for p in range(4):
                        v.tensor_tensor(khTm[p][:], khT[:], pmaskc[:, p:p + 1].broadcast_to([128, 128]), ALU.mult)
                    v.drain()
                    c_dv.tag(v.nop())

                @b.tensor
                def _(te, cc=cc, tt=tt):
                    c_ev.wait(te)
                    c_dv.wait(te)
                    te.matmul(ps_m[:, 0:128], b1[:], a1[:], start=True, stop=True, skip_group_check=True)
                    te.matmul(ps_m[:, 128:256], b1b[:], a1[:], start=True, stop=True, skip_group_check=True)
                    te.matmul(ps_m2[:, 0:128], b2[:], a1[:], start=True, stop=True, skip_group_check=True)
                    te.matmul(ps_m2[:, 128:256], b2b[:], a1[:], start=True, stop=True, skip_group_check=True)
                    for p in range(4):
                        ins = te.matmul(ps_kv[:, 128 * p:128 * (p + 1)], khTm[p][:], vT[:],
                                        start=True, stop=True, skip_group_check=True)
                    c_mm.tag(ins)

                if LV < 4:
                    continue

                @b.vector
                def _(v, cc=cc, tt=tt):
                    c_mm.wait(v)
                    for h in range(2):
                        v.tensor_tensor(msb[h][:], ps_m[:, 128 * h:128 * (h + 1)], self.mk_diag[:], ALU.mult)
                        v.drain()
                        v.tensor_tensor(mtmp[:], ps_m2[:, 128 * h:128 * (h + 1)], self.mk_off[:], ALU.mult)
                        v.drain()
                        v.tensor_tensor(msb[h][:], msb[h][:], mtmp[:], ALU.add)
                        v.drain()
                    for p in range(4):
                        v.scalar_tensor_tensor(shist[0:64, 64 * (p + 1):64 * (p + 2)],
                                               shist[0:64, 64 * p:64 * (p + 1)],
                                               wpow[0:64, 32:33],
                                               ps_kv[0:64, 128 * p:128 * p + 64], ALU.mult, ALU.add)
                        v.scalar_tensor_tensor(shist[64:128, 64 * (p + 1):64 * (p + 2)],
                                               shist[64:128, 64 * p:64 * (p + 1)],
                                               wpow[64:128, 32:33],
                                               ps_kv[64:128, 128 * p + 64:128 * (p + 1)], ALU.mult, ALU.add)
                        v.drain()
                    c_dv.tag(v.nop())

                if LV < 5:
                    continue

                @b.tensor
                def _(te, cc=cc, tt=tt):
                    c_dv.wait(te)
                    c_ev.wait(te)
                    for h in range(2):
                        te.matmul(ps_y9[64 * h:64 * (h + 1), 0:128], vT[:, 64 * h:64 * (h + 1)],
                                  msb[h][:], start=True, stop=False,
                                  tile_position=(0, 64 * h), skip_group_check=True)
                    for h in range(2):
                        for p in range(4):
                            ins = te.matmul(ps_y9[64 * h:64 * (h + 1), 32 * p:32 * (p + 1)],
                                            shist[:, 64 * p:64 * (p + 1)],
                                            r2h[h][:, 32 * p:32 * (p + 1)],
                                            start=False, stop=(h == 1 and p == 3),
                                            tile_position=(0, 64 * h), skip_group_check=True)
                    c_mm.tag(ins)

                @b.scalar
                def _(sc, cc=cc, tt=tt, sl=sl):
                    c_mm.wait(sc)
                    c_ev.tag(sc.copy(out=y9stg[:, sl], in_=ps_y9[:, 0:128]))


                @b.vector
                def _(v, cc=cc, tt=tt):
                    c_ev.wait(v)
                    v.tensor_copy(shist[:, 0:64], shist[:, 256:320])
                    v.drain()
                    c_dv.tag(v.nop())

            for dout in range(NT):
                for ci in range(8):
                    @b.tensor
                    def _(te, cc=cc, dout=dout, ci=ci):
                        c_ev.wait(te)
                        c_dv.wait(te)
                        c_mm.tag(te.matmul(ps_p1[:, 512 * (ci % 2):512 * (ci % 2) + 512],
                                           wo1slab[:, 128 * dout:128 * (dout + 1)],
                                           y9stg[:, 512 * ci:512 * (ci + 1)],
                                           start=True, stop=True))

                    @b.scalar
                    def _(sc, cc=cc, dout=dout, ci=ci):
                        c_mm.wait(sc)
                        c_sp.wait(sc)
                        c_ev.tag(sc.activation(p1stg[:], ps_p1[:, 512 * (ci % 2):512 * (ci % 2) + 512], ACTF.Copy))

                    @b.sync
                    def _(sy, cc=cc, dout=dout, ci=ci):
                        c_ev.wait(sy)
                        gt = 4096 * cc + 512 * ci
                        owner = gt // TC
                        lt = gt - TC * owner
                        c_sp.tag(sy, sy.dma_start(
                            out=d["partial1_dram"][D * owner + 128 * dout:D * owner + 128 * (dout + 1), lt:lt + 512],
                            in_=p1stg[:]), 16)
        p1_mark = c_sp.mark()
        s9.close()

        @b.gpsimd
        def _(g):
            c_sp.wait(g, mark=p1_mark)
            self.c_coll.tag(g, g.collective_compute(
                "ReduceScatter", ALU.add, ins=[d["partial1_dram"]], outs=[d["rs1_dram"]],
                replica_groups=[list(range(NCORE))]), 1)
        self.rs1_mark = self.c_coll.mark()

    def s10_wo(self):
        pass

    def s11_l1_ffn(self):
        b, d = self.b, self.dram
        c_wst, c_mm, c_ev, c_dv, c_ld, c_sp = self.c_wst, self.c_mm, self.c_ev, self.c_dv, self.c_ld, self.c_sp
        from bass_rust import AP as _AP
        s11 = ExitStack()
        r1t = self.sb("r1t", [128, 64], stack=s11)
        lg1 = self.sb("lg1", [8, TC], F32, stack=s11)
        lgt1 = self.sb("lgt1", [128, 128], F32, stack=s11)
        mx1 = self.sb("mx1", [128, 16], F32, stack=s11)
        mx2 = self.sb("mx2", [128, 16], F32, stack=s11)
        g1v = self.sb("g1v", [128, 16], F32, stack=s11)
        g2v = self.sb("g2v", [128, 16], F32, stack=s11)
        oh1 = self.sb("oh1", [128, 128], F32, stack=s11)
        oh2 = self.sb("oh2", [128, 128], F32, stack=s11)
        l2t = self.sb("l2t", [128, 128], F32, stack=s11)
        poscum = [self.sb(f"poscum{k}", [128, 128], F32, stack=s11) for k in range(2)]
        carry1 = self.sb("carry1", [1, 2 * NE], F32, stack=s11)
        cntsb = self.sb("cntsb", [NCORE, 2 * NE], F32, stack=s11)
        prefsb = self.sb("prefsb", [128, 2 * NE], F32, stack=s11)
        gfin = self.sb("gfin", [128, 128], F32, stack=s11)
        gstg = self.sb("gstg", [8, TC], stack=s11)
        grep1 = self.sb("grep1", [128, TC], stack=s11)
        utrif = self.sb("utrif", [128, 128], F32, stack=s11)
        ps_l1 = self.psum("ps_l1", [8, 512], F32, stack=s11)
        ps_t1 = self.psum("ps_t1", [128, 512], F32, stack=s11)
        ps_c1 = self.psum("ps_c1", [128, 512], F32, stack=s11)

        @b.gpsimd
        def _(g):
            c_mm.wait(g)
            c_dv.wait(g)
            for k in range(NT):
                c_wst.tag(g, g.dma_start(out=r1t[:, 8 * k:8 * k + 8], in_=d["r1_wT"][128 * k:128 * (k + 1), :]), 16)
        r1_mark = c_wst.mark()

        @b.vector
        def _(v):
            self.c_gp.wait(v)
            v.tensor_copy(utrif[:], self.utri[:])
            v.drain()
            c_dv.tag(v.nop())

        for ci in range(4):
            @b.tensor
            def _(te, ci=ci):
                c_wst.wait(te, mark=r1_mark)
                c_ev.wait(te)
                c_dv.wait(te)
                for k in range(NT):
                    ins = te.matmul(ps_l1[:, 0:512], r1t[:, 8 * k:8 * k + 8],
                                    self.xn[k][:, 512 * ci:512 * (ci + 1)],
                                    start=(k == 0), stop=(k == NT - 1))
                c_mm.tag(ins)

            @b.scalar
            def _(sc, ci=ci):
                c_mm.wait(sc)
                c_ev.tag(sc.copy(out=lg1[:, 512 * ci:512 * (ci + 1)], in_=ps_l1[:, 0:512]))

        for i in range(16):
            @b.tensor
            def _(te, i=i):
                c_ev.wait(te)
                c_dv.wait(te)
                c_mm.tag(te.transpose(ps_t1[0:128, 0:8], lg1[0:8, 128 * i:128 * (i + 1)], self.identf[0:8, 0:8]))

            @b.vector
            def _(v, i=i):
                c_mm.wait(v)
                c_dv.tag(v.tensor_copy(lgt1[:, 8 * i:8 * i + 8], ps_t1[:, 0:8]))

        @b.vector
        def _(v):
            c_dv.wait(v)
            lgt3 = _AP(lgt1[:].tensor, lgt1[:].offset, [[128, 128], [8, 16], [1, 8]])
            mx1b = _AP(mx1[:].tensor, mx1[:].offset, [[16, 128], [1, 16], [0, 8]])
            mx2b = _AP(mx2[:].tensor, mx2[:].offset, [[16, 128], [1, 16], [0, 8]])
            v.tensor_reduce(mx1[:], lgt3, axis=AX.X, op=ALU.max)
            v.drain()
            v.tensor_tensor(oh1[:], lgt3, mx1b, ALU.is_ge)
            v.drain()
            v.tensor_scalar(l2t[:], oh1[:], -1e30, None, ALU.mult)
            v.drain()
            v.tensor_tensor(l2t[:], lgt1[:], l2t[:], ALU.add)
            v.drain()
            l2t3 = _AP(l2t[:].tensor, l2t[:].offset, [[128, 128], [8, 16], [1, 8]])
            v.tensor_reduce(mx2[:], l2t3, axis=AX.X, op=ALU.max)
            v.drain()
            v.tensor_tensor(oh2[:], l2t3, mx2b, ALU.is_ge)
            v.drain()
            v.tensor_tensor(g2v[:], mx2[:], mx1[:], ALU.subtract)
            v.drain()
            c_dv.tag(v.nop())

        @b.scalar
        def _(sc):
            c_dv.wait(sc)
            c_ev.tag(sc.activation(g2v[:], g2v[:], ACTF.Exp))

        @b.vector
        def _(v):
            c_ev.wait(v)
            v.tensor_scalar(g1v[:], g2v[:], 1.0, None, ALU.add)
            v.drain()
            v.reciprocal(g1v[:], g1v[:])
            v.drain()
            v.tensor_tensor(g2v[:], g2v[:], g1v[:], ALU.mult)
            v.drain()
            c_dv.tag(v.nop())

        @b.vector
        def _(v):
            c_dv.wait(v)
            v.memset(carry1[:], 0.0)
            v.drain()
            c_dv.tag(v.nop())

        for i in range(16):
            for k in range(2):
                oh = [oh1, oh2][k]

                @b.tensor
                def _(te, i=i, k=k, oh=oh):
                    c_dv.wait(te)
                    te.matmul(ps_c1[:, 8 * k:8 * (k + 1)], utrif[:], oh[:, 8 * i:8 * i + 8],
                              start=True, stop=False, skip_group_check=True)
                    c_mm.tag(te.matmul(ps_c1[:, 8 * k:8 * (k + 1)], self.onesf_row[:], carry1[0:1, 8 * k:8 * (k + 1)],
                                       start=False, stop=True, skip_group_check=True))

                @b.vector
                def _(v, i=i, k=k):
                    c_mm.wait(v)
                    v.tensor_copy(poscum[k][:, 8 * i:8 * i + 8], ps_c1[:, 8 * k:8 * (k + 1)])
                    v.drain()
                    c_dv.tag(v.nop())

                @b.tensor
                def _(te, i=i, k=k):
                    c_dv.wait(te)
                    c_mm.tag(te.matmul(ps_c1[0:1, 16 + 8 * k:24 + 8 * k], self.sel127[:],
                                       poscum[k][:, 8 * i:8 * i + 8], start=True, stop=True,
                                       skip_group_check=True))

                @b.vector
                def _(v, i=i, k=k):
                    c_mm.wait(v)
                    v.tensor_copy(carry1[0:1, 8 * k:8 * (k + 1)], ps_c1[0:1, 16 + 8 * k:24 + 8 * k])
                    v.drain()
                    c_dv.tag(v.nop())

        @b.sync
        def _(sy):
            c_dv.wait(sy)
            c_sp.tag(sy, sy.dma_start(out=d["cnt_dram"], in_=carry1[0:1, :]), 16)
        cnt_mark = c_sp.mark()

        @b.gpsimd
        def _(g):
            c_sp.wait(g, mark=cnt_mark)
            self.c_coll.tag(g, g.collective_compute(
                "AllGather", ALU.bypass, ins=[d["cnt_dram"]], outs=[d["cnt_all"]],
                replica_groups=[list(range(NCORE))]), 1)
        cnt_ag = self.c_coll.mark()

        @b.sync
        def _(sy):
            self.c_coll.wait(sy, mark=cnt_ag)
            c_ld.tag(sy, sy.dma_start(out=cntsb[:], in_=d["cnt_all"]), 16)
        cnt_ld = c_ld.mark()

        @b.tensor
        def _(te):
            c_ld.wait(te, mark=cnt_ld)
            c_dv.wait(te)
            c_mm.tag(te.matmul(ps_c1[0:1, 0:16], self.sm["pmask"][:], cntsb[:], start=True, stop=True, skip_group_check=True))

        @b.vector
        def _(v):
            c_mm.wait(v)
            v.tensor_copy(prefsb[0:1, :], ps_c1[0:1, 0:16])
            v.drain()
            c_dv.tag(v.nop())

        @b.tensor
        def _(te):
            c_dv.wait(te)
            c_mm.tag(te.matmul(ps_c1[:, 0:16], self.onesf_row[:], prefsb[0:1, :], start=True, stop=True, skip_group_check=True))

        @b.vector
        def _(v):
            c_mm.wait(v)
            v.tensor_copy(prefsb[:], ps_c1[:, 0:16])
            v.drain()
            for k in range(2):
                pref3 = _AP(prefsb[:].tensor, prefsb[:].offset + 8 * k, [[2 * NE, 128], [0, 16], [1, 8]])
                gk3 = _AP([g1v, g2v][k][:].tensor, [g1v, g2v][k][:].offset, [[16, 128], [1, 16], [0, 8]])
                dst = [oh1, oh2][k]
                v.tensor_tensor(poscum[k][:], poscum[k][:], pref3, ALU.add)
                v.drain()
                v.tensor_scalar(poscum[k][:], poscum[k][:], CAP + 0.5, None, ALU.is_le)
                v.drain()
                v.tensor_tensor(dst[:], dst[:], poscum[k][:], ALU.mult)
                v.drain()
                v.tensor_tensor(dst[:], dst[:], gk3, ALU.mult)
                v.drain()
            v.tensor_tensor(gfin[:], oh1[:], oh2[:], ALU.add)
            v.drain()
            c_dv.tag(v.nop())

        for i in range(16):
            @b.tensor
            def _(te, i=i):
                c_dv.wait(te)
                c_ev.wait(te)
                c_mm.tag(te.transpose(ps_t1[0:8, 0:128], gfin[:, 8 * i:8 * i + 8], self.identf[:]))

            @b.scalar
            def _(sc, i=i):
                c_mm.wait(sc)
                c_ev.tag(sc.copy(out=gstg[0:8, 128 * i:128 * (i + 1)], in_=ps_t1[0:8, 0:128]))

        @b.sync
        def _(sy):
            c_ev.wait(sy)
            c_sp.tag(sy, sy.dma_start(out=d["g1_dram"], in_=gstg[0:8, :]), 16)
        g1_mark = c_sp.mark()
        s11.close()

        s11b = ExitStack()
        w1sl = [self.sb(f"w1s{k}", [128, 2048], stack=s11b) for k in range(NT)]
        w2sl = [self.sb(f"w2s{k}", [128, D], stack=s11b) for k in range(16)]
        hsb = [self.sb(f"hsb{f}", [128, 512], stack=s11b) for f in range(16)]
        grep_b = self.sb("grep1b", [128, TC], stack=s11b)
        tmp11 = self.sb("tmp11", [128, 512], stack=s11b)
        ps_h = self.psum("ps_h", [128, 1024], F32, stack=s11b)
        ps_y1 = self.psum("ps_y1", [128, 2048], F32, stack=s11b)

        gate_marks = {}
        for e in range(NE):
            for fh in range(2):
                @b.gpsimd
                def _(g, e=e, fh=fh):
                    c_mm.wait(g)
                    if e == 0 and fh == 0:
                        c_sp.wait(g, mark=g1_mark)
                        c_dv.wait(g)
                        c_ev.wait(g)
                    for k in range(NT):
                        c_wst.tag(g, g.dma_start(out=w1sl[k][:],
                                                 in_=d["w1"][e, 128 * k:128 * (k + 1), 2048 * fh:2048 * (fh + 1)]), 16)
                    for k in range(16):
                        c_wst.tag(g, g.dma_start(out=w2sl[k][:],
                                                 in_=d["w2"][e, 2048 * fh + 128 * k:2048 * fh + 128 * (k + 1), 0:D]), 16)
                w_mark = c_wst.mark()

                if fh == 0:
                    @b.sync
                    def _(sy, e=e):
                        c_sp.wait(sy, mark=g1_mark)
                        if e > 0:
                            c_dv.wait(sy, mark=gate_marks.get(e - 1))
                        c_ld.tag(sy, sy.dma_start(out=grep_b[:],
                                                  in_=d["g1_dram"][e:e + 1, :].broadcast_to([128, TC])), 16)
                    self._s11_gemark = c_ld.mark()

                for tc4 in range(4):
                    prev_y_mark = getattr(self, "_s11_prev_y", 0)
                    mm_h = []
                    ev_h = []

                    @b.tensor
                    def _(te, e=e, fh=fh, tc4=tc4, prev_y_mark=prev_y_mark):
                        c_wst.wait(te, mark=w_mark)
                        ev_base = c_ev.n
                        for f in range(16):
                            if f == 0:
                                c_ev.wait(te)
                            elif f >= 2:
                                te.wait_ge(c_ev.sem, ev_base + f - 1)
                            for k in range(NT):
                                ins = te.matmul(ps_h[:, 512 * (f % 2):512 * (f % 2) + 512],
                                                w1sl[k][:, 128 * f:128 * (f + 1)],
                                                self.xn[k][:, 512 * tc4:512 * (tc4 + 1)],
                                                start=(k == 0), stop=(k == NT - 1))
                            c_mm.tag(ins)
                            mm_h.append(c_mm.n)

                    @b.scalar
                    def _(sc, e=e, fh=fh, tc4=tc4, prev_y_mark=prev_y_mark):
                        sc.wait_ge(c_mm.sem, prev_y_mark)
                        for f in range(16):
                            sc.wait_ge(c_mm.sem, mm_h[f])
                            c_ev.tag(sc.activation(hsb[f][:], ps_h[:, 512 * (f % 2):512 * (f % 2) + 512], ACTF.Gelu))
                            ev_h.append(c_ev.n)

                    @b.vector
                    def _(v, e=e, fh=fh, tc4=tc4):
                        c_ld.wait(v, mark=self._s11_gemark)
                        for f in range(16):
                            v.wait_ge(c_ev.sem, ev_h[f])
                            v.tensor_tensor(hsb[f][:], hsb[f][:], grep_b[:, 512 * tc4:512 * (tc4 + 1)], ALU.mult)
                        v.drain()
                        c_dv.tag(v.nop())

                    dv_base = c_dv.n
                    mm_y = []

                    @b.tensor
                    def _(te, e=e, fh=fh, tc4=tc4, dv_base=dv_base):
                        c_dv.wait(te)
                        j = 0
                        for dhalf in range(2):
                            for d4 in range(4):
                                dout = 4 * dhalf + d4
                                if dhalf == 1:
                                    te.wait_ge(c_dv.sem, dv_base + d4 + 1)
                                for k in range(16):
                                    ins = te.matmul(ps_y1[:, 512 * d4:512 * (d4 + 1)],
                                                    w2sl[k][:, 128 * dout:128 * (dout + 1)],
                                                    hsb[k][:], start=(k == 0), stop=(k == 15))
                                c_mm.tag(ins)
                                mm_y.append((dout, d4, c_mm.n))
                                j += 1

                    @b.vector
                    def _(v, e=e, fh=fh, tc4=tc4):
                        for dout, d4, mmv in mm_y:
                            v.wait_ge(c_mm.sem, mmv)
                            if e == 0 and fh == 0:
                                c_dv.tag(v.tensor_copy(self.facc[dout][:, 512 * tc4:512 * (tc4 + 1)],
                                                       ps_y1[:, 512 * d4:512 * (d4 + 1)]))
                            else:
                                c_dv.tag(v.tensor_tensor(self.facc[dout][:, 512 * tc4:512 * (tc4 + 1)],
                                                         self.facc[dout][:, 512 * tc4:512 * (tc4 + 1)],
                                                         ps_y1[:, 512 * d4:512 * (d4 + 1)], ALU.add))
                        v.drain()
                    self._s11_prev_y = c_mm.n
                gate_marks[e] = c_dv.n
        s11b.close()


    def s12_final_sum(self):
        b, d = self.b, self.dram
        c_mm, c_ev, c_dv, c_ld, c_sp = self.c_mm, self.c_ev, self.c_dv, self.c_ld, self.c_sp
        s12 = ExitStack()
        rs1t = self.sb("rs1t", [128, TC], stack=s12)
        x1ld = self.sb("x1ld", [128, TC], stack=s12)
        outf = [self.sb(f"outf{j}", [128, TC], F32, stack=s12) for j in range(NT)]
        outT = self.sb("outT", [128, D], F32, stack=s12)
        outT2 = self.sb("outT2", [128, D], F32, stack=s12)
        ps12 = self.psum("ps12", [128, 512], F32, stack=s12)

        for j in range(NT):
            @b.sync
            def _(sy, j=j):
                self.c_coll.wait(sy, mark=self.rs1_mark)
                c_dv.wait(sy)
                c_ld.tag(sy, sy.dma_start(out=rs1t[:], in_=d["rs1_dram"][128 * j:128 * (j + 1), :]), 16)
                c_ld.tag(sy, sy.dma_start(out=x1ld[:], in_=d["x1_dram"][128 * j:128 * (j + 1), :]), 16)
            r_mark = c_ld.mark()

            @b.vector
            def _(v, j=j):
                c_ld.wait(v, mark=r_mark)
                c_dv.wait(v)
                v.tensor_tensor(outf[j][:], rs1t[:], x1ld[:], ALU.add)
                v.drain()
                v.tensor_tensor(outf[j][:], outf[j][:], self.facc[j][:], ALU.add)
                v.drain()
                c_dv.tag(v.nop())

        for tg in range(16):
            ot = [outT, outT2][tg % 2]
            for j in range(NT):
                @b.tensor
                def _(te, tg=tg, j=j):
                    c_dv.wait(te)
                    c_ev.wait(te)
                    c_mm.tag(te.transpose(ps12[:, 128 * (j % 2):128 * (j % 2) + 128],
                                          outf[j][:, 128 * tg:128 * (tg + 1)], self.identf[:]))

                @b.scalar
                def _(sc, tg=tg, j=j, ot=ot):
                    c_mm.wait(sc)
                    if j == 0:
                        self.c_st.wait(sc)
                    c_ev.tag(sc.copy(out=ot[:, 128 * j:128 * (j + 1)], in_=ps12[:, 128 * (j % 2):128 * (j % 2) + 128]))

            @b.sync
            def _(sy, tg=tg, ot=ot):
                c_ev.wait(sy)
                self.c_st.tag(sy, sy.dma_start(out=d["out"][128 * tg:128 * (tg + 1), :], in_=ot[:]), 16)
        s12.close()

    def s_final(self):
        b, d = self.b, self.dram
        if self.stop_after is not None:
            zout = self.sb("zout", [128, D], F32)

            @b.vector
            def _(v):
                self.c_mm.wait(v)
                self.c_ev.wait(v)
                self.c_dv.tag(v.memset(zout[:], 0.0))

            @b.sync
            def _(sy):
                self.c_dv.wait(sy)
                for i in range(TC // 128):
                    self.c_st.tag(sy, sy.dma_start(out=d["out"][128 * i:128 * (i + 1), :], in_=zout[:]), 16)

        @b.sync
        def _(sy):
            self.c_dv.wait(sy)
            self.c_ev.wait(sy)
            self.c_mm.wait(sy)
            self.c_sp.wait(sy)
            self.c_ldb.wait(sy)
            self.c_ld.wait(sy)
            self.c_coll.wait(sy)
            self.c_st.wait(sy)


def build(dbg=(), stop_after=None):
    mk = MK(dbg=dbg, stop_after=stop_after)
    nc = mk.build()
    return nc, mk


def stage_inputs(ins):
    x = np.ascontiguousarray(ins["L0_attn_x"].reshape(TFULL, D))
    maps = []
    com = {
        "w_inproj": ins["L0_attn_in_proj_w"],
        "rms_w": ins["L0_attn_rms_w"].reshape(D, 1),
        "conv_w": ins["L0_attn_conv_w"],
        "conv_b": ins["L0_attn_conv_b"].reshape(DI, 1),
        "bcd_w": ins["L0_attn_bcd_w"],
        "dup_w": ins["L0_attn_dup_w"],
        "dup_b": ins["L0_attn_dup_b"].reshape(DI, 1),
        "lma": ins["L0_attn_log_minus_A"],
        "d_param": ins["L0_attn_D_param"].reshape(DI, 1),
        "w_outproj": ins["L0_attn_out_proj_w"],
        "r0_wT": np.ascontiguousarray(ins["L0_ffn_router_w"].T),
        "r0_b": ins["L0_ffn_router_b"].reshape(1, NE),
        "exp0_wT": np.ascontiguousarray(ins["L0_ffn_expert_w"].transpose(0, 2, 1)),
        "exp0_b": np.ascontiguousarray(ins["L0_ffn_expert_b"].reshape(NE, D, 1)),
        "wkT": np.ascontiguousarray(ins["L1_attn_Wk"].T),
        "wvT": np.ascontiguousarray(ins["L1_attn_Wv"].T),
        "wrT": np.ascontiguousarray(ins["L1_attn_Wr"].T),
        "woT": np.ascontiguousarray(ins["L1_attn_Wo"].T),
        "tdec": ins["L1_attn_time_decay"].reshape(D, 1),
        "tfirst": ins["L1_attn_time_first"].reshape(D, 1),
        "r1_wT": np.ascontiguousarray(ins["L1_ffn_router_w"].T),
        "w1": ins["L1_ffn_w1"],
        "w2": ins["L1_ffn_w2"],
    }
    com = {k: np.ascontiguousarray(v, dtype=np.float32) for k, v in com.items()}
    for c in range(NCORE):
        lo = c * TC
        xt = np.zeros((D, TC + 3), np.float32)
        xt[:, 3:] = x[lo:lo + TC].T
        if lo % 8192 != 0:
            xt[:, 0:3] = x[lo - 3:lo].T
        pm = np.zeros((NE, 1), np.float32)
        pm[:c] = 1.0
        m = dict(com)
        m["x_t"] = xt
        m["pmask"] = pm
        maps.append(m)
    return maps


_CACHED = {}


def kernel(**inputs):
    from concourse.bass_utils import run_bass_kernel_spmd
    if "nc" not in _CACHED:
        nc, _ = build(dbg=())
        _CACHED["nc"] = nc
    nc = _CACHED["nc"]
    maps = stage_inputs(inputs)
    res = run_bass_kernel_spmd(nc, maps, core_ids=list(range(NCORE)))
    out = np.concatenate([r["out"] for r in res.results], axis=0)
    return np.ascontiguousarray(out.reshape(2, 8192, D)).astype(np.float32)
